# version 11
# baseline (speedup 1.0000x reference)
import math
import numpy as np
import concourse.bass as bass
import concourse.mybir as mybir

F32 = mybir.dt.float32
BF16 = mybir.dt.bfloat16
FP16 = mybir.dt.float16
AF = mybir.ActivationFunctionType
ALU = mybir.AluOpType
AX = mybir.AxisListType

D = 1024
NCH = 8
H = 16
HD = 64
EPS_LN = 1e-5
EPS_RMS = 1e-6


def get_slopes(n):
    def pow2(n):
        start = 2 ** (-2 ** (-(math.log2(n) - 3)))
        return [start * start ** i for i in range(n)]
    if math.log2(n).is_integer():
        return pow2(n)
    c = 2 ** math.floor(math.log2(n))
    return pow2(c) + get_slopes(2 * c)[0::2][:n - c]


class Buf:
    __slots__ = ("name", "w", "r")

    def __init__(self, name):
        self.name = name
        self.w = None
        self.r = {}


class Q:

    def __init__(self, nc, name, nslots=12):
        self.slots = []
        for i in range(nslots):
            s = nc.alloc_semaphore("%s%d" % (name, i))
            _SemById[s.num] = s
            self.slots.append([s, 0])
        self.i = 0

    def take(self):
        slot = self.slots[self.i]
        self.i = (self.i + 1) % len(self.slots)
        return slot


class Eng:

    def __init__(self, ctx, name, ename, sem, self_ordered):
        self.ctx = ctx
        self.name = name
        self.ename = ename
        self.sem = sem
        self.n = 0
        self.self_ordered = self_ordered
        self.known = {}
        self.prog = []
        self._waits = []

    def _dep(self, ev):
        if ev is None:
            return
        sem, val = ev
        k = sem.num
        if sem is self.sem and self.self_ordered:
            return
        if self.known.get(k, 0) >= val:
            return
        self._waits.append((sem, val))
        snap = self.ctx.snapshots.get((k, val))
        if snap:
            for k2, v2 in snap.items():
                if self.known.get(k2, 0) < v2:
                    self.known[k2] = v2
        if self.known.get(k, 0) < val:
            self.known[k] = val

    def _pre(self, reads, writes):
        self._waits = []
        for b in reads:
            self._dep(b.w)
        for b in writes:
            for k, v in b.r.items():
                self._dep((_SemById[k], v))
            self._dep(b.w)

    def _snapshot(self, ev):
        snap = dict(self.known)
        if self.self_ordered:
            snap[self.sem.num] = self.n
        k, v = ev[0].num, ev[1]
        if snap.get(k, 0) < v:
            snap[k] = v
        self.ctx.snapshots[(k, v)] = snap

    def _post(self, ev, reads, writes):
        for b in reads:
            k = ev[0].num
            if b.r.get(k, 0) < ev[1]:
                b.r[k] = ev[1]
        for b in writes:
            b.w = ev
            b.r = {}

    def emit(self, fn, reads=(), writes=(), q=None):
        return self.emit_multi([fn], reads, writes, q)

    def emit_multi(self, fns, reads=(), writes=(), q=None):
        self._pre(reads, writes)
        if q is not None:
            slot = q.take()
            self._dep((slot[0], slot[1]))
            slot[1] += 16
            ev = (slot[0], slot[1])
            self.prog.append((self._waits, fns, slot[0], 16))
        else:
            self.n += 1
            ev = (self.sem, self.n)
            self.prog.append((self._waits, fns, self.sem, 1))
        self._waits = []
        self._snapshot(ev)
        self._post(ev, reads, writes)
        return ev

    def render(self, e):
        for waits, fns, incsem, incval in self.prog:
            for (s, v) in waits:
                e.wait_ge(s, v)
            inst = None
            for fn in fns:
                inst = fn(e)
            if inst is not None:
                inst.then_inc(incsem, incval)


_SemById = {}


class Ctx:
    def __init__(self, nc):
        self.nc = nc
        self.snapshots = {}
        self.sems = {}
        for nm in ("pe", "act", "dve", "pool", "sp"):
            s = nc.alloc_semaphore("s_" + nm)
            _SemById[s.num] = s
            self.sems[nm] = s
        self.pe = Eng(self, "pe", "tensor", self.sems["pe"], False)
        self.act = Eng(self, "act", "scalar", self.sems["act"], False)
        self.dve = Eng(self, "dve", "vector", self.sems["dve"], False)
        self.pool = Eng(self, "pool", "gpsimd", self.sems["pool"], False)
        self.sp = Eng(self, "sp", "sync", self.sems["sp"], False)
        self.engines = [self.pe, self.act, self.dve, self.pool, self.sp]
        self.qsp = Q(nc, "qsp")
        self.qact = Q(nc, "qact")
        self.qpool = Q(nc, "qpool")
        self.bufs = []

    def uname(self, n):
        self._uc = getattr(self, "_uc", 0) + 1
        return "%s_u%d" % (n, self._uc)

    def buf(self, name):
        b = Buf(name)
        b.r = dict(getattr(self, "barrier", {}))
        self.bufs.append(b)
        return b

    def scope_barrier(self):
        bar = {}
        for e in self.engines:
            if e.n > 0:
                bar[e.sem.num] = e.n
        for q in (self.qsp, self.qact, self.qpool):
            for slot in q.slots:
                if slot[1] > 0:
                    bar[slot[0].num] = slot[1]
        self.barrier = bar

    def sub(self):
        from contextlib import ExitStack as _ES

        class _Scope:
            def __init__(s):
                s.st = _ES()

            def __enter__(s):
                return s.st.__enter__()

            def __exit__(s, *a):
                r = s.st.__exit__(*a)
                self.scope_barrier()
                return r
        return _Scope()

    def finish(self):
        for e in self.engines:
            e._waits = []
            for other in self.engines:
                if other is not e and other.n > 0:
                    e._dep((other.sem, other.n))
            for q in (self.qsp, self.qact, self.qpool):
                for slot in q.slots:
                    if slot[1] > 0:
                        e._dep((slot[0], slot[1]))
            if e._waits:
                e.prog.append((e._waits, [], None, 0))
                e._waits = []

    def render_all(self):
        with self.nc.Block() as blk:
            for eng in self.engines:
                getattr(blk, eng.ename)(lambda e, eng=eng: eng.render(e))


class Consts:

    def __init__(self, ctx, stack):
        nc = ctx.nc
        en = stack.enter_context
        self.id_bf = en(nc.sbuf_tensor("c_id_bf", [128, 128], BF16))
        self.id_f = en(nc.sbuf_tensor("c_id_f", [128, 128], F32))
        self.ones_col = en(nc.sbuf_tensor("c_ones_col", [128, 1], F32))
        self.ones_row = en(nc.sbuf_tensor("c_ones_row", [1, 512], F32))
        self.b_id_bf = ctx.buf("id_bf")
        self.b_id_f = ctx.buf("id_f")
        self.b_ones_col = ctx.buf("ones_col")
        self.b_ones_row = ctx.buf("ones_row")
        p, b = ctx.pool, self
        p.emit(lambda e: e.memset(self.id_bf[:, :], 0.0), writes=[b.b_id_bf])
        p.emit(lambda e: e.affine_select(out=self.id_bf[:, :], in_=self.id_bf[:, :],
                                         compare_op=ALU.not_equal, fill=1.0, base=0,
                                         pattern=[[-1, 128]], channel_multiplier=1),
               reads=[b.b_id_bf], writes=[b.b_id_bf])
        p.emit(lambda e: e.memset(self.id_f[:, :], 0.0), writes=[b.b_id_f])
        p.emit(lambda e: e.affine_select(out=self.id_f[:, :], in_=self.id_f[:, :],
                                         compare_op=ALU.not_equal, fill=1.0, base=0,
                                         pattern=[[-1, 128]], channel_multiplier=1),
               reads=[b.b_id_f], writes=[b.b_id_f])
        p.emit(lambda e: e.memset(self.ones_col[:, :], 1.0), writes=[b.b_ones_col])
        p.emit(lambda e: e.memset(self.ones_row[:, :], 1.0), writes=[b.b_ones_row])
        self.ones_row_bf = en(nc.sbuf_tensor("c_ones_row_bf", [1, 512], BF16))
        self.b_ones_row_bf = ctx.buf("ones_row_bf")
        p.emit(lambda e: e.memset(self.ones_row_bf[:, :], 1.0), writes=[b.b_ones_row_bf])
        self.ones_col_bf = en(nc.sbuf_tensor("c_ones_col_bf", [128, 1], BF16))
        self.b_ones_col_bf = ctx.buf("ones_col_bf")
        p.emit(lambda e: e.memset(self.ones_col_bf[:, :], 1.0), writes=[b.b_ones_col_bf])
        self.negbig = en(nc.sbuf_tensor("c_negbig", [1, 512], F32))
        self.b_negbig = ctx.buf("negbig")
        p.emit(lambda e: e.memset(self.negbig[:, :], -1e9), writes=[b.b_negbig])
        self.fullmask = en(nc.sbuf_tensor("c_fullmask", [128, 128], BF16))
        self.b_fullmask = ctx.buf("fullmask")
        p.emit(lambda e: e.memset(self.fullmask[:, :], -1e9), writes=[b.b_fullmask])
        self.masklt = en(nc.sbuf_tensor("c_masklt", [128, 128], BF16))
        self.b_masklt = ctx.buf("masklt")
        p.emit(lambda e: e.memset(self.masklt[:, :], 0.0), writes=[b.b_masklt])
        p.emit(lambda e: e.affine_select(out=self.masklt[:, :], in_=self.masklt[:, :],
                                         compare_op=ALU.is_ge, fill=-1e9, base=0,
                                         pattern=[[1, 128]], channel_multiplier=-1),
               reads=[b.b_masklt], writes=[b.b_masklt])
        self.maskut = en(nc.sbuf_tensor("c_maskut", [128, 128], BF16))
        self.b_maskut = ctx.buf("maskut")
        p.emit(lambda e: e.memset(self.maskut[:, :], 0.0), writes=[b.b_maskut])
        p.emit(lambda e: e.affine_select(out=self.maskut[:, :], in_=self.maskut[:, :],
                                         compare_op=ALU.is_ge, fill=-1e9, base=0,
                                         pattern=[[-1, 128]], channel_multiplier=1),
               reads=[b.b_maskut], writes=[b.b_maskut])
        self.eps = en(nc.sbuf_tensor("c_eps", [128, 2], F32))
        self.b_eps = ctx.buf("eps")
        p.emit(lambda e: e.memset(self.eps[:, 0:1], 1e-5), writes=[b.b_eps])
        p.emit(lambda e: e.memset(self.eps[:, 1:2], 1e-6), reads=[b.b_eps], writes=[b.b_eps])


def ln_stats(ctx, consts, xt_ap, b_xt, mr_ap, b_mr, tmp_ap, b_tmp, width=D):
    nchunk = (width + 511) // 512
    for i in range(nchunk):
        lo, hi = i * 512, min((i + 1) * 512, width)
        ctx.dve.emit(lambda e, i=i, lo=lo, hi=hi: e.bn_stats(tmp_ap[:, 6 * i:6 * i + 6], xt_ap[:, lo:hi]),
                     reads=[b_xt], writes=[b_tmp])
    ctx.dve.emit(lambda e: e.bn_aggr(mr_ap[:, 0:2], tmp_ap[:, 0:6 * nchunk]),
                 reads=[b_tmp], writes=[b_mr])
    ctx.act.emit(lambda e: e.activation(out=tmp_ap[:, 12:13], in_=mr_ap[:, 1:2], func=AF.Ln,
                                        bias=consts.eps[0:mr_ap.shape[0], 0:1]),
                 reads=[b_mr, consts.b_eps], writes=[b_tmp])
    ctx.act.emit(lambda e: e.activation(out=mr_ap[:, 1:2], in_=tmp_ap[:, 12:13], func=AF.Exp, scale=-0.5),
                 reads=[b_tmp], writes=[b_mr])


def transpose128(ctx, consts, dst_ap, b_dst, src_ap, b_src, psum_ap, b_psum, dtype="bf"):
    ident = consts.id_bf if dtype == "bf" else consts.id_f
    b_id = consts.b_id_bf if dtype == "bf" else consts.b_id_f
    ctx.pe.emit(lambda e: e.transpose(psum_ap, src_ap, ident[:, :]),
                reads=[b_src, b_id], writes=[b_psum])
    ctx.dve.emit(lambda e: e.tensor_copy(dst_ap, psum_ap),
                 reads=[b_psum], writes=[b_dst])


def load_weight_T(ctx, consts, stack, name, w_hbm, b_whbm, rows, cols, tp_psum, b_tp):
    nc = ctx.nc
    ra, ca = rows // 128, cols // 128
    nat = [stack.enter_context(nc.sbuf_tensor(name + "_nat%d" % i, [128, cols], BF16)) for i in range(2)]
    b_nat = [ctx.buf(name + "_nat%d" % i) for i in range(2)]
    wT = stack.enter_context(nc.sbuf_tensor(name + "_T", [128, ca * rows], BF16))
    b_wT = ctx.buf(name + "_T")
    for ai in range(ra):
        st, b_st = nat[ai % 2], b_nat[ai % 2]
        ctx.pool.emit(lambda e, ai=ai, st=st: e.dma_start(out=st[:, :], in_=w_hbm[ai * 128:(ai + 1) * 128, :]),
                      reads=[b_whbm], writes=[b_st], q=ctx.qpool)
        for bi in range(ca):
            transpose128(ctx, consts,
                         wT[:, bi * rows + ai * 128: bi * rows + (ai + 1) * 128], b_wT,
                         st[:, bi * 128:(bi + 1) * 128], b_st,
                         tp_psum, b_tp, "bf")
    return wT, b_wT


def ln_apply_cast(ctx, out_ap, b_out, x_ap, b_x, mr_ap, b_mr):
    ctx.dve.emit(lambda e: e.tensor_scalar(out_ap, x_ap, mr_ap[:, 0:1], mr_ap[:, 1:2],
                                           ALU.subtract, ALU.mult),
                 reads=[b_x, b_mr], writes=[b_out])


def build_phase2(ctx, consts, stack, T, io):
    nc = ctx.nc
    nt = T // 128
    ntc = T // 512
    en = stack.enter_context
    xin, b_xin = io["x_in"], io["b_x_in"]
    xout, b_xout = io["x_out"], io["b_x_out"]

    xn2T = en(nc.sbuf_tensor("p2_xnT", [128, NCH * T], BF16)); b_xn2T = ctx.buf("p2_xnT")
    obuf, b_obuf = xn2T, b_xn2T
    u = en(nc.sbuf_tensor("p2_u", [128, T], F32)); b_u = ctx.buf("p2_u")
    xt = [en(nc.sbuf_tensor("p2_xt%d" % i, [128, D], F32)) for i in range(2)]
    b_xt = [ctx.buf("p2_xt0"), ctx.buf("p2_xt1")]
    xnb = en(nc.sbuf_tensor("p2_xnb", [128, D], BF16)); b_xnb = ctx.buf("p2_xnb")
    lnst = en(nc.sbuf_tensor("p2_lnst", [128, nt * 2], F32)); b_lnst = ctx.buf("p2_lnst")
    sttmp = en(nc.sbuf_tensor("p2_sttmp", [128, 16], F32)); b_sttmp = ctx.buf("p2_sttmp")
    rows = en(nc.sbuf_tensor("p2_rows", [1, 2 * D + 64], F32)); b_rows = ctx.buf("p2_rows")
    lamexp = en(nc.sbuf_tensor("p2_lamexp", [1, 128], F32)); b_lamexp = ctx.buf("p2_lamexp")
    lamcol = en(nc.sbuf_tensor("p2_lamcol", [128, NCH], F32)); b_lamcol = ctx.buf("p2_lamcol")
    qkcol = en(nc.sbuf_tensor("p2_qkcol", [128, NCH], F32)); b_qkcol = ctx.buf("p2_qkcol")
    sel = en(nc.sbuf_tensor("p2_sel", [128, 2], F32)); b_sel = ctx.buf("p2_sel")
    selT = en(nc.sbuf_tensor("p2_selT", [2, 128], F32)); b_selT = ctx.buf("p2_selT")
    s2 = en(nc.sbuf_tensor("p2_s2", [2, NCH], F32)); b_s2 = ctx.buf("p2_s2")
    ccol = en(nc.sbuf_tensor("p2_ccol", [128, NCH], F32)); b_ccol = ctx.buf("p2_ccol")
    ssq = en(nc.sbuf_tensor("p2_ssq", [1, T], F32)); b_ssq = ctx.buf("p2_ssq")
    rrow = en(nc.sbuf_tensor("p2_rrow", [1, T], F32)); b_rrow = ctx.buf("p2_rrow")
    rstdc = en(nc.sbuf_tensor("p2_rstdc", [128, nt], F32)); b_rstdc = ctx.buf("p2_rstdc")
    nwcol = en(nc.sbuf_tensor("p2_nwcol", [128, NCH], F32)); b_nwcol = ctx.buf("p2_nwcol")
    ao = en(nc.sbuf_tensor("p2_ao", [128, D], F32)); b_ao = ctx.buf("p2_ao")
    t1 = en(nc.sbuf_tensor("p2_t1", [128, D], F32)); b_t1 = ctx.buf("p2_t1")
    ptp = [en(nc.psum_tensor("p2_ptp%d" % i, [128, 128], BF16)) for i in range(2)]
    b_ptp = [ctx.buf("p2_ptp%d" % i) for i in range(2)]
    ptf = en(nc.psum_tensor("p2_ptf", [128, 512], F32)); b_ptf = ctx.buf("p2_ptf")
    pmm = [en(nc.psum_tensor("p2_pmm%d" % i, [128, 512], F32)) for i in range(2)]
    b_pmm = [ctx.buf("p2_pmm%d" % i) for i in range(2)]

    for t in range(nt):
        ctx.sp.emit(lambda e, t=t: e.dma_start(out=xt[t % 2][:, :], in_=xin[t * 128:(t + 1) * 128, :]),
                    reads=[b_xin], writes=[b_xt[t % 2]], q=ctx.qsp)
        mr = lnst[:, 2 * t:2 * t + 2]
        ln_stats(ctx, consts, xt[t % 2][:, :], b_xt[t % 2], mr, b_lnst, sttmp[:, :], b_sttmp)
        ctx.dve.emit(lambda e, t=t: e.tensor_scalar(xnb[:, :], xt[t % 2][:, :],
                                                    lnst[:, 2 * t:2 * t + 1], lnst[:, 2 * t + 1:2 * t + 2],
                                                    ALU.subtract, ALU.mult),
                     reads=[b_xt[t % 2], b_lnst], writes=[b_xnb])
        for c in range(NCH):
            transpose128(ctx, consts,
                         xn2T[:, c * T + t * 128: c * T + (t + 1) * 128], b_xn2T,
                         xnb[:, c * 128:(c + 1) * 128], b_xnb,
                         ptp[c % 2][:, :], b_ptp[c % 2], "bf")

    q_h, b_q = io["L1_attn_q"], io["b_L1_attn_q"]
    k_h, b_k = io["L1_attn_k"], io["b_L1_attn_k"]
    ld_h, b_ld = io["L1_attn_log_decay"], io["b_L1_attn_log_decay"]
    nw_h, b_nw = io["L1_attn_norm_weight"], io["b_L1_attn_norm_weight"]
    ow_h, b_ow = io["L1_attn_o_w"], io["b_L1_attn_o_w"]

    ctx.sp.emit(lambda e: e.dma_start(out=rows[0:1, 0:16], in_=ld_h[None, :]),
                reads=[b_ld], writes=[b_rows], q=ctx.qsp)
    ctx.sp.emit(lambda e: e.dma_start(out=rows[0:1, 64:64 + D], in_=q_h[None, :]),
                reads=[b_q], writes=[b_rows], q=ctx.qsp)
    ctx.sp.emit(lambda e: e.dma_start(out=rows[0:1, 64 + D:64 + 2 * D], in_=k_h[None, :]),
                reads=[b_k], writes=[b_rows], q=ctx.qsp)
    ctx.act.emit(lambda e: e.activation(out=rows[0:1, 16:32], in_=rows[0:1, 0:16], func=AF.Sigmoid),
                 reads=[b_rows], writes=[b_rows])
    ctx.dve.emit(lambda e: e.tensor_mul(rows[0:1, 64:64 + D], rows[0:1, 64:64 + D], rows[0:1, 64 + D:64 + 2 * D]),
                 reads=[b_rows], writes=[b_rows])
    for c in range(NCH):
        ctx.pe.emit_multi([
            (lambda e, c=c: e.matmul(ptf[0:1, 0:64], rows[0:1, 16 + 2 * c:17 + 2 * c],
                                     consts.ones_row[0:1, 0:64], start=True, stop=True)),
            (lambda e, c=c: e.matmul(ptf[0:1, 64:128], rows[0:1, 17 + 2 * c:18 + 2 * c],
                                     consts.ones_row[0:1, 0:64], start=True, stop=True)),
        ], reads=[b_rows, consts.b_ones_row], writes=[b_ptf])
        ctx.dve.emit(lambda e: e.tensor_copy(lamexp[0:1, :], ptf[0:1, 0:128]),
                     reads=[b_ptf], writes=[b_lamexp])
        ctx.pe.emit(lambda e: e.transpose(ptf[:, 0:1], lamexp[0:1, :], consts.id_f[0:1, 0:1]),
                    reads=[b_lamexp, consts.b_id_f], writes=[b_ptf])
        ctx.dve.emit(lambda e, c=c: e.tensor_copy(lamcol[:, c:c + 1], ptf[:, 0:1]),
                     reads=[b_ptf], writes=[b_lamcol])
    for c in range(NCH):
        ctx.pe.emit(lambda e, c=c: e.transpose(ptf[:, 0:1], rows[0:1, 64 + c * 128:64 + (c + 1) * 128],
                                               consts.id_f[0:1, 0:1]),
                    reads=[b_rows, consts.b_id_f], writes=[b_ptf])
        ctx.dve.emit(lambda e, c=c: e.tensor_copy(qkcol[:, c:c + 1], ptf[:, 0:1]),
                     reads=[b_ptf], writes=[b_qkcol])
    ctx.pool.emit(lambda e: e.memset(sel[:, :], 0.0), writes=[b_sel])
    ctx.pool.emit(lambda e: e.memset(sel[0:64, 0:1], 1.0), reads=[b_sel], writes=[b_sel])
    ctx.pool.emit(lambda e: e.memset(sel[64:128, 1:2], 1.0), reads=[b_sel], writes=[b_sel])
    ctx.pe.emit(lambda e: e.matmul(ptf[0:2, 0:NCH], sel[:, :], qkcol[:, :], start=True, stop=True),
                reads=[b_sel, b_qkcol], writes=[b_ptf])
    ctx.dve.emit(lambda e: e.tensor_copy(s2[:, :], ptf[0:2, 0:NCH]),
                 reads=[b_ptf], writes=[b_s2])
    ctx.pe.emit(lambda e: e.transpose(ptf[0:2, 0:128], sel[:, :], consts.id_f[:, :]),
                reads=[b_sel, consts.b_id_f], writes=[b_ptf])
    ctx.dve.emit(lambda e: e.tensor_copy(selT[:, :], ptf[0:2, 0:128]),
                 reads=[b_ptf], writes=[b_selT])
    ctx.pe.emit(lambda e: e.matmul(ptf[:, 0:NCH], selT[:, :], s2[:, :], start=True, stop=True),
                reads=[b_selT, b_s2], writes=[b_ptf])
    ctx.dve.emit(lambda e: e.tensor_copy(ccol[:, :], ptf[:, 0:NCH]),
                 reads=[b_ptf], writes=[b_ccol])

    for c in range(NCH):
        ctx.dve.emit(lambda e, c=c: e.tensor_tensor_scan(u[:, :], lamcol[:, c:c + 1].broadcast_to((128, T)),
                                                         xn2T[:, c * T:(c + 1) * T], 0.0, ALU.mult, ALU.add),
                     reads=[b_lamcol, b_xn2T], writes=[b_u])
        ctx.dve.emit(lambda e, c=c: e.tensor_scalar_mul(obuf[:, c * T:(c + 1) * T], u[:, :], ccol[:, c:c + 1]),
                     reads=[b_u, b_ccol], writes=[b_obuf])

    ctx.dve.emit(lambda e: e.memset(ssq[0:1, :], 0.0), writes=[b_ssq])
    for c in range(NCH):
        ctx.act.emit(lambda e, c=c: e.activation(out=u[:, :], in_=obuf[:, c * T:(c + 1) * T], func=AF.Square),
                     reads=[b_obuf], writes=[b_u])
        for tc in range(ntc):
            pm = pmm[tc % 2]
            bpm = b_pmm[tc % 2]
            ctx.pe.emit(lambda e, tc=tc, pm=pm: e.matmul(pm[0:1, :], consts.ones_col[:, :], u[:, tc * 512:(tc + 1) * 512],
                                                  start=True, stop=True),
                        reads=[b_u, consts.b_ones_col], writes=[bpm])
            ctx.dve.emit(lambda e, tc=tc, pm=pm: e.tensor_add(ssq[0:1, tc * 512:(tc + 1) * 512],
                                                       ssq[0:1, tc * 512:(tc + 1) * 512], pm[0:1, :]),
                         reads=[bpm, b_ssq], writes=[b_ssq])

    ctx.act.emit(lambda e: e.activation(out=rrow[0:1, 0:T], in_=ssq[0:1, :], func=AF.Ln, scale=1.0 / D,
                                        bias=consts.eps[0:1, 1:2]),
                 reads=[b_ssq, consts.b_eps], writes=[b_rrow])
    ctx.act.emit(lambda e: e.activation(out=rrow[0:1, 0:T], in_=rrow[0:1, 0:T], func=AF.Exp, scale=-0.5),
                 reads=[b_rrow], writes=[b_rrow])
    for t in range(nt):
        ctx.pe.emit(lambda e, t=t: e.transpose(ptf[:, 0:1], rrow[0:1, t * 128:(t + 1) * 128],
                                               consts.id_f[0:1, 0:1]),
                    reads=[b_rrow, consts.b_id_f], writes=[b_ptf])
        ctx.dve.emit(lambda e, t=t: e.tensor_copy(rstdc[:, t:t + 1], ptf[:, 0:1]),
                     reads=[b_ptf], writes=[b_rstdc])

    owT, b_owT = load_weight_T(ctx, consts, stack, "p2_ow", ow_h, b_ow, D, D, ptp[0][:, :], b_ptp[0])
    ctx.sp.emit(lambda e: e.dma_start(out=rows[0:1, 64:64 + D], in_=nw_h[None, :]),
                reads=[b_nw], writes=[b_rows], q=ctx.qsp)
    for c in range(NCH):
        ctx.pe.emit(lambda e, c=c: e.transpose(ptf[:, 0:1], rows[0:1, 64 + c * 128:64 + (c + 1) * 128],
                                               consts.id_f[0:1, 0:1]),
                    reads=[b_rows, consts.b_id_f], writes=[b_ptf])
        ctx.dve.emit(lambda e, c=c: e.tensor_copy(nwcol[:, c:c + 1], ptf[:, 0:1]),
                     reads=[b_ptf], writes=[b_nwcol])
    for bi in range(NCH):
        ctx.dve.emit(lambda e, bi=bi: e.tensor_scalar_mul(owT[:, bi * D:(bi + 1) * D],
                                                          owT[:, bi * D:(bi + 1) * D], nwcol[:, bi:bi + 1]),
                     reads=[b_owT, b_nwcol], writes=[b_owT])

    for t in range(nt):
        for half in range(2):
            pm, bpm = pmm[half], b_pmm[half]
            fns = []
            for ci in range(NCH):
                fns.append(lambda e, ci=ci, half=half, t=t, pm=pm: e.matmul(
                    pm[:, :],
                    obuf[:, ci * T + t * 128: ci * T + (t + 1) * 128],
                    owT[:, ci * D + half * 512: ci * D + (half + 1) * 512],
                    start=(ci == 0), stop=(ci == NCH - 1)))
            ctx.pe.emit_multi(fns, reads=[b_obuf, b_owT], writes=[bpm])
            ctx.dve.emit(lambda e, half=half, t=t, pm=pm: e.tensor_scalar_mul(
                ao[:, half * 512:(half + 1) * 512], pm[:, :], rstdc[:, t:t + 1]),
                reads=[bpm, b_rstdc], writes=[b_ao])
        mr = sttmp[:, 13:15]
        ln_stats(ctx, consts, ao[:, :], b_ao, mr, b_sttmp, sttmp[:, :], b_sttmp)
        ctx.dve.emit(lambda e: e.tensor_scalar(t1[:, :], ao[:, :], sttmp[:, 13:14], sttmp[:, 14:15],
                                               ALU.subtract, ALU.mult),
                     reads=[b_ao, b_sttmp], writes=[b_t1])
        ctx.sp.emit(lambda e, t=t: e.dma_start(out=xt[t % 2][:, :], in_=xin[t * 128:(t + 1) * 128, :]),
                    reads=[b_xin], writes=[b_xt[t % 2]], q=ctx.qsp)
        ctx.dve.emit(lambda e, t=t: e.tensor_add(t1[:, :], t1[:, :], xt[t % 2][:, :]),
                     reads=[b_xt[t % 2], b_t1], writes=[b_t1])
        ctx.sp.emit(lambda e, t=t: e.dma_start(out=xout[t * 128:(t + 1) * 128, :], in_=t1[:, :]),
                    reads=[b_t1], writes=[b_xout], q=ctx.qsp)


def emit_weight_T(ctx, consts, wT, b_wT, stage, b_stage, w_hbm, b_w, rows, cols, tp_psum, b_tp,
                  row0=0, col0=0):
    ra, ca = rows // 128, cols // 128
    for ai in range(ra):
        st, b_st = stage[ai % 2], b_stage[ai % 2]
        ctx.pool.emit(lambda e, ai=ai, st=st: e.dma_start(
            out=st[:, 0:cols], in_=w_hbm[row0 + ai * 128:row0 + (ai + 1) * 128, col0:col0 + cols]),
            reads=[b_w], writes=[b_st], q=ctx.qpool)
        for bi in range(ca):
            transpose128(ctx, consts,
                         wT[:, bi * rows + ai * 128: bi * rows + (ai + 1) * 128], b_wT,
                         st[:, bi * 128:(bi + 1) * 128], b_st,
                         tp_psum, b_tp, "bf")


def emit_weight_T3(ctx, consts, wT, b_wT, stages, b_stages, cnt, w_hbm, b_w, rows, cols,
                   ptp2, b_ptp2, row0=0, col0=0):
    ra, ca = rows // 128, cols // 128
    for ai in range(ra):
        k = cnt[0] % len(stages)
        cnt[0] += 1
        st, b_st = stages[k], b_stages[k]
        ctx.pool.emit(lambda e, ai=ai, st=st: e.dma_start(
            out=st[:, 0:cols], in_=w_hbm[row0 + ai * 128:row0 + (ai + 1) * 128, col0:col0 + cols]),
            reads=[b_w], writes=[b_st], q=ctx.qpool)
        for b0 in range(0, ca, 4):
            bn = min(4, ca - b0)
            pb, bpb = ptp2[(ai + b0 // 4) % 2], b_ptp2[(ai + b0 // 4) % 2]
            for j in range(bn):
                ctx.pe.emit(lambda e, b0=b0, j=j, pb=pb, st=st: e.transpose(
                    pb[:, j * 128:(j + 1) * 128],
                    st[:, (b0 + j) * 128:(b0 + j + 1) * 128],
                    consts.id_bf[:, :]),
                    reads=[b_st, consts.b_id_bf], writes=[bpb])
            wTr = wT[:, 0:ca * rows].rearrange("p (b a) -> p b a", b=ca)
            ctx.dve.emit(lambda e, ai=ai, b0=b0, bn=bn, pb=pb, wTr=wTr: e.tensor_copy(
                wTr[:, b0:b0 + bn, ai * 128:(ai + 1) * 128],
                pb[:, 0:bn * 128].rearrange("p (b a) -> p b a", b=bn)),
                reads=[bpb], writes=[b_wT])


def emit_weight_T2(ctx, consts, wT, b_wT, nat, b_nat, w_hbm, b_w, rows, cols,
                   ptp2, b_ptp2, row0=0, col0=0):
    ra, ca = rows // 128, cols // 128
    if col0 == 0 and cols == w_hbm.shape[1]:
        ctx.pool.emit(lambda e: e.dma_start(
            out=nat[:, 0:ra * cols].rearrange("p (a c) -> p a c", a=ra),
            in_=w_hbm[row0:row0 + rows, :].rearrange("(a p) c -> p a c", p=128)),
            reads=[b_w], writes=[b_nat], q=ctx.qpool)
    else:
        for ai in range(ra):
            ctx.pool.emit(lambda e, ai=ai: e.dma_start(
                out=nat[:, ai * cols:(ai + 1) * cols],
                in_=w_hbm[row0 + ai * 128:row0 + (ai + 1) * 128, col0:col0 + cols]),
                reads=[b_w], writes=[b_nat], q=ctx.qpool)
    wTr = wT[:, 0:ca * rows].rearrange("p (b a) -> p b a", b=ca)
    nb = 0
    for bi in range(ca):
        for a0 in range(0, ra, 4):
            an = min(4, ra - a0)
            pb, bpb = ptp2[nb % 2], b_ptp2[nb % 2]
            nb += 1
            for j in range(an):
                ctx.pe.emit(lambda e, bi=bi, a0=a0, j=j, pb=pb: e.transpose(
                    pb[:, j * 128:(j + 1) * 128],
                    nat[:, (a0 + j) * cols + bi * 128:(a0 + j) * cols + (bi + 1) * 128],
                    consts.id_bf[:, :]),
                    reads=[b_nat, consts.b_id_bf], writes=[bpb])
            ctx.dve.emit(lambda e, bi=bi, a0=a0, an=an, pb=pb: e.tensor_copy(
                wTr[:, bi, a0 * 128:(a0 + an) * 128], pb[:, 0:an * 128]),
                reads=[bpb], writes=[b_wT])


def emit_ln_pass(ctx, consts, T, xin, b_xin, xt, b_xt, xnb, b_xnb, lnst, b_lnst,
                 sttmp, b_sttmp, xnT, b_xnT, ptp, b_ptp, r0=0):
    nt = T // 128
    for t in range(nt):
        ctx.sp.emit(lambda e, t=t: e.dma_start(out=xt[t % 2][:, :],
                                               in_=xin[r0 + t * 128:r0 + (t + 1) * 128, :]),
                    reads=[b_xin], writes=[b_xt[t % 2]], q=ctx.qsp)
        mr = lnst[:, 2 * t:2 * t + 2]
        ln_stats(ctx, consts, xt[t % 2][:, :], b_xt[t % 2], mr, b_lnst, sttmp[:, :], b_sttmp)
        ctx.dve.emit(lambda e, t=t: e.tensor_scalar(xnb[:, :], xt[t % 2][:, :],
                                                    lnst[:, 2 * t:2 * t + 1], lnst[:, 2 * t + 1:2 * t + 2],
                                                    ALU.subtract, ALU.mult),
                     reads=[b_xt[t % 2], b_lnst], writes=[b_xnb])
        for c in range(NCH):
            transpose128(ctx, consts,
                         xnT[:, c * T + t * 128: c * T + (t + 1) * 128], b_xnT,
                         xnb[:, c * 128:(c + 1) * 128], b_xnb,
                         ptp[c % 2][:, 0:128], b_ptp[c % 2], "bf")


def emit_finale(ctx, consts, T, t, src_ap, b_src, xin, b_xin, xout, b_xout,
                xt, b_xt, t1, b_t1, sttmp, b_sttmp, r0=0):
    mr = sttmp[:, 13:15]
    ln_stats(ctx, consts, src_ap, b_src, mr, b_sttmp, sttmp[:, :], b_sttmp)
    ctx.dve.emit(lambda e: e.tensor_scalar(t1[:, :], src_ap, sttmp[:, 13:14], sttmp[:, 14:15],
                                           ALU.subtract, ALU.mult),
                 reads=[b_src, b_sttmp], writes=[b_t1])
    ctx.sp.emit(lambda e, t=t: e.dma_start(out=xt[:, :], in_=xin[r0 + t * 128:r0 + (t + 1) * 128, :]),
                reads=[b_xin], writes=[b_xt], q=ctx.qsp)
    ctx.dve.emit(lambda e: e.tensor_add(t1[:, :], t1[:, :], xt[:, :]),
                 reads=[b_xt, b_t1], writes=[b_t1])
    ctx.sp.emit(lambda e, t=t: e.dma_start(out=xout[r0 + t * 128:r0 + (t + 1) * 128, :], in_=t1[:, :]),
                reads=[b_t1], writes=[b_xout], q=ctx.qsp)


def build_phase1(ctx, consts, stack, Tfull, io):
    nhalf = 2 if Tfull >= 4096 else 1
    T = Tfull // nhalf
    for half in range(nhalf):
        with ctx.sub() as hstack:
            _build_phase1_half(ctx, consts, hstack, T, half * T, io)


def _build_phase1_half(ctx, consts, stack, T, r0, io):
    nc = ctx.nc
    nt = T // 128
    ntc = T // 512
    en = stack.enter_context
    xin, b_xin = io["x_in"], io["b_x_in"]
    xout, b_xout = io["x_out"], io["b_x_out"]
    E = 8

    xnT = en(nc.sbuf_tensor(ctx.uname("p1_xnT"), [128, NCH * T], BF16)); b_xnT = ctx.buf("p1_xnT")
    ybuf = en(nc.sbuf_tensor(ctx.uname("p1_y"), [128, nt * D], FP16)); b_y = ctx.buf("p1_y")
    xt = [en(nc.sbuf_tensor(ctx.uname("p1_xt%d" % i), [128, D], F32)) for i in range(2)]
    b_xt = [ctx.buf("p1_xt0"), ctx.buf("p1_xt1")]
    xnb = en(nc.sbuf_tensor(ctx.uname("p1_xnb"), [128, D], BF16)); b_xnb = ctx.buf("p1_xnb")
    lnst = en(nc.sbuf_tensor(ctx.uname("p1_lnst"), [128, nt * 2], F32)); b_lnst = ctx.buf("p1_lnst")
    sttmp = en(nc.sbuf_tensor(ctx.uname("p1_sttmp"), [128, 16], F32)); b_sttmp = ctx.buf("p1_sttmp")
    t1 = en(nc.sbuf_tensor(ctx.uname("p1_t1"), [128, D], F32)); b_t1 = ctx.buf("p1_t1")
    gwT = en(nc.sbuf_tensor(ctx.uname("p1_gwT"), [128, NCH * E], BF16)); b_gwT = ctx.buf("p1_gwT")
    gnat = en(nc.sbuf_tensor(ctx.uname("p1_gnat"), [8, D], BF16)); b_gnat = ctx.buf("p1_gnat")
    sc = en(nc.sbuf_tensor(ctx.uname("p1_sc"), [128, 4 * E + 4], F32)); b_sc = ctx.buf("p1_sc")
    cbuf = en(nc.sbuf_tensor(ctx.uname("p1_cbuf"), [128, nt * E], F32)); b_cbuf = ctx.buf("p1_cbuf")
    mbuf = en(nc.sbuf_tensor(ctx.uname("p1_mbuf"), [128, nt * E], F32)); b_mbuf = ctx.buf("p1_mbuf")
    crow = en(nc.sbuf_tensor(ctx.uname("p1_crow"), [1, T], BF16)); b_crow = ctx.buf("p1_crow")
    w1T = en(nc.sbuf_tensor(ctx.uname("p1_w1T"), [128, NCH * D], BF16)); b_w1T = ctx.buf("p1_w1T")
    w2T = en(nc.sbuf_tensor(ctx.uname("p1_w2T"), [128, NCH * D], BF16)); b_w2T = ctx.buf("p1_w2T")
    w3T = en(nc.sbuf_tensor(ctx.uname("p1_w3T"), [128, NCH * D], BF16)); b_w3T = ctx.buf("p1_w3T")
    nat2 = [en(nc.sbuf_tensor(ctx.uname("p1_nat%d" % i), [128, D], BF16)) for i in range(3)]
    b_nat2 = [ctx.buf("p1_nat%d" % i) for i in range(3)]
    natcnt = [0]
    brow = en(nc.sbuf_tensor(ctx.uname("p1_brow"), [1, 2 * D], BF16)); b_brow = ctx.buf("p1_brow")
    b1col = en(nc.sbuf_tensor(ctx.uname("p1_b1col"), [128, NCH], F32)); b_b1col = ctx.buf("p1_b1col")
    xcT2 = [en(nc.sbuf_tensor(ctx.uname("p1_xcT%d" % i), [128, NCH * 512], BF16)) for i in range(2)]
    b_xcT2 = [ctx.buf("p1_xcT%d" % i) for i in range(2)]
    hT = en(nc.sbuf_tensor(ctx.uname("p1_hT"), [128, NCH * 512], BF16)); b_hT = ctx.buf("p1_hT")
    x3sb = en(nc.sbuf_tensor(ctx.uname("p1_x3sb"), [128, D], F32)); b_x3sb = ctx.buf("p1_x3sb")
    t2 = en(nc.sbuf_tensor(ctx.uname("p1_t2"), [128, 512], F32)); b_t2 = ctx.buf("p1_t2")
    ptp = [en(nc.psum_tensor(ctx.uname("p1_ptp%d" % i), [128, 512], BF16)) for i in range(2)]
    b_ptp = [ctx.buf("p1_ptp%d" % i) for i in range(2)]
    pcb = en(nc.psum_tensor(ctx.uname("p1_pcb"), [128, 512], F32)); b_pcb = ctx.buf("p1_pcb")
    pz = [en(nc.psum_tensor(ctx.uname("p1_pz%d" % i), [128, 512], F32)) for i in range(2)]
    b_pz = [ctx.buf("p1_pz%d" % i) for i in range(2)]
    pw = [en(nc.psum_tensor(ctx.uname("p1_pw%d" % i), [128, 512], F32)) for i in range(3)]
    b_pw = [ctx.buf("p1_pw%d" % i) for i in range(3)]

    emit_ln_pass(ctx, consts, T, xin, b_xin, xt, b_xt, xnb, b_xnb, lnst, b_lnst,
                 sttmp, b_sttmp, xnT, b_xnT, ptp, b_ptp, r0=r0)

    gw, b_gw = io["L0_ffn_gate_w"], io["b_L0_ffn_gate_w"]
    ctx.pool.emit(lambda e: e.dma_start(out=gnat[:, :], in_=gw[:, :]),
                  reads=[b_gw], writes=[b_gnat], q=ctx.qpool)
    for ci in range(NCH):
        ctx.pe.emit(lambda e, ci=ci: e.transpose(ptp[0][:, 0:8], gnat[0:8, ci * 128:(ci + 1) * 128],
                                                 consts.id_bf[0:8, 0:8]),
                    reads=[b_gnat, consts.b_id_bf], writes=[b_ptp[0]])
        ctx.dve.emit(lambda e, ci=ci: e.tensor_copy(gwT[:, ci * E:(ci + 1) * E], ptp[0][0:128, 0:8]),
                     reads=[b_ptp[0]], writes=[b_gwT])
    for t in range(nt):
        fns = []
        for ci in range(NCH):
            fns.append(lambda e, ci=ci, t=t: e.matmul(
                pcb[:, 0:E], xnT[:, ci * T + t * 128: ci * T + (t + 1) * 128],
                gwT[:, ci * E:(ci + 1) * E], start=(ci == 0), stop=(ci == NCH - 1)))
        ctx.pe.emit_multi(fns, reads=[b_xnT, b_gwT], writes=[b_pcb])
        L, M1, M2, Z = sc[:, 0:E], sc[:, 4 * E:4 * E + 1], sc[:, 4 * E + 1:4 * E + 2], sc[:, 4 * E + 2:4 * E + 3]
        A1, A2, MS = sc[:, E:2 * E], sc[:, 2 * E:3 * E], sc[:, 3 * E:4 * E]
        ctx.dve.emit(lambda e: e.tensor_copy(L, pcb[:, 0:E]), reads=[b_pcb], writes=[b_sc])
        ctx.dve.emit(lambda e: e.reduce_max(M1, L, axis=AX.X, negate=True), reads=[b_sc], writes=[b_sc])
        ctx.act.emit(lambda e: e.activation(out=L, in_=L, func=AF.Exp, bias=M1, accum_out=Z),
                     reads=[b_sc], writes=[b_sc])
        ctx.dve.emit(lambda e: e.reciprocal(M1, Z), reads=[b_sc], writes=[b_sc])
        ctx.dve.emit(lambda e: e.tensor_scalar_mul(L, L, M1), reads=[b_sc], writes=[b_sc])
        ctx.dve.emit(lambda e: e.reduce_max(M1, L, axis=AX.X), reads=[b_sc], writes=[b_sc])
        ctx.dve.emit(lambda e: e.tensor_single_scalar(A1, L, M1, ALU.is_ge), reads=[b_sc], writes=[b_sc])
        ctx.dve.emit(lambda e: e.scalar_tensor_tensor(MS, A1, -1e30, L, ALU.mult, ALU.add),
                     reads=[b_sc], writes=[b_sc])
        ctx.dve.emit(lambda e: e.reduce_max(M2, MS, axis=AX.X), reads=[b_sc], writes=[b_sc])
        ctx.dve.emit(lambda e: e.tensor_single_scalar(A2, MS, M2, ALU.is_ge), reads=[b_sc], writes=[b_sc])
        ctx.dve.emit(lambda e, t=t: e.tensor_scalar_mul(cbuf[:, t * E:(t + 1) * E], A1, M1),
                     reads=[b_sc], writes=[b_cbuf])
        ctx.dve.emit(lambda e, t=t: e.tensor_scalar(MS, A2, M2, None, ALU.mult),
                     reads=[b_sc], writes=[b_sc])
        ctx.dve.emit(lambda e, t=t: e.tensor_add(cbuf[:, t * E:(t + 1) * E], cbuf[:, t * E:(t + 1) * E], MS),
                     reads=[b_sc, b_cbuf], writes=[b_cbuf])
        ctx.dve.emit(lambda e, t=t: e.tensor_add(mbuf[:, t * E:(t + 1) * E], A1, A2),
                     reads=[b_sc], writes=[b_mbuf])

    ctx.pool.emit(lambda e: e.memset(ybuf[:, :], 0.0), writes=[b_y])

    w1h, b_w1h = io["L0_ffn_w1"], io["b_L0_ffn_w1"]
    w2h, b_w2h = io["L0_ffn_w2"], io["b_L0_ffn_w2"]
    w3h, b_w3h = io["L0_ffn_w3"], io["b_L0_ffn_w3"]
    b1h, b_b1h = io["L0_ffn_b1"], io["b_L0_ffn_b1"]
    b2h, b_b2h = io["L0_ffn_b2"], io["b_L0_ffn_b2"]
    b3h, b_b3h = io["L0_ffn_b3"], io["b_L0_ffn_b3"]
    for ex in range(E):
        ctx.pool.emit(lambda e, ex=ex: e.dma_start(out=brow[0:1, 0:D], in_=b1h[ex][None, :]),
                      reads=[b_b1h], writes=[b_brow], q=ctx.qpool)
        for c in range(NCH):
            ctx.pe.emit(lambda e, c=c: e.transpose(ptp[0][:, 0:1], brow[0:1, c * 128:(c + 1) * 128],
                                                   consts.id_bf[0:1, 0:1]),
                        reads=[b_brow, consts.b_id_bf], writes=[b_ptp[0]])
            ctx.dve.emit(lambda e, c=c: e.tensor_copy(b1col[:, c:c + 1], ptp[0][:, 0:1]),
                         reads=[b_ptp[0]], writes=[b_b1col])
        ctx.pool.emit(lambda e, ex=ex: e.dma_start(out=brow[0:1, 0:D], in_=b2h[ex][None, :]),
                      reads=[b_b2h, b_b1col], writes=[b_brow], q=ctx.qpool)
        ctx.pool.emit(lambda e, ex=ex: e.dma_start(out=brow[0:1, D:2 * D], in_=b3h[ex][None, :]),
                      reads=[b_b3h], writes=[b_brow], q=ctx.qpool)
        emit_weight_T3(ctx, consts, w1T, b_w1T, nat2, b_nat2, natcnt, w1h[ex], b_w1h, D, D, ptp, b_ptp)
        emit_weight_T3(ctx, consts, w2T, b_w2T, nat2, b_nat2, natcnt, w2h[ex], b_w2h, D, D, ptp, b_ptp)
        emit_weight_T3(ctx, consts, w3T, b_w3T, nat2, b_nat2, natcnt, w3h[ex], b_w3h, D, D, ptp, b_ptp)
        for t in range(nt):
            ctx.pe.emit(lambda e, t=t, ex=ex: e.transpose(pcb[0:1, 0:128],
                                                          cbuf[:, t * E + ex:t * E + ex + 1],
                                                          consts.id_f[:, :]),
                        reads=[b_cbuf, consts.b_id_f], writes=[b_pcb])
            ctx.dve.emit(lambda e, t=t: e.tensor_copy(crow[0:1, t * 128:(t + 1) * 128], pcb[0:1, 0:128]),
                         reads=[b_pcb], writes=[b_crow])
        def emit_xct(tc):
            xcT, b_xcT = xcT2[tc % 2], b_xcT2[tc % 2]
            ctx.pe.emit(lambda e, tc=tc: e.matmul(pcb[:, :], consts.ones_row_bf[0:1, 0:128],
                                                  crow[0:1, tc * 512:(tc + 1) * 512], start=True, stop=True),
                        reads=[b_crow, consts.b_ones_row_bf], writes=[b_pcb])
            for ci in range(NCH):
                ctx.dve.emit(lambda e, ci=ci, tc=tc, xcT=xcT: e.tensor_mul(
                    xcT[:, ci * 512:(ci + 1) * 512],
                    xnT[:, ci * T + tc * 512: ci * T + (tc + 1) * 512], pcb[:, :]),
                    reads=[b_xnT, b_pcb], writes=[b_xcT])

        emit_xct(0)
        for tc in range(ntc):
            xcT, b_xcT = xcT2[tc % 2], b_xcT2[tc % 2]
            for hc in range(NCH):
                pb, bpb = pz[hc % 2], b_pz[hc % 2]
                fns = []
                for ci in range(NCH):
                    fns.append(lambda e, ci=ci, hc=hc, pb=pb, xcT=xcT: e.matmul(
                        pb[:, :], w1T[:, ci * D + hc * 128: ci * D + (hc + 1) * 128],
                        xcT[:, ci * 512:(ci + 1) * 512], start=(ci == 0), stop=(ci == NCH - 1)))
                ctx.pe.emit_multi(fns, reads=[b_w1T, b_xcT], writes=[bpb])
                ctx.act.emit(lambda e, hc=hc, pb=pb: e.activation(
                    out=hT[:, hc * 512:(hc + 1) * 512], in_=pb[:, :], func=AF.Silu,
                    bias=b1col[:, hc:hc + 1]),
                    reads=[bpb, b_b1col], writes=[b_hT])
            if tc + 1 < ntc:
                emit_xct(tc + 1)
            for s4 in range(4):
                t = tc * 4 + s4
                for half in range(2):
                    pb, bpb = pw[half], b_pw[half]
                    fns = []
                    for ci in range(NCH):
                        fns.append(lambda e, ci=ci, s4=s4, half=half, pb=pb, xcT=xcT: e.matmul(
                            pb[:, :], xcT[:, ci * 512 + s4 * 128: ci * 512 + (s4 + 1) * 128],
                            w3T[:, ci * D + half * 512: ci * D + (half + 1) * 512],
                            start=(ci == 0), stop=False))
                    fns.append(lambda e, half=half, pb=pb: e.matmul(
                        pb[:, :], consts.ones_row_bf[0:1, 0:128],
                        brow[0:1, D + half * 512: D + (half + 1) * 512], start=False, stop=True))
                    ctx.pe.emit_multi(fns, reads=[b_xcT, b_w3T, b_brow, consts.b_ones_row_bf], writes=[bpb])
                    ctx.act.emit(lambda e, half=half, pb=pb: e.activation(
                        out=x3sb[:, half * 512:(half + 1) * 512], in_=pb[:, :], func=AF.Copy),
                        reads=[bpb], writes=[b_x3sb])
                for half in range(2):
                    pb, bpb = pw[2], b_pw[2]
                    fns = []
                    for hc in range(NCH):
                        fns.append(lambda e, hc=hc, s4=s4, half=half, pb=pb: e.matmul(
                            pb[:, :], hT[:, hc * 512 + s4 * 128: hc * 512 + (s4 + 1) * 128],
                            w2T[:, hc * D + half * 512: hc * D + (half + 1) * 512],
                            start=(hc == 0), stop=False))
                    fns.append(lambda e, half=half, pb=pb: e.matmul(
                        pb[:, :], consts.ones_row_bf[0:1, 0:128],
                        brow[0:1, 0 + half * 512: 0 + (half + 1) * 512], start=False, stop=True))
                    ctx.pe.emit_multi(fns, reads=[b_hT, b_w2T, b_brow, consts.b_ones_row_bf], writes=[bpb])
                    ctx.dve.emit(lambda e, half=half, pb=pb: e.tensor_mul(
                        t2[:, :], pb[:, :], x3sb[:, half * 512:(half + 1) * 512]),
                        reads=[bpb, b_x3sb], writes=[b_t2])
                    ctx.dve.emit(lambda e, t=t, half=half, ex=ex: e.scalar_tensor_tensor(
                        ybuf[:, t * D + half * 512: t * D + (half + 1) * 512],
                        t2[:, :], mbuf[:, t * E + ex:t * E + ex + 1],
                        ybuf[:, t * D + half * 512: t * D + (half + 1) * 512],
                        ALU.mult, ALU.add),
                        reads=[b_t2, b_mbuf, b_y], writes=[b_y])

    sw1h, b_sw1h = io["L0_ffn_sw1"], io["b_L0_ffn_sw1"]
    sw2h, b_sw2h = io["L0_ffn_sw2"], io["b_L0_ffn_sw2"]
    sw3h, b_sw3h = io["L0_ffn_sw3"], io["b_L0_ffn_sw3"]
    IC = 256
    nic = (2 * D) // IC
    z1sT, b_z1sT = w1T, b_w1T
    z2sT, b_z2sT = w2T, b_w2T
    swT, b_swT = xcT, b_xcT
    sw2T_, b_sw2T_ = hT, b_hT
    sw3T_, b_sw3T_ = w3T, b_w3T
    for ic in range(nic):
        emit_weight_T3(ctx, consts, swT, b_swT, nat2, b_nat2, natcnt, sw1h, b_sw1h, IC, D, ptp, b_ptp, row0=ic * IC)
        emit_weight_T3(ctx, consts, sw2T_, b_sw2T_, nat2, b_nat2, natcnt, sw2h, b_sw2h, IC, D, ptp, b_ptp, row0=ic * IC)
        emit_weight_T3(ctx, consts, sw3T_, b_sw3T_, nat2, b_nat2, natcnt, sw3h, b_sw3h, D, IC, ptp, b_ptp, col0=ic * IC)
        for isub in range(IC // 128):
            for tc in range(ntc):
                pb, bpb = pz[tc % 2], b_pz[tc % 2]
                fns = []
                for ci in range(NCH):
                    fns.append(lambda e, ci=ci, isub=isub, tc=tc, pb=pb: e.matmul(
                        pb[:, :], swT[:, ci * IC + isub * 128: ci * IC + (isub + 1) * 128],
                        xnT[:, ci * T + tc * 512: ci * T + (tc + 1) * 512],
                        start=(ci == 0), stop=(ci == NCH - 1)))
                ctx.pe.emit_multi(fns, reads=[b_swT, b_xnT], writes=[bpb])
                ctx.act.emit(lambda e, isub=isub, tc=tc, pb=pb: e.activation(
                    out=z1sT[:, isub * T + tc * 512: isub * T + (tc + 1) * 512], in_=pb[:, :], func=AF.Silu),
                    reads=[bpb], writes=[b_z1sT])
                pb2, bpb2 = pw[tc % 2], b_pw[tc % 2]
                fns = []
                for ci in range(NCH):
                    fns.append(lambda e, ci=ci, isub=isub, tc=tc, pb2=pb2: e.matmul(
                        pb2[:, :], sw2T_[:, ci * IC + isub * 128: ci * IC + (isub + 1) * 128],
                        xnT[:, ci * T + tc * 512: ci * T + (tc + 1) * 512],
                        start=(ci == 0), stop=(ci == NCH - 1)))
                ctx.pe.emit_multi(fns, reads=[b_sw2T_, b_xnT], writes=[bpb2])
                ctx.dve.emit(lambda e, isub=isub, tc=tc, pb2=pb2: e.tensor_mul(
                    z1sT[:, isub * T + tc * 512: isub * T + (tc + 1) * 512],
                    z1sT[:, isub * T + tc * 512: isub * T + (tc + 1) * 512], pb2[:, :]),
                    reads=[b_z1sT, bpb2], writes=[b_z1sT])
        for t in range(nt):
            for half in range(2):
                pb, bpb = pw[2], b_pw[2]
                fns = []
                for isub in range(IC // 128):
                    fns.append(lambda e, isub=isub, t=t, half=half, pb=pb: e.matmul(
                        pb[:, :], z1sT[:, isub * T + t * 128: isub * T + (t + 1) * 128],
                        sw3T_[:, isub * D + half * 512: isub * D + (half + 1) * 512],
                        start=(isub == 0), stop=(isub == IC // 128 - 1)))
                ctx.pe.emit_multi(fns, reads=[b_z1sT, b_sw3T_], writes=[bpb])
                ctx.dve.emit(lambda e, t=t, half=half, pb=pb: e.tensor_add(
                    ybuf[:, t * D + half * 512: t * D + (half + 1) * 512],
                    ybuf[:, t * D + half * 512: t * D + (half + 1) * 512], pb[:, :]),
                    reads=[bpb, b_y], writes=[b_y])

    for t in range(nt):
        emit_finale(ctx, consts, T, t, ybuf[:, t * D:(t + 1) * D], b_y, xin, b_xin, xout, b_xout,
                    xt[t % 2], b_xt[t % 2], t1, b_t1, sttmp, b_sttmp, r0=r0)


def build_phase0(ctx, consts, stack, T, io, tag="p0"):
    nc = ctx.nc
    nt = T // 128
    ntc = T // 512
    en = stack.enter_context
    xin, b_xin = io["x_in"], io["b_x_in"]
    xout, b_xout = io["x_out"], io["b_x_out"]
    slopes = get_slopes(H)

    qt_hbm = nc.dram_tensor(tag + "_qt", [D, T], BF16, kind="Internal").ap()
    kt_hbm = nc.dram_tensor(tag + "_kt", [D, T], BF16, kind="Internal").ap()
    v_hbm = nc.dram_tensor(tag + "_v", [T, D], BF16, kind="Internal").ap()
    b_qt = ctx.buf(tag + "_qt"); b_kt = ctx.buf(tag + "_kt"); b_v = ctx.buf(tag + "_v")

    lnst = en(nc.sbuf_tensor(tag + "_lnst", [128, nt * 2], F32)); b_lnst = ctx.buf(tag + "_lnst")
    sttmp = en(nc.sbuf_tensor(tag + "_sttmp", [128, 16], F32)); b_sttmp = ctx.buf(tag + "_sttmp")
    xt = [en(nc.sbuf_tensor(tag + "_xt%d" % i, [128, D], F32)) for i in range(2)]
    b_xt = [ctx.buf(tag + "_xt0"), ctx.buf(tag + "_xt1")]
    kpos = en(nc.sbuf_tensor(tag + "_kpos", [1, T], F32)); b_kpos = ctx.buf(tag + "_kpos")
    cones, b_cones = None, None
    ctx.pool.emit(lambda e: e.iota(kpos[0:1, :], pattern=[[1, T]], base=0, channel_multiplier=0,
                                   allow_small_or_imprecise_dtypes=True), writes=[b_kpos])

    with ctx.sub() as st0:
        en0 = st0.enter_context
        xnT = en0(nc.sbuf_tensor(tag + "_xnT", [128, NCH * T], BF16)); b_xnT = ctx.buf(tag + "_xnT")
        xnb = en0(nc.sbuf_tensor(tag + "_xnb", [128, D], BF16)); b_xnb = ctx.buf(tag + "_xnb")
        wq = en0(nc.sbuf_tensor(tag + "_wq", [128, NCH * D], BF16)); b_wq = ctx.buf(tag + "_wq")
        wk = en0(nc.sbuf_tensor(tag + "_wk", [128, NCH * D], BF16)); b_wk = ctx.buf(tag + "_wk")
        wv = en0(nc.sbuf_tensor(tag + "_wv", [128, NCH * D], BF16)); b_wv = ctx.buf(tag + "_wv")
        qst = en0(nc.sbuf_tensor(tag + "_qst", [128, 512], BF16)); b_qst = ctx.buf(tag + "_qst")
        kst = en0(nc.sbuf_tensor(tag + "_kst", [128, 512], BF16)); b_kst = ctx.buf(tag + "_kst")
        vst = en0(nc.sbuf_tensor(tag + "_vst", [128, 512], BF16)); b_vst = ctx.buf(tag + "_vst")
        ptp = [en0(nc.psum_tensor(tag + "_ptp%d" % i, [128, 128], BF16)) for i in range(2)]
        b_ptp = [ctx.buf(tag + "_ptp%d" % i) for i in range(2)]
        pqk = [en0(nc.psum_tensor(tag + "_pqk%d" % i, [128, 512], F32)) for i in range(2)]
        b_pqk = [ctx.buf(tag + "_pqk%d" % i) for i in range(2)]

        for name, wsb, b_wsb in (("L0_attn_Wq", wq, b_wq), ("L0_attn_Wk", wk, b_wk),
                                 ("L0_attn_Wv", wv, b_wv)):
            wh, b_wh = io[name], io["b_" + name]
            ctx.pool.emit(lambda e, wh=wh, wsb=wsb: e.dma_start(
                out=wsb[:, :].rearrange("p (a c) -> p a c", a=NCH),
                in_=wh[:, :].rearrange("(a p) c -> p a c", p=128)),
                reads=[b_wh], writes=[b_wsb], q=ctx.qpool)

        emit_ln_pass(ctx, consts, T, xin, b_xin, xt, b_xt, xnb, b_xnb, lnst, b_lnst,
                     sttmp, b_sttmp, xnT, b_xnT, ptp, b_ptp)

        for name, wsb, b_wsb, dst, b_dst, stg, b_stg, scale in (
                ("q", wq, b_wq, qt_hbm, b_qt, qst, b_qst, 0.125),
                ("k", wk, b_wk, kt_hbm, b_kt, kst, b_kst, 1.0)):
            for co in range(NCH):
                for tc in range(ntc):
                    pb, bpb = pqk[tc % 2], b_pqk[tc % 2]
                    fns = []
                    for ci in range(NCH):
                        fns.append(lambda e, ci=ci, co=co, tc=tc, pb=pb, wsb=wsb: e.matmul(
                            pb[:, :], wsb[:, ci * D + co * 128: ci * D + (co + 1) * 128],
                            xnT[:, ci * T + tc * 512: ci * T + (tc + 1) * 512],
                            start=(ci == 0), stop=(ci == NCH - 1)))
                    ctx.pe.emit_multi(fns, reads=[b_wsb, b_xnT], writes=[bpb])
                    ctx.act.emit(lambda e, pb=pb, stg=stg, scale=scale: e.activation(
                        out=stg[:, :], in_=pb[:, :], func=AF.Copy, scale=scale),
                        reads=[bpb], writes=[b_stg])
                    ctx.sp.emit(lambda e, co=co, tc=tc, dst=dst, stg=stg: e.dma_start(
                        out=dst[co * 128:(co + 1) * 128, tc * 512:(tc + 1) * 512], in_=stg[:, :]),
                        reads=[b_stg], writes=[b_dst], q=ctx.qsp)
        for t in range(nt):
            for half in range(2):
                pb, bpb = pqk[half], b_pqk[half]
                fns = []
                for ci in range(NCH):
                    fns.append(lambda e, ci=ci, t=t, half=half, pb=pb: e.matmul(
                        pb[:, :], xnT[:, ci * T + t * 128: ci * T + (t + 1) * 128],
                        wv[:, ci * D + half * 512: ci * D + (half + 1) * 512],
                        start=(ci == 0), stop=(ci == NCH - 1)))
                ctx.pe.emit_multi(fns, reads=[b_wv, b_xnT], writes=[bpb])
                ctx.dve.emit(lambda e, pb=pb: e.tensor_copy(vst[:, :], pb[:, :]),
                             reads=[bpb], writes=[b_vst])
                ctx.sp.emit(lambda e, t=t, half=half: e.dma_start(
                    out=v_hbm[t * 128:(t + 1) * 128, half * 512:(half + 1) * 512], in_=vst[:, :]),
                    reads=[b_vst], writes=[b_v], q=ctx.qsp)

    with ctx.sub() as st12:
        ctxfull = st12.enter_context(nc.sbuf_tensor(tag + "_ctx", [128, nt * D], BF16))
        b_ctx = ctx.buf(tag + "_ctx")
        _build_phase0_bc(ctx, consts, st12, T, io, tag, ctxfull, b_ctx, lnst, b_lnst,
                         sttmp, b_sttmp, xt, b_xt, kpos, b_kpos, cones, b_cones,
                         qt_hbm, b_qt, kt_hbm, b_kt, v_hbm, b_v)


def _build_phase0_bc(ctx, consts, stack, T, io, tag, ctxfull, b_ctx, lnst, b_lnst,
                     sttmp, b_sttmp, xt, b_xt, kpos, b_kpos, cones, b_cones,
                     qt_hbm, b_qt, kt_hbm, b_kt, v_hbm, b_v):
    nc = ctx.nc
    nt = T // 128
    nqc = T // 512
    slopes = get_slopes(H)
    xin, b_xin = io["x_in"], io["b_x_in"]
    xout, b_xout = io["x_out"], io["b_x_out"]
    with ctx.sub() as st1:
        en1 = st1.enter_context
        qaug = en1(nc.sbuf_tensor(tag + "_qaug", [128, 2 * T], BF16)); b_qaug = ctx.buf(tag + "_qaug")
        kaug = en1(nc.sbuf_tensor(tag + "_kaug", [128, 2 * T], BF16)); b_kaug = ctx.buf(tag + "_kaug")
        vp = en1(nc.sbuf_tensor(tag + "_vp", [128, nt * 128], BF16)); b_vp = ctx.buf(tag + "_vp")
        sq = en1(nc.sbuf_tensor(tag + "_sq", [64, T], BF16)); b_sq = ctx.buf(tag + "_sq")
        rows = en1(nc.sbuf_tensor(tag + "_rows", [1, 2 * T + 8], F32)); b_rows = ctx.buf(tag + "_rows")
        ckbf = en1(nc.sbuf_tensor(tag + "_ckbf", [1, T], BF16)); b_ckbf = ctx.buf(tag + "_ckbf")
        ptile = [[en1(nc.sbuf_tensor(tag + "_pt%d%d" % (i, j), [128, 512], BF16)) for j in range(2)]
                 for i in range(2)]
        b_ptile = [[ctx.buf(tag + "_pt%d%d" % (i, j)) for j in range(2)] for i in range(2)]
        zrow = en1(nc.sbuf_tensor(tag + "_zrow", [1, 2 * 512], F32)); b_zrow = ctx.buf(tag + "_zrow")
        zbc = en1(nc.sbuf_tensor(tag + "_zbc", [128, 512], F32)); b_zbc = ctx.buf(tag + "_zbc")
        mzc = en1(nc.sbuf_tensor(tag + "_mzc", [128, nt], F32)); b_mzc = ctx.buf(tag + "_mzc")
        pst = [[en1(nc.psum_tensor(tag + "_pst%d%d" % (i, j), [128, 512], F32)) for j in range(2)]
               for i in range(2)]
        b_pst = [[ctx.buf(tag + "_pst%d%d" % (i, j)) for j in range(2)] for i in range(2)]
        pctx = en1(nc.psum_tensor(tag + "_pctx", [128, 512], F32)); b_pctx = ctx.buf(tag + "_pctx")
        pz = en1(nc.psum_tensor(tag + "_pz", [128, 512], F32)); b_pz = ctx.buf(tag + "_pz")

        ctx.pool.emit(lambda e: e.memset(qaug[64:65, :], 1.0), writes=[b_qaug])
        for i in range(2 * T // 512):
            ctx.sp.emit(lambda e, i=i: e.dma_start(out=qaug[65:66, i * 512:(i + 1) * 512],
                                                   in_=consts.ones_row_bf[0:1, :]),
                        reads=[consts.b_ones_row_bf], writes=[b_qaug], q=ctx.qsp)
            ctx.sp.emit(lambda e, i=i: e.dma_start(out=kaug[66:67, i * 512:(i + 1) * 512],
                                                   in_=consts.ones_row_bf[0:1, :]),
                        reads=[consts.b_ones_row_bf], writes=[b_kaug], q=ctx.qsp)

        for pair in range(NCH):
            ctx.sp.emit(lambda e, pair=pair: e.dma_start(
                out=qaug[0:64, :].rearrange("p (two t) -> p two t", two=2),
                in_=qt_hbm[pair * 128:(pair + 1) * 128, :].rearrange("(two p) t -> p two t", p=64)),
                reads=[b_qt], writes=[b_qaug], q=ctx.qsp)
            ctx.sp.emit(lambda e, pair=pair: e.dma_start(
                out=kaug[0:64, :].rearrange("p (two t) -> p two t", two=2),
                in_=kt_hbm[pair * 128:(pair + 1) * 128, :].rearrange("(two p) t -> p two t", p=64)),
                reads=[b_kt], writes=[b_kaug], q=ctx.qsp)
            ctx.sp.emit(lambda e, pair=pair: e.dma_start(
                out=vp[:, :].rearrange("p (kt c) -> p kt c", c=128),
                in_=v_hbm[:, pair * 128:(pair + 1) * 128].rearrange("(kt p) c -> p kt c", p=128)),
                reads=[b_v], writes=[b_vp], q=ctx.qsp)
            for hb in range(2):
                c_h = slopes[2 * pair + hb] / 8.0
                ctx.act.emit(lambda e, c_h=c_h: e.activation(
                    out=rows[0:1, 0:T], in_=kpos[0:1, :], func=AF.Copy, scale=c_h),
                    reads=[b_kpos], writes=[b_rows])
                ctx.dve.emit(lambda e: e.tensor_copy(ckbf[0:1, :], rows[0:1, 0:T]),
                             reads=[b_rows], writes=[b_ckbf])
                ctx.dve.emit(lambda e, hb=hb: e.tensor_copy(kaug[64:65, hb * T:(hb + 1) * T],
                                                            ckbf[0:1, :]),
                             reads=[b_ckbf], writes=[b_kaug])
                ctx.dve.emit(lambda e: e.tensor_sub(rows[0:1, T:2 * T], rows[0:1, 0:T],
                                                    ckbf[0:1, :]),
                             reads=[b_rows, b_ckbf], writes=[b_rows])
                ctx.pool.emit(lambda e, hb=hb: e.dma_start(out=kaug[65:66, hb * T:(hb + 1) * T],
                                                           in_=rows[0:1, T:2 * T]),
                              reads=[b_rows], writes=[b_kaug], q=ctx.qpool)
                ctx.act.emit(lambda e, hb=hb: e.activation(
                    out=sq[0:64, :], in_=qaug[0:64, hb * T:(hb + 1) * T], func=AF.Square),
                    reads=[b_qaug], writes=[b_sq])
                for i in range(T // 512):
                    pb = pz if i % 2 == 0 else pctx
                    bpb = b_pz if i % 2 == 0 else b_pctx
                    ctx.pe.emit(lambda e, i=i, pb=pb: e.matmul(
                        pb[0:1, :], consts.ones_col_bf[0:64, 0:1], sq[0:64, i * 512:(i + 1) * 512],
                        start=True, stop=True),
                        reads=[b_sq, consts.b_ones_col_bf], writes=[bpb])
                    ctx.dve.emit(lambda e, i=i, pb=pb: e.tensor_copy(
                        rows[0:1, i * 512:(i + 1) * 512], pb[0:1, :]),
                        reads=[bpb], writes=[b_rows])
                ctx.act.emit(lambda e, hb=hb: e.activation(
                    out=sq[0:64, :], in_=kaug[0:64, hb * T:(hb + 1) * T], func=AF.Square),
                    reads=[b_kaug], writes=[b_sq])
                for i in range(T // 512):
                    pb = pz if i % 2 == 0 else pctx
                    bpb = b_pz if i % 2 == 0 else b_pctx
                    ctx.pe.emit(lambda e, i=i, pb=pb: e.matmul(
                        pb[0:1, :], consts.ones_col_bf[0:64, 0:1], sq[0:64, i * 512:(i + 1) * 512],
                        start=True, stop=True),
                        reads=[b_sq, consts.b_ones_col_bf], writes=[bpb])
                    ctx.dve.emit(lambda e, i=i, pb=pb: e.tensor_copy(
                        rows[0:1, T + i * 512:T + (i + 1) * 512], pb[0:1, :]),
                        reads=[bpb], writes=[b_rows])
                ctx.dve.emit(lambda e: e.reduce_max(rows[0:1, 2 * T:2 * T + 1],
                                                    rows[0:1, T:2 * T], axis=AX.X),
                             reads=[b_rows], writes=[b_rows])
                ctx.act.emit(lambda e: e.activation(out=rows[0:1, 2 * T + 1:2 * T + 2],
                                                    in_=rows[0:1, 2 * T:2 * T + 1], func=AF.Ln),
                             reads=[b_rows], writes=[b_rows])
                ctx.act.emit(lambda e: e.activation(out=rows[0:1, 2 * T + 1:2 * T + 2],
                                                    in_=rows[0:1, 2 * T + 1:2 * T + 2],
                                                    func=AF.Copy, scale=0.5),
                             reads=[b_rows], writes=[b_rows])
                ctx.act.emit(lambda e: e.activation(out=rows[0:1, 0:T], in_=rows[0:1, 0:T],
                                                    func=AF.Ln),
                             reads=[b_rows], writes=[b_rows])
                ctx.act.emit(lambda e: e.activation(out=rows[0:1, 0:T], in_=rows[0:1, 0:T],
                                                    func=AF.Exp, scale=0.5,
                                                    bias=rows[0:1, 2 * T + 1:2 * T + 2]),
                             reads=[b_rows], writes=[b_rows])
                ctx.act.emit(lambda e, c_h=c_h: e.activation(
                    out=rows[0:1, T:2 * T], in_=kpos[0:1, :], func=AF.Copy, scale=c_h),
                    reads=[b_kpos], writes=[b_rows])
                ctx.dve.emit(lambda e: e.tensor_add(rows[0:1, 0:T], rows[0:1, 0:T],
                                                    rows[0:1, T:2 * T]),
                             reads=[b_rows], writes=[b_rows])
                ctx.dve.emit(lambda e: e.tensor_scalar(rows[0:1, 0:T], rows[0:1, 0:T],
                                                       0.5, -1.0, ALU.add, ALU.mult),
                             reads=[b_rows], writes=[b_rows])
                ctx.pool.emit(lambda e, hb=hb: e.dma_start(
                    out=qaug[66:67, hb * T:(hb + 1) * T], in_=rows[0:1, 0:T]),
                    reads=[b_rows], writes=[b_qaug], q=ctx.qpool)

            for qc in range(nqc):
                q0 = qc * 512
                ctxsl = ctxfull[:, pair * T + q0: pair * T + q0 + 512]
                klast = 4 * qc + 3
                def emit_cz(kt):
                    par = kt % 2
                    ctx.pe.emit_multi([
                        (lambda e, kt=kt, par=par: e.matmul(
                            pctx[0:64, :], vp[:, kt * 128: kt * 128 + 64],
                            ptile[0][par][:, 0:512],
                            start=(kt == 0), stop=(kt == klast), tile_position=(0, 0))),
                        (lambda e, kt=kt, par=par: e.matmul(
                            pctx[64:128, :], vp[:, kt * 128 + 64: kt * 128 + 128],
                            ptile[1][par][:, 0:512],
                            start=(kt == 0), stop=(kt == klast), tile_position=(0, 64),
                            skip_group_check=True)),
                    ], reads=[b_vp, b_ptile[0][par], b_ptile[1][par]], writes=[b_pctx])
                    ctx.pe.emit_multi([
                        (lambda e, kt=kt, par=par: e.matmul(
                            pz[0:1, :], consts.ones_col_bf[:, 0:1], ptile[0][par][:, 0:512],
                            start=(kt == 0), stop=(kt == klast), tile_position=(0, 0))),
                        (lambda e, kt=kt, par=par: e.matmul(
                            pz[32:33, :], consts.ones_col_bf[:, 0:1], ptile[1][par][:, 0:512],
                            start=(kt == 0), stop=(kt == klast), tile_position=(0, 32),
                            skip_group_check=True)),
                    ], reads=[consts.b_ones_col_bf, b_ptile[0][par], b_ptile[1][par]],
                        writes=[b_pz])

                for kt in range(4 * qc + 4):
                    jd = kt - 4 * qc
                    par = kt % 2
                    for hb in range(2):
                        pb, bpb = pst[hb][par], b_pst[hb][par]
                        fns = [
                            (lambda e, kt=kt, q0=q0, hb=hb, pb=pb: e.matmul(
                                pb[:, :],
                                kaug[0:67, hb * T + kt * 128: hb * T + (kt + 1) * 128],
                                qaug[0:67, hb * T + q0: hb * T + q0 + 512],
                                start=True, stop=not (0 <= jd <= 3))),
                        ]
                        if 0 <= jd <= 3:
                            for jj in range(jd):
                                fns.append(lambda e, jj=jj, pb=pb: e.matmul(
                                    pb[:, jj * 128:(jj + 1) * 128],
                                    consts.id_bf[:, :], consts.fullmask[:, :],
                                    start=False, stop=False))
                            fns.append(lambda e, jd=jd, pb=pb: e.matmul(
                                pb[:, jd * 128:(jd + 1) * 128],
                                consts.id_bf[:, :], consts.masklt[:, :],
                                start=False, stop=True))
                        ctx.pe.emit_multi(fns, reads=[b_kaug, b_qaug, consts.b_id_bf,
                                                      consts.b_masklt, consts.b_fullmask],
                                          writes=[bpb])
                        ctx.act.emit(lambda e, hb=hb, par=par, pb=pb: e.activation(
                            out=ptile[hb][par][:, 0:512], in_=pb[:, :], func=AF.Exp),
                            reads=[bpb], writes=[b_ptile[hb][par]])
                    if kt > 0:
                        emit_cz(kt - 1)
                emit_cz(4 * qc + 3)
                ctx.dve.emit(lambda e: e.tensor_copy(zrow[0:1, 0:512], pz[0:1, :]),
                             reads=[b_pz], writes=[b_zrow])
                ctx.dve.emit(lambda e: e.tensor_copy(zrow[0:1, 512:1024], pz[32:33, :]),
                             reads=[b_pz], writes=[b_zrow])
                ctx.act.emit(lambda e: e.activation(out=zrow[0:1, :], in_=zrow[0:1, :], func=AF.Ln),
                             reads=[b_zrow], writes=[b_zrow])
                ctx.act.emit(lambda e: e.activation(out=zrow[0:1, :], in_=zrow[0:1, :], func=AF.Exp,
                                                    scale=-1.0),
                             reads=[b_zrow], writes=[b_zrow])
                ctx.pe.emit_multi([
                    (lambda e: e.matmul(pz[0:64, :], consts.ones_row[0:1, 0:64],
                                        zrow[0:1, 0:512], start=True, stop=True,
                                        tile_position=(0, 0))),
                    (lambda e: e.matmul(pz[64:128, :], consts.ones_row[0:1, 0:64],
                                        zrow[0:1, 512:1024], start=True, stop=True,
                                        tile_position=(0, 64), skip_group_check=True)),
                ], reads=[consts.b_ones_row, b_zrow], writes=[b_pz])
                ctx.dve.emit(lambda e: e.tensor_copy(zbc[:, :], pz[:, :]),
                             reads=[b_pz], writes=[b_zbc])
                ctx.dve.emit(lambda e, ctxsl=ctxsl: e.tensor_mul(ctxsl, pctx[:, :], zbc[:, :]),
                             reads=[b_pctx, b_zbc], writes=[b_ctx])

    with ctx.sub() as st2:
        en2 = st2.enter_context
        wo = en2(nc.sbuf_tensor(tag + "_wo", [128, NCH * D], BF16)); b_wo = ctx.buf(tag + "_wo")
        borow = en2(nc.sbuf_tensor(tag + "_bo", [1, D], F32)); b_bo = ctx.buf(tag + "_bo")
        cts = en2(nc.sbuf_tensor(tag + "_cts", [128, NCH * 128], BF16)); b_cts = ctx.buf(tag + "_cts")
        ao = en2(nc.sbuf_tensor(tag + "_ao", [128, D], F32)); b_ao = ctx.buf(tag + "_ao")
        t1 = en2(nc.sbuf_tensor(tag + "_t1", [128, D], F32)); b_t1 = ctx.buf(tag + "_t1")
        ptc = [en2(nc.psum_tensor(tag + "_ptc%d" % i, [128, 128], BF16)) for i in range(2)]
        b_ptc = [ctx.buf(tag + "_ptc%d" % i) for i in range(2)]
        ppr = [en2(nc.psum_tensor(tag + "_ppr%d" % i, [128, 512], F32)) for i in range(2)]
        b_ppr = [ctx.buf(tag + "_ppr%d" % i) for i in range(2)]

        woh, b_woh = io["L0_attn_Wo"], io["b_L0_attn_Wo"]
        boh, b_boh = io["L0_attn_bo"], io["b_L0_attn_bo"]
        ctx.pool.emit(lambda e: e.dma_start(
            out=wo[:, :].rearrange("p (a c) -> p a c", a=NCH),
            in_=woh[:, :].rearrange("(a p) c -> p a c", p=128)),
            reads=[b_woh], writes=[b_wo], q=ctx.qpool)
        ctx.sp.emit(lambda e: e.dma_start(out=borow[0:1, :], in_=boh[None, :]),
                    reads=[b_boh], writes=[b_bo], q=ctx.qsp)
        for t in range(nt):
            for half in range(2):
                pb, bpb = ppr[half], b_ppr[half]
                fns = []
                for ci in range(NCH):
                    fns.append(lambda e, ci=ci, t=t, half=half, pb=pb: e.matmul(
                        pb[:, :], ctxfull[:, ci * T + t * 128: ci * T + (t + 1) * 128],
                        wo[:, ci * D + half * 512: ci * D + (half + 1) * 512],
                        start=(ci == 0), stop=False))
                fns.append(lambda e, half=half, pb=pb: e.matmul(
                    pb[:, :], consts.ones_row[0:1, 0:128],
                    borow[0:1, half * 512:(half + 1) * 512], start=False, stop=True))
                ctx.pe.emit_multi(fns, reads=[b_ctx, b_wo, b_bo, consts.b_ones_row], writes=[bpb])
                ctx.dve.emit(lambda e, half=half, pb=pb: e.tensor_copy(ao[:, half * 512:(half + 1) * 512],
                                                                      pb[:, :]),
                             reads=[bpb], writes=[b_ao])
            emit_finale(ctx, consts, T, t, ao[:, :], b_ao, xin, b_xin, xout, b_xout,
                        xt[t % 2], b_xt[t % 2], t1, b_t1, sttmp, b_sttmp)


from contextlib import ExitStack as _ExitStack


def _substack(stack):
    return _ExitStack()


def build_phase3(ctx, consts, stack, T, io, tag="p3"):
    nc = ctx.nc
    nt = T // 128
    ntc = T // 512
    en = stack.enter_context
    xin, b_xin = io["x_in"], io["b_x_in"]
    xout, b_xout = io["x_out"], io["b_x_out"]
    NH, RAILS, RW = 4, 64, 16
    ZT = NH * RAILS * 64
    ZC = 1024
    nzc = ZT // ZC
    NG = RAILS // 8

    xnT = en(nc.sbuf_tensor(tag + "_xnT", [128, NCH * T], BF16)); b_xnT = ctx.buf(tag + "_xnT")
    acc = en(nc.sbuf_tensor(tag + "_acc", [128, nt * D], FP16)); b_acc = ctx.buf(tag + "_acc")
    lnst = en(nc.sbuf_tensor(tag + "_lnst", [128, nt * 2], F32)); b_lnst = ctx.buf(tag + "_lnst")
    sttmp = en(nc.sbuf_tensor(tag + "_sttmp", [128, 16], F32)); b_sttmp = ctx.buf(tag + "_sttmp")
    xt = [en(nc.sbuf_tensor(tag + "_xt%d" % i, [128, D], F32)) for i in range(2)]
    b_xt = [ctx.buf(tag + "_xt0"), ctx.buf(tag + "_xt1")]
    xnb = en(nc.sbuf_tensor(tag + "_xnb", [128, D], BF16)); b_xnb = ctx.buf(tag + "_xnb")
    t1 = en(nc.sbuf_tensor(tag + "_t1", [128, D], F32)); b_t1 = ctx.buf(tag + "_t1")
    cdiag = en(nc.sbuf_tensor(tag + "_cdiag", [128, 32 * 128], BF16)); b_cdiag = ctx.buf(tag + "_cdiag")
    hT2 = [en(nc.sbuf_tensor(tag + "_hT%d" % i, [128, (ZC // 128) * 512], BF16)) for i in range(2)]
    b_hT2 = [ctx.buf(tag + "_hT%d" % i) for i in range(2)]
    wcT = en(nc.sbuf_tensor(tag + "_wcT", [128, (ZC // 128) * D], BF16)); b_wcT = ctx.buf(tag + "_wcT")
    wcst = [en(nc.sbuf_tensor(tag + "_wcst%d" % i, [128, ZC], BF16)) for i in range(2)]
    b_wcst = [ctx.buf(tag + "_wcst0"), ctx.buf(tag + "_wcst1")]
    ao = en(nc.sbuf_tensor(tag + "_ao", [128, D], F32)); b_ao = ctx.buf(tag + "_ao")

    ptp = [en(nc.psum_tensor(tag + "_ptp%d" % i, [128, 512], BF16)) for i in range(2)]
    b_ptp = [ctx.buf(tag + "_ptp%d" % i) for i in range(2)]
    pzr = [en(nc.psum_tensor(tag + "_pzr%d" % i, [128, 512], F32)) for i in range(2)]
    b_pzr = [ctx.buf(tag + "_pzr%d" % i) for i in range(2)]
    ppr = [en(nc.psum_tensor(tag + "_ppr%d" % i, [128, 512], F32)) for i in range(2)]
    b_ppr = [ctx.buf(tag + "_ppr%d" % i) for i in range(2)]

    rmh, b_rmh = io["L1_ffn_rail_maker"], io["b_L1_ffn_rail_maker"]
    ruph, b_ruph = io["L1_ffn_rail_up_proj"], io["b_L1_ffn_rail_up_proj"]
    wch, b_wch = io["L1_ffn_w_cproj"], io["b_L1_ffn_w_cproj"]
    rm_flat = rmh.rearrange("h r i j -> (h r i) j")
    rup_flat = ruph.rearrange("h r k j -> (h r k) j")

    emit_ln_pass(ctx, consts, T, xin, b_xin, xt, b_xt, xnb, b_xnb, lnst, b_lnst,
                 sttmp, b_sttmp, xnT, b_xnT, ptp, b_ptp)

    bstack = ctx.sub(); bst = bstack.__enter__()
    enb = bst.enter_context
    rmdiag = enb(nc.sbuf_tensor(tag + "_rmd", [128, 128], BF16)); b_rmd = ctx.buf(tag + "_rmd")
    rupd = enb(nc.sbuf_tensor(tag + "_rupd", [128, 128], BF16)); b_rupd = ctx.buf(tag + "_rupd")
    rst = enb(nc.sbuf_tensor(tag + "_rst", [128, 16], BF16)); b_rst = ctx.buf(tag + "_rst")
    rtr = enb(nc.sbuf_tensor(tag + "_rtr", [16, 128], BF16)); b_rtr = ctx.buf(tag + "_rtr")
    rmTt = enb(nc.sbuf_tensor(tag + "_rmTt", [128, 128], BF16)); b_rmTt = ctx.buf(tag + "_rmTt")
    csc = enb(nc.sbuf_tensor(tag + "_csc", [32, 128], BF16)); b_csc = ctx.buf(tag + "_csc")
    for h in range(NH):
        for g in range(NG):
            gi = h * NG + g
            ctx.pool.emit(lambda e: e.memset(rmdiag[:, :], 0.0), writes=[b_rmd])
            ctx.pool.emit(lambda e: e.memset(rupd[:, :], 0.0), reads=[b_rmd] if False else [],
                          writes=[b_rupd])
            row0 = (h * RAILS + g * 8) * 16
            ctx.pool.emit(lambda e, row0=row0: e.dma_start(out=rst[:, :],
                                                           in_=rm_flat[row0:row0 + 128, :]),
                          reads=[b_rmh], writes=[b_rst], q=ctx.qpool)
            for b in range(8):
                ctx.sp.emit(lambda e, b=b: e.dma_start(
                    out=rmdiag[16 * b:16 * (b + 1), 16 * b:16 * (b + 1)],
                    in_=rst[16 * b:16 * (b + 1), :]),
                    reads=[b_rst], writes=[b_rmd], q=ctx.qsp)
            krow0 = (h * RAILS + g * 8) * 64
            for s in range(4):
                ctx.pool.emit(lambda e, krow0=krow0, s=s: e.dma_start(
                    out=rst[:, :], in_=rup_flat[krow0 + s * 128: krow0 + (s + 1) * 128, :]),
                    reads=[b_ruph], writes=[b_rst], q=ctx.qpool)
                ctx.pe.emit(lambda e: e.transpose(ptp[0][0:16, 0:128], rst[:, :], consts.id_bf[:, :]),
                            reads=[b_rst, consts.b_id_bf], writes=[b_ptp[0]])
                ctx.dve.emit(lambda e: e.tensor_copy(rtr[:, :], ptp[0][0:16, 0:128]),
                             reads=[b_ptp[0]], writes=[b_rtr])
                ctx.sp.emit(lambda e, s=s: e.dma_start(
                    out=rupd[32 * s:32 * s + 16, 0:64], in_=rtr[:, 0:64]),
                    reads=[b_rtr], writes=[b_rupd], q=ctx.qsp)
                ctx.sp.emit(lambda e, s=s: e.dma_start(
                    out=rupd[32 * s + 16:32 * s + 32, 64:128], in_=rtr[:, 64:128]),
                    reads=[b_rtr], writes=[b_rupd], q=ctx.qsp)
            ctx.pe.emit(lambda e: e.transpose(ptp[0][:, 0:128], rmdiag[:, :], consts.id_bf[:, :]),
                        reads=[b_rmd, consts.b_id_bf], writes=[b_ptp[0]])
            ctx.dve.emit(lambda e: e.tensor_copy(rmTt[:, :], ptp[0][:, 0:128]),
                         reads=[b_ptp[0]], writes=[b_rmTt])
            for s in range(4):
                pzb, b_pzb = pzr[s % 2], b_pzr[s % 2]
                ctx.pe.emit(lambda e, s=s, pzb=pzb: e.matmul(
                    pzb[0:32, 0:128], rmTt[32 * s:32 * s + 32, 32 * s:32 * s + 32],
                    rupd[32 * s:32 * s + 32, 0:128],
                    start=True, stop=True, tile_position=(32 * s, 0)),
                    reads=[b_rmTt, b_rupd], writes=[b_pzb])
                if s < 3:
                    ctx.dve.emit(lambda e, gi=gi, s=s, pzb=pzb: e.tensor_copy(
                        cdiag[32 * s:32 * s + 32, gi * 128:(gi + 1) * 128], pzb[0:32, 0:128]),
                        reads=[b_pzb], writes=[b_cdiag])
                else:
                    ctx.dve.emit(lambda e, pzb=pzb: e.tensor_copy(csc[:, :], pzb[0:32, 0:128]),
                                 reads=[b_pzb], writes=[b_csc])
                    ctx.sp.emit(lambda e, gi=gi: e.dma_start(
                        out=cdiag[96:128, gi * 128:(gi + 1) * 128], in_=csc[:, :]),
                        reads=[b_csc], writes=[b_cdiag], q=ctx.qsp)
    bstack.__exit__(None, None, None)

    ctx.pool.emit(lambda e: e.memset(acc[:, :], 0.0), writes=[b_acc])

    nsub = ZC // 128
    ngc = ZC // 512
    for zc in range(nzc):
        h = zc // (nzc // NH)
        wcTr = wcT[:, :].rearrange("p (z d) -> p z d", z=nsub)
        nb = 0
        for ci in range(NCH):
            st, b_st = wcst[ci % 2], b_wcst[ci % 2]
            ctx.pool.emit(lambda e, ci=ci, zc=zc, st=st: e.dma_start(
                out=st[:, :], in_=wch[ci * 128:(ci + 1) * 128, zc * ZC:(zc + 1) * ZC]),
                reads=[b_wch], writes=[b_st], q=ctx.qpool)
            for z0 in range(0, nsub, 4):
                zn = min(4, nsub - z0)
                pb, bpb = ptp[nb % 2], b_ptp[nb % 2]
                nb += 1
                for j in range(zn):
                    ctx.pe.emit(lambda e, z0=z0, j=j, pb=pb, st=st: e.transpose(
                        pb[:, j * 128:(j + 1) * 128], st[:, (z0 + j) * 128:(z0 + j + 1) * 128],
                        consts.id_bf[:, :]),
                        reads=[b_st, consts.b_id_bf], writes=[bpb])
                ctx.dve.emit(lambda e, ci=ci, z0=z0, zn=zn, pb=pb: e.tensor_copy(
                    wcTr[:, z0:z0 + zn, ci * 128:(ci + 1) * 128], pb[:, 0:zn * 128]
                    .rearrange("p (z d) -> p z d", z=zn)),
                    reads=[bpb], writes=[b_wcT])
        def emit_rails(tc):
            hT, b_hT = hT2[tc % 2], b_hT2[tc % 2]
            for gl in range(ngc):
                g = (zc % (nzc // NH)) * ngc + gl
                gi = h * NG + g
                for s in range(4):
                    pzb, b_pzb = pzr[s % 2], b_pzr[s % 2]
                    ctx.pe.emit(lambda e, gi=gi, g=g, tc=tc, s=s, pzb=pzb: e.matmul(
                        pzb[:, :], cdiag[32 * s:32 * s + 32, gi * 128:(gi + 1) * 128],
                        xnT[32 * s:32 * s + 32, g * T + tc * 512: g * T + (tc + 1) * 512],
                        start=True, stop=True, tile_position=(32 * s, 0)),
                        reads=[b_cdiag, b_xnT], writes=[b_pzb])
                    ctx.act.emit(lambda e, gl=gl, s=s, pzb=pzb, hT=hT: e.activation(
                        out=hT[:, (gl * 4 + s) * 512:(gl * 4 + s + 1) * 512], in_=pzb[:, :],
                        func=AF.Gelu),
                        reads=[b_pzb], writes=[b_hT])

        emit_rails(0)
        for tc in range(ntc):
            if tc + 1 < ntc:
                emit_rails(tc + 1)
            hT, b_hT = hT2[tc % 2], b_hT2[tc % 2]
            for t4 in range(4):
                t = tc * 4 + t4
                for half in range(2):
                    pb, bpb = ppr[half], b_ppr[half]
                    fns = []
                    for zs in range(nsub):
                        fns.append(lambda e, zs=zs, t4=t4, half=half, pb=pb, hT=hT: e.matmul(
                            pb[:, :], hT[:, zs * 512 + t4 * 128: zs * 512 + (t4 + 1) * 128],
                            wcT[:, zs * D + half * 512: zs * D + (half + 1) * 512],
                            start=(zs == 0), stop=(zs == nsub - 1)))
                    ctx.pe.emit_multi(fns, reads=[b_hT, b_wcT], writes=[bpb])
                    ctx.dve.emit(lambda e, t=t, half=half, pb=pb: e.tensor_add(
                        acc[:, t * D + half * 512: t * D + (half + 1) * 512],
                        acc[:, t * D + half * 512: t * D + (half + 1) * 512], pb[:, :]),
                        reads=[bpb, b_acc], writes=[b_acc])

    for t in range(nt):
        ctx.dve.emit(lambda e, t=t: e.tensor_copy(ao[:, :], acc[:, t * D:(t + 1) * D]),
                     reads=[b_acc], writes=[b_ao])
        emit_finale(ctx, consts, T, t, ao[:, :], b_ao, xin, b_xin, xout, b_xout,
                    xt[t % 2], b_xt[t % 2], t1, b_t1, sttmp, b_sttmp)



PHASE_WEIGHTS = {
    0: ["L0_attn_Wq", "L0_attn_Wk", "L0_attn_Wv", "L0_attn_Wo", "L0_attn_bo"],
    1: ["L0_ffn_gate_w", "L0_ffn_w1", "L0_ffn_b1", "L0_ffn_w2", "L0_ffn_b2",
        "L0_ffn_w3", "L0_ffn_b3", "L0_ffn_sw1", "L0_ffn_sw2", "L0_ffn_sw3"],
    2: ["L1_attn_q", "L1_attn_k", "L1_attn_log_decay", "L1_attn_norm_weight", "L1_attn_o_w"],
    3: ["L1_ffn_rail_maker", "L1_ffn_rail_up_proj", "L1_ffn_w_cproj"],
}
_SHAPES = {
    "L0_attn_Wq": [1024, 1024], "L0_attn_Wk": [1024, 1024], "L0_attn_Wv": [1024, 1024],
    "L0_attn_Wo": [1024, 1024], "L0_attn_bo": [1024],
    "L0_ffn_gate_w": [8, 1024], "L0_ffn_w1": [8, 1024, 1024], "L0_ffn_b1": [8, 1024],
    "L0_ffn_w2": [8, 1024, 1024], "L0_ffn_b2": [8, 1024], "L0_ffn_w3": [8, 1024, 1024],
    "L0_ffn_b3": [8, 1024], "L0_ffn_sw1": [2048, 1024], "L0_ffn_sw2": [2048, 1024],
    "L0_ffn_sw3": [1024, 2048],
    "L1_attn_q": [1024], "L1_attn_k": [1024], "L1_attn_log_decay": [16],
    "L1_attn_norm_weight": [1024], "L1_attn_o_w": [1024, 1024],
    "L1_ffn_rail_maker": [4, 64, 16, 16], "L1_ffn_rail_up_proj": [4, 64, 64, 16],
    "L1_ffn_w_cproj": [1024, 16384],
}
ALL_WEIGHTS = [nm for p in range(4) for nm in PHASE_WEIGHTS[p]]


def build_fused(T):
    from contextlib import ExitStack
    nc = bass.Bass("TRN2", target_bir_lowering=False, debug=False)
    ctx = Ctx(nc)
    io = {}
    for nm in ALL_WEIGHTS:
        io[nm] = nc.dram_tensor(nm, _SHAPES[nm], F32, kind="ExternalInput").ap()
        io["b_" + nm] = ctx.buf(nm)
    x0 = nc.dram_tensor("x_in", [T, D], F32, kind="ExternalInput").ap()
    y = nc.dram_tensor("x_out", [T, D], F32, kind="ExternalOutput").ap()
    xmid = [nc.dram_tensor("xmid%d" % i, [T, D], F32, kind="Internal").ap() for i in range(3)]
    b_x = [ctx.buf("x%d" % i) for i in range(5)]
    chain = [x0, xmid[0], xmid[1], xmid[2], y]
    builders = [build_phase0, build_phase1, build_phase2, build_phase3]
    with ExitStack() as stack:
        consts = Consts(ctx, stack)
        for p in range(4):
            io["x_in"], io["b_x_in"] = chain[p], b_x[p]
            io["x_out"], io["b_x_out"] = chain[p + 1], b_x[p + 1]
            with ctx.sub() as pstack:
                builders[p](ctx, consts, pstack, T, io)
        ctx.finish()
        ctx.render_all()
    nc.finalize()
    return nc


def kernel(**inputs):
    from concourse.bass_utils import run_bass_kernel_spmd
    NCORES = 8
    T = 4096
    x = np.ascontiguousarray(np.asarray(inputs["L0_attn_x"], dtype=np.float32))
    nc = build_fused(T)
    wmap = {nm: np.ascontiguousarray(np.asarray(inputs[nm], dtype=np.float32))
            for nm in ALL_WEIGHTS}
    in_maps = []
    for i in range(NCORES):
        m = {"x_in": np.ascontiguousarray(x[i])}
        m.update(wmap)
        in_maps.append(m)
    res = run_bass_kernel_spmd(nc, in_maps, core_ids=list(range(NCORES)))
    return np.stack([np.asarray(res.results[i]["x_out"]) for i in range(NCORES)], axis=0)


# version 12
# speedup vs baseline: 1.0451x; 1.0451x over previous
import math
import numpy as np
import concourse.bass as bass
import concourse.mybir as mybir

F32 = mybir.dt.float32
BF16 = mybir.dt.bfloat16
FP16 = mybir.dt.float16
AF = mybir.ActivationFunctionType
ALU = mybir.AluOpType
AX = mybir.AxisListType

D = 1024
NCH = 8
H = 16
HD = 64
EPS_LN = 1e-5
EPS_RMS = 1e-6


def get_slopes(n):
    def pow2(n):
        start = 2 ** (-2 ** (-(math.log2(n) - 3)))
        return [start * start ** i for i in range(n)]
    if math.log2(n).is_integer():
        return pow2(n)
    c = 2 ** math.floor(math.log2(n))
    return pow2(c) + get_slopes(2 * c)[0::2][:n - c]


class Buf:
    __slots__ = ("name", "w", "r")

    def __init__(self, name):
        self.name = name
        self.w = None
        self.r = {}


class Q:

    def __init__(self, nc, name, nslots=12):
        self.slots = []
        for i in range(nslots):
            s = nc.alloc_semaphore("%s%d" % (name, i))
            _SemById[s.num] = s
            self.slots.append([s, 0])
        self.i = 0

    def take(self):
        slot = self.slots[self.i]
        self.i = (self.i + 1) % len(self.slots)
        return slot


class Eng:

    def __init__(self, ctx, name, ename, sem, self_ordered):
        self.ctx = ctx
        self.name = name
        self.ename = ename
        self.sem = sem
        self.n = 0
        self.self_ordered = self_ordered
        self.known = {}
        self.prog = []
        self._waits = []

    def _dep(self, ev):
        if ev is None:
            return
        sem, val = ev
        k = sem.num
        if sem is self.sem and self.self_ordered:
            return
        if self.known.get(k, 0) >= val:
            return
        self._waits.append((sem, val))
        snap = self.ctx.snapshots.get((k, val))
        if snap:
            for k2, v2 in snap.items():
                if self.known.get(k2, 0) < v2:
                    self.known[k2] = v2
        if self.known.get(k, 0) < val:
            self.known[k] = val

    def _pre(self, reads, writes):
        self._waits = []
        for b in reads:
            self._dep(b.w)
        for b in writes:
            for k, v in b.r.items():
                self._dep((_SemById[k], v))
            self._dep(b.w)

    def _snapshot(self, ev):
        snap = dict(self.known)
        if self.self_ordered:
            snap[self.sem.num] = self.n
        k, v = ev[0].num, ev[1]
        if snap.get(k, 0) < v:
            snap[k] = v
        self.ctx.snapshots[(k, v)] = snap

    def _post(self, ev, reads, writes):
        for b in reads:
            k = ev[0].num
            if b.r.get(k, 0) < ev[1]:
                b.r[k] = ev[1]
        for b in writes:
            b.w = ev
            b.r = {}

    def emit(self, fn, reads=(), writes=(), q=None):
        return self.emit_multi([fn], reads, writes, q)

    def emit_multi(self, fns, reads=(), writes=(), q=None):
        self._pre(reads, writes)
        if q is not None:
            slot = q.take()
            self._dep((slot[0], slot[1]))
            slot[1] += 16
            ev = (slot[0], slot[1])
            self.prog.append((self._waits, fns, slot[0], 16))
        else:
            self.n += 1
            ev = (self.sem, self.n)
            self.prog.append((self._waits, fns, self.sem, 1))
        self._waits = []
        self._snapshot(ev)
        self._post(ev, reads, writes)
        return ev

    def render(self, e):
        for waits, fns, incsem, incval in self.prog:
            for (s, v) in waits:
                e.wait_ge(s, v)
            inst = None
            for fn in fns:
                inst = fn(e)
            if inst is not None:
                inst.then_inc(incsem, incval)


_SemById = {}


class Ctx:
    def __init__(self, nc):
        self.nc = nc
        self.snapshots = {}
        self.sems = {}
        for nm in ("pe", "act", "dve", "pool", "sp"):
            s = nc.alloc_semaphore("s_" + nm)
            _SemById[s.num] = s
            self.sems[nm] = s
        self.pe = Eng(self, "pe", "tensor", self.sems["pe"], False)
        self.act = Eng(self, "act", "scalar", self.sems["act"], False)
        self.dve = Eng(self, "dve", "vector", self.sems["dve"], False)
        self.pool = Eng(self, "pool", "gpsimd", self.sems["pool"], False)
        self.sp = Eng(self, "sp", "sync", self.sems["sp"], False)
        self.engines = [self.pe, self.act, self.dve, self.pool, self.sp]
        self.qsp = Q(nc, "qsp")
        self.qact = Q(nc, "qact")
        self.qpool = Q(nc, "qpool")
        self.bufs = []

    def uname(self, n):
        self._uc = getattr(self, "_uc", 0) + 1
        return "%s_u%d" % (n, self._uc)

    def buf(self, name):
        b = Buf(name)
        b.r = dict(getattr(self, "barrier", {}))
        self.bufs.append(b)
        return b

    def scope_barrier(self):
        bar = {}
        for e in self.engines:
            if e.n > 0:
                bar[e.sem.num] = e.n
        for q in (self.qsp, self.qact, self.qpool):
            for slot in q.slots:
                if slot[1] > 0:
                    bar[slot[0].num] = slot[1]
        self.barrier = bar

    def sub(self):
        from contextlib import ExitStack as _ES

        class _Scope:
            def __init__(s):
                s.st = _ES()

            def __enter__(s):
                return s.st.__enter__()

            def __exit__(s, *a):
                r = s.st.__exit__(*a)
                self.scope_barrier()
                return r
        return _Scope()

    def finish(self):
        for e in self.engines:
            e._waits = []
            for other in self.engines:
                if other is not e and other.n > 0:
                    e._dep((other.sem, other.n))
            for q in (self.qsp, self.qact, self.qpool):
                for slot in q.slots:
                    if slot[1] > 0:
                        e._dep((slot[0], slot[1]))
            if e._waits:
                e.prog.append((e._waits, [], None, 0))
                e._waits = []

    def render_all(self):
        with self.nc.Block() as blk:
            for eng in self.engines:
                getattr(blk, eng.ename)(lambda e, eng=eng: eng.render(e))


class Consts:

    def __init__(self, ctx, stack):
        nc = ctx.nc
        en = stack.enter_context
        self.id_bf = en(nc.sbuf_tensor("c_id_bf", [128, 128], BF16))
        self.id_f = en(nc.sbuf_tensor("c_id_f", [128, 128], F32))
        self.ones_col = en(nc.sbuf_tensor("c_ones_col", [128, 1], F32))
        self.ones_row = en(nc.sbuf_tensor("c_ones_row", [1, 512], F32))
        self.b_id_bf = ctx.buf("id_bf")
        self.b_id_f = ctx.buf("id_f")
        self.b_ones_col = ctx.buf("ones_col")
        self.b_ones_row = ctx.buf("ones_row")
        p, b = ctx.pool, self
        p.emit(lambda e: e.memset(self.id_bf[:, :], 0.0), writes=[b.b_id_bf])
        p.emit(lambda e: e.affine_select(out=self.id_bf[:, :], in_=self.id_bf[:, :],
                                         compare_op=ALU.not_equal, fill=1.0, base=0,
                                         pattern=[[-1, 128]], channel_multiplier=1),
               reads=[b.b_id_bf], writes=[b.b_id_bf])
        p.emit(lambda e: e.memset(self.id_f[:, :], 0.0), writes=[b.b_id_f])
        p.emit(lambda e: e.affine_select(out=self.id_f[:, :], in_=self.id_f[:, :],
                                         compare_op=ALU.not_equal, fill=1.0, base=0,
                                         pattern=[[-1, 128]], channel_multiplier=1),
               reads=[b.b_id_f], writes=[b.b_id_f])
        p.emit(lambda e: e.memset(self.ones_col[:, :], 1.0), writes=[b.b_ones_col])
        p.emit(lambda e: e.memset(self.ones_row[:, :], 1.0), writes=[b.b_ones_row])
        self.ones_row_bf = en(nc.sbuf_tensor("c_ones_row_bf", [1, 512], BF16))
        self.b_ones_row_bf = ctx.buf("ones_row_bf")
        p.emit(lambda e: e.memset(self.ones_row_bf[:, :], 1.0), writes=[b.b_ones_row_bf])
        self.ones_col_bf = en(nc.sbuf_tensor("c_ones_col_bf", [128, 1], BF16))
        self.b_ones_col_bf = ctx.buf("ones_col_bf")
        p.emit(lambda e: e.memset(self.ones_col_bf[:, :], 1.0), writes=[b.b_ones_col_bf])
        self.negbig = en(nc.sbuf_tensor("c_negbig", [1, 512], F32))
        self.b_negbig = ctx.buf("negbig")
        p.emit(lambda e: e.memset(self.negbig[:, :], -1e9), writes=[b.b_negbig])
        self.fullmask = en(nc.sbuf_tensor("c_fullmask", [128, 128], BF16))
        self.b_fullmask = ctx.buf("fullmask")
        p.emit(lambda e: e.memset(self.fullmask[:, :], -1e9), writes=[b.b_fullmask])
        self.masklt = en(nc.sbuf_tensor("c_masklt", [128, 128], BF16))
        self.b_masklt = ctx.buf("masklt")
        p.emit(lambda e: e.memset(self.masklt[:, :], 0.0), writes=[b.b_masklt])
        p.emit(lambda e: e.affine_select(out=self.masklt[:, :], in_=self.masklt[:, :],
                                         compare_op=ALU.is_ge, fill=-1e9, base=0,
                                         pattern=[[1, 128]], channel_multiplier=-1),
               reads=[b.b_masklt], writes=[b.b_masklt])
        self.maskut = en(nc.sbuf_tensor("c_maskut", [128, 128], BF16))
        self.b_maskut = ctx.buf("maskut")
        p.emit(lambda e: e.memset(self.maskut[:, :], 0.0), writes=[b.b_maskut])
        p.emit(lambda e: e.affine_select(out=self.maskut[:, :], in_=self.maskut[:, :],
                                         compare_op=ALU.is_ge, fill=-1e9, base=0,
                                         pattern=[[-1, 128]], channel_multiplier=1),
               reads=[b.b_maskut], writes=[b.b_maskut])
        self.eps = en(nc.sbuf_tensor("c_eps", [128, 2], F32))
        self.b_eps = ctx.buf("eps")
        p.emit(lambda e: e.memset(self.eps[:, 0:1], 1e-5), writes=[b.b_eps])
        p.emit(lambda e: e.memset(self.eps[:, 1:2], 1e-6), reads=[b.b_eps], writes=[b.b_eps])


def ln_stats(ctx, consts, xt_ap, b_xt, mr_ap, b_mr, tmp_ap, b_tmp, width=D):
    nchunk = (width + 511) // 512
    for i in range(nchunk):
        lo, hi = i * 512, min((i + 1) * 512, width)
        ctx.dve.emit(lambda e, i=i, lo=lo, hi=hi: e.bn_stats(tmp_ap[:, 6 * i:6 * i + 6], xt_ap[:, lo:hi]),
                     reads=[b_xt], writes=[b_tmp])
    ctx.dve.emit(lambda e: e.bn_aggr(mr_ap[:, 0:2], tmp_ap[:, 0:6 * nchunk]),
                 reads=[b_tmp], writes=[b_mr])
    ctx.act.emit(lambda e: e.activation(out=tmp_ap[:, 12:13], in_=mr_ap[:, 1:2], func=AF.Ln,
                                        bias=consts.eps[0:mr_ap.shape[0], 0:1]),
                 reads=[b_mr, consts.b_eps], writes=[b_tmp])
    ctx.act.emit(lambda e: e.activation(out=mr_ap[:, 1:2], in_=tmp_ap[:, 12:13], func=AF.Exp, scale=-0.5),
                 reads=[b_tmp], writes=[b_mr])


def transpose128(ctx, consts, dst_ap, b_dst, src_ap, b_src, psum_ap, b_psum, dtype="bf"):
    ident = consts.id_bf if dtype == "bf" else consts.id_f
    b_id = consts.b_id_bf if dtype == "bf" else consts.b_id_f
    ctx.pe.emit(lambda e: e.transpose(psum_ap, src_ap, ident[:, :]),
                reads=[b_src, b_id], writes=[b_psum])
    ctx.dve.emit(lambda e: e.tensor_copy(dst_ap, psum_ap),
                 reads=[b_psum], writes=[b_dst])


def load_weight_T(ctx, consts, stack, name, w_hbm, b_whbm, rows, cols, tp_psum, b_tp):
    nc = ctx.nc
    ra, ca = rows // 128, cols // 128
    nat = [stack.enter_context(nc.sbuf_tensor(name + "_nat%d" % i, [128, cols], BF16)) for i in range(2)]
    b_nat = [ctx.buf(name + "_nat%d" % i) for i in range(2)]
    wT = stack.enter_context(nc.sbuf_tensor(name + "_T", [128, ca * rows], BF16))
    b_wT = ctx.buf(name + "_T")
    for ai in range(ra):
        st, b_st = nat[ai % 2], b_nat[ai % 2]
        ctx.pool.emit(lambda e, ai=ai, st=st: e.dma_start(out=st[:, :], in_=w_hbm[ai * 128:(ai + 1) * 128, :]),
                      reads=[b_whbm], writes=[b_st], q=ctx.qpool)
        for bi in range(ca):
            transpose128(ctx, consts,
                         wT[:, bi * rows + ai * 128: bi * rows + (ai + 1) * 128], b_wT,
                         st[:, bi * 128:(bi + 1) * 128], b_st,
                         tp_psum, b_tp, "bf")
    return wT, b_wT


def ln_apply_cast(ctx, out_ap, b_out, x_ap, b_x, mr_ap, b_mr):
    ctx.dve.emit(lambda e: e.tensor_scalar(out_ap, x_ap, mr_ap[:, 0:1], mr_ap[:, 1:2],
                                           ALU.subtract, ALU.mult),
                 reads=[b_x, b_mr], writes=[b_out])


def build_phase2(ctx, consts, stack, T, io):
    nc = ctx.nc
    nt = T // 128
    ntc = T // 512
    en = stack.enter_context
    xin, b_xin = io["x_in"], io["b_x_in"]
    xout, b_xout = io["x_out"], io["b_x_out"]

    xn2T = en(nc.sbuf_tensor("p2_xnT", [128, NCH * T], BF16)); b_xn2T = ctx.buf("p2_xnT")
    obuf, b_obuf = xn2T, b_xn2T
    u = en(nc.sbuf_tensor("p2_u", [128, T], F32)); b_u = ctx.buf("p2_u")
    xt = [en(nc.sbuf_tensor("p2_xt%d" % i, [128, D], F32)) for i in range(2)]
    b_xt = [ctx.buf("p2_xt0"), ctx.buf("p2_xt1")]
    xnb = en(nc.sbuf_tensor("p2_xnb", [128, D], BF16)); b_xnb = ctx.buf("p2_xnb")
    lnst = en(nc.sbuf_tensor("p2_lnst", [128, nt * 2], F32)); b_lnst = ctx.buf("p2_lnst")
    sttmp = en(nc.sbuf_tensor("p2_sttmp", [128, 16], F32)); b_sttmp = ctx.buf("p2_sttmp")
    rows = en(nc.sbuf_tensor("p2_rows", [1, 2 * D + 64], F32)); b_rows = ctx.buf("p2_rows")
    lamexp = en(nc.sbuf_tensor("p2_lamexp", [1, 128], F32)); b_lamexp = ctx.buf("p2_lamexp")
    lamcol = en(nc.sbuf_tensor("p2_lamcol", [128, NCH], F32)); b_lamcol = ctx.buf("p2_lamcol")
    qkcol = en(nc.sbuf_tensor("p2_qkcol", [128, NCH], F32)); b_qkcol = ctx.buf("p2_qkcol")
    sel = en(nc.sbuf_tensor("p2_sel", [128, 2], F32)); b_sel = ctx.buf("p2_sel")
    selT = en(nc.sbuf_tensor("p2_selT", [2, 128], F32)); b_selT = ctx.buf("p2_selT")
    s2 = en(nc.sbuf_tensor("p2_s2", [2, NCH], F32)); b_s2 = ctx.buf("p2_s2")
    ccol = en(nc.sbuf_tensor("p2_ccol", [128, NCH], F32)); b_ccol = ctx.buf("p2_ccol")
    ssq = en(nc.sbuf_tensor("p2_ssq", [1, T], F32)); b_ssq = ctx.buf("p2_ssq")
    rrow = en(nc.sbuf_tensor("p2_rrow", [1, T], F32)); b_rrow = ctx.buf("p2_rrow")
    rstdc = en(nc.sbuf_tensor("p2_rstdc", [128, nt], F32)); b_rstdc = ctx.buf("p2_rstdc")
    nwcol = en(nc.sbuf_tensor("p2_nwcol", [128, NCH], F32)); b_nwcol = ctx.buf("p2_nwcol")
    ao = en(nc.sbuf_tensor("p2_ao", [128, D], F32)); b_ao = ctx.buf("p2_ao")
    t1 = en(nc.sbuf_tensor("p2_t1", [128, D], F32)); b_t1 = ctx.buf("p2_t1")
    ptp = [en(nc.psum_tensor("p2_ptp%d" % i, [128, 128], BF16)) for i in range(2)]
    b_ptp = [ctx.buf("p2_ptp%d" % i) for i in range(2)]
    ptf = en(nc.psum_tensor("p2_ptf", [128, 512], F32)); b_ptf = ctx.buf("p2_ptf")
    pmm = [en(nc.psum_tensor("p2_pmm%d" % i, [128, 512], F32)) for i in range(2)]
    b_pmm = [ctx.buf("p2_pmm%d" % i) for i in range(2)]

    for t in range(nt):
        ctx.sp.emit(lambda e, t=t: e.dma_start(out=xt[t % 2][:, :], in_=xin[t * 128:(t + 1) * 128, :]),
                    reads=[b_xin], writes=[b_xt[t % 2]], q=ctx.qsp)
        mr = lnst[:, 2 * t:2 * t + 2]
        ln_stats(ctx, consts, xt[t % 2][:, :], b_xt[t % 2], mr, b_lnst, sttmp[:, :], b_sttmp)
        ctx.dve.emit(lambda e, t=t: e.tensor_scalar(xnb[:, :], xt[t % 2][:, :],
                                                    lnst[:, 2 * t:2 * t + 1], lnst[:, 2 * t + 1:2 * t + 2],
                                                    ALU.subtract, ALU.mult),
                     reads=[b_xt[t % 2], b_lnst], writes=[b_xnb])
        for c in range(NCH):
            transpose128(ctx, consts,
                         xn2T[:, c * T + t * 128: c * T + (t + 1) * 128], b_xn2T,
                         xnb[:, c * 128:(c + 1) * 128], b_xnb,
                         ptp[c % 2][:, :], b_ptp[c % 2], "bf")

    q_h, b_q = io["L1_attn_q"], io["b_L1_attn_q"]
    k_h, b_k = io["L1_attn_k"], io["b_L1_attn_k"]
    ld_h, b_ld = io["L1_attn_log_decay"], io["b_L1_attn_log_decay"]
    nw_h, b_nw = io["L1_attn_norm_weight"], io["b_L1_attn_norm_weight"]
    ow_h, b_ow = io["L1_attn_o_w"], io["b_L1_attn_o_w"]

    ctx.sp.emit(lambda e: e.dma_start(out=rows[0:1, 0:16], in_=ld_h[None, :]),
                reads=[b_ld], writes=[b_rows], q=ctx.qsp)
    ctx.sp.emit(lambda e: e.dma_start(out=rows[0:1, 64:64 + D], in_=q_h[None, :]),
                reads=[b_q], writes=[b_rows], q=ctx.qsp)
    ctx.sp.emit(lambda e: e.dma_start(out=rows[0:1, 64 + D:64 + 2 * D], in_=k_h[None, :]),
                reads=[b_k], writes=[b_rows], q=ctx.qsp)
    ctx.act.emit(lambda e: e.activation(out=rows[0:1, 16:32], in_=rows[0:1, 0:16], func=AF.Sigmoid),
                 reads=[b_rows], writes=[b_rows])
    ctx.dve.emit(lambda e: e.tensor_mul(rows[0:1, 64:64 + D], rows[0:1, 64:64 + D], rows[0:1, 64 + D:64 + 2 * D]),
                 reads=[b_rows], writes=[b_rows])
    for c in range(NCH):
        ctx.pe.emit_multi([
            (lambda e, c=c: e.matmul(ptf[0:1, 0:64], rows[0:1, 16 + 2 * c:17 + 2 * c],
                                     consts.ones_row[0:1, 0:64], start=True, stop=True)),
            (lambda e, c=c: e.matmul(ptf[0:1, 64:128], rows[0:1, 17 + 2 * c:18 + 2 * c],
                                     consts.ones_row[0:1, 0:64], start=True, stop=True)),
        ], reads=[b_rows, consts.b_ones_row], writes=[b_ptf])
        ctx.dve.emit(lambda e: e.tensor_copy(lamexp[0:1, :], ptf[0:1, 0:128]),
                     reads=[b_ptf], writes=[b_lamexp])
        ctx.pe.emit(lambda e: e.transpose(ptf[:, 0:1], lamexp[0:1, :], consts.id_f[0:1, 0:1]),
                    reads=[b_lamexp, consts.b_id_f], writes=[b_ptf])
        ctx.dve.emit(lambda e, c=c: e.tensor_copy(lamcol[:, c:c + 1], ptf[:, 0:1]),
                     reads=[b_ptf], writes=[b_lamcol])
    for c in range(NCH):
        ctx.pe.emit(lambda e, c=c: e.transpose(ptf[:, 0:1], rows[0:1, 64 + c * 128:64 + (c + 1) * 128],
                                               consts.id_f[0:1, 0:1]),
                    reads=[b_rows, consts.b_id_f], writes=[b_ptf])
        ctx.dve.emit(lambda e, c=c: e.tensor_copy(qkcol[:, c:c + 1], ptf[:, 0:1]),
                     reads=[b_ptf], writes=[b_qkcol])
    ctx.pool.emit(lambda e: e.memset(sel[:, :], 0.0), writes=[b_sel])
    ctx.pool.emit(lambda e: e.memset(sel[0:64, 0:1], 1.0), reads=[b_sel], writes=[b_sel])
    ctx.pool.emit(lambda e: e.memset(sel[64:128, 1:2], 1.0), reads=[b_sel], writes=[b_sel])
    ctx.pe.emit(lambda e: e.matmul(ptf[0:2, 0:NCH], sel[:, :], qkcol[:, :], start=True, stop=True),
                reads=[b_sel, b_qkcol], writes=[b_ptf])
    ctx.dve.emit(lambda e: e.tensor_copy(s2[:, :], ptf[0:2, 0:NCH]),
                 reads=[b_ptf], writes=[b_s2])
    ctx.pe.emit(lambda e: e.transpose(ptf[0:2, 0:128], sel[:, :], consts.id_f[:, :]),
                reads=[b_sel, consts.b_id_f], writes=[b_ptf])
    ctx.dve.emit(lambda e: e.tensor_copy(selT[:, :], ptf[0:2, 0:128]),
                 reads=[b_ptf], writes=[b_selT])
    ctx.pe.emit(lambda e: e.matmul(ptf[:, 0:NCH], selT[:, :], s2[:, :], start=True, stop=True),
                reads=[b_selT, b_s2], writes=[b_ptf])
    ctx.dve.emit(lambda e: e.tensor_copy(ccol[:, :], ptf[:, 0:NCH]),
                 reads=[b_ptf], writes=[b_ccol])

    for c in range(NCH):
        ctx.dve.emit(lambda e, c=c: e.tensor_tensor_scan(u[:, :], lamcol[:, c:c + 1].broadcast_to((128, T)),
                                                         xn2T[:, c * T:(c + 1) * T], 0.0, ALU.mult, ALU.add),
                     reads=[b_lamcol, b_xn2T], writes=[b_u])
        ctx.dve.emit(lambda e, c=c: e.tensor_scalar_mul(obuf[:, c * T:(c + 1) * T], u[:, :], ccol[:, c:c + 1]),
                     reads=[b_u, b_ccol], writes=[b_obuf])

    ctx.dve.emit(lambda e: e.memset(ssq[0:1, :], 0.0), writes=[b_ssq])
    for c in range(NCH):
        ctx.act.emit(lambda e, c=c: e.activation(out=u[:, :], in_=obuf[:, c * T:(c + 1) * T], func=AF.Square),
                     reads=[b_obuf], writes=[b_u])
        for tc in range(ntc):
            pm = pmm[tc % 2]
            bpm = b_pmm[tc % 2]
            ctx.pe.emit(lambda e, tc=tc, pm=pm: e.matmul(pm[0:1, :], consts.ones_col[:, :], u[:, tc * 512:(tc + 1) * 512],
                                                  start=True, stop=True),
                        reads=[b_u, consts.b_ones_col], writes=[bpm])
            ctx.dve.emit(lambda e, tc=tc, pm=pm: e.tensor_add(ssq[0:1, tc * 512:(tc + 1) * 512],
                                                       ssq[0:1, tc * 512:(tc + 1) * 512], pm[0:1, :]),
                         reads=[bpm, b_ssq], writes=[b_ssq])

    ctx.act.emit(lambda e: e.activation(out=rrow[0:1, 0:T], in_=ssq[0:1, :], func=AF.Ln, scale=1.0 / D,
                                        bias=consts.eps[0:1, 1:2]),
                 reads=[b_ssq, consts.b_eps], writes=[b_rrow])
    ctx.act.emit(lambda e: e.activation(out=rrow[0:1, 0:T], in_=rrow[0:1, 0:T], func=AF.Exp, scale=-0.5),
                 reads=[b_rrow], writes=[b_rrow])
    for t in range(nt):
        ctx.pe.emit(lambda e, t=t: e.transpose(ptf[:, 0:1], rrow[0:1, t * 128:(t + 1) * 128],
                                               consts.id_f[0:1, 0:1]),
                    reads=[b_rrow, consts.b_id_f], writes=[b_ptf])
        ctx.dve.emit(lambda e, t=t: e.tensor_copy(rstdc[:, t:t + 1], ptf[:, 0:1]),
                     reads=[b_ptf], writes=[b_rstdc])

    owT, b_owT = load_weight_T(ctx, consts, stack, "p2_ow", ow_h, b_ow, D, D, ptp[0][:, :], b_ptp[0])
    ctx.sp.emit(lambda e: e.dma_start(out=rows[0:1, 64:64 + D], in_=nw_h[None, :]),
                reads=[b_nw], writes=[b_rows], q=ctx.qsp)
    for c in range(NCH):
        ctx.pe.emit(lambda e, c=c: e.transpose(ptf[:, 0:1], rows[0:1, 64 + c * 128:64 + (c + 1) * 128],
                                               consts.id_f[0:1, 0:1]),
                    reads=[b_rows, consts.b_id_f], writes=[b_ptf])
        ctx.dve.emit(lambda e, c=c: e.tensor_copy(nwcol[:, c:c + 1], ptf[:, 0:1]),
                     reads=[b_ptf], writes=[b_nwcol])
    for bi in range(NCH):
        ctx.dve.emit(lambda e, bi=bi: e.tensor_scalar_mul(owT[:, bi * D:(bi + 1) * D],
                                                          owT[:, bi * D:(bi + 1) * D], nwcol[:, bi:bi + 1]),
                     reads=[b_owT, b_nwcol], writes=[b_owT])

    for t in range(nt):
        for half in range(2):
            pm, bpm = pmm[half], b_pmm[half]
            fns = []
            for ci in range(NCH):
                fns.append(lambda e, ci=ci, half=half, t=t, pm=pm: e.matmul(
                    pm[:, :],
                    obuf[:, ci * T + t * 128: ci * T + (t + 1) * 128],
                    owT[:, ci * D + half * 512: ci * D + (half + 1) * 512],
                    start=(ci == 0), stop=(ci == NCH - 1)))
            ctx.pe.emit_multi(fns, reads=[b_obuf, b_owT], writes=[bpm])
            ctx.dve.emit(lambda e, half=half, t=t, pm=pm: e.tensor_scalar_mul(
                ao[:, half * 512:(half + 1) * 512], pm[:, :], rstdc[:, t:t + 1]),
                reads=[bpm, b_rstdc], writes=[b_ao])
        mr = sttmp[:, 13:15]
        ln_stats(ctx, consts, ao[:, :], b_ao, mr, b_sttmp, sttmp[:, :], b_sttmp)
        ctx.dve.emit(lambda e: e.tensor_scalar(t1[:, :], ao[:, :], sttmp[:, 13:14], sttmp[:, 14:15],
                                               ALU.subtract, ALU.mult),
                     reads=[b_ao, b_sttmp], writes=[b_t1])
        ctx.sp.emit(lambda e, t=t: e.dma_start(out=xt[t % 2][:, :], in_=xin[t * 128:(t + 1) * 128, :]),
                    reads=[b_xin], writes=[b_xt[t % 2]], q=ctx.qsp)
        ctx.dve.emit(lambda e, t=t: e.tensor_add(t1[:, :], t1[:, :], xt[t % 2][:, :]),
                     reads=[b_xt[t % 2], b_t1], writes=[b_t1])
        ctx.sp.emit(lambda e, t=t: e.dma_start(out=xout[t * 128:(t + 1) * 128, :], in_=t1[:, :]),
                    reads=[b_t1], writes=[b_xout], q=ctx.qsp)


def emit_weight_T(ctx, consts, wT, b_wT, stage, b_stage, w_hbm, b_w, rows, cols, tp_psum, b_tp,
                  row0=0, col0=0):
    ra, ca = rows // 128, cols // 128
    for ai in range(ra):
        st, b_st = stage[ai % 2], b_stage[ai % 2]
        ctx.pool.emit(lambda e, ai=ai, st=st: e.dma_start(
            out=st[:, 0:cols], in_=w_hbm[row0 + ai * 128:row0 + (ai + 1) * 128, col0:col0 + cols]),
            reads=[b_w], writes=[b_st], q=ctx.qpool)
        for bi in range(ca):
            transpose128(ctx, consts,
                         wT[:, bi * rows + ai * 128: bi * rows + (ai + 1) * 128], b_wT,
                         st[:, bi * 128:(bi + 1) * 128], b_st,
                         tp_psum, b_tp, "bf")


def emit_weight_T3(ctx, consts, wT, b_wT, stages, b_stages, cnt, w_hbm, b_w, rows, cols,
                   ptp2, b_ptp2, row0=0, col0=0):
    ra, ca = rows // 128, cols // 128
    for ai in range(ra):
        k = cnt[0] % len(stages)
        cnt[0] += 1
        st, b_st = stages[k], b_stages[k]
        ctx.pool.emit(lambda e, ai=ai, st=st: e.dma_start(
            out=st[:, 0:cols], in_=w_hbm[row0 + ai * 128:row0 + (ai + 1) * 128, col0:col0 + cols]),
            reads=[b_w], writes=[b_st], q=ctx.qpool)
        for b0 in range(0, ca, 4):
            bn = min(4, ca - b0)
            pb, bpb = ptp2[(ai + b0 // 4) % 2], b_ptp2[(ai + b0 // 4) % 2]
            for j in range(bn):
                ctx.pe.emit(lambda e, b0=b0, j=j, pb=pb, st=st: e.transpose(
                    pb[:, j * 128:(j + 1) * 128],
                    st[:, (b0 + j) * 128:(b0 + j + 1) * 128],
                    consts.id_bf[:, :]),
                    reads=[b_st, consts.b_id_bf], writes=[bpb])
            wTr = wT[:, 0:ca * rows].rearrange("p (b a) -> p b a", b=ca)
            ctx.dve.emit(lambda e, ai=ai, b0=b0, bn=bn, pb=pb, wTr=wTr: e.tensor_copy(
                wTr[:, b0:b0 + bn, ai * 128:(ai + 1) * 128],
                pb[:, 0:bn * 128].rearrange("p (b a) -> p b a", b=bn)),
                reads=[bpb], writes=[b_wT])


def emit_weight_T2(ctx, consts, wT, b_wT, nat, b_nat, w_hbm, b_w, rows, cols,
                   ptp2, b_ptp2, row0=0, col0=0):
    ra, ca = rows // 128, cols // 128
    if col0 == 0 and cols == w_hbm.shape[1]:
        ctx.pool.emit(lambda e: e.dma_start(
            out=nat[:, 0:ra * cols].rearrange("p (a c) -> p a c", a=ra),
            in_=w_hbm[row0:row0 + rows, :].rearrange("(a p) c -> p a c", p=128)),
            reads=[b_w], writes=[b_nat], q=ctx.qpool)
    else:
        for ai in range(ra):
            ctx.pool.emit(lambda e, ai=ai: e.dma_start(
                out=nat[:, ai * cols:(ai + 1) * cols],
                in_=w_hbm[row0 + ai * 128:row0 + (ai + 1) * 128, col0:col0 + cols]),
                reads=[b_w], writes=[b_nat], q=ctx.qpool)
    wTr = wT[:, 0:ca * rows].rearrange("p (b a) -> p b a", b=ca)
    nb = 0
    for bi in range(ca):
        for a0 in range(0, ra, 4):
            an = min(4, ra - a0)
            pb, bpb = ptp2[nb % 2], b_ptp2[nb % 2]
            nb += 1
            for j in range(an):
                ctx.pe.emit(lambda e, bi=bi, a0=a0, j=j, pb=pb: e.transpose(
                    pb[:, j * 128:(j + 1) * 128],
                    nat[:, (a0 + j) * cols + bi * 128:(a0 + j) * cols + (bi + 1) * 128],
                    consts.id_bf[:, :]),
                    reads=[b_nat, consts.b_id_bf], writes=[bpb])
            ctx.dve.emit(lambda e, bi=bi, a0=a0, an=an, pb=pb: e.tensor_copy(
                wTr[:, bi, a0 * 128:(a0 + an) * 128], pb[:, 0:an * 128]),
                reads=[bpb], writes=[b_wT])


def emit_ln_pass(ctx, consts, T, xin, b_xin, xt, b_xt, xnb, b_xnb, lnst, b_lnst,
                 sttmp, b_sttmp, xnT, b_xnT, ptp, b_ptp, r0=0):
    nt = T // 128
    for t in range(nt):
        ctx.sp.emit(lambda e, t=t: e.dma_start(out=xt[t % 2][:, :],
                                               in_=xin[r0 + t * 128:r0 + (t + 1) * 128, :]),
                    reads=[b_xin], writes=[b_xt[t % 2]], q=ctx.qsp)
        mr = lnst[:, 2 * t:2 * t + 2]
        ln_stats(ctx, consts, xt[t % 2][:, :], b_xt[t % 2], mr, b_lnst, sttmp[:, :], b_sttmp)
        ctx.dve.emit(lambda e, t=t: e.tensor_scalar(xnb[:, :], xt[t % 2][:, :],
                                                    lnst[:, 2 * t:2 * t + 1], lnst[:, 2 * t + 1:2 * t + 2],
                                                    ALU.subtract, ALU.mult),
                     reads=[b_xt[t % 2], b_lnst], writes=[b_xnb])
        for c in range(NCH):
            transpose128(ctx, consts,
                         xnT[:, c * T + t * 128: c * T + (t + 1) * 128], b_xnT,
                         xnb[:, c * 128:(c + 1) * 128], b_xnb,
                         ptp[c % 2][:, 0:128], b_ptp[c % 2], "bf")


def emit_finale(ctx, consts, T, t, src_ap, b_src, xin, b_xin, xout, b_xout,
                xt, b_xt, t1, b_t1, sttmp, b_sttmp, r0=0):
    mr = sttmp[:, 13:15]
    ln_stats(ctx, consts, src_ap, b_src, mr, b_sttmp, sttmp[:, :], b_sttmp)
    ctx.dve.emit(lambda e: e.tensor_scalar(t1[:, :], src_ap, sttmp[:, 13:14], sttmp[:, 14:15],
                                           ALU.subtract, ALU.mult),
                 reads=[b_src, b_sttmp], writes=[b_t1])
    ctx.sp.emit(lambda e, t=t: e.dma_start(out=xt[:, :], in_=xin[r0 + t * 128:r0 + (t + 1) * 128, :]),
                reads=[b_xin], writes=[b_xt], q=ctx.qsp)
    ctx.dve.emit(lambda e: e.tensor_add(t1[:, :], t1[:, :], xt[:, :]),
                 reads=[b_xt, b_t1], writes=[b_t1])
    ctx.sp.emit(lambda e, t=t: e.dma_start(out=xout[r0 + t * 128:r0 + (t + 1) * 128, :], in_=t1[:, :]),
                reads=[b_t1], writes=[b_xout], q=ctx.qsp)


def build_phase1(ctx, consts, stack, Tfull, io):
    with ctx.sub() as hstack:
        _build_phase1_half(ctx, consts, hstack, Tfull, 0, io)


def _build_phase1_half(ctx, consts, stack, T, r0, io):
    nc = ctx.nc
    nt = T // 128
    ntc = T // 512
    en = stack.enter_context
    xin, b_xin = io["x_in"], io["b_x_in"]
    xout, b_xout = io["x_out"], io["b_x_out"]
    E = 8

    xnT = en(nc.sbuf_tensor(ctx.uname("p1_xnT"), [128, NCH * T], BF16)); b_xnT = ctx.buf("p1_xnT")
    y_hbm = nc.dram_tensor(ctx.uname("p1_yhbm"), [T, D], F32, kind="Internal").ap()
    b_yh = [ctx.buf("p1_yh%d" % t) for t in range(nt)]
    ctrb = [en(nc.sbuf_tensor(ctx.uname("p1_ctrb%d" % i), [128, D], F32)) for i in range(3)]
    b_ctrb = [ctx.buf("p1_ctrb%d" % i) for i in range(3)]
    ctrbi = [0]
    xt = [en(nc.sbuf_tensor(ctx.uname("p1_xt%d" % i), [128, D], F32)) for i in range(2)]
    b_xt = [ctx.buf("p1_xt0"), ctx.buf("p1_xt1")]
    xnb = en(nc.sbuf_tensor(ctx.uname("p1_xnb"), [128, D], BF16)); b_xnb = ctx.buf("p1_xnb")
    lnst = en(nc.sbuf_tensor(ctx.uname("p1_lnst"), [128, nt * 2], F32)); b_lnst = ctx.buf("p1_lnst")
    sttmp = en(nc.sbuf_tensor(ctx.uname("p1_sttmp"), [128, 16], F32)); b_sttmp = ctx.buf("p1_sttmp")
    t1 = en(nc.sbuf_tensor(ctx.uname("p1_t1"), [128, D], F32)); b_t1 = ctx.buf("p1_t1")
    gwT = en(nc.sbuf_tensor(ctx.uname("p1_gwT"), [128, NCH * E], BF16)); b_gwT = ctx.buf("p1_gwT")
    gnat = en(nc.sbuf_tensor(ctx.uname("p1_gnat"), [8, D], BF16)); b_gnat = ctx.buf("p1_gnat")
    sc = en(nc.sbuf_tensor(ctx.uname("p1_sc"), [128, 4 * E + 4], F32)); b_sc = ctx.buf("p1_sc")
    cbuf = en(nc.sbuf_tensor(ctx.uname("p1_cbuf"), [128, nt * E], F32)); b_cbuf = ctx.buf("p1_cbuf")
    mbuf = en(nc.sbuf_tensor(ctx.uname("p1_mbuf"), [128, nt * E], F32)); b_mbuf = ctx.buf("p1_mbuf")
    crow = en(nc.sbuf_tensor(ctx.uname("p1_crow"), [1, T], BF16)); b_crow = ctx.buf("p1_crow")
    w1T = en(nc.sbuf_tensor(ctx.uname("p1_w1T"), [128, NCH * D], BF16)); b_w1T = ctx.buf("p1_w1T")
    w2T = en(nc.sbuf_tensor(ctx.uname("p1_w2T"), [128, NCH * D], BF16)); b_w2T = ctx.buf("p1_w2T")
    w3T = en(nc.sbuf_tensor(ctx.uname("p1_w3T"), [128, NCH * D], BF16)); b_w3T = ctx.buf("p1_w3T")
    nat2 = [en(nc.sbuf_tensor(ctx.uname("p1_nat%d" % i), [128, D], BF16)) for i in range(3)]
    b_nat2 = [ctx.buf("p1_nat%d" % i) for i in range(3)]
    natcnt = [0]
    brow = en(nc.sbuf_tensor(ctx.uname("p1_brow"), [1, 2 * D], BF16)); b_brow = ctx.buf("p1_brow")
    b1col = en(nc.sbuf_tensor(ctx.uname("p1_b1col"), [128, NCH], F32)); b_b1col = ctx.buf("p1_b1col")
    xcT2 = [en(nc.sbuf_tensor(ctx.uname("p1_xcT%d" % i), [128, NCH * 512], BF16)) for i in range(2)]
    b_xcT2 = [ctx.buf("p1_xcT%d" % i) for i in range(2)]
    hT = en(nc.sbuf_tensor(ctx.uname("p1_hT"), [128, NCH * 512], BF16)); b_hT = ctx.buf("p1_hT")
    x3sb = en(nc.sbuf_tensor(ctx.uname("p1_x3sb"), [128, D], F32)); b_x3sb = ctx.buf("p1_x3sb")
    t2 = en(nc.sbuf_tensor(ctx.uname("p1_t2"), [128, 512], F32)); b_t2 = ctx.buf("p1_t2")
    ptp = [en(nc.psum_tensor(ctx.uname("p1_ptp%d" % i), [128, 512], BF16)) for i in range(2)]
    b_ptp = [ctx.buf("p1_ptp%d" % i) for i in range(2)]
    pcb = en(nc.psum_tensor(ctx.uname("p1_pcb"), [128, 512], F32)); b_pcb = ctx.buf("p1_pcb")
    pz = [en(nc.psum_tensor(ctx.uname("p1_pz%d" % i), [128, 512], F32)) for i in range(2)]
    b_pz = [ctx.buf("p1_pz%d" % i) for i in range(2)]
    pw = [en(nc.psum_tensor(ctx.uname("p1_pw%d" % i), [128, 512], F32)) for i in range(3)]
    b_pw = [ctx.buf("p1_pw%d" % i) for i in range(3)]

    emit_ln_pass(ctx, consts, T, xin, b_xin, xt, b_xt, xnb, b_xnb, lnst, b_lnst,
                 sttmp, b_sttmp, xnT, b_xnT, ptp, b_ptp, r0=r0)

    gw, b_gw = io["L0_ffn_gate_w"], io["b_L0_ffn_gate_w"]
    ctx.pool.emit(lambda e: e.dma_start(out=gnat[:, :], in_=gw[:, :]),
                  reads=[b_gw], writes=[b_gnat], q=ctx.qpool)
    for ci in range(NCH):
        ctx.pe.emit(lambda e, ci=ci: e.transpose(ptp[0][:, 0:8], gnat[0:8, ci * 128:(ci + 1) * 128],
                                                 consts.id_bf[0:8, 0:8]),
                    reads=[b_gnat, consts.b_id_bf], writes=[b_ptp[0]])
        ctx.dve.emit(lambda e, ci=ci: e.tensor_copy(gwT[:, ci * E:(ci + 1) * E], ptp[0][0:128, 0:8]),
                     reads=[b_ptp[0]], writes=[b_gwT])
    for t in range(nt):
        fns = []
        for ci in range(NCH):
            fns.append(lambda e, ci=ci, t=t: e.matmul(
                pcb[:, 0:E], xnT[:, ci * T + t * 128: ci * T + (t + 1) * 128],
                gwT[:, ci * E:(ci + 1) * E], start=(ci == 0), stop=(ci == NCH - 1)))
        ctx.pe.emit_multi(fns, reads=[b_xnT, b_gwT], writes=[b_pcb])
        L, M1, M2, Z = sc[:, 0:E], sc[:, 4 * E:4 * E + 1], sc[:, 4 * E + 1:4 * E + 2], sc[:, 4 * E + 2:4 * E + 3]
        A1, A2, MS = sc[:, E:2 * E], sc[:, 2 * E:3 * E], sc[:, 3 * E:4 * E]
        ctx.dve.emit(lambda e: e.tensor_copy(L, pcb[:, 0:E]), reads=[b_pcb], writes=[b_sc])
        ctx.dve.emit(lambda e: e.reduce_max(M1, L, axis=AX.X, negate=True), reads=[b_sc], writes=[b_sc])
        ctx.act.emit(lambda e: e.activation(out=L, in_=L, func=AF.Exp, bias=M1, accum_out=Z),
                     reads=[b_sc], writes=[b_sc])
        ctx.dve.emit(lambda e: e.reciprocal(M1, Z), reads=[b_sc], writes=[b_sc])
        ctx.dve.emit(lambda e: e.tensor_scalar_mul(L, L, M1), reads=[b_sc], writes=[b_sc])
        ctx.dve.emit(lambda e: e.reduce_max(M1, L, axis=AX.X), reads=[b_sc], writes=[b_sc])
        ctx.dve.emit(lambda e: e.tensor_single_scalar(A1, L, M1, ALU.is_ge), reads=[b_sc], writes=[b_sc])
        ctx.dve.emit(lambda e: e.scalar_tensor_tensor(MS, A1, -1e30, L, ALU.mult, ALU.add),
                     reads=[b_sc], writes=[b_sc])
        ctx.dve.emit(lambda e: e.reduce_max(M2, MS, axis=AX.X), reads=[b_sc], writes=[b_sc])
        ctx.dve.emit(lambda e: e.tensor_single_scalar(A2, MS, M2, ALU.is_ge), reads=[b_sc], writes=[b_sc])
        ctx.dve.emit(lambda e, t=t: e.tensor_scalar_mul(cbuf[:, t * E:(t + 1) * E], A1, M1),
                     reads=[b_sc], writes=[b_cbuf])
        ctx.dve.emit(lambda e, t=t: e.tensor_scalar(MS, A2, M2, None, ALU.mult),
                     reads=[b_sc], writes=[b_sc])
        ctx.dve.emit(lambda e, t=t: e.tensor_add(cbuf[:, t * E:(t + 1) * E], cbuf[:, t * E:(t + 1) * E], MS),
                     reads=[b_sc, b_cbuf], writes=[b_cbuf])
        ctx.dve.emit(lambda e, t=t: e.tensor_add(mbuf[:, t * E:(t + 1) * E], A1, A2),
                     reads=[b_sc], writes=[b_mbuf])


    w1h, b_w1h = io["L0_ffn_w1"], io["b_L0_ffn_w1"]
    w2h, b_w2h = io["L0_ffn_w2"], io["b_L0_ffn_w2"]
    w3h, b_w3h = io["L0_ffn_w3"], io["b_L0_ffn_w3"]
    b1h, b_b1h = io["L0_ffn_b1"], io["b_L0_ffn_b1"]
    b2h, b_b2h = io["L0_ffn_b2"], io["b_L0_ffn_b2"]
    b3h, b_b3h = io["L0_ffn_b3"], io["b_L0_ffn_b3"]
    for ex in range(E):
        ctx.pool.emit(lambda e, ex=ex: e.dma_start(out=brow[0:1, 0:D], in_=b1h[ex][None, :]),
                      reads=[b_b1h], writes=[b_brow], q=ctx.qpool)
        for c in range(NCH):
            ctx.pe.emit(lambda e, c=c: e.transpose(ptp[0][:, 0:1], brow[0:1, c * 128:(c + 1) * 128],
                                                   consts.id_bf[0:1, 0:1]),
                        reads=[b_brow, consts.b_id_bf], writes=[b_ptp[0]])
            ctx.dve.emit(lambda e, c=c: e.tensor_copy(b1col[:, c:c + 1], ptp[0][:, 0:1]),
                         reads=[b_ptp[0]], writes=[b_b1col])
        ctx.pool.emit(lambda e, ex=ex: e.dma_start(out=brow[0:1, 0:D], in_=b2h[ex][None, :]),
                      reads=[b_b2h, b_b1col], writes=[b_brow], q=ctx.qpool)
        ctx.pool.emit(lambda e, ex=ex: e.dma_start(out=brow[0:1, D:2 * D], in_=b3h[ex][None, :]),
                      reads=[b_b3h], writes=[b_brow], q=ctx.qpool)
        emit_weight_T3(ctx, consts, w1T, b_w1T, nat2, b_nat2, natcnt, w1h[ex], b_w1h, D, D, ptp, b_ptp)
        emit_weight_T3(ctx, consts, w2T, b_w2T, nat2, b_nat2, natcnt, w2h[ex], b_w2h, D, D, ptp, b_ptp)
        emit_weight_T3(ctx, consts, w3T, b_w3T, nat2, b_nat2, natcnt, w3h[ex], b_w3h, D, D, ptp, b_ptp)
        for t in range(nt):
            ctx.pe.emit(lambda e, t=t, ex=ex: e.transpose(pcb[0:1, 0:128],
                                                          cbuf[:, t * E + ex:t * E + ex + 1],
                                                          consts.id_f[:, :]),
                        reads=[b_cbuf, consts.b_id_f], writes=[b_pcb])
            ctx.dve.emit(lambda e, t=t: e.tensor_copy(crow[0:1, t * 128:(t + 1) * 128], pcb[0:1, 0:128]),
                         reads=[b_pcb], writes=[b_crow])
        def emit_xct(tc):
            xcT, b_xcT = xcT2[tc % 2], b_xcT2[tc % 2]
            ctx.pe.emit(lambda e, tc=tc: e.matmul(pcb[:, :], consts.ones_row_bf[0:1, 0:128],
                                                  crow[0:1, tc * 512:(tc + 1) * 512], start=True, stop=True),
                        reads=[b_crow, consts.b_ones_row_bf], writes=[b_pcb])
            for ci in range(NCH):
                ctx.dve.emit(lambda e, ci=ci, tc=tc, xcT=xcT: e.tensor_mul(
                    xcT[:, ci * 512:(ci + 1) * 512],
                    xnT[:, ci * T + tc * 512: ci * T + (tc + 1) * 512], pcb[:, :]),
                    reads=[b_xnT, b_pcb], writes=[b_xcT])

        emit_xct(0)
        for tc in range(ntc):
            xcT, b_xcT = xcT2[tc % 2], b_xcT2[tc % 2]
            for hc in range(NCH):
                pb, bpb = pz[hc % 2], b_pz[hc % 2]
                fns = []
                for ci in range(NCH):
                    fns.append(lambda e, ci=ci, hc=hc, pb=pb, xcT=xcT: e.matmul(
                        pb[:, :], w1T[:, ci * D + hc * 128: ci * D + (hc + 1) * 128],
                        xcT[:, ci * 512:(ci + 1) * 512], start=(ci == 0), stop=(ci == NCH - 1)))
                ctx.pe.emit_multi(fns, reads=[b_w1T, b_xcT], writes=[bpb])
                ctx.act.emit(lambda e, hc=hc, pb=pb: e.activation(
                    out=hT[:, hc * 512:(hc + 1) * 512], in_=pb[:, :], func=AF.Silu,
                    bias=b1col[:, hc:hc + 1]),
                    reads=[bpb, b_b1col], writes=[b_hT])
            if tc + 1 < ntc:
                emit_xct(tc + 1)
            for s4 in range(4):
                t = tc * 4 + s4
                for half in range(2):
                    pb, bpb = pw[half], b_pw[half]
                    fns = []
                    for ci in range(NCH):
                        fns.append(lambda e, ci=ci, s4=s4, half=half, pb=pb, xcT=xcT: e.matmul(
                            pb[:, :], xcT[:, ci * 512 + s4 * 128: ci * 512 + (s4 + 1) * 128],
                            w3T[:, ci * D + half * 512: ci * D + (half + 1) * 512],
                            start=(ci == 0), stop=False))
                    fns.append(lambda e, half=half, pb=pb: e.matmul(
                        pb[:, :], consts.ones_row_bf[0:1, 0:128],
                        brow[0:1, D + half * 512: D + (half + 1) * 512], start=False, stop=True))
                    ctx.pe.emit_multi(fns, reads=[b_xcT, b_w3T, b_brow, consts.b_ones_row_bf], writes=[bpb])
                    ctx.act.emit(lambda e, half=half, pb=pb: e.activation(
                        out=x3sb[:, half * 512:(half + 1) * 512], in_=pb[:, :], func=AF.Copy),
                        reads=[bpb], writes=[b_x3sb])
                for half in range(2):
                    pb, bpb = pw[2], b_pw[2]
                    fns = []
                    for hc in range(NCH):
                        fns.append(lambda e, hc=hc, s4=s4, half=half, pb=pb: e.matmul(
                            pb[:, :], hT[:, hc * 512 + s4 * 128: hc * 512 + (s4 + 1) * 128],
                            w2T[:, hc * D + half * 512: hc * D + (half + 1) * 512],
                            start=(hc == 0), stop=False))
                    fns.append(lambda e, half=half, pb=pb: e.matmul(
                        pb[:, :], consts.ones_row_bf[0:1, 0:128],
                        brow[0:1, 0 + half * 512: 0 + (half + 1) * 512], start=False, stop=True))
                    ctx.pe.emit_multi(fns, reads=[b_hT, b_w2T, b_brow, consts.b_ones_row_bf], writes=[bpb])
                    ctx.dve.emit(lambda e, half=half, pb=pb: e.tensor_mul(
                        t2[:, :], pb[:, :], x3sb[:, half * 512:(half + 1) * 512]),
                        reads=[bpb, b_x3sb], writes=[b_t2])
                    cb, b_cb = ctrb[ctrbi[0] % 3], b_ctrb[ctrbi[0] % 3]
                    ctx.dve.emit(lambda e, t=t, half=half, ex=ex, cb=cb: e.tensor_scalar_mul(
                        cb[:, half * 512:(half + 1) * 512],
                        t2[:, :], mbuf[:, t * E + ex:t * E + ex + 1]),
                        reads=[b_t2, b_mbuf], writes=[b_cb])
                ctrbi[0] += 1
                ctx.pool.emit(lambda e, t=t, ex=ex, cb=cb: e.dma_start(
                    out=y_hbm[t * 128:(t + 1) * 128, :], in_=cb[:, :],
                    accum_op=(ALU.bypass if ex == 0 else ALU.add)),
                    reads=[b_cb], writes=[b_yh[t]], q=ctx.qpool)

    sw1h, b_sw1h = io["L0_ffn_sw1"], io["b_L0_ffn_sw1"]
    sw2h, b_sw2h = io["L0_ffn_sw2"], io["b_L0_ffn_sw2"]
    sw3h, b_sw3h = io["L0_ffn_sw3"], io["b_L0_ffn_sw3"]
    IC = 256
    nic = (2 * D) // IC
    z1sT, b_z1sT = w1T, b_w1T
    z2sT, b_z2sT = w2T, b_w2T
    swT, b_swT = xcT, b_xcT
    sw2T_, b_sw2T_ = hT, b_hT
    sw3T_, b_sw3T_ = w3T, b_w3T
    for ic in range(nic):
        emit_weight_T3(ctx, consts, swT, b_swT, nat2, b_nat2, natcnt, sw1h, b_sw1h, IC, D, ptp, b_ptp, row0=ic * IC)
        emit_weight_T3(ctx, consts, sw2T_, b_sw2T_, nat2, b_nat2, natcnt, sw2h, b_sw2h, IC, D, ptp, b_ptp, row0=ic * IC)
        emit_weight_T3(ctx, consts, sw3T_, b_sw3T_, nat2, b_nat2, natcnt, sw3h, b_sw3h, D, IC, ptp, b_ptp, col0=ic * IC)
        for isub in range(IC // 128):
            for tc in range(ntc):
                pb, bpb = pz[tc % 2], b_pz[tc % 2]
                fns = []
                for ci in range(NCH):
                    fns.append(lambda e, ci=ci, isub=isub, tc=tc, pb=pb: e.matmul(
                        pb[:, :], swT[:, ci * IC + isub * 128: ci * IC + (isub + 1) * 128],
                        xnT[:, ci * T + tc * 512: ci * T + (tc + 1) * 512],
                        start=(ci == 0), stop=(ci == NCH - 1)))
                ctx.pe.emit_multi(fns, reads=[b_swT, b_xnT], writes=[bpb])
                ctx.act.emit(lambda e, isub=isub, tc=tc, pb=pb: e.activation(
                    out=z1sT[:, isub * T + tc * 512: isub * T + (tc + 1) * 512], in_=pb[:, :], func=AF.Silu),
                    reads=[bpb], writes=[b_z1sT])
                pb2, bpb2 = pw[tc % 2], b_pw[tc % 2]
                fns = []
                for ci in range(NCH):
                    fns.append(lambda e, ci=ci, isub=isub, tc=tc, pb2=pb2: e.matmul(
                        pb2[:, :], sw2T_[:, ci * IC + isub * 128: ci * IC + (isub + 1) * 128],
                        xnT[:, ci * T + tc * 512: ci * T + (tc + 1) * 512],
                        start=(ci == 0), stop=(ci == NCH - 1)))
                ctx.pe.emit_multi(fns, reads=[b_sw2T_, b_xnT], writes=[bpb2])
                ctx.dve.emit(lambda e, isub=isub, tc=tc, pb2=pb2: e.tensor_mul(
                    z1sT[:, isub * T + tc * 512: isub * T + (tc + 1) * 512],
                    z1sT[:, isub * T + tc * 512: isub * T + (tc + 1) * 512], pb2[:, :]),
                    reads=[b_z1sT, bpb2], writes=[b_z1sT])
        for t in range(nt):
            for half in range(2):
                pb, bpb = pw[2], b_pw[2]
                fns = []
                for isub in range(IC // 128):
                    fns.append(lambda e, isub=isub, t=t, half=half, pb=pb: e.matmul(
                        pb[:, :], z1sT[:, isub * T + t * 128: isub * T + (t + 1) * 128],
                        sw3T_[:, isub * D + half * 512: isub * D + (half + 1) * 512],
                        start=(isub == 0), stop=(isub == IC // 128 - 1)))
                ctx.pe.emit_multi(fns, reads=[b_z1sT, b_sw3T_], writes=[bpb])
                cb, b_cb = ctrb[ctrbi[0] % 3], b_ctrb[ctrbi[0] % 3]
                ctx.dve.emit(lambda e, half=half, pb=pb, cb=cb: e.tensor_copy(
                    cb[:, half * 512:(half + 1) * 512], pb[:, :]),
                    reads=[bpb], writes=[b_cb])
                if half == 1:
                    ctrbi[0] += 1
                    ctx.pool.emit(lambda e, t=t, cb=cb: e.dma_start(
                        out=y_hbm[t * 128:(t + 1) * 128, :], in_=cb[:, :], accum_op=ALU.add),
                        reads=[b_cb], writes=[b_yh[t]], q=ctx.qpool)

    for t in range(nt):
        yb, b_yb = ctrb[t % 2], b_ctrb[t % 2]
        ctx.sp.emit(lambda e, t=t, yb=yb: e.dma_start(out=yb[:, :], in_=y_hbm[t * 128:(t + 1) * 128, :]),
                    reads=[b_yh[t]], writes=[b_yb], q=ctx.qsp)
        emit_finale(ctx, consts, T, t, yb[:, :], b_yb, xin, b_xin, xout, b_xout,
                    xt[t % 2], b_xt[t % 2], t1, b_t1, sttmp, b_sttmp, r0=r0)


def build_phase0(ctx, consts, stack, T, io, tag="p0"):
    nc = ctx.nc
    nt = T // 128
    ntc = T // 512
    en = stack.enter_context
    xin, b_xin = io["x_in"], io["b_x_in"]
    xout, b_xout = io["x_out"], io["b_x_out"]
    slopes = get_slopes(H)

    qt_hbm = nc.dram_tensor(tag + "_qt", [D, T], BF16, kind="Internal").ap()
    kt_hbm = nc.dram_tensor(tag + "_kt", [D, T], BF16, kind="Internal").ap()
    v_hbm = nc.dram_tensor(tag + "_v", [T, D], BF16, kind="Internal").ap()
    b_qt = ctx.buf(tag + "_qt"); b_kt = ctx.buf(tag + "_kt"); b_v = ctx.buf(tag + "_v")

    lnst = en(nc.sbuf_tensor(tag + "_lnst", [128, nt * 2], F32)); b_lnst = ctx.buf(tag + "_lnst")
    sttmp = en(nc.sbuf_tensor(tag + "_sttmp", [128, 16], F32)); b_sttmp = ctx.buf(tag + "_sttmp")
    xt = [en(nc.sbuf_tensor(tag + "_xt%d" % i, [128, D], F32)) for i in range(2)]
    b_xt = [ctx.buf(tag + "_xt0"), ctx.buf(tag + "_xt1")]
    kpos = en(nc.sbuf_tensor(tag + "_kpos", [1, T], F32)); b_kpos = ctx.buf(tag + "_kpos")
    cones, b_cones = None, None
    ctx.pool.emit(lambda e: e.iota(kpos[0:1, :], pattern=[[1, T]], base=0, channel_multiplier=0,
                                   allow_small_or_imprecise_dtypes=True), writes=[b_kpos])

    with ctx.sub() as st0:
        en0 = st0.enter_context
        xnT = en0(nc.sbuf_tensor(tag + "_xnT", [128, NCH * T], BF16)); b_xnT = ctx.buf(tag + "_xnT")
        xnb = en0(nc.sbuf_tensor(tag + "_xnb", [128, D], BF16)); b_xnb = ctx.buf(tag + "_xnb")
        wq = en0(nc.sbuf_tensor(tag + "_wq", [128, NCH * D], BF16)); b_wq = ctx.buf(tag + "_wq")
        wk = en0(nc.sbuf_tensor(tag + "_wk", [128, NCH * D], BF16)); b_wk = ctx.buf(tag + "_wk")
        wv = en0(nc.sbuf_tensor(tag + "_wv", [128, NCH * D], BF16)); b_wv = ctx.buf(tag + "_wv")
        qst = en0(nc.sbuf_tensor(tag + "_qst", [128, 512], BF16)); b_qst = ctx.buf(tag + "_qst")
        kst = en0(nc.sbuf_tensor(tag + "_kst", [128, 512], BF16)); b_kst = ctx.buf(tag + "_kst")
        vst = en0(nc.sbuf_tensor(tag + "_vst", [128, 512], BF16)); b_vst = ctx.buf(tag + "_vst")
        ptp = [en0(nc.psum_tensor(tag + "_ptp%d" % i, [128, 128], BF16)) for i in range(2)]
        b_ptp = [ctx.buf(tag + "_ptp%d" % i) for i in range(2)]
        pqk = [en0(nc.psum_tensor(tag + "_pqk%d" % i, [128, 512], F32)) for i in range(2)]
        b_pqk = [ctx.buf(tag + "_pqk%d" % i) for i in range(2)]

        for name, wsb, b_wsb in (("L0_attn_Wq", wq, b_wq), ("L0_attn_Wk", wk, b_wk),
                                 ("L0_attn_Wv", wv, b_wv)):
            wh, b_wh = io[name], io["b_" + name]
            ctx.pool.emit(lambda e, wh=wh, wsb=wsb: e.dma_start(
                out=wsb[:, :].rearrange("p (a c) -> p a c", a=NCH),
                in_=wh[:, :].rearrange("(a p) c -> p a c", p=128)),
                reads=[b_wh], writes=[b_wsb], q=ctx.qpool)

        emit_ln_pass(ctx, consts, T, xin, b_xin, xt, b_xt, xnb, b_xnb, lnst, b_lnst,
                     sttmp, b_sttmp, xnT, b_xnT, ptp, b_ptp)

        for name, wsb, b_wsb, dst, b_dst, stg, b_stg, scale in (
                ("q", wq, b_wq, qt_hbm, b_qt, qst, b_qst, 0.125),
                ("k", wk, b_wk, kt_hbm, b_kt, kst, b_kst, 1.0)):
            for co in range(NCH):
                for tc in range(ntc):
                    pb, bpb = pqk[tc % 2], b_pqk[tc % 2]
                    fns = []
                    for ci in range(NCH):
                        fns.append(lambda e, ci=ci, co=co, tc=tc, pb=pb, wsb=wsb: e.matmul(
                            pb[:, :], wsb[:, ci * D + co * 128: ci * D + (co + 1) * 128],
                            xnT[:, ci * T + tc * 512: ci * T + (tc + 1) * 512],
                            start=(ci == 0), stop=(ci == NCH - 1)))
                    ctx.pe.emit_multi(fns, reads=[b_wsb, b_xnT], writes=[bpb])
                    ctx.act.emit(lambda e, pb=pb, stg=stg, scale=scale: e.activation(
                        out=stg[:, :], in_=pb[:, :], func=AF.Copy, scale=scale),
                        reads=[bpb], writes=[b_stg])
                    ctx.sp.emit(lambda e, co=co, tc=tc, dst=dst, stg=stg: e.dma_start(
                        out=dst[co * 128:(co + 1) * 128, tc * 512:(tc + 1) * 512], in_=stg[:, :]),
                        reads=[b_stg], writes=[b_dst], q=ctx.qsp)
        for t in range(nt):
            for half in range(2):
                pb, bpb = pqk[half], b_pqk[half]
                fns = []
                for ci in range(NCH):
                    fns.append(lambda e, ci=ci, t=t, half=half, pb=pb: e.matmul(
                        pb[:, :], xnT[:, ci * T + t * 128: ci * T + (t + 1) * 128],
                        wv[:, ci * D + half * 512: ci * D + (half + 1) * 512],
                        start=(ci == 0), stop=(ci == NCH - 1)))
                ctx.pe.emit_multi(fns, reads=[b_wv, b_xnT], writes=[bpb])
                ctx.dve.emit(lambda e, pb=pb: e.tensor_copy(vst[:, :], pb[:, :]),
                             reads=[bpb], writes=[b_vst])
                ctx.sp.emit(lambda e, t=t, half=half: e.dma_start(
                    out=v_hbm[t * 128:(t + 1) * 128, half * 512:(half + 1) * 512], in_=vst[:, :]),
                    reads=[b_vst], writes=[b_v], q=ctx.qsp)

    with ctx.sub() as st12:
        ctxfull = st12.enter_context(nc.sbuf_tensor(tag + "_ctx", [128, nt * D], BF16))
        b_ctx = ctx.buf(tag + "_ctx")
        _build_phase0_bc(ctx, consts, st12, T, io, tag, ctxfull, b_ctx, lnst, b_lnst,
                         sttmp, b_sttmp, xt, b_xt, kpos, b_kpos, cones, b_cones,
                         qt_hbm, b_qt, kt_hbm, b_kt, v_hbm, b_v)


def _build_phase0_bc(ctx, consts, stack, T, io, tag, ctxfull, b_ctx, lnst, b_lnst,
                     sttmp, b_sttmp, xt, b_xt, kpos, b_kpos, cones, b_cones,
                     qt_hbm, b_qt, kt_hbm, b_kt, v_hbm, b_v):
    nc = ctx.nc
    nt = T // 128
    nqc = T // 512
    slopes = get_slopes(H)
    xin, b_xin = io["x_in"], io["b_x_in"]
    xout, b_xout = io["x_out"], io["b_x_out"]
    with ctx.sub() as st1:
        en1 = st1.enter_context
        qaug = en1(nc.sbuf_tensor(tag + "_qaug", [128, 2 * T], BF16)); b_qaug = ctx.buf(tag + "_qaug")
        kaug = en1(nc.sbuf_tensor(tag + "_kaug", [128, 2 * T], BF16)); b_kaug = ctx.buf(tag + "_kaug")
        vp = en1(nc.sbuf_tensor(tag + "_vp", [128, nt * 128], BF16)); b_vp = ctx.buf(tag + "_vp")
        sq = en1(nc.sbuf_tensor(tag + "_sq", [64, T], BF16)); b_sq = ctx.buf(tag + "_sq")
        rows = en1(nc.sbuf_tensor(tag + "_rows", [1, 2 * T + 8], F32)); b_rows = ctx.buf(tag + "_rows")
        ckbf = en1(nc.sbuf_tensor(tag + "_ckbf", [1, T], BF16)); b_ckbf = ctx.buf(tag + "_ckbf")
        ptile = [[en1(nc.sbuf_tensor(tag + "_pt%d%d" % (i, j), [128, 512], BF16)) for j in range(2)]
                 for i in range(2)]
        b_ptile = [[ctx.buf(tag + "_pt%d%d" % (i, j)) for j in range(2)] for i in range(2)]
        zrow = en1(nc.sbuf_tensor(tag + "_zrow", [1, 2 * 512], F32)); b_zrow = ctx.buf(tag + "_zrow")
        zbc = en1(nc.sbuf_tensor(tag + "_zbc", [128, 512], F32)); b_zbc = ctx.buf(tag + "_zbc")
        mzc = en1(nc.sbuf_tensor(tag + "_mzc", [128, nt], F32)); b_mzc = ctx.buf(tag + "_mzc")
        pst = [[en1(nc.psum_tensor(tag + "_pst%d%d" % (i, j), [128, 512], F32)) for j in range(2)]
               for i in range(2)]
        b_pst = [[ctx.buf(tag + "_pst%d%d" % (i, j)) for j in range(2)] for i in range(2)]
        pctx = en1(nc.psum_tensor(tag + "_pctx", [128, 512], F32)); b_pctx = ctx.buf(tag + "_pctx")
        pz = en1(nc.psum_tensor(tag + "_pz", [128, 512], F32)); b_pz = ctx.buf(tag + "_pz")

        ctx.pool.emit(lambda e: e.memset(qaug[64:65, :], 1.0), writes=[b_qaug])
        for i in range(2 * T // 512):
            ctx.sp.emit(lambda e, i=i: e.dma_start(out=qaug[65:66, i * 512:(i + 1) * 512],
                                                   in_=consts.ones_row_bf[0:1, :]),
                        reads=[consts.b_ones_row_bf], writes=[b_qaug], q=ctx.qsp)
            ctx.sp.emit(lambda e, i=i: e.dma_start(out=kaug[66:67, i * 512:(i + 1) * 512],
                                                   in_=consts.ones_row_bf[0:1, :]),
                        reads=[consts.b_ones_row_bf], writes=[b_kaug], q=ctx.qsp)

        for pair in range(NCH):
            ctx.sp.emit(lambda e, pair=pair: e.dma_start(
                out=qaug[0:64, :].rearrange("p (two t) -> p two t", two=2),
                in_=qt_hbm[pair * 128:(pair + 1) * 128, :].rearrange("(two p) t -> p two t", p=64)),
                reads=[b_qt], writes=[b_qaug], q=ctx.qsp)
            ctx.sp.emit(lambda e, pair=pair: e.dma_start(
                out=kaug[0:64, :].rearrange("p (two t) -> p two t", two=2),
                in_=kt_hbm[pair * 128:(pair + 1) * 128, :].rearrange("(two p) t -> p two t", p=64)),
                reads=[b_kt], writes=[b_kaug], q=ctx.qsp)
            ctx.sp.emit(lambda e, pair=pair: e.dma_start(
                out=vp[:, :].rearrange("p (kt c) -> p kt c", c=128),
                in_=v_hbm[:, pair * 128:(pair + 1) * 128].rearrange("(kt p) c -> p kt c", p=128)),
                reads=[b_v], writes=[b_vp], q=ctx.qsp)
            for hb in range(2):
                c_h = slopes[2 * pair + hb] / 8.0
                ctx.act.emit(lambda e, c_h=c_h: e.activation(
                    out=rows[0:1, 0:T], in_=kpos[0:1, :], func=AF.Copy, scale=c_h),
                    reads=[b_kpos], writes=[b_rows])
                ctx.dve.emit(lambda e: e.tensor_copy(ckbf[0:1, :], rows[0:1, 0:T]),
                             reads=[b_rows], writes=[b_ckbf])
                ctx.dve.emit(lambda e, hb=hb: e.tensor_copy(kaug[64:65, hb * T:(hb + 1) * T],
                                                            ckbf[0:1, :]),
                             reads=[b_ckbf], writes=[b_kaug])
                ctx.dve.emit(lambda e: e.tensor_sub(rows[0:1, T:2 * T], rows[0:1, 0:T],
                                                    ckbf[0:1, :]),
                             reads=[b_rows, b_ckbf], writes=[b_rows])
                ctx.pool.emit(lambda e, hb=hb: e.dma_start(out=kaug[65:66, hb * T:(hb + 1) * T],
                                                           in_=rows[0:1, T:2 * T]),
                              reads=[b_rows], writes=[b_kaug], q=ctx.qpool)
                ctx.act.emit(lambda e, hb=hb: e.activation(
                    out=sq[0:64, :], in_=qaug[0:64, hb * T:(hb + 1) * T], func=AF.Square),
                    reads=[b_qaug], writes=[b_sq])
                for i in range(T // 512):
                    pb = pz if i % 2 == 0 else pctx
                    bpb = b_pz if i % 2 == 0 else b_pctx
                    ctx.pe.emit(lambda e, i=i, pb=pb: e.matmul(
                        pb[0:1, :], consts.ones_col_bf[0:64, 0:1], sq[0:64, i * 512:(i + 1) * 512],
                        start=True, stop=True),
                        reads=[b_sq, consts.b_ones_col_bf], writes=[bpb])
                    ctx.dve.emit(lambda e, i=i, pb=pb: e.tensor_copy(
                        rows[0:1, i * 512:(i + 1) * 512], pb[0:1, :]),
                        reads=[bpb], writes=[b_rows])
                ctx.act.emit(lambda e, hb=hb: e.activation(
                    out=sq[0:64, :], in_=kaug[0:64, hb * T:(hb + 1) * T], func=AF.Square),
                    reads=[b_kaug], writes=[b_sq])
                for i in range(T // 512):
                    pb = pz if i % 2 == 0 else pctx
                    bpb = b_pz if i % 2 == 0 else b_pctx
                    ctx.pe.emit(lambda e, i=i, pb=pb: e.matmul(
                        pb[0:1, :], consts.ones_col_bf[0:64, 0:1], sq[0:64, i * 512:(i + 1) * 512],
                        start=True, stop=True),
                        reads=[b_sq, consts.b_ones_col_bf], writes=[bpb])
                    ctx.dve.emit(lambda e, i=i, pb=pb: e.tensor_copy(
                        rows[0:1, T + i * 512:T + (i + 1) * 512], pb[0:1, :]),
                        reads=[bpb], writes=[b_rows])
                ctx.dve.emit(lambda e: e.reduce_max(rows[0:1, 2 * T:2 * T + 1],
                                                    rows[0:1, T:2 * T], axis=AX.X),
                             reads=[b_rows], writes=[b_rows])
                ctx.act.emit(lambda e: e.activation(out=rows[0:1, 2 * T + 1:2 * T + 2],
                                                    in_=rows[0:1, 2 * T:2 * T + 1], func=AF.Ln),
                             reads=[b_rows], writes=[b_rows])
                ctx.act.emit(lambda e: e.activation(out=rows[0:1, 2 * T + 1:2 * T + 2],
                                                    in_=rows[0:1, 2 * T + 1:2 * T + 2],
                                                    func=AF.Copy, scale=0.5),
                             reads=[b_rows], writes=[b_rows])
                ctx.act.emit(lambda e: e.activation(out=rows[0:1, 0:T], in_=rows[0:1, 0:T],
                                                    func=AF.Ln),
                             reads=[b_rows], writes=[b_rows])
                ctx.act.emit(lambda e: e.activation(out=rows[0:1, 0:T], in_=rows[0:1, 0:T],
                                                    func=AF.Exp, scale=0.5,
                                                    bias=rows[0:1, 2 * T + 1:2 * T + 2]),
                             reads=[b_rows], writes=[b_rows])
                ctx.act.emit(lambda e, c_h=c_h: e.activation(
                    out=rows[0:1, T:2 * T], in_=kpos[0:1, :], func=AF.Copy, scale=c_h),
                    reads=[b_kpos], writes=[b_rows])
                ctx.dve.emit(lambda e: e.tensor_add(rows[0:1, 0:T], rows[0:1, 0:T],
                                                    rows[0:1, T:2 * T]),
                             reads=[b_rows], writes=[b_rows])
                ctx.dve.emit(lambda e: e.tensor_scalar(rows[0:1, 0:T], rows[0:1, 0:T],
                                                       0.5, -1.0, ALU.add, ALU.mult),
                             reads=[b_rows], writes=[b_rows])
                ctx.pool.emit(lambda e, hb=hb: e.dma_start(
                    out=qaug[66:67, hb * T:(hb + 1) * T], in_=rows[0:1, 0:T]),
                    reads=[b_rows], writes=[b_qaug], q=ctx.qpool)

            for qc in range(nqc):
                q0 = qc * 512
                ctxsl = ctxfull[:, pair * T + q0: pair * T + q0 + 512]
                klast = 4 * qc + 3
                def emit_cz(kt):
                    par = kt % 2
                    ctx.pe.emit_multi([
                        (lambda e, kt=kt, par=par: e.matmul(
                            pctx[0:64, :], vp[:, kt * 128: kt * 128 + 64],
                            ptile[0][par][:, 0:512],
                            start=(kt == 0), stop=(kt == klast), tile_position=(0, 0))),
                        (lambda e, kt=kt, par=par: e.matmul(
                            pctx[64:128, :], vp[:, kt * 128 + 64: kt * 128 + 128],
                            ptile[1][par][:, 0:512],
                            start=(kt == 0), stop=(kt == klast), tile_position=(0, 64),
                            skip_group_check=True)),
                    ], reads=[b_vp, b_ptile[0][par], b_ptile[1][par]], writes=[b_pctx])
                    ctx.pe.emit_multi([
                        (lambda e, kt=kt, par=par: e.matmul(
                            pz[0:1, :], consts.ones_col_bf[:, 0:1], ptile[0][par][:, 0:512],
                            start=(kt == 0), stop=(kt == klast), tile_position=(0, 0))),
                        (lambda e, kt=kt, par=par: e.matmul(
                            pz[32:33, :], consts.ones_col_bf[:, 0:1], ptile[1][par][:, 0:512],
                            start=(kt == 0), stop=(kt == klast), tile_position=(0, 32),
                            skip_group_check=True)),
                    ], reads=[consts.b_ones_col_bf, b_ptile[0][par], b_ptile[1][par]],
                        writes=[b_pz])

                for kt in range(4 * qc + 4):
                    jd = kt - 4 * qc
                    par = kt % 2
                    for hb in range(2):
                        pb, bpb = pst[hb][par], b_pst[hb][par]
                        fns = [
                            (lambda e, kt=kt, q0=q0, hb=hb, pb=pb: e.matmul(
                                pb[:, :],
                                kaug[0:67, hb * T + kt * 128: hb * T + (kt + 1) * 128],
                                qaug[0:67, hb * T + q0: hb * T + q0 + 512],
                                start=True, stop=not (0 <= jd <= 3))),
                        ]
                        if 0 <= jd <= 3:
                            for jj in range(jd):
                                fns.append(lambda e, jj=jj, pb=pb: e.matmul(
                                    pb[:, jj * 128:(jj + 1) * 128],
                                    consts.id_bf[:, :], consts.fullmask[:, :],
                                    start=False, stop=False))
                            fns.append(lambda e, jd=jd, pb=pb: e.matmul(
                                pb[:, jd * 128:(jd + 1) * 128],
                                consts.id_bf[:, :], consts.masklt[:, :],
                                start=False, stop=True))
                        ctx.pe.emit_multi(fns, reads=[b_kaug, b_qaug, consts.b_id_bf,
                                                      consts.b_masklt, consts.b_fullmask],
                                          writes=[bpb])
                        ctx.act.emit(lambda e, hb=hb, par=par, pb=pb: e.activation(
                            out=ptile[hb][par][:, 0:512], in_=pb[:, :], func=AF.Exp),
                            reads=[bpb], writes=[b_ptile[hb][par]])
                    if kt > 0:
                        emit_cz(kt - 1)
                emit_cz(4 * qc + 3)
                ctx.dve.emit(lambda e: e.tensor_copy(zrow[0:1, 0:512], pz[0:1, :]),
                             reads=[b_pz], writes=[b_zrow])
                ctx.dve.emit(lambda e: e.tensor_copy(zrow[0:1, 512:1024], pz[32:33, :]),
                             reads=[b_pz], writes=[b_zrow])
                ctx.act.emit(lambda e: e.activation(out=zrow[0:1, :], in_=zrow[0:1, :], func=AF.Ln),
                             reads=[b_zrow], writes=[b_zrow])
                ctx.act.emit(lambda e: e.activation(out=zrow[0:1, :], in_=zrow[0:1, :], func=AF.Exp,
                                                    scale=-1.0),
                             reads=[b_zrow], writes=[b_zrow])
                ctx.pe.emit_multi([
                    (lambda e: e.matmul(pz[0:64, :], consts.ones_row[0:1, 0:64],
                                        zrow[0:1, 0:512], start=True, stop=True,
                                        tile_position=(0, 0))),
                    (lambda e: e.matmul(pz[64:128, :], consts.ones_row[0:1, 0:64],
                                        zrow[0:1, 512:1024], start=True, stop=True,
                                        tile_position=(0, 64), skip_group_check=True)),
                ], reads=[consts.b_ones_row, b_zrow], writes=[b_pz])
                ctx.dve.emit(lambda e: e.tensor_copy(zbc[:, :], pz[:, :]),
                             reads=[b_pz], writes=[b_zbc])
                ctx.dve.emit(lambda e, ctxsl=ctxsl: e.tensor_mul(ctxsl, pctx[:, :], zbc[:, :]),
                             reads=[b_pctx, b_zbc], writes=[b_ctx])

    with ctx.sub() as st2:
        en2 = st2.enter_context
        wo = en2(nc.sbuf_tensor(tag + "_wo", [128, NCH * D], BF16)); b_wo = ctx.buf(tag + "_wo")
        borow = en2(nc.sbuf_tensor(tag + "_bo", [1, D], F32)); b_bo = ctx.buf(tag + "_bo")
        cts = en2(nc.sbuf_tensor(tag + "_cts", [128, NCH * 128], BF16)); b_cts = ctx.buf(tag + "_cts")
        ao = en2(nc.sbuf_tensor(tag + "_ao", [128, D], F32)); b_ao = ctx.buf(tag + "_ao")
        t1 = en2(nc.sbuf_tensor(tag + "_t1", [128, D], F32)); b_t1 = ctx.buf(tag + "_t1")
        ptc = [en2(nc.psum_tensor(tag + "_ptc%d" % i, [128, 128], BF16)) for i in range(2)]
        b_ptc = [ctx.buf(tag + "_ptc%d" % i) for i in range(2)]
        ppr = [en2(nc.psum_tensor(tag + "_ppr%d" % i, [128, 512], F32)) for i in range(2)]
        b_ppr = [ctx.buf(tag + "_ppr%d" % i) for i in range(2)]

        woh, b_woh = io["L0_attn_Wo"], io["b_L0_attn_Wo"]
        boh, b_boh = io["L0_attn_bo"], io["b_L0_attn_bo"]
        ctx.pool.emit(lambda e: e.dma_start(
            out=wo[:, :].rearrange("p (a c) -> p a c", a=NCH),
            in_=woh[:, :].rearrange("(a p) c -> p a c", p=128)),
            reads=[b_woh], writes=[b_wo], q=ctx.qpool)
        ctx.sp.emit(lambda e: e.dma_start(out=borow[0:1, :], in_=boh[None, :]),
                    reads=[b_boh], writes=[b_bo], q=ctx.qsp)
        for t in range(nt):
            for half in range(2):
                pb, bpb = ppr[half], b_ppr[half]
                fns = []
                for ci in range(NCH):
                    fns.append(lambda e, ci=ci, t=t, half=half, pb=pb: e.matmul(
                        pb[:, :], ctxfull[:, ci * T + t * 128: ci * T + (t + 1) * 128],
                        wo[:, ci * D + half * 512: ci * D + (half + 1) * 512],
                        start=(ci == 0), stop=False))
                fns.append(lambda e, half=half, pb=pb: e.matmul(
                    pb[:, :], consts.ones_row[0:1, 0:128],
                    borow[0:1, half * 512:(half + 1) * 512], start=False, stop=True))
                ctx.pe.emit_multi(fns, reads=[b_ctx, b_wo, b_bo, consts.b_ones_row], writes=[bpb])
                ctx.dve.emit(lambda e, half=half, pb=pb: e.tensor_copy(ao[:, half * 512:(half + 1) * 512],
                                                                      pb[:, :]),
                             reads=[bpb], writes=[b_ao])
            emit_finale(ctx, consts, T, t, ao[:, :], b_ao, xin, b_xin, xout, b_xout,
                        xt[t % 2], b_xt[t % 2], t1, b_t1, sttmp, b_sttmp)


from contextlib import ExitStack as _ExitStack


def _substack(stack):
    return _ExitStack()


def build_phase3(ctx, consts, stack, T, io, tag="p3"):
    nc = ctx.nc
    nt = T // 128
    ntc = T // 512
    en = stack.enter_context
    xin, b_xin = io["x_in"], io["b_x_in"]
    xout, b_xout = io["x_out"], io["b_x_out"]
    NH, RAILS, RW = 4, 64, 16
    ZT = NH * RAILS * 64
    ZC = 1024
    nzc = ZT // ZC
    NG = RAILS // 8

    xnT = en(nc.sbuf_tensor(tag + "_xnT", [128, NCH * T], BF16)); b_xnT = ctx.buf(tag + "_xnT")
    acc = en(nc.sbuf_tensor(tag + "_acc", [128, nt * D], FP16)); b_acc = ctx.buf(tag + "_acc")
    lnst = en(nc.sbuf_tensor(tag + "_lnst", [128, nt * 2], F32)); b_lnst = ctx.buf(tag + "_lnst")
    sttmp = en(nc.sbuf_tensor(tag + "_sttmp", [128, 16], F32)); b_sttmp = ctx.buf(tag + "_sttmp")
    xt = [en(nc.sbuf_tensor(tag + "_xt%d" % i, [128, D], F32)) for i in range(2)]
    b_xt = [ctx.buf(tag + "_xt0"), ctx.buf(tag + "_xt1")]
    xnb = en(nc.sbuf_tensor(tag + "_xnb", [128, D], BF16)); b_xnb = ctx.buf(tag + "_xnb")
    t1 = en(nc.sbuf_tensor(tag + "_t1", [128, D], F32)); b_t1 = ctx.buf(tag + "_t1")
    cdiag = en(nc.sbuf_tensor(tag + "_cdiag", [128, 32 * 128], BF16)); b_cdiag = ctx.buf(tag + "_cdiag")
    hT2 = [en(nc.sbuf_tensor(tag + "_hT%d" % i, [128, (ZC // 128) * 512], BF16)) for i in range(2)]
    b_hT2 = [ctx.buf(tag + "_hT%d" % i) for i in range(2)]
    wcT = en(nc.sbuf_tensor(tag + "_wcT", [128, (ZC // 128) * D], BF16)); b_wcT = ctx.buf(tag + "_wcT")
    wcst = [en(nc.sbuf_tensor(tag + "_wcst%d" % i, [128, ZC], BF16)) for i in range(2)]
    b_wcst = [ctx.buf(tag + "_wcst0"), ctx.buf(tag + "_wcst1")]
    ao = en(nc.sbuf_tensor(tag + "_ao", [128, D], F32)); b_ao = ctx.buf(tag + "_ao")

    ptp = [en(nc.psum_tensor(tag + "_ptp%d" % i, [128, 512], BF16)) for i in range(2)]
    b_ptp = [ctx.buf(tag + "_ptp%d" % i) for i in range(2)]
    pzr = [en(nc.psum_tensor(tag + "_pzr%d" % i, [128, 512], F32)) for i in range(2)]
    b_pzr = [ctx.buf(tag + "_pzr%d" % i) for i in range(2)]
    ppr = [en(nc.psum_tensor(tag + "_ppr%d" % i, [128, 512], F32)) for i in range(2)]
    b_ppr = [ctx.buf(tag + "_ppr%d" % i) for i in range(2)]

    rmh, b_rmh = io["L1_ffn_rail_maker"], io["b_L1_ffn_rail_maker"]
    ruph, b_ruph = io["L1_ffn_rail_up_proj"], io["b_L1_ffn_rail_up_proj"]
    wch, b_wch = io["L1_ffn_w_cproj"], io["b_L1_ffn_w_cproj"]
    rm_flat = rmh.rearrange("h r i j -> (h r i) j")
    rup_flat = ruph.rearrange("h r k j -> (h r k) j")

    emit_ln_pass(ctx, consts, T, xin, b_xin, xt, b_xt, xnb, b_xnb, lnst, b_lnst,
                 sttmp, b_sttmp, xnT, b_xnT, ptp, b_ptp)

    bstack = ctx.sub(); bst = bstack.__enter__()
    enb = bst.enter_context
    rmdiag = enb(nc.sbuf_tensor(tag + "_rmd", [128, 128], BF16)); b_rmd = ctx.buf(tag + "_rmd")
    rupd = enb(nc.sbuf_tensor(tag + "_rupd", [128, 128], BF16)); b_rupd = ctx.buf(tag + "_rupd")
    rst = enb(nc.sbuf_tensor(tag + "_rst", [128, 16], BF16)); b_rst = ctx.buf(tag + "_rst")
    rtr = enb(nc.sbuf_tensor(tag + "_rtr", [16, 128], BF16)); b_rtr = ctx.buf(tag + "_rtr")
    rmTt = enb(nc.sbuf_tensor(tag + "_rmTt", [128, 128], BF16)); b_rmTt = ctx.buf(tag + "_rmTt")
    csc = enb(nc.sbuf_tensor(tag + "_csc", [32, 128], BF16)); b_csc = ctx.buf(tag + "_csc")
    for h in range(NH):
        for g in range(NG):
            gi = h * NG + g
            ctx.pool.emit(lambda e: e.memset(rmdiag[:, :], 0.0), writes=[b_rmd])
            ctx.pool.emit(lambda e: e.memset(rupd[:, :], 0.0), reads=[b_rmd] if False else [],
                          writes=[b_rupd])
            row0 = (h * RAILS + g * 8) * 16
            ctx.pool.emit(lambda e, row0=row0: e.dma_start(out=rst[:, :],
                                                           in_=rm_flat[row0:row0 + 128, :]),
                          reads=[b_rmh], writes=[b_rst], q=ctx.qpool)
            for b in range(8):
                ctx.sp.emit(lambda e, b=b: e.dma_start(
                    out=rmdiag[16 * b:16 * (b + 1), 16 * b:16 * (b + 1)],
                    in_=rst[16 * b:16 * (b + 1), :]),
                    reads=[b_rst], writes=[b_rmd], q=ctx.qsp)
            krow0 = (h * RAILS + g * 8) * 64
            for s in range(4):
                ctx.pool.emit(lambda e, krow0=krow0, s=s: e.dma_start(
                    out=rst[:, :], in_=rup_flat[krow0 + s * 128: krow0 + (s + 1) * 128, :]),
                    reads=[b_ruph], writes=[b_rst], q=ctx.qpool)
                ctx.pe.emit(lambda e: e.transpose(ptp[0][0:16, 0:128], rst[:, :], consts.id_bf[:, :]),
                            reads=[b_rst, consts.b_id_bf], writes=[b_ptp[0]])
                ctx.dve.emit(lambda e: e.tensor_copy(rtr[:, :], ptp[0][0:16, 0:128]),
                             reads=[b_ptp[0]], writes=[b_rtr])
                ctx.sp.emit(lambda e, s=s: e.dma_start(
                    out=rupd[32 * s:32 * s + 16, 0:64], in_=rtr[:, 0:64]),
                    reads=[b_rtr], writes=[b_rupd], q=ctx.qsp)
                ctx.sp.emit(lambda e, s=s: e.dma_start(
                    out=rupd[32 * s + 16:32 * s + 32, 64:128], in_=rtr[:, 64:128]),
                    reads=[b_rtr], writes=[b_rupd], q=ctx.qsp)
            ctx.pe.emit(lambda e: e.transpose(ptp[0][:, 0:128], rmdiag[:, :], consts.id_bf[:, :]),
                        reads=[b_rmd, consts.b_id_bf], writes=[b_ptp[0]])
            ctx.dve.emit(lambda e: e.tensor_copy(rmTt[:, :], ptp[0][:, 0:128]),
                         reads=[b_ptp[0]], writes=[b_rmTt])
            for s in range(4):
                pzb, b_pzb = pzr[s % 2], b_pzr[s % 2]
                ctx.pe.emit(lambda e, s=s, pzb=pzb: e.matmul(
                    pzb[0:32, 0:128], rmTt[32 * s:32 * s + 32, 32 * s:32 * s + 32],
                    rupd[32 * s:32 * s + 32, 0:128],
                    start=True, stop=True, tile_position=(32 * s, 0)),
                    reads=[b_rmTt, b_rupd], writes=[b_pzb])
                if s < 3:
                    ctx.dve.emit(lambda e, gi=gi, s=s, pzb=pzb: e.tensor_copy(
                        cdiag[32 * s:32 * s + 32, gi * 128:(gi + 1) * 128], pzb[0:32, 0:128]),
                        reads=[b_pzb], writes=[b_cdiag])
                else:
                    ctx.dve.emit(lambda e, pzb=pzb: e.tensor_copy(csc[:, :], pzb[0:32, 0:128]),
                                 reads=[b_pzb], writes=[b_csc])
                    ctx.sp.emit(lambda e, gi=gi: e.dma_start(
                        out=cdiag[96:128, gi * 128:(gi + 1) * 128], in_=csc[:, :]),
                        reads=[b_csc], writes=[b_cdiag], q=ctx.qsp)
    bstack.__exit__(None, None, None)

    ctx.pool.emit(lambda e: e.memset(acc[:, :], 0.0), writes=[b_acc])

    nsub = ZC // 128
    ngc = ZC // 512
    for zc in range(nzc):
        h = zc // (nzc // NH)
        wcTr = wcT[:, :].rearrange("p (z d) -> p z d", z=nsub)
        nb = 0
        for ci in range(NCH):
            st, b_st = wcst[ci % 2], b_wcst[ci % 2]
            ctx.pool.emit(lambda e, ci=ci, zc=zc, st=st: e.dma_start(
                out=st[:, :], in_=wch[ci * 128:(ci + 1) * 128, zc * ZC:(zc + 1) * ZC]),
                reads=[b_wch], writes=[b_st], q=ctx.qpool)
            for z0 in range(0, nsub, 4):
                zn = min(4, nsub - z0)
                pb, bpb = ptp[nb % 2], b_ptp[nb % 2]
                nb += 1
                for j in range(zn):
                    ctx.pe.emit(lambda e, z0=z0, j=j, pb=pb, st=st: e.transpose(
                        pb[:, j * 128:(j + 1) * 128], st[:, (z0 + j) * 128:(z0 + j + 1) * 128],
                        consts.id_bf[:, :]),
                        reads=[b_st, consts.b_id_bf], writes=[bpb])
                ctx.dve.emit(lambda e, ci=ci, z0=z0, zn=zn, pb=pb: e.tensor_copy(
                    wcTr[:, z0:z0 + zn, ci * 128:(ci + 1) * 128], pb[:, 0:zn * 128]
                    .rearrange("p (z d) -> p z d", z=zn)),
                    reads=[bpb], writes=[b_wcT])
        def emit_rails(tc):
            hT, b_hT = hT2[tc % 2], b_hT2[tc % 2]
            for gl in range(ngc):
                g = (zc % (nzc // NH)) * ngc + gl
                gi = h * NG + g
                for s in range(4):
                    pzb, b_pzb = pzr[s % 2], b_pzr[s % 2]
                    ctx.pe.emit(lambda e, gi=gi, g=g, tc=tc, s=s, pzb=pzb: e.matmul(
                        pzb[:, :], cdiag[32 * s:32 * s + 32, gi * 128:(gi + 1) * 128],
                        xnT[32 * s:32 * s + 32, g * T + tc * 512: g * T + (tc + 1) * 512],
                        start=True, stop=True, tile_position=(32 * s, 0)),
                        reads=[b_cdiag, b_xnT], writes=[b_pzb])
                    ctx.act.emit(lambda e, gl=gl, s=s, pzb=pzb, hT=hT: e.activation(
                        out=hT[:, (gl * 4 + s) * 512:(gl * 4 + s + 1) * 512], in_=pzb[:, :],
                        func=AF.Gelu),
                        reads=[b_pzb], writes=[b_hT])

        emit_rails(0)
        for tc in range(ntc):
            if tc + 1 < ntc:
                emit_rails(tc + 1)
            hT, b_hT = hT2[tc % 2], b_hT2[tc % 2]
            for t4 in range(4):
                t = tc * 4 + t4
                for half in range(2):
                    pb, bpb = ppr[half], b_ppr[half]
                    fns = []
                    for zs in range(nsub):
                        fns.append(lambda e, zs=zs, t4=t4, half=half, pb=pb, hT=hT: e.matmul(
                            pb[:, :], hT[:, zs * 512 + t4 * 128: zs * 512 + (t4 + 1) * 128],
                            wcT[:, zs * D + half * 512: zs * D + (half + 1) * 512],
                            start=(zs == 0), stop=(zs == nsub - 1)))
                    ctx.pe.emit_multi(fns, reads=[b_hT, b_wcT], writes=[bpb])
                    ctx.dve.emit(lambda e, t=t, half=half, pb=pb: e.tensor_add(
                        acc[:, t * D + half * 512: t * D + (half + 1) * 512],
                        acc[:, t * D + half * 512: t * D + (half + 1) * 512], pb[:, :]),
                        reads=[bpb, b_acc], writes=[b_acc])

    for t in range(nt):
        ctx.dve.emit(lambda e, t=t: e.tensor_copy(ao[:, :], acc[:, t * D:(t + 1) * D]),
                     reads=[b_acc], writes=[b_ao])
        emit_finale(ctx, consts, T, t, ao[:, :], b_ao, xin, b_xin, xout, b_xout,
                    xt[t % 2], b_xt[t % 2], t1, b_t1, sttmp, b_sttmp)



PHASE_WEIGHTS = {
    0: ["L0_attn_Wq", "L0_attn_Wk", "L0_attn_Wv", "L0_attn_Wo", "L0_attn_bo"],
    1: ["L0_ffn_gate_w", "L0_ffn_w1", "L0_ffn_b1", "L0_ffn_w2", "L0_ffn_b2",
        "L0_ffn_w3", "L0_ffn_b3", "L0_ffn_sw1", "L0_ffn_sw2", "L0_ffn_sw3"],
    2: ["L1_attn_q", "L1_attn_k", "L1_attn_log_decay", "L1_attn_norm_weight", "L1_attn_o_w"],
    3: ["L1_ffn_rail_maker", "L1_ffn_rail_up_proj", "L1_ffn_w_cproj"],
}
_SHAPES = {
    "L0_attn_Wq": [1024, 1024], "L0_attn_Wk": [1024, 1024], "L0_attn_Wv": [1024, 1024],
    "L0_attn_Wo": [1024, 1024], "L0_attn_bo": [1024],
    "L0_ffn_gate_w": [8, 1024], "L0_ffn_w1": [8, 1024, 1024], "L0_ffn_b1": [8, 1024],
    "L0_ffn_w2": [8, 1024, 1024], "L0_ffn_b2": [8, 1024], "L0_ffn_w3": [8, 1024, 1024],
    "L0_ffn_b3": [8, 1024], "L0_ffn_sw1": [2048, 1024], "L0_ffn_sw2": [2048, 1024],
    "L0_ffn_sw3": [1024, 2048],
    "L1_attn_q": [1024], "L1_attn_k": [1024], "L1_attn_log_decay": [16],
    "L1_attn_norm_weight": [1024], "L1_attn_o_w": [1024, 1024],
    "L1_ffn_rail_maker": [4, 64, 16, 16], "L1_ffn_rail_up_proj": [4, 64, 64, 16],
    "L1_ffn_w_cproj": [1024, 16384],
}
ALL_WEIGHTS = [nm for p in range(4) for nm in PHASE_WEIGHTS[p]]


def build_fused(T):
    from contextlib import ExitStack
    nc = bass.Bass("TRN2", target_bir_lowering=False, debug=False)
    ctx = Ctx(nc)
    io = {}
    for nm in ALL_WEIGHTS:
        io[nm] = nc.dram_tensor(nm, _SHAPES[nm], F32, kind="ExternalInput").ap()
        io["b_" + nm] = ctx.buf(nm)
    x0 = nc.dram_tensor("x_in", [T, D], F32, kind="ExternalInput").ap()
    y = nc.dram_tensor("x_out", [T, D], F32, kind="ExternalOutput").ap()
    xmid = [nc.dram_tensor("xmid%d" % i, [T, D], F32, kind="Internal").ap() for i in range(3)]
    b_x = [ctx.buf("x%d" % i) for i in range(5)]
    chain = [x0, xmid[0], xmid[1], xmid[2], y]
    builders = [build_phase0, build_phase1, build_phase2, build_phase3]
    with ExitStack() as stack:
        consts = Consts(ctx, stack)
        for p in range(4):
            io["x_in"], io["b_x_in"] = chain[p], b_x[p]
            io["x_out"], io["b_x_out"] = chain[p + 1], b_x[p + 1]
            with ctx.sub() as pstack:
                builders[p](ctx, consts, pstack, T, io)
        ctx.finish()
        ctx.render_all()
    nc.finalize()
    return nc


def kernel(**inputs):
    from concourse.bass_utils import run_bass_kernel_spmd
    NCORES = 8
    T = 4096
    x = np.ascontiguousarray(np.asarray(inputs["L0_attn_x"], dtype=np.float32))
    nc = build_fused(T)
    wmap = {nm: np.ascontiguousarray(np.asarray(inputs[nm], dtype=np.float32))
            for nm in ALL_WEIGHTS}
    in_maps = []
    for i in range(NCORES):
        m = {"x_in": np.ascontiguousarray(x[i])}
        m.update(wmap)
        in_maps.append(m)
    res = run_bass_kernel_spmd(nc, in_maps, core_ids=list(range(NCORES)))
    return np.stack([np.asarray(res.results[i]["x_out"]) for i in range(NCORES)], axis=0)


# version 13
# speedup vs baseline: 1.0526x; 1.0072x over previous
import math
import numpy as np
import concourse.bass as bass
import concourse.mybir as mybir

F32 = mybir.dt.float32
BF16 = mybir.dt.bfloat16
FP16 = mybir.dt.float16
AF = mybir.ActivationFunctionType
ALU = mybir.AluOpType
AX = mybir.AxisListType

D = 1024
NCH = 8
H = 16
HD = 64
EPS_LN = 1e-5
EPS_RMS = 1e-6


def get_slopes(n):
    def pow2(n):
        start = 2 ** (-2 ** (-(math.log2(n) - 3)))
        return [start * start ** i for i in range(n)]
    if math.log2(n).is_integer():
        return pow2(n)
    c = 2 ** math.floor(math.log2(n))
    return pow2(c) + get_slopes(2 * c)[0::2][:n - c]


class Buf:
    __slots__ = ("name", "w", "r")

    def __init__(self, name):
        self.name = name
        self.w = None
        self.r = {}


class Q:

    def __init__(self, nc, name, nslots=12):
        self.slots = []
        for i in range(nslots):
            s = nc.alloc_semaphore("%s%d" % (name, i))
            _SemById[s.num] = s
            self.slots.append([s, 0])
        self.i = 0

    def take(self):
        slot = self.slots[self.i]
        self.i = (self.i + 1) % len(self.slots)
        return slot


class Eng:

    def __init__(self, ctx, name, ename, sem, self_ordered):
        self.ctx = ctx
        self.name = name
        self.ename = ename
        self.sem = sem
        self.n = 0
        self.self_ordered = self_ordered
        self.known = {}
        self.prog = []
        self._waits = []

    def _dep(self, ev):
        if ev is None:
            return
        sem, val = ev
        k = sem.num
        if sem is self.sem and self.self_ordered:
            return
        if self.known.get(k, 0) >= val:
            return
        self._waits.append((sem, val))
        snap = self.ctx.snapshots.get((k, val))
        if snap:
            for k2, v2 in snap.items():
                if self.known.get(k2, 0) < v2:
                    self.known[k2] = v2
        if self.known.get(k, 0) < val:
            self.known[k] = val

    def _pre(self, reads, writes):
        self._waits = []
        for b in reads:
            self._dep(b.w)
        for b in writes:
            for k, v in b.r.items():
                self._dep((_SemById[k], v))
            self._dep(b.w)

    def _snapshot(self, ev):
        snap = dict(self.known)
        if self.self_ordered:
            snap[self.sem.num] = self.n
        k, v = ev[0].num, ev[1]
        if snap.get(k, 0) < v:
            snap[k] = v
        self.ctx.snapshots[(k, v)] = snap

    def _post(self, ev, reads, writes):
        for b in reads:
            k = ev[0].num
            if b.r.get(k, 0) < ev[1]:
                b.r[k] = ev[1]
        for b in writes:
            b.w = ev
            b.r = {}

    def emit(self, fn, reads=(), writes=(), q=None):
        return self.emit_multi([fn], reads, writes, q)

    def emit_multi(self, fns, reads=(), writes=(), q=None):
        self._pre(reads, writes)
        if q is not None:
            slot = q.take()
            self._dep((slot[0], slot[1]))
            slot[1] += 16
            ev = (slot[0], slot[1])
            self.prog.append((self._waits, fns, slot[0], 16))
        else:
            self.n += 1
            ev = (self.sem, self.n)
            self.prog.append((self._waits, fns, self.sem, 1))
        self._waits = []
        self._snapshot(ev)
        self._post(ev, reads, writes)
        return ev

    def render(self, e):
        for waits, fns, incsem, incval in self.prog:
            for (s, v) in waits:
                e.wait_ge(s, v)
            inst = None
            for fn in fns:
                inst = fn(e)
            if inst is not None:
                inst.then_inc(incsem, incval)


_SemById = {}


class Ctx:
    def __init__(self, nc):
        self.nc = nc
        self.snapshots = {}
        self.sems = {}
        for nm in ("pe", "act", "dve", "pool", "sp"):
            s = nc.alloc_semaphore("s_" + nm)
            _SemById[s.num] = s
            self.sems[nm] = s
        self.pe = Eng(self, "pe", "tensor", self.sems["pe"], False)
        self.act = Eng(self, "act", "scalar", self.sems["act"], False)
        self.dve = Eng(self, "dve", "vector", self.sems["dve"], False)
        self.pool = Eng(self, "pool", "gpsimd", self.sems["pool"], False)
        self.sp = Eng(self, "sp", "sync", self.sems["sp"], False)
        self.engines = [self.pe, self.act, self.dve, self.pool, self.sp]
        self.qsp = Q(nc, "qsp")
        self.qact = Q(nc, "qact")
        self.qpool = Q(nc, "qpool")
        self.bufs = []

    def uname(self, n):
        self._uc = getattr(self, "_uc", 0) + 1
        return "%s_u%d" % (n, self._uc)

    def buf(self, name):
        b = Buf(name)
        b.r = dict(getattr(self, "barrier", {}))
        self.bufs.append(b)
        return b

    def scope_barrier(self):
        bar = {}
        for e in self.engines:
            if e.n > 0:
                bar[e.sem.num] = e.n
        for q in (self.qsp, self.qact, self.qpool):
            for slot in q.slots:
                if slot[1] > 0:
                    bar[slot[0].num] = slot[1]
        self.barrier = bar

    def sub(self):
        from contextlib import ExitStack as _ES

        class _Scope:
            def __init__(s):
                s.st = _ES()

            def __enter__(s):
                return s.st.__enter__()

            def __exit__(s, *a):
                r = s.st.__exit__(*a)
                self.scope_barrier()
                return r
        return _Scope()

    def finish(self):
        for e in self.engines:
            e._waits = []
            for other in self.engines:
                if other is not e and other.n > 0:
                    e._dep((other.sem, other.n))
            for q in (self.qsp, self.qact, self.qpool):
                for slot in q.slots:
                    if slot[1] > 0:
                        e._dep((slot[0], slot[1]))
            if e._waits:
                e.prog.append((e._waits, [], None, 0))
                e._waits = []

    def render_all(self):
        with self.nc.Block() as blk:
            for eng in self.engines:
                getattr(blk, eng.ename)(lambda e, eng=eng: eng.render(e))


class Consts:

    def __init__(self, ctx, stack):
        nc = ctx.nc
        en = stack.enter_context
        self.id_bf = en(nc.sbuf_tensor("c_id_bf", [128, 128], BF16))
        self.id_f = en(nc.sbuf_tensor("c_id_f", [128, 128], F32))
        self.ones_col = en(nc.sbuf_tensor("c_ones_col", [128, 1], F32))
        self.ones_row = en(nc.sbuf_tensor("c_ones_row", [1, 512], F32))
        self.b_id_bf = ctx.buf("id_bf")
        self.b_id_f = ctx.buf("id_f")
        self.b_ones_col = ctx.buf("ones_col")
        self.b_ones_row = ctx.buf("ones_row")
        p, b = ctx.pool, self
        p.emit(lambda e: e.memset(self.id_bf[:, :], 0.0), writes=[b.b_id_bf])
        p.emit(lambda e: e.affine_select(out=self.id_bf[:, :], in_=self.id_bf[:, :],
                                         compare_op=ALU.not_equal, fill=1.0, base=0,
                                         pattern=[[-1, 128]], channel_multiplier=1),
               reads=[b.b_id_bf], writes=[b.b_id_bf])
        p.emit(lambda e: e.memset(self.id_f[:, :], 0.0), writes=[b.b_id_f])
        p.emit(lambda e: e.affine_select(out=self.id_f[:, :], in_=self.id_f[:, :],
                                         compare_op=ALU.not_equal, fill=1.0, base=0,
                                         pattern=[[-1, 128]], channel_multiplier=1),
               reads=[b.b_id_f], writes=[b.b_id_f])
        p.emit(lambda e: e.memset(self.ones_col[:, :], 1.0), writes=[b.b_ones_col])
        p.emit(lambda e: e.memset(self.ones_row[:, :], 1.0), writes=[b.b_ones_row])
        self.ones_row_bf = en(nc.sbuf_tensor("c_ones_row_bf", [1, 512], BF16))
        self.b_ones_row_bf = ctx.buf("ones_row_bf")
        p.emit(lambda e: e.memset(self.ones_row_bf[:, :], 1.0), writes=[b.b_ones_row_bf])
        self.ones_col_bf = en(nc.sbuf_tensor("c_ones_col_bf", [128, 1], BF16))
        self.b_ones_col_bf = ctx.buf("ones_col_bf")
        p.emit(lambda e: e.memset(self.ones_col_bf[:, :], 1.0), writes=[b.b_ones_col_bf])
        self.negbig = en(nc.sbuf_tensor("c_negbig", [1, 512], F32))
        self.b_negbig = ctx.buf("negbig")
        p.emit(lambda e: e.memset(self.negbig[:, :], -1e9), writes=[b.b_negbig])
        self.fullmask = en(nc.sbuf_tensor("c_fullmask", [128, 128], BF16))
        self.b_fullmask = ctx.buf("fullmask")
        p.emit(lambda e: e.memset(self.fullmask[:, :], -1e9), writes=[b.b_fullmask])
        self.masklt = en(nc.sbuf_tensor("c_masklt", [128, 128], BF16))
        self.b_masklt = ctx.buf("masklt")
        p.emit(lambda e: e.memset(self.masklt[:, :], 0.0), writes=[b.b_masklt])
        p.emit(lambda e: e.affine_select(out=self.masklt[:, :], in_=self.masklt[:, :],
                                         compare_op=ALU.is_ge, fill=-1e9, base=0,
                                         pattern=[[1, 128]], channel_multiplier=-1),
               reads=[b.b_masklt], writes=[b.b_masklt])
        self.maskut = en(nc.sbuf_tensor("c_maskut", [128, 128], BF16))
        self.b_maskut = ctx.buf("maskut")
        p.emit(lambda e: e.memset(self.maskut[:, :], 0.0), writes=[b.b_maskut])
        p.emit(lambda e: e.affine_select(out=self.maskut[:, :], in_=self.maskut[:, :],
                                         compare_op=ALU.is_ge, fill=-1e9, base=0,
                                         pattern=[[-1, 128]], channel_multiplier=1),
               reads=[b.b_maskut], writes=[b.b_maskut])
        self.eps = en(nc.sbuf_tensor("c_eps", [128, 2], F32))
        self.b_eps = ctx.buf("eps")
        p.emit(lambda e: e.memset(self.eps[:, 0:1], 1e-5), writes=[b.b_eps])
        p.emit(lambda e: e.memset(self.eps[:, 1:2], 1e-6), reads=[b.b_eps], writes=[b.b_eps])


def ln_stats(ctx, consts, xt_ap, b_xt, mr_ap, b_mr, tmp_ap, b_tmp, width=D):
    nchunk = (width + 511) // 512
    for i in range(nchunk):
        lo, hi = i * 512, min((i + 1) * 512, width)
        ctx.dve.emit(lambda e, i=i, lo=lo, hi=hi: e.bn_stats(tmp_ap[:, 6 * i:6 * i + 6], xt_ap[:, lo:hi]),
                     reads=[b_xt], writes=[b_tmp])
    ctx.dve.emit(lambda e: e.bn_aggr(mr_ap[:, 0:2], tmp_ap[:, 0:6 * nchunk]),
                 reads=[b_tmp], writes=[b_mr])
    ctx.act.emit(lambda e: e.activation(out=tmp_ap[:, 12:13], in_=mr_ap[:, 1:2], func=AF.Ln,
                                        bias=consts.eps[0:mr_ap.shape[0], 0:1]),
                 reads=[b_mr, consts.b_eps], writes=[b_tmp])
    ctx.act.emit(lambda e: e.activation(out=mr_ap[:, 1:2], in_=tmp_ap[:, 12:13], func=AF.Exp, scale=-0.5),
                 reads=[b_tmp], writes=[b_mr])


def transpose128(ctx, consts, dst_ap, b_dst, src_ap, b_src, psum_ap, b_psum, dtype="bf"):
    ident = consts.id_bf if dtype == "bf" else consts.id_f
    b_id = consts.b_id_bf if dtype == "bf" else consts.b_id_f
    ctx.pe.emit(lambda e: e.transpose(psum_ap, src_ap, ident[:, :]),
                reads=[b_src, b_id], writes=[b_psum])
    ctx.dve.emit(lambda e: e.tensor_copy(dst_ap, psum_ap),
                 reads=[b_psum], writes=[b_dst])


def load_weight_T(ctx, consts, stack, name, w_hbm, b_whbm, rows, cols, tp_psum, b_tp):
    nc = ctx.nc
    ra, ca = rows // 128, cols // 128
    nat = [stack.enter_context(nc.sbuf_tensor(name + "_nat%d" % i, [128, cols], BF16)) for i in range(2)]
    b_nat = [ctx.buf(name + "_nat%d" % i) for i in range(2)]
    wT = stack.enter_context(nc.sbuf_tensor(name + "_T", [128, ca * rows], BF16))
    b_wT = ctx.buf(name + "_T")
    for ai in range(ra):
        st, b_st = nat[ai % 2], b_nat[ai % 2]
        ctx.pool.emit(lambda e, ai=ai, st=st: e.dma_start(out=st[:, :], in_=w_hbm[ai * 128:(ai + 1) * 128, :]),
                      reads=[b_whbm], writes=[b_st], q=ctx.qpool)
        for bi in range(ca):
            transpose128(ctx, consts,
                         wT[:, bi * rows + ai * 128: bi * rows + (ai + 1) * 128], b_wT,
                         st[:, bi * 128:(bi + 1) * 128], b_st,
                         tp_psum, b_tp, "bf")
    return wT, b_wT


def ln_apply_cast(ctx, out_ap, b_out, x_ap, b_x, mr_ap, b_mr):
    ctx.dve.emit(lambda e: e.tensor_scalar(out_ap, x_ap, mr_ap[:, 0:1], mr_ap[:, 1:2],
                                           ALU.subtract, ALU.mult),
                 reads=[b_x, b_mr], writes=[b_out])


def build_phase2(ctx, consts, stack, T, io):
    nc = ctx.nc
    nt = T // 128
    ntc = T // 512
    en = stack.enter_context
    xin, b_xin = io["x_in"], io["b_x_in"]
    xout, b_xout = io["x_out"], io["b_x_out"]

    xn2T = en(nc.sbuf_tensor("p2_xnT", [128, NCH * T], BF16)); b_xn2T = ctx.buf("p2_xnT")
    obuf, b_obuf = xn2T, b_xn2T
    u = en(nc.sbuf_tensor("p2_u", [128, T], F32)); b_u = ctx.buf("p2_u")
    xt = [en(nc.sbuf_tensor("p2_xt%d" % i, [128, D], F32)) for i in range(2)]
    b_xt = [ctx.buf("p2_xt0"), ctx.buf("p2_xt1")]
    xnb = en(nc.sbuf_tensor("p2_xnb", [128, D], BF16)); b_xnb = ctx.buf("p2_xnb")
    lnst = en(nc.sbuf_tensor("p2_lnst", [128, nt * 2], F32)); b_lnst = ctx.buf("p2_lnst")
    sttmp = en(nc.sbuf_tensor("p2_sttmp", [128, 16], F32)); b_sttmp = ctx.buf("p2_sttmp")
    rows = en(nc.sbuf_tensor("p2_rows", [1, 2 * D + 64], F32)); b_rows = ctx.buf("p2_rows")
    lamexp = en(nc.sbuf_tensor("p2_lamexp", [1, 128], F32)); b_lamexp = ctx.buf("p2_lamexp")
    lamcol = en(nc.sbuf_tensor("p2_lamcol", [128, NCH], F32)); b_lamcol = ctx.buf("p2_lamcol")
    qkcol = en(nc.sbuf_tensor("p2_qkcol", [128, NCH], F32)); b_qkcol = ctx.buf("p2_qkcol")
    sel = en(nc.sbuf_tensor("p2_sel", [128, 2], F32)); b_sel = ctx.buf("p2_sel")
    selT = en(nc.sbuf_tensor("p2_selT", [2, 128], F32)); b_selT = ctx.buf("p2_selT")
    s2 = en(nc.sbuf_tensor("p2_s2", [2, NCH], F32)); b_s2 = ctx.buf("p2_s2")
    ccol = en(nc.sbuf_tensor("p2_ccol", [128, NCH], F32)); b_ccol = ctx.buf("p2_ccol")
    ssq = en(nc.sbuf_tensor("p2_ssq", [1, T], F32)); b_ssq = ctx.buf("p2_ssq")
    rrow = en(nc.sbuf_tensor("p2_rrow", [1, T], F32)); b_rrow = ctx.buf("p2_rrow")
    rstdc = en(nc.sbuf_tensor("p2_rstdc", [128, nt], F32)); b_rstdc = ctx.buf("p2_rstdc")
    nwcol = en(nc.sbuf_tensor("p2_nwcol", [128, NCH], F32)); b_nwcol = ctx.buf("p2_nwcol")
    ao = en(nc.sbuf_tensor("p2_ao", [128, D], F32)); b_ao = ctx.buf("p2_ao")
    t1 = en(nc.sbuf_tensor("p2_t1", [128, D], F32)); b_t1 = ctx.buf("p2_t1")
    ptp = [en(nc.psum_tensor("p2_ptp%d" % i, [128, 128], BF16)) for i in range(2)]
    b_ptp = [ctx.buf("p2_ptp%d" % i) for i in range(2)]
    ptf = en(nc.psum_tensor("p2_ptf", [128, 512], F32)); b_ptf = ctx.buf("p2_ptf")
    pmm = [en(nc.psum_tensor("p2_pmm%d" % i, [128, 512], F32)) for i in range(2)]
    b_pmm = [ctx.buf("p2_pmm%d" % i) for i in range(2)]

    for t in range(nt):
        ctx.sp.emit(lambda e, t=t: e.dma_start(out=xt[t % 2][:, :], in_=xin[t * 128:(t + 1) * 128, :]),
                    reads=[b_xin], writes=[b_xt[t % 2]], q=ctx.qsp)
        mr = lnst[:, 2 * t:2 * t + 2]
        ln_stats(ctx, consts, xt[t % 2][:, :], b_xt[t % 2], mr, b_lnst, sttmp[:, :], b_sttmp)
        ctx.dve.emit(lambda e, t=t: e.tensor_scalar(xnb[:, :], xt[t % 2][:, :],
                                                    lnst[:, 2 * t:2 * t + 1], lnst[:, 2 * t + 1:2 * t + 2],
                                                    ALU.subtract, ALU.mult),
                     reads=[b_xt[t % 2], b_lnst], writes=[b_xnb])
        for c in range(NCH):
            transpose128(ctx, consts,
                         xn2T[:, c * T + t * 128: c * T + (t + 1) * 128], b_xn2T,
                         xnb[:, c * 128:(c + 1) * 128], b_xnb,
                         ptp[c % 2][:, :], b_ptp[c % 2], "bf")

    q_h, b_q = io["L1_attn_q"], io["b_L1_attn_q"]
    k_h, b_k = io["L1_attn_k"], io["b_L1_attn_k"]
    ld_h, b_ld = io["L1_attn_log_decay"], io["b_L1_attn_log_decay"]
    nw_h, b_nw = io["L1_attn_norm_weight"], io["b_L1_attn_norm_weight"]
    ow_h, b_ow = io["L1_attn_o_w"], io["b_L1_attn_o_w"]

    ctx.sp.emit(lambda e: e.dma_start(out=rows[0:1, 0:16], in_=ld_h[None, :]),
                reads=[b_ld], writes=[b_rows], q=ctx.qsp)
    ctx.sp.emit(lambda e: e.dma_start(out=rows[0:1, 64:64 + D], in_=q_h[None, :]),
                reads=[b_q], writes=[b_rows], q=ctx.qsp)
    ctx.sp.emit(lambda e: e.dma_start(out=rows[0:1, 64 + D:64 + 2 * D], in_=k_h[None, :]),
                reads=[b_k], writes=[b_rows], q=ctx.qsp)
    ctx.act.emit(lambda e: e.activation(out=rows[0:1, 16:32], in_=rows[0:1, 0:16], func=AF.Sigmoid),
                 reads=[b_rows], writes=[b_rows])
    ctx.dve.emit(lambda e: e.tensor_mul(rows[0:1, 64:64 + D], rows[0:1, 64:64 + D], rows[0:1, 64 + D:64 + 2 * D]),
                 reads=[b_rows], writes=[b_rows])
    for c in range(NCH):
        ctx.pe.emit_multi([
            (lambda e, c=c: e.matmul(ptf[0:1, 0:64], rows[0:1, 16 + 2 * c:17 + 2 * c],
                                     consts.ones_row[0:1, 0:64], start=True, stop=True)),
            (lambda e, c=c: e.matmul(ptf[0:1, 64:128], rows[0:1, 17 + 2 * c:18 + 2 * c],
                                     consts.ones_row[0:1, 0:64], start=True, stop=True)),
        ], reads=[b_rows, consts.b_ones_row], writes=[b_ptf])
        ctx.dve.emit(lambda e: e.tensor_copy(lamexp[0:1, :], ptf[0:1, 0:128]),
                     reads=[b_ptf], writes=[b_lamexp])
        ctx.pe.emit(lambda e: e.transpose(ptf[:, 0:1], lamexp[0:1, :], consts.id_f[0:1, 0:1]),
                    reads=[b_lamexp, consts.b_id_f], writes=[b_ptf])
        ctx.dve.emit(lambda e, c=c: e.tensor_copy(lamcol[:, c:c + 1], ptf[:, 0:1]),
                     reads=[b_ptf], writes=[b_lamcol])
    for c in range(NCH):
        ctx.pe.emit(lambda e, c=c: e.transpose(ptf[:, 0:1], rows[0:1, 64 + c * 128:64 + (c + 1) * 128],
                                               consts.id_f[0:1, 0:1]),
                    reads=[b_rows, consts.b_id_f], writes=[b_ptf])
        ctx.dve.emit(lambda e, c=c: e.tensor_copy(qkcol[:, c:c + 1], ptf[:, 0:1]),
                     reads=[b_ptf], writes=[b_qkcol])
    ctx.pool.emit(lambda e: e.memset(sel[:, :], 0.0), writes=[b_sel])
    ctx.pool.emit(lambda e: e.memset(sel[0:64, 0:1], 1.0), reads=[b_sel], writes=[b_sel])
    ctx.pool.emit(lambda e: e.memset(sel[64:128, 1:2], 1.0), reads=[b_sel], writes=[b_sel])
    ctx.pe.emit(lambda e: e.matmul(ptf[0:2, 0:NCH], sel[:, :], qkcol[:, :], start=True, stop=True),
                reads=[b_sel, b_qkcol], writes=[b_ptf])
    ctx.dve.emit(lambda e: e.tensor_copy(s2[:, :], ptf[0:2, 0:NCH]),
                 reads=[b_ptf], writes=[b_s2])
    ctx.pe.emit(lambda e: e.transpose(ptf[0:2, 0:128], sel[:, :], consts.id_f[:, :]),
                reads=[b_sel, consts.b_id_f], writes=[b_ptf])
    ctx.dve.emit(lambda e: e.tensor_copy(selT[:, :], ptf[0:2, 0:128]),
                 reads=[b_ptf], writes=[b_selT])
    ctx.pe.emit(lambda e: e.matmul(ptf[:, 0:NCH], selT[:, :], s2[:, :], start=True, stop=True),
                reads=[b_selT, b_s2], writes=[b_ptf])
    ctx.dve.emit(lambda e: e.tensor_copy(ccol[:, :], ptf[:, 0:NCH]),
                 reads=[b_ptf], writes=[b_ccol])

    for c in range(NCH):
        ctx.dve.emit(lambda e, c=c: e.tensor_tensor_scan(u[:, :], lamcol[:, c:c + 1].broadcast_to((128, T)),
                                                         xn2T[:, c * T:(c + 1) * T], 0.0, ALU.mult, ALU.add),
                     reads=[b_lamcol, b_xn2T], writes=[b_u])
        ctx.dve.emit(lambda e, c=c: e.tensor_scalar_mul(obuf[:, c * T:(c + 1) * T], u[:, :], ccol[:, c:c + 1]),
                     reads=[b_u, b_ccol], writes=[b_obuf])

    ctx.dve.emit(lambda e: e.memset(ssq[0:1, :], 0.0), writes=[b_ssq])
    for c in range(NCH):
        ctx.act.emit(lambda e, c=c: e.activation(out=u[:, :], in_=obuf[:, c * T:(c + 1) * T], func=AF.Square),
                     reads=[b_obuf], writes=[b_u])
        for tc in range(ntc):
            pm = pmm[tc % 2]
            bpm = b_pmm[tc % 2]
            ctx.pe.emit(lambda e, tc=tc, pm=pm: e.matmul(pm[0:1, :], consts.ones_col[:, :], u[:, tc * 512:(tc + 1) * 512],
                                                  start=True, stop=True),
                        reads=[b_u, consts.b_ones_col], writes=[bpm])
            ctx.dve.emit(lambda e, tc=tc, pm=pm: e.tensor_add(ssq[0:1, tc * 512:(tc + 1) * 512],
                                                       ssq[0:1, tc * 512:(tc + 1) * 512], pm[0:1, :]),
                         reads=[bpm, b_ssq], writes=[b_ssq])

    ctx.act.emit(lambda e: e.activation(out=rrow[0:1, 0:T], in_=ssq[0:1, :], func=AF.Ln, scale=1.0 / D,
                                        bias=consts.eps[0:1, 1:2]),
                 reads=[b_ssq, consts.b_eps], writes=[b_rrow])
    ctx.act.emit(lambda e: e.activation(out=rrow[0:1, 0:T], in_=rrow[0:1, 0:T], func=AF.Exp, scale=-0.5),
                 reads=[b_rrow], writes=[b_rrow])
    for t in range(nt):
        ctx.pe.emit(lambda e, t=t: e.transpose(ptf[:, 0:1], rrow[0:1, t * 128:(t + 1) * 128],
                                               consts.id_f[0:1, 0:1]),
                    reads=[b_rrow, consts.b_id_f], writes=[b_ptf])
        ctx.dve.emit(lambda e, t=t: e.tensor_copy(rstdc[:, t:t + 1], ptf[:, 0:1]),
                     reads=[b_ptf], writes=[b_rstdc])

    owT, b_owT = load_weight_T(ctx, consts, stack, "p2_ow", ow_h, b_ow, D, D, ptp[0][:, :], b_ptp[0])
    ctx.sp.emit(lambda e: e.dma_start(out=rows[0:1, 64:64 + D], in_=nw_h[None, :]),
                reads=[b_nw], writes=[b_rows], q=ctx.qsp)
    for c in range(NCH):
        ctx.pe.emit(lambda e, c=c: e.transpose(ptf[:, 0:1], rows[0:1, 64 + c * 128:64 + (c + 1) * 128],
                                               consts.id_f[0:1, 0:1]),
                    reads=[b_rows, consts.b_id_f], writes=[b_ptf])
        ctx.dve.emit(lambda e, c=c: e.tensor_copy(nwcol[:, c:c + 1], ptf[:, 0:1]),
                     reads=[b_ptf], writes=[b_nwcol])
    for bi in range(NCH):
        ctx.dve.emit(lambda e, bi=bi: e.tensor_scalar_mul(owT[:, bi * D:(bi + 1) * D],
                                                          owT[:, bi * D:(bi + 1) * D], nwcol[:, bi:bi + 1]),
                     reads=[b_owT, b_nwcol], writes=[b_owT])

    for t in range(nt):
        for half in range(2):
            pm, bpm = pmm[half], b_pmm[half]
            fns = []
            for ci in range(NCH):
                fns.append(lambda e, ci=ci, half=half, t=t, pm=pm: e.matmul(
                    pm[:, :],
                    obuf[:, ci * T + t * 128: ci * T + (t + 1) * 128],
                    owT[:, ci * D + half * 512: ci * D + (half + 1) * 512],
                    start=(ci == 0), stop=(ci == NCH - 1)))
            ctx.pe.emit_multi(fns, reads=[b_obuf, b_owT], writes=[bpm])
            ctx.dve.emit(lambda e, half=half, t=t, pm=pm: e.tensor_scalar_mul(
                ao[:, half * 512:(half + 1) * 512], pm[:, :], rstdc[:, t:t + 1]),
                reads=[bpm, b_rstdc], writes=[b_ao])
        mr = sttmp[:, 13:15]
        ln_stats(ctx, consts, ao[:, :], b_ao, mr, b_sttmp, sttmp[:, :], b_sttmp)
        ctx.dve.emit(lambda e: e.tensor_scalar(t1[:, :], ao[:, :], sttmp[:, 13:14], sttmp[:, 14:15],
                                               ALU.subtract, ALU.mult),
                     reads=[b_ao, b_sttmp], writes=[b_t1])
        ctx.sp.emit(lambda e, t=t: e.dma_start(out=xt[t % 2][:, :], in_=xin[t * 128:(t + 1) * 128, :]),
                    reads=[b_xin], writes=[b_xt[t % 2]], q=ctx.qsp)
        ctx.dve.emit(lambda e, t=t: e.tensor_add(t1[:, :], t1[:, :], xt[t % 2][:, :]),
                     reads=[b_xt[t % 2], b_t1], writes=[b_t1])
        ctx.sp.emit(lambda e, t=t: e.dma_start(out=xout[t * 128:(t + 1) * 128, :], in_=t1[:, :]),
                    reads=[b_t1], writes=[b_xout], q=ctx.qsp)


def emit_weight_T(ctx, consts, wT, b_wT, stage, b_stage, w_hbm, b_w, rows, cols, tp_psum, b_tp,
                  row0=0, col0=0):
    ra, ca = rows // 128, cols // 128
    for ai in range(ra):
        st, b_st = stage[ai % 2], b_stage[ai % 2]
        ctx.pool.emit(lambda e, ai=ai, st=st: e.dma_start(
            out=st[:, 0:cols], in_=w_hbm[row0 + ai * 128:row0 + (ai + 1) * 128, col0:col0 + cols]),
            reads=[b_w], writes=[b_st], q=ctx.qpool)
        for bi in range(ca):
            transpose128(ctx, consts,
                         wT[:, bi * rows + ai * 128: bi * rows + (ai + 1) * 128], b_wT,
                         st[:, bi * 128:(bi + 1) * 128], b_st,
                         tp_psum, b_tp, "bf")


def emit_weight_T3(ctx, consts, wT, b_wT, stages, b_stages, cnt, w_hbm, b_w, rows, cols,
                   ptp2, b_ptp2, row0=0, col0=0):
    ra, ca = rows // 128, cols // 128
    for ai in range(ra):
        k = cnt[0] % len(stages)
        cnt[0] += 1
        st, b_st = stages[k], b_stages[k]
        ctx.pool.emit(lambda e, ai=ai, st=st: e.dma_start(
            out=st[:, 0:cols], in_=w_hbm[row0 + ai * 128:row0 + (ai + 1) * 128, col0:col0 + cols]),
            reads=[b_w], writes=[b_st], q=ctx.qpool)
        for b0 in range(0, ca, 4):
            bn = min(4, ca - b0)
            pb, bpb = ptp2[(ai + b0 // 4) % 2], b_ptp2[(ai + b0 // 4) % 2]
            for j in range(bn):
                ctx.pe.emit(lambda e, b0=b0, j=j, pb=pb, st=st: e.transpose(
                    pb[:, j * 128:(j + 1) * 128],
                    st[:, (b0 + j) * 128:(b0 + j + 1) * 128],
                    consts.id_bf[:, :]),
                    reads=[b_st, consts.b_id_bf], writes=[bpb])
            wTr = wT[:, 0:ca * rows].rearrange("p (b a) -> p b a", b=ca)
            ctx.dve.emit(lambda e, ai=ai, b0=b0, bn=bn, pb=pb, wTr=wTr: e.tensor_copy(
                wTr[:, b0:b0 + bn, ai * 128:(ai + 1) * 128],
                pb[:, 0:bn * 128].rearrange("p (b a) -> p b a", b=bn)),
                reads=[bpb], writes=[b_wT])


def emit_weight_T2(ctx, consts, wT, b_wT, nat, b_nat, w_hbm, b_w, rows, cols,
                   ptp2, b_ptp2, row0=0, col0=0):
    ra, ca = rows // 128, cols // 128
    if col0 == 0 and cols == w_hbm.shape[1]:
        ctx.pool.emit(lambda e: e.dma_start(
            out=nat[:, 0:ra * cols].rearrange("p (a c) -> p a c", a=ra),
            in_=w_hbm[row0:row0 + rows, :].rearrange("(a p) c -> p a c", p=128)),
            reads=[b_w], writes=[b_nat], q=ctx.qpool)
    else:
        for ai in range(ra):
            ctx.pool.emit(lambda e, ai=ai: e.dma_start(
                out=nat[:, ai * cols:(ai + 1) * cols],
                in_=w_hbm[row0 + ai * 128:row0 + (ai + 1) * 128, col0:col0 + cols]),
                reads=[b_w], writes=[b_nat], q=ctx.qpool)
    wTr = wT[:, 0:ca * rows].rearrange("p (b a) -> p b a", b=ca)
    nb = 0
    for bi in range(ca):
        for a0 in range(0, ra, 4):
            an = min(4, ra - a0)
            pb, bpb = ptp2[nb % 2], b_ptp2[nb % 2]
            nb += 1
            for j in range(an):
                ctx.pe.emit(lambda e, bi=bi, a0=a0, j=j, pb=pb: e.transpose(
                    pb[:, j * 128:(j + 1) * 128],
                    nat[:, (a0 + j) * cols + bi * 128:(a0 + j) * cols + (bi + 1) * 128],
                    consts.id_bf[:, :]),
                    reads=[b_nat, consts.b_id_bf], writes=[bpb])
            ctx.dve.emit(lambda e, bi=bi, a0=a0, an=an, pb=pb: e.tensor_copy(
                wTr[:, bi, a0 * 128:(a0 + an) * 128], pb[:, 0:an * 128]),
                reads=[bpb], writes=[b_wT])


def emit_ln_pass(ctx, consts, T, xin, b_xin, xt, b_xt, xnb, b_xnb, lnst, b_lnst,
                 sttmp, b_sttmp, xnT, b_xnT, ptp, b_ptp, r0=0):
    nt = T // 128
    for t in range(nt):
        ctx.sp.emit(lambda e, t=t: e.dma_start(out=xt[t % 2][:, :],
                                               in_=xin[r0 + t * 128:r0 + (t + 1) * 128, :]),
                    reads=[b_xin], writes=[b_xt[t % 2]], q=ctx.qsp)
        mr = lnst[:, 2 * t:2 * t + 2]
        ln_stats(ctx, consts, xt[t % 2][:, :], b_xt[t % 2], mr, b_lnst, sttmp[:, :], b_sttmp)
        ctx.dve.emit(lambda e, t=t: e.tensor_scalar(xnb[:, :], xt[t % 2][:, :],
                                                    lnst[:, 2 * t:2 * t + 1], lnst[:, 2 * t + 1:2 * t + 2],
                                                    ALU.subtract, ALU.mult),
                     reads=[b_xt[t % 2], b_lnst], writes=[b_xnb])
        for c in range(NCH):
            transpose128(ctx, consts,
                         xnT[:, c * T + t * 128: c * T + (t + 1) * 128], b_xnT,
                         xnb[:, c * 128:(c + 1) * 128], b_xnb,
                         ptp[c % 2][:, 0:128], b_ptp[c % 2], "bf")


def emit_finale(ctx, consts, T, t, src_ap, b_src, xin, b_xin, xout, b_xout,
                xt, b_xt, t1, b_t1, sttmp, b_sttmp, r0=0):
    mr = sttmp[:, 13:15]
    ln_stats(ctx, consts, src_ap, b_src, mr, b_sttmp, sttmp[:, :], b_sttmp)
    ctx.dve.emit(lambda e: e.tensor_scalar(t1[:, :], src_ap, sttmp[:, 13:14], sttmp[:, 14:15],
                                           ALU.subtract, ALU.mult),
                 reads=[b_src, b_sttmp], writes=[b_t1])
    ctx.sp.emit(lambda e, t=t: e.dma_start(out=xt[:, :], in_=xin[r0 + t * 128:r0 + (t + 1) * 128, :]),
                reads=[b_xin], writes=[b_xt], q=ctx.qsp)
    ctx.dve.emit(lambda e: e.tensor_add(t1[:, :], t1[:, :], xt[:, :]),
                 reads=[b_xt, b_t1], writes=[b_t1])
    ctx.sp.emit(lambda e, t=t: e.dma_start(out=xout[r0 + t * 128:r0 + (t + 1) * 128, :], in_=t1[:, :]),
                reads=[b_t1], writes=[b_xout], q=ctx.qsp)


def build_phase1(ctx, consts, stack, Tfull, io):
    with ctx.sub() as hstack:
        _build_phase1_half(ctx, consts, hstack, Tfull, 0, io)


def _build_phase1_half(ctx, consts, stack, T, r0, io):
    nc = ctx.nc
    nt = T // 128
    ntc = T // 512
    en = stack.enter_context
    xin, b_xin = io["x_in"], io["b_x_in"]
    xout, b_xout = io["x_out"], io["b_x_out"]
    E = 8

    xnT = en(nc.sbuf_tensor(ctx.uname("p1_xnT"), [128, NCH * T], BF16)); b_xnT = ctx.buf("p1_xnT")
    y_hbm = nc.dram_tensor(ctx.uname("p1_yhbm"), [T, D], F32, kind="Internal").ap()
    b_yh = [ctx.buf("p1_yh%d" % t) for t in range(nt)]
    ctrb = [en(nc.sbuf_tensor(ctx.uname("p1_ctrb%d" % i), [128, D], F32)) for i in range(3)]
    b_ctrb = [ctx.buf("p1_ctrb%d" % i) for i in range(3)]
    ctrbi = [0]
    xt = [en(nc.sbuf_tensor(ctx.uname("p1_xt%d" % i), [128, D], F32)) for i in range(2)]
    b_xt = [ctx.buf("p1_xt0"), ctx.buf("p1_xt1")]
    xnb = en(nc.sbuf_tensor(ctx.uname("p1_xnb"), [128, D], BF16)); b_xnb = ctx.buf("p1_xnb")
    lnst = en(nc.sbuf_tensor(ctx.uname("p1_lnst"), [128, nt * 2], F32)); b_lnst = ctx.buf("p1_lnst")
    sttmp = en(nc.sbuf_tensor(ctx.uname("p1_sttmp"), [128, 16], F32)); b_sttmp = ctx.buf("p1_sttmp")
    t1 = en(nc.sbuf_tensor(ctx.uname("p1_t1"), [128, D], F32)); b_t1 = ctx.buf("p1_t1")
    gwT = en(nc.sbuf_tensor(ctx.uname("p1_gwT"), [128, NCH * E], BF16)); b_gwT = ctx.buf("p1_gwT")
    gnat = en(nc.sbuf_tensor(ctx.uname("p1_gnat"), [8, D], BF16)); b_gnat = ctx.buf("p1_gnat")
    sc = en(nc.sbuf_tensor(ctx.uname("p1_sc"), [128, 4 * E + 4], F32)); b_sc = ctx.buf("p1_sc")
    cbuf = en(nc.sbuf_tensor(ctx.uname("p1_cbuf"), [128, nt * E], F32)); b_cbuf = ctx.buf("p1_cbuf")
    mbuf = en(nc.sbuf_tensor(ctx.uname("p1_mbuf"), [128, nt * E], F32)); b_mbuf = ctx.buf("p1_mbuf")
    crow = en(nc.sbuf_tensor(ctx.uname("p1_crow"), [1, T], BF16)); b_crow = ctx.buf("p1_crow")
    w1T = en(nc.sbuf_tensor(ctx.uname("p1_w1T"), [128, NCH * D], BF16)); b_w1T = ctx.buf("p1_w1T")
    w2T = en(nc.sbuf_tensor(ctx.uname("p1_w2T"), [128, NCH * D], BF16)); b_w2T = ctx.buf("p1_w2T")
    w3T = en(nc.sbuf_tensor(ctx.uname("p1_w3T"), [128, NCH * D], BF16)); b_w3T = ctx.buf("p1_w3T")
    nat2 = [en(nc.sbuf_tensor(ctx.uname("p1_nat%d" % i), [128, D], BF16)) for i in range(3)]
    b_nat2 = [ctx.buf("p1_nat%d" % i) for i in range(3)]
    natcnt = [0]
    brow = en(nc.sbuf_tensor(ctx.uname("p1_brow"), [1, 2 * D], BF16)); b_brow = ctx.buf("p1_brow")
    b1col = en(nc.sbuf_tensor(ctx.uname("p1_b1col"), [128, NCH], F32)); b_b1col = ctx.buf("p1_b1col")
    xcT2 = [en(nc.sbuf_tensor(ctx.uname("p1_xcT%d" % i), [128, NCH * 512], BF16)) for i in range(2)]
    b_xcT2 = [ctx.buf("p1_xcT%d" % i) for i in range(2)]
    hT = en(nc.sbuf_tensor(ctx.uname("p1_hT"), [128, NCH * 512], BF16)); b_hT = ctx.buf("p1_hT")
    x3sb = en(nc.sbuf_tensor(ctx.uname("p1_x3sb"), [128, D], F32)); b_x3sb = ctx.buf("p1_x3sb")
    t2 = en(nc.sbuf_tensor(ctx.uname("p1_t2"), [128, 512], F32)); b_t2 = ctx.buf("p1_t2")
    ptp = [en(nc.psum_tensor(ctx.uname("p1_ptp%d" % i), [128, 512], BF16)) for i in range(2)]
    b_ptp = [ctx.buf("p1_ptp%d" % i) for i in range(2)]
    pcb = en(nc.psum_tensor(ctx.uname("p1_pcb"), [128, 512], F32)); b_pcb = ctx.buf("p1_pcb")
    pz = [en(nc.psum_tensor(ctx.uname("p1_pz%d" % i), [128, 512], F32)) for i in range(2)]
    b_pz = [ctx.buf("p1_pz%d" % i) for i in range(2)]
    pw = [en(nc.psum_tensor(ctx.uname("p1_pw%d" % i), [128, 512], F32)) for i in range(3)]
    b_pw = [ctx.buf("p1_pw%d" % i) for i in range(3)]

    emit_ln_pass(ctx, consts, T, xin, b_xin, xt, b_xt, xnb, b_xnb, lnst, b_lnst,
                 sttmp, b_sttmp, xnT, b_xnT, ptp, b_ptp, r0=r0)

    gw, b_gw = io["L0_ffn_gate_w"], io["b_L0_ffn_gate_w"]
    ctx.pool.emit(lambda e: e.dma_start(out=gnat[:, :], in_=gw[:, :]),
                  reads=[b_gw], writes=[b_gnat], q=ctx.qpool)
    for ci in range(NCH):
        ctx.pe.emit(lambda e, ci=ci: e.transpose(ptp[0][:, 0:8], gnat[0:8, ci * 128:(ci + 1) * 128],
                                                 consts.id_bf[0:8, 0:8]),
                    reads=[b_gnat, consts.b_id_bf], writes=[b_ptp[0]])
        ctx.dve.emit(lambda e, ci=ci: e.tensor_copy(gwT[:, ci * E:(ci + 1) * E], ptp[0][0:128, 0:8]),
                     reads=[b_ptp[0]], writes=[b_gwT])
    for t in range(nt):
        fns = []
        for ci in range(NCH):
            fns.append(lambda e, ci=ci, t=t: e.matmul(
                pcb[:, 0:E], xnT[:, ci * T + t * 128: ci * T + (t + 1) * 128],
                gwT[:, ci * E:(ci + 1) * E], start=(ci == 0), stop=(ci == NCH - 1)))
        ctx.pe.emit_multi(fns, reads=[b_xnT, b_gwT], writes=[b_pcb])
        L, M1, M2, Z = sc[:, 0:E], sc[:, 4 * E:4 * E + 1], sc[:, 4 * E + 1:4 * E + 2], sc[:, 4 * E + 2:4 * E + 3]
        A1, A2, MS = sc[:, E:2 * E], sc[:, 2 * E:3 * E], sc[:, 3 * E:4 * E]
        ctx.dve.emit(lambda e: e.tensor_copy(L, pcb[:, 0:E]), reads=[b_pcb], writes=[b_sc])
        ctx.dve.emit(lambda e: e.reduce_max(M1, L, axis=AX.X, negate=True), reads=[b_sc], writes=[b_sc])
        ctx.act.emit(lambda e: e.activation(out=L, in_=L, func=AF.Exp, bias=M1, accum_out=Z),
                     reads=[b_sc], writes=[b_sc])
        ctx.dve.emit(lambda e: e.reciprocal(M1, Z), reads=[b_sc], writes=[b_sc])
        ctx.dve.emit(lambda e: e.tensor_scalar_mul(L, L, M1), reads=[b_sc], writes=[b_sc])
        ctx.dve.emit(lambda e: e.reduce_max(M1, L, axis=AX.X), reads=[b_sc], writes=[b_sc])
        ctx.dve.emit(lambda e: e.tensor_single_scalar(A1, L, M1, ALU.is_ge), reads=[b_sc], writes=[b_sc])
        ctx.dve.emit(lambda e: e.scalar_tensor_tensor(MS, A1, -1e30, L, ALU.mult, ALU.add),
                     reads=[b_sc], writes=[b_sc])
        ctx.dve.emit(lambda e: e.reduce_max(M2, MS, axis=AX.X), reads=[b_sc], writes=[b_sc])
        ctx.dve.emit(lambda e: e.tensor_single_scalar(A2, MS, M2, ALU.is_ge), reads=[b_sc], writes=[b_sc])
        ctx.dve.emit(lambda e, t=t: e.tensor_scalar_mul(cbuf[:, t * E:(t + 1) * E], A1, M1),
                     reads=[b_sc], writes=[b_cbuf])
        ctx.dve.emit(lambda e, t=t: e.tensor_scalar(MS, A2, M2, None, ALU.mult),
                     reads=[b_sc], writes=[b_sc])
        ctx.dve.emit(lambda e, t=t: e.tensor_add(cbuf[:, t * E:(t + 1) * E], cbuf[:, t * E:(t + 1) * E], MS),
                     reads=[b_sc, b_cbuf], writes=[b_cbuf])
        ctx.dve.emit(lambda e, t=t: e.tensor_add(mbuf[:, t * E:(t + 1) * E], A1, A2),
                     reads=[b_sc], writes=[b_mbuf])


    w1h, b_w1h = io["L0_ffn_w1"], io["b_L0_ffn_w1"]
    w2h, b_w2h = io["L0_ffn_w2"], io["b_L0_ffn_w2"]
    w3h, b_w3h = io["L0_ffn_w3"], io["b_L0_ffn_w3"]
    b1h, b_b1h = io["L0_ffn_b1"], io["b_L0_ffn_b1"]
    b2h, b_b2h = io["L0_ffn_b2"], io["b_L0_ffn_b2"]
    b3h, b_b3h = io["L0_ffn_b3"], io["b_L0_ffn_b3"]
    for ex in range(E):
        ctx.pool.emit(lambda e, ex=ex: e.dma_start(out=brow[0:1, 0:D], in_=b1h[ex][None, :]),
                      reads=[b_b1h], writes=[b_brow], q=ctx.qpool)
        for c in range(NCH):
            ctx.pe.emit(lambda e, c=c: e.transpose(ptp[0][:, 0:1], brow[0:1, c * 128:(c + 1) * 128],
                                                   consts.id_bf[0:1, 0:1]),
                        reads=[b_brow, consts.b_id_bf], writes=[b_ptp[0]])
            ctx.dve.emit(lambda e, c=c: e.tensor_copy(b1col[:, c:c + 1], ptp[0][:, 0:1]),
                         reads=[b_ptp[0]], writes=[b_b1col])
        ctx.pool.emit(lambda e, ex=ex: e.dma_start(out=brow[0:1, 0:D], in_=b2h[ex][None, :]),
                      reads=[b_b2h, b_b1col], writes=[b_brow], q=ctx.qpool)
        ctx.pool.emit(lambda e, ex=ex: e.dma_start(out=brow[0:1, D:2 * D], in_=b3h[ex][None, :]),
                      reads=[b_b3h], writes=[b_brow], q=ctx.qpool)
        emit_weight_T3(ctx, consts, w1T, b_w1T, nat2, b_nat2, natcnt, w1h[ex], b_w1h, D, D, ptp, b_ptp)
        emit_weight_T3(ctx, consts, w2T, b_w2T, nat2, b_nat2, natcnt, w2h[ex], b_w2h, D, D, ptp, b_ptp)
        emit_weight_T3(ctx, consts, w3T, b_w3T, nat2, b_nat2, natcnt, w3h[ex], b_w3h, D, D, ptp, b_ptp)
        for t in range(nt):
            ctx.pe.emit(lambda e, t=t, ex=ex: e.transpose(pcb[0:1, 0:128],
                                                          cbuf[:, t * E + ex:t * E + ex + 1],
                                                          consts.id_f[:, :]),
                        reads=[b_cbuf, consts.b_id_f], writes=[b_pcb])
            ctx.dve.emit(lambda e, t=t: e.tensor_copy(crow[0:1, t * 128:(t + 1) * 128], pcb[0:1, 0:128]),
                         reads=[b_pcb], writes=[b_crow])
        def emit_xct(tc):
            xcT, b_xcT = xcT2[tc % 2], b_xcT2[tc % 2]
            ctx.pe.emit(lambda e, tc=tc: e.matmul(pcb[:, :], consts.ones_row_bf[0:1, 0:128],
                                                  crow[0:1, tc * 512:(tc + 1) * 512], start=True, stop=True),
                        reads=[b_crow, consts.b_ones_row_bf], writes=[b_pcb])
            for ci in range(NCH):
                ctx.dve.emit(lambda e, ci=ci, tc=tc, xcT=xcT: e.tensor_mul(
                    xcT[:, ci * 512:(ci + 1) * 512],
                    xnT[:, ci * T + tc * 512: ci * T + (tc + 1) * 512], pcb[:, :]),
                    reads=[b_xnT, b_pcb], writes=[b_xcT])

        emit_xct(0)
        for tc in range(ntc):
            xcT, b_xcT = xcT2[tc % 2], b_xcT2[tc % 2]
            for hc in range(NCH):
                pb, bpb = pz[hc % 2], b_pz[hc % 2]
                fns = []
                for ci in range(NCH):
                    fns.append(lambda e, ci=ci, hc=hc, pb=pb, xcT=xcT: e.matmul(
                        pb[:, :], w1T[:, ci * D + hc * 128: ci * D + (hc + 1) * 128],
                        xcT[:, ci * 512:(ci + 1) * 512], start=(ci == 0), stop=(ci == NCH - 1)))
                ctx.pe.emit_multi(fns, reads=[b_w1T, b_xcT], writes=[bpb])
                ctx.act.emit(lambda e, hc=hc, pb=pb: e.activation(
                    out=hT[:, hc * 512:(hc + 1) * 512], in_=pb[:, :], func=AF.Silu,
                    bias=b1col[:, hc:hc + 1]),
                    reads=[bpb, b_b1col], writes=[b_hT])
            if tc + 1 < ntc:
                emit_xct(tc + 1)
            for s4 in range(4):
                t = tc * 4 + s4
                for half in range(2):
                    pb, bpb = pw[half], b_pw[half]
                    fns = []
                    for ci in range(NCH):
                        fns.append(lambda e, ci=ci, s4=s4, half=half, pb=pb, xcT=xcT: e.matmul(
                            pb[:, :], xcT[:, ci * 512 + s4 * 128: ci * 512 + (s4 + 1) * 128],
                            w3T[:, ci * D + half * 512: ci * D + (half + 1) * 512],
                            start=(ci == 0), stop=False))
                    fns.append(lambda e, half=half, pb=pb: e.matmul(
                        pb[:, :], consts.ones_row_bf[0:1, 0:128],
                        brow[0:1, D + half * 512: D + (half + 1) * 512], start=False, stop=True))
                    ctx.pe.emit_multi(fns, reads=[b_xcT, b_w3T, b_brow, consts.b_ones_row_bf], writes=[bpb])
                    ctx.act.emit(lambda e, half=half, pb=pb: e.activation(
                        out=x3sb[:, half * 512:(half + 1) * 512], in_=pb[:, :], func=AF.Copy),
                        reads=[bpb], writes=[b_x3sb])
                for half in range(2):
                    pb, bpb = pw[2], b_pw[2]
                    fns = []
                    for hc in range(NCH):
                        fns.append(lambda e, hc=hc, s4=s4, half=half, pb=pb: e.matmul(
                            pb[:, :], hT[:, hc * 512 + s4 * 128: hc * 512 + (s4 + 1) * 128],
                            w2T[:, hc * D + half * 512: hc * D + (half + 1) * 512],
                            start=(hc == 0), stop=False))
                    fns.append(lambda e, half=half, pb=pb: e.matmul(
                        pb[:, :], consts.ones_row_bf[0:1, 0:128],
                        brow[0:1, 0 + half * 512: 0 + (half + 1) * 512], start=False, stop=True))
                    ctx.pe.emit_multi(fns, reads=[b_hT, b_w2T, b_brow, consts.b_ones_row_bf], writes=[bpb])
                    ctx.dve.emit(lambda e, half=half, pb=pb: e.tensor_mul(
                        t2[:, :], pb[:, :], x3sb[:, half * 512:(half + 1) * 512]),
                        reads=[bpb, b_x3sb], writes=[b_t2])
                    cb, b_cb = ctrb[ctrbi[0] % 3], b_ctrb[ctrbi[0] % 3]
                    ctx.dve.emit(lambda e, t=t, half=half, ex=ex, cb=cb: e.tensor_scalar_mul(
                        cb[:, half * 512:(half + 1) * 512],
                        t2[:, :], mbuf[:, t * E + ex:t * E + ex + 1]),
                        reads=[b_t2, b_mbuf], writes=[b_cb])
                ctrbi[0] += 1
                ctx.pool.emit(lambda e, t=t, ex=ex, cb=cb: e.dma_start(
                    out=y_hbm[t * 128:(t + 1) * 128, :], in_=cb[:, :],
                    accum_op=(ALU.bypass if ex == 0 else ALU.add)),
                    reads=[b_cb], writes=[b_yh[t]], q=ctx.qpool)

    sw1h, b_sw1h = io["L0_ffn_sw1"], io["b_L0_ffn_sw1"]
    sw2h, b_sw2h = io["L0_ffn_sw2"], io["b_L0_ffn_sw2"]
    sw3h, b_sw3h = io["L0_ffn_sw3"], io["b_L0_ffn_sw3"]
    IC = 256
    nic = (2 * D) // IC
    z1sT, b_z1sT = w1T, b_w1T
    z2sT, b_z2sT = w2T, b_w2T
    swT, b_swT = xcT, b_xcT
    sw2T_, b_sw2T_ = hT, b_hT
    sw3T_, b_sw3T_ = w3T, b_w3T
    for ic in range(nic):
        emit_weight_T3(ctx, consts, swT, b_swT, nat2, b_nat2, natcnt, sw1h, b_sw1h, IC, D, ptp, b_ptp, row0=ic * IC)
        emit_weight_T3(ctx, consts, sw2T_, b_sw2T_, nat2, b_nat2, natcnt, sw2h, b_sw2h, IC, D, ptp, b_ptp, row0=ic * IC)
        emit_weight_T3(ctx, consts, sw3T_, b_sw3T_, nat2, b_nat2, natcnt, sw3h, b_sw3h, D, IC, ptp, b_ptp, col0=ic * IC)
        for isub in range(IC // 128):
            for tc in range(ntc):
                pb, bpb = pz[tc % 2], b_pz[tc % 2]
                fns = []
                for ci in range(NCH):
                    fns.append(lambda e, ci=ci, isub=isub, tc=tc, pb=pb: e.matmul(
                        pb[:, :], swT[:, ci * IC + isub * 128: ci * IC + (isub + 1) * 128],
                        xnT[:, ci * T + tc * 512: ci * T + (tc + 1) * 512],
                        start=(ci == 0), stop=(ci == NCH - 1)))
                ctx.pe.emit_multi(fns, reads=[b_swT, b_xnT], writes=[bpb])
                ctx.act.emit(lambda e, isub=isub, tc=tc, pb=pb: e.activation(
                    out=z1sT[:, isub * T + tc * 512: isub * T + (tc + 1) * 512], in_=pb[:, :], func=AF.Silu),
                    reads=[bpb], writes=[b_z1sT])
                pb2, bpb2 = pw[tc % 2], b_pw[tc % 2]
                fns = []
                for ci in range(NCH):
                    fns.append(lambda e, ci=ci, isub=isub, tc=tc, pb2=pb2: e.matmul(
                        pb2[:, :], sw2T_[:, ci * IC + isub * 128: ci * IC + (isub + 1) * 128],
                        xnT[:, ci * T + tc * 512: ci * T + (tc + 1) * 512],
                        start=(ci == 0), stop=(ci == NCH - 1)))
                ctx.pe.emit_multi(fns, reads=[b_sw2T_, b_xnT], writes=[bpb2])
                ctx.dve.emit(lambda e, isub=isub, tc=tc, pb2=pb2: e.tensor_mul(
                    z1sT[:, isub * T + tc * 512: isub * T + (tc + 1) * 512],
                    z1sT[:, isub * T + tc * 512: isub * T + (tc + 1) * 512], pb2[:, :]),
                    reads=[b_z1sT, bpb2], writes=[b_z1sT])
        for t in range(nt):
            for half in range(2):
                pb, bpb = pw[2], b_pw[2]
                fns = []
                for isub in range(IC // 128):
                    fns.append(lambda e, isub=isub, t=t, half=half, pb=pb: e.matmul(
                        pb[:, :], z1sT[:, isub * T + t * 128: isub * T + (t + 1) * 128],
                        sw3T_[:, isub * D + half * 512: isub * D + (half + 1) * 512],
                        start=(isub == 0), stop=(isub == IC // 128 - 1)))
                ctx.pe.emit_multi(fns, reads=[b_z1sT, b_sw3T_], writes=[bpb])
                cb, b_cb = ctrb[ctrbi[0] % 3], b_ctrb[ctrbi[0] % 3]
                ctx.dve.emit(lambda e, half=half, pb=pb, cb=cb: e.tensor_copy(
                    cb[:, half * 512:(half + 1) * 512], pb[:, :]),
                    reads=[bpb], writes=[b_cb])
                if half == 1:
                    ctrbi[0] += 1
                    ctx.pool.emit(lambda e, t=t, cb=cb: e.dma_start(
                        out=y_hbm[t * 128:(t + 1) * 128, :], in_=cb[:, :], accum_op=ALU.add),
                        reads=[b_cb], writes=[b_yh[t]], q=ctx.qpool)

    for t in range(nt):
        yb, b_yb = ctrb[t % 2], b_ctrb[t % 2]
        ctx.sp.emit(lambda e, t=t, yb=yb: e.dma_start(out=yb[:, :], in_=y_hbm[t * 128:(t + 1) * 128, :]),
                    reads=[b_yh[t]], writes=[b_yb], q=ctx.qsp)
        emit_finale(ctx, consts, T, t, yb[:, :], b_yb, xin, b_xin, xout, b_xout,
                    xt[t % 2], b_xt[t % 2], t1, b_t1, sttmp, b_sttmp, r0=r0)


def build_phase0(ctx, consts, stack, T, io, tag="p0"):
    nc = ctx.nc
    nt = T // 128
    ntc = T // 512
    en = stack.enter_context
    xin, b_xin = io["x_in"], io["b_x_in"]
    xout, b_xout = io["x_out"], io["b_x_out"]
    slopes = get_slopes(H)

    qt_hbm = nc.dram_tensor(tag + "_qt", [D, T], BF16, kind="Internal").ap()
    kt_hbm = nc.dram_tensor(tag + "_kt", [D, T], BF16, kind="Internal").ap()
    v_hbm = nc.dram_tensor(tag + "_v", [T, D], BF16, kind="Internal").ap()
    b_qt = ctx.buf(tag + "_qt"); b_kt = ctx.buf(tag + "_kt"); b_v = ctx.buf(tag + "_v")

    lnst = en(nc.sbuf_tensor(tag + "_lnst", [128, nt * 2], F32)); b_lnst = ctx.buf(tag + "_lnst")
    sttmp = en(nc.sbuf_tensor(tag + "_sttmp", [128, 16], F32)); b_sttmp = ctx.buf(tag + "_sttmp")
    xt = [en(nc.sbuf_tensor(tag + "_xt%d" % i, [128, D], F32)) for i in range(2)]
    b_xt = [ctx.buf(tag + "_xt0"), ctx.buf(tag + "_xt1")]
    kpos = en(nc.sbuf_tensor(tag + "_kpos", [1, T], F32)); b_kpos = ctx.buf(tag + "_kpos")
    cones, b_cones = None, None
    ctx.pool.emit(lambda e: e.iota(kpos[0:1, :], pattern=[[1, T]], base=0, channel_multiplier=0,
                                   allow_small_or_imprecise_dtypes=True), writes=[b_kpos])

    with ctx.sub() as st0:
        en0 = st0.enter_context
        xnT = en0(nc.sbuf_tensor(tag + "_xnT", [128, NCH * T], BF16)); b_xnT = ctx.buf(tag + "_xnT")
        xnb = en0(nc.sbuf_tensor(tag + "_xnb", [128, D], BF16)); b_xnb = ctx.buf(tag + "_xnb")
        wq = en0(nc.sbuf_tensor(tag + "_wq", [128, NCH * D], BF16)); b_wq = ctx.buf(tag + "_wq")
        wk = en0(nc.sbuf_tensor(tag + "_wk", [128, NCH * D], BF16)); b_wk = ctx.buf(tag + "_wk")
        wv = en0(nc.sbuf_tensor(tag + "_wv", [128, NCH * D], BF16)); b_wv = ctx.buf(tag + "_wv")
        qst = en0(nc.sbuf_tensor(tag + "_qst", [128, 512], BF16)); b_qst = ctx.buf(tag + "_qst")
        kst = en0(nc.sbuf_tensor(tag + "_kst", [128, 512], BF16)); b_kst = ctx.buf(tag + "_kst")
        vst = en0(nc.sbuf_tensor(tag + "_vst", [128, 512], BF16)); b_vst = ctx.buf(tag + "_vst")
        ptp = [en0(nc.psum_tensor(tag + "_ptp%d" % i, [128, 128], BF16)) for i in range(2)]
        b_ptp = [ctx.buf(tag + "_ptp%d" % i) for i in range(2)]
        pqk = [en0(nc.psum_tensor(tag + "_pqk%d" % i, [128, 512], F32)) for i in range(2)]
        b_pqk = [ctx.buf(tag + "_pqk%d" % i) for i in range(2)]

        for name, wsb, b_wsb in (("L0_attn_Wq", wq, b_wq), ("L0_attn_Wk", wk, b_wk),
                                 ("L0_attn_Wv", wv, b_wv)):
            wh, b_wh = io[name], io["b_" + name]
            ctx.pool.emit(lambda e, wh=wh, wsb=wsb: e.dma_start(
                out=wsb[:, :].rearrange("p (a c) -> p a c", a=NCH),
                in_=wh[:, :].rearrange("(a p) c -> p a c", p=128)),
                reads=[b_wh], writes=[b_wsb], q=ctx.qpool)

        emit_ln_pass(ctx, consts, T, xin, b_xin, xt, b_xt, xnb, b_xnb, lnst, b_lnst,
                     sttmp, b_sttmp, xnT, b_xnT, ptp, b_ptp)

        for name, wsb, b_wsb, dst, b_dst, stg, b_stg, scale in (
                ("q", wq, b_wq, qt_hbm, b_qt, qst, b_qst, 0.125),
                ("k", wk, b_wk, kt_hbm, b_kt, kst, b_kst, 1.0)):
            for co in range(NCH):
                for tc in range(ntc):
                    pb, bpb = pqk[tc % 2], b_pqk[tc % 2]
                    fns = []
                    for ci in range(NCH):
                        fns.append(lambda e, ci=ci, co=co, tc=tc, pb=pb, wsb=wsb: e.matmul(
                            pb[:, :], wsb[:, ci * D + co * 128: ci * D + (co + 1) * 128],
                            xnT[:, ci * T + tc * 512: ci * T + (tc + 1) * 512],
                            start=(ci == 0), stop=(ci == NCH - 1)))
                    ctx.pe.emit_multi(fns, reads=[b_wsb, b_xnT], writes=[bpb])
                    ctx.act.emit(lambda e, pb=pb, stg=stg, scale=scale: e.activation(
                        out=stg[:, :], in_=pb[:, :], func=AF.Copy, scale=scale),
                        reads=[bpb], writes=[b_stg])
                    ctx.sp.emit(lambda e, co=co, tc=tc, dst=dst, stg=stg: e.dma_start(
                        out=dst[co * 128:(co + 1) * 128, tc * 512:(tc + 1) * 512], in_=stg[:, :]),
                        reads=[b_stg], writes=[b_dst], q=ctx.qsp)
        for t in range(nt):
            for half in range(2):
                pb, bpb = pqk[half], b_pqk[half]
                fns = []
                for ci in range(NCH):
                    fns.append(lambda e, ci=ci, t=t, half=half, pb=pb: e.matmul(
                        pb[:, :], xnT[:, ci * T + t * 128: ci * T + (t + 1) * 128],
                        wv[:, ci * D + half * 512: ci * D + (half + 1) * 512],
                        start=(ci == 0), stop=(ci == NCH - 1)))
                ctx.pe.emit_multi(fns, reads=[b_wv, b_xnT], writes=[bpb])
                ctx.dve.emit(lambda e, pb=pb: e.tensor_copy(vst[:, :], pb[:, :]),
                             reads=[bpb], writes=[b_vst])
                ctx.sp.emit(lambda e, t=t, half=half: e.dma_start(
                    out=v_hbm[t * 128:(t + 1) * 128, half * 512:(half + 1) * 512], in_=vst[:, :]),
                    reads=[b_vst], writes=[b_v], q=ctx.qsp)

    with ctx.sub() as st12:
        ctxfull = st12.enter_context(nc.sbuf_tensor(tag + "_ctx", [128, nt * D], BF16))
        b_ctx = ctx.buf(tag + "_ctx")
        _build_phase0_bc(ctx, consts, st12, T, io, tag, ctxfull, b_ctx, lnst, b_lnst,
                         sttmp, b_sttmp, xt, b_xt, kpos, b_kpos, cones, b_cones,
                         qt_hbm, b_qt, kt_hbm, b_kt, v_hbm, b_v)


def _build_phase0_bc(ctx, consts, stack, T, io, tag, ctxfull, b_ctx, lnst, b_lnst,
                     sttmp, b_sttmp, xt, b_xt, kpos, b_kpos, cones, b_cones,
                     qt_hbm, b_qt, kt_hbm, b_kt, v_hbm, b_v):
    nc = ctx.nc
    nt = T // 128
    nqc = T // 512
    slopes = get_slopes(H)
    xin, b_xin = io["x_in"], io["b_x_in"]
    xout, b_xout = io["x_out"], io["b_x_out"]
    with ctx.sub() as st1:
        en1 = st1.enter_context
        qaug = en1(nc.sbuf_tensor(tag + "_qaug", [128, 2 * T], BF16)); b_qaug = ctx.buf(tag + "_qaug")
        kaug = en1(nc.sbuf_tensor(tag + "_kaug", [128, 2 * T], BF16)); b_kaug = ctx.buf(tag + "_kaug")
        vp = en1(nc.sbuf_tensor(tag + "_vp", [128, nt * 128], BF16)); b_vp = ctx.buf(tag + "_vp")
        sq = en1(nc.sbuf_tensor(tag + "_sq", [64, T], BF16)); b_sq = ctx.buf(tag + "_sq")
        rows = en1(nc.sbuf_tensor(tag + "_rows", [1, 2 * T + 8], F32)); b_rows = ctx.buf(tag + "_rows")
        ckbf = en1(nc.sbuf_tensor(tag + "_ckbf", [1, T], BF16)); b_ckbf = ctx.buf(tag + "_ckbf")
        ptile = [[en1(nc.sbuf_tensor(tag + "_pt%d%d" % (i, j), [128, 512], BF16)) for j in range(2)]
                 for i in range(2)]
        b_ptile = [[ctx.buf(tag + "_pt%d%d" % (i, j)) for j in range(2)] for i in range(2)]
        zrow = en1(nc.sbuf_tensor(tag + "_zrow", [1, 2 * 512], F32)); b_zrow = ctx.buf(tag + "_zrow")
        zbc = en1(nc.sbuf_tensor(tag + "_zbc", [128, 512], F32)); b_zbc = ctx.buf(tag + "_zbc")
        mzc = en1(nc.sbuf_tensor(tag + "_mzc", [128, nt], F32)); b_mzc = ctx.buf(tag + "_mzc")
        pst = [[en1(nc.psum_tensor(tag + "_pst%d%d" % (i, j), [128, 512], F32)) for j in range(2)]
               for i in range(2)]
        b_pst = [[ctx.buf(tag + "_pst%d%d" % (i, j)) for j in range(2)] for i in range(2)]
        pctx = en1(nc.psum_tensor(tag + "_pctx", [128, 512], F32)); b_pctx = ctx.buf(tag + "_pctx")
        pz = en1(nc.psum_tensor(tag + "_pz", [128, 512], F32)); b_pz = ctx.buf(tag + "_pz")

        ctx.pool.emit(lambda e: e.memset(qaug[64:65, :], 1.0), writes=[b_qaug])
        for i in range(2 * T // 512):
            ctx.sp.emit(lambda e, i=i: e.dma_start(out=qaug[65:66, i * 512:(i + 1) * 512],
                                                   in_=consts.ones_row_bf[0:1, :]),
                        reads=[consts.b_ones_row_bf], writes=[b_qaug], q=ctx.qsp)
            ctx.sp.emit(lambda e, i=i: e.dma_start(out=kaug[66:67, i * 512:(i + 1) * 512],
                                                   in_=consts.ones_row_bf[0:1, :]),
                        reads=[consts.b_ones_row_bf], writes=[b_kaug], q=ctx.qsp)

        for pair in range(NCH):
            ctx.sp.emit(lambda e, pair=pair: e.dma_start(
                out=qaug[0:64, :].rearrange("p (two t) -> p two t", two=2),
                in_=qt_hbm[pair * 128:(pair + 1) * 128, :].rearrange("(two p) t -> p two t", p=64)),
                reads=[b_qt], writes=[b_qaug], q=ctx.qsp)
            ctx.sp.emit(lambda e, pair=pair: e.dma_start(
                out=kaug[0:64, :].rearrange("p (two t) -> p two t", two=2),
                in_=kt_hbm[pair * 128:(pair + 1) * 128, :].rearrange("(two p) t -> p two t", p=64)),
                reads=[b_kt], writes=[b_kaug], q=ctx.qsp)
            ctx.sp.emit(lambda e, pair=pair: e.dma_start(
                out=vp[:, :].rearrange("p (kt c) -> p kt c", c=128),
                in_=v_hbm[:, pair * 128:(pair + 1) * 128].rearrange("(kt p) c -> p kt c", p=128)),
                reads=[b_v], writes=[b_vp], q=ctx.qsp)
            for hb in range(2):
                c_h = slopes[2 * pair + hb] / 8.0
                ctx.act.emit(lambda e, c_h=c_h: e.activation(
                    out=rows[0:1, 0:T], in_=kpos[0:1, :], func=AF.Copy, scale=c_h),
                    reads=[b_kpos], writes=[b_rows])
                ctx.dve.emit(lambda e: e.tensor_copy(ckbf[0:1, :], rows[0:1, 0:T]),
                             reads=[b_rows], writes=[b_ckbf])
                ctx.dve.emit(lambda e, hb=hb: e.tensor_copy(kaug[64:65, hb * T:(hb + 1) * T],
                                                            ckbf[0:1, :]),
                             reads=[b_ckbf], writes=[b_kaug])
                ctx.dve.emit(lambda e: e.tensor_sub(rows[0:1, T:2 * T], rows[0:1, 0:T],
                                                    ckbf[0:1, :]),
                             reads=[b_rows, b_ckbf], writes=[b_rows])
                ctx.pool.emit(lambda e, hb=hb: e.dma_start(out=kaug[65:66, hb * T:(hb + 1) * T],
                                                           in_=rows[0:1, T:2 * T]),
                              reads=[b_rows], writes=[b_kaug], q=ctx.qpool)
                ctx.act.emit(lambda e, hb=hb: e.activation(
                    out=sq[0:64, :], in_=qaug[0:64, hb * T:(hb + 1) * T], func=AF.Square),
                    reads=[b_qaug], writes=[b_sq])
                for i in range(T // 512):
                    pb = pz if i % 2 == 0 else pctx
                    bpb = b_pz if i % 2 == 0 else b_pctx
                    ctx.pe.emit(lambda e, i=i, pb=pb: e.matmul(
                        pb[0:1, :], consts.ones_col_bf[0:64, 0:1], sq[0:64, i * 512:(i + 1) * 512],
                        start=True, stop=True),
                        reads=[b_sq, consts.b_ones_col_bf], writes=[bpb])
                    ctx.dve.emit(lambda e, i=i, pb=pb: e.tensor_copy(
                        rows[0:1, i * 512:(i + 1) * 512], pb[0:1, :]),
                        reads=[bpb], writes=[b_rows])
                ctx.act.emit(lambda e, hb=hb: e.activation(
                    out=sq[0:64, :], in_=kaug[0:64, hb * T:(hb + 1) * T], func=AF.Square),
                    reads=[b_kaug], writes=[b_sq])
                for i in range(T // 512):
                    pb = pz if i % 2 == 0 else pctx
                    bpb = b_pz if i % 2 == 0 else b_pctx
                    ctx.pe.emit(lambda e, i=i, pb=pb: e.matmul(
                        pb[0:1, :], consts.ones_col_bf[0:64, 0:1], sq[0:64, i * 512:(i + 1) * 512],
                        start=True, stop=True),
                        reads=[b_sq, consts.b_ones_col_bf], writes=[bpb])
                    ctx.dve.emit(lambda e, i=i, pb=pb: e.tensor_copy(
                        rows[0:1, T + i * 512:T + (i + 1) * 512], pb[0:1, :]),
                        reads=[bpb], writes=[b_rows])
                ctx.dve.emit(lambda e: e.reduce_max(rows[0:1, 2 * T:2 * T + 1],
                                                    rows[0:1, T:2 * T], axis=AX.X),
                             reads=[b_rows], writes=[b_rows])
                ctx.act.emit(lambda e: e.activation(out=rows[0:1, 2 * T + 1:2 * T + 2],
                                                    in_=rows[0:1, 2 * T:2 * T + 1], func=AF.Ln),
                             reads=[b_rows], writes=[b_rows])
                ctx.act.emit(lambda e: e.activation(out=rows[0:1, 2 * T + 1:2 * T + 2],
                                                    in_=rows[0:1, 2 * T + 1:2 * T + 2],
                                                    func=AF.Copy, scale=0.5),
                             reads=[b_rows], writes=[b_rows])
                ctx.act.emit(lambda e: e.activation(out=rows[0:1, 0:T], in_=rows[0:1, 0:T],
                                                    func=AF.Ln),
                             reads=[b_rows], writes=[b_rows])
                ctx.act.emit(lambda e: e.activation(out=rows[0:1, 0:T], in_=rows[0:1, 0:T],
                                                    func=AF.Exp, scale=0.5,
                                                    bias=rows[0:1, 2 * T + 1:2 * T + 2]),
                             reads=[b_rows], writes=[b_rows])
                ctx.act.emit(lambda e, c_h=c_h: e.activation(
                    out=rows[0:1, T:2 * T], in_=kpos[0:1, :], func=AF.Copy, scale=c_h),
                    reads=[b_kpos], writes=[b_rows])
                ctx.dve.emit(lambda e: e.tensor_add(rows[0:1, 0:T], rows[0:1, 0:T],
                                                    rows[0:1, T:2 * T]),
                             reads=[b_rows], writes=[b_rows])
                ctx.dve.emit(lambda e: e.tensor_scalar(rows[0:1, 0:T], rows[0:1, 0:T],
                                                       0.5, -1.0, ALU.add, ALU.mult),
                             reads=[b_rows], writes=[b_rows])
                ctx.pool.emit(lambda e, hb=hb: e.dma_start(
                    out=qaug[66:67, hb * T:(hb + 1) * T], in_=rows[0:1, 0:T]),
                    reads=[b_rows], writes=[b_qaug], q=ctx.qpool)

            for qc in range(nqc):
                q0 = qc * 512
                ctxsl = ctxfull[:, pair * T + q0: pair * T + q0 + 512]
                klast = 4 * qc + 3
                def emit_cz(kt):
                    par = kt % 2
                    ctx.pe.emit_multi([
                        (lambda e, kt=kt, par=par: e.matmul(
                            pctx[0:64, :], vp[:, kt * 128: kt * 128 + 64],
                            ptile[0][par][:, 0:512],
                            start=(kt == 0), stop=(kt == klast), tile_position=(0, 0))),
                        (lambda e, kt=kt, par=par: e.matmul(
                            pctx[64:128, :], vp[:, kt * 128 + 64: kt * 128 + 128],
                            ptile[1][par][:, 0:512],
                            start=(kt == 0), stop=(kt == klast), tile_position=(0, 64),
                            skip_group_check=True)),
                    ], reads=[b_vp, b_ptile[0][par], b_ptile[1][par]], writes=[b_pctx])
                    ctx.pe.emit_multi([
                        (lambda e, kt=kt, par=par: e.matmul(
                            pz[0:1, :], consts.ones_col_bf[:, 0:1], ptile[0][par][:, 0:512],
                            start=(kt == 0), stop=(kt == klast), tile_position=(0, 0))),
                        (lambda e, kt=kt, par=par: e.matmul(
                            pz[32:33, :], consts.ones_col_bf[:, 0:1], ptile[1][par][:, 0:512],
                            start=(kt == 0), stop=(kt == klast), tile_position=(0, 32),
                            skip_group_check=True)),
                    ], reads=[consts.b_ones_col_bf, b_ptile[0][par], b_ptile[1][par]],
                        writes=[b_pz])

                for kt in range(4 * qc + 4):
                    jd = kt - 4 * qc
                    par = kt % 2
                    for hb in range(2):
                        pb, bpb = pst[hb][par], b_pst[hb][par]
                        fns = [
                            (lambda e, kt=kt, q0=q0, hb=hb, pb=pb: e.matmul(
                                pb[:, :],
                                kaug[0:67, hb * T + kt * 128: hb * T + (kt + 1) * 128],
                                qaug[0:67, hb * T + q0: hb * T + q0 + 512],
                                start=True, stop=not (0 <= jd <= 3))),
                        ]
                        if 0 <= jd <= 3:
                            for jj in range(jd):
                                fns.append(lambda e, jj=jj, pb=pb: e.matmul(
                                    pb[:, jj * 128:(jj + 1) * 128],
                                    consts.id_bf[:, :], consts.fullmask[:, :],
                                    start=False, stop=False))
                            fns.append(lambda e, jd=jd, pb=pb: e.matmul(
                                pb[:, jd * 128:(jd + 1) * 128],
                                consts.id_bf[:, :], consts.masklt[:, :],
                                start=False, stop=True))
                        ctx.pe.emit_multi(fns, reads=[b_kaug, b_qaug, consts.b_id_bf,
                                                      consts.b_masklt, consts.b_fullmask],
                                          writes=[bpb])
                        ctx.act.emit(lambda e, hb=hb, par=par, pb=pb: e.activation(
                            out=ptile[hb][par][:, 0:512], in_=pb[:, :], func=AF.Exp),
                            reads=[bpb], writes=[b_ptile[hb][par]])
                    if kt > 0:
                        emit_cz(kt - 1)
                emit_cz(4 * qc + 3)
                ctx.dve.emit(lambda e: e.tensor_copy(zrow[0:1, 0:512], pz[0:1, :]),
                             reads=[b_pz], writes=[b_zrow])
                ctx.dve.emit(lambda e: e.tensor_copy(zrow[0:1, 512:1024], pz[32:33, :]),
                             reads=[b_pz], writes=[b_zrow])
                ctx.act.emit(lambda e: e.activation(out=zrow[0:1, :], in_=zrow[0:1, :], func=AF.Ln),
                             reads=[b_zrow], writes=[b_zrow])
                ctx.act.emit(lambda e: e.activation(out=zrow[0:1, :], in_=zrow[0:1, :], func=AF.Exp,
                                                    scale=-1.0),
                             reads=[b_zrow], writes=[b_zrow])
                ctx.pe.emit_multi([
                    (lambda e: e.matmul(pz[0:64, :], consts.ones_row[0:1, 0:64],
                                        zrow[0:1, 0:512], start=True, stop=True,
                                        tile_position=(0, 0))),
                    (lambda e: e.matmul(pz[64:128, :], consts.ones_row[0:1, 0:64],
                                        zrow[0:1, 512:1024], start=True, stop=True,
                                        tile_position=(0, 64), skip_group_check=True)),
                ], reads=[consts.b_ones_row, b_zrow], writes=[b_pz])
                ctx.dve.emit(lambda e: e.tensor_copy(zbc[:, :], pz[:, :]),
                             reads=[b_pz], writes=[b_zbc])
                ctx.dve.emit(lambda e, ctxsl=ctxsl: e.tensor_mul(ctxsl, pctx[:, :], zbc[:, :]),
                             reads=[b_pctx, b_zbc], writes=[b_ctx])

    with ctx.sub() as st2:
        en2 = st2.enter_context
        wo = en2(nc.sbuf_tensor(tag + "_wo", [128, NCH * D], BF16)); b_wo = ctx.buf(tag + "_wo")
        borow = en2(nc.sbuf_tensor(tag + "_bo", [1, D], F32)); b_bo = ctx.buf(tag + "_bo")
        cts = en2(nc.sbuf_tensor(tag + "_cts", [128, NCH * 128], BF16)); b_cts = ctx.buf(tag + "_cts")
        ao = en2(nc.sbuf_tensor(tag + "_ao", [128, D], F32)); b_ao = ctx.buf(tag + "_ao")
        t1 = en2(nc.sbuf_tensor(tag + "_t1", [128, D], F32)); b_t1 = ctx.buf(tag + "_t1")
        ptc = [en2(nc.psum_tensor(tag + "_ptc%d" % i, [128, 128], BF16)) for i in range(2)]
        b_ptc = [ctx.buf(tag + "_ptc%d" % i) for i in range(2)]
        ppr = [en2(nc.psum_tensor(tag + "_ppr%d" % i, [128, 512], F32)) for i in range(2)]
        b_ppr = [ctx.buf(tag + "_ppr%d" % i) for i in range(2)]

        woh, b_woh = io["L0_attn_Wo"], io["b_L0_attn_Wo"]
        boh, b_boh = io["L0_attn_bo"], io["b_L0_attn_bo"]
        ctx.pool.emit(lambda e: e.dma_start(
            out=wo[:, :].rearrange("p (a c) -> p a c", a=NCH),
            in_=woh[:, :].rearrange("(a p) c -> p a c", p=128)),
            reads=[b_woh], writes=[b_wo], q=ctx.qpool)
        ctx.sp.emit(lambda e: e.dma_start(out=borow[0:1, :], in_=boh[None, :]),
                    reads=[b_boh], writes=[b_bo], q=ctx.qsp)
        for t in range(nt):
            for half in range(2):
                pb, bpb = ppr[half], b_ppr[half]
                fns = []
                for ci in range(NCH):
                    fns.append(lambda e, ci=ci, t=t, half=half, pb=pb: e.matmul(
                        pb[:, :], ctxfull[:, ci * T + t * 128: ci * T + (t + 1) * 128],
                        wo[:, ci * D + half * 512: ci * D + (half + 1) * 512],
                        start=(ci == 0), stop=False))
                fns.append(lambda e, half=half, pb=pb: e.matmul(
                    pb[:, :], consts.ones_row[0:1, 0:128],
                    borow[0:1, half * 512:(half + 1) * 512], start=False, stop=True))
                ctx.pe.emit_multi(fns, reads=[b_ctx, b_wo, b_bo, consts.b_ones_row], writes=[bpb])
                ctx.dve.emit(lambda e, half=half, pb=pb: e.tensor_copy(ao[:, half * 512:(half + 1) * 512],
                                                                      pb[:, :]),
                             reads=[bpb], writes=[b_ao])
            emit_finale(ctx, consts, T, t, ao[:, :], b_ao, xin, b_xin, xout, b_xout,
                        xt[t % 2], b_xt[t % 2], t1, b_t1, sttmp, b_sttmp)


from contextlib import ExitStack as _ExitStack


def _substack(stack):
    return _ExitStack()


def build_phase3(ctx, consts, stack, T, io, tag="p3"):
    nc = ctx.nc
    nt = T // 128
    ntc = T // 512
    en = stack.enter_context
    xin, b_xin = io["x_in"], io["b_x_in"]
    xout, b_xout = io["x_out"], io["b_x_out"]
    NH, RAILS, RW = 4, 64, 16
    ZT = NH * RAILS * 64
    ZC = 1024
    nzc = ZT // ZC
    NG = RAILS // 8

    xnT = en(nc.sbuf_tensor(tag + "_xnT", [128, NCH * T], BF16)); b_xnT = ctx.buf(tag + "_xnT")
    acc = en(nc.sbuf_tensor(tag + "_acc", [128, nt * D], FP16)); b_acc = ctx.buf(tag + "_acc")
    lnst = en(nc.sbuf_tensor(tag + "_lnst", [128, nt * 2], F32)); b_lnst = ctx.buf(tag + "_lnst")
    sttmp = en(nc.sbuf_tensor(tag + "_sttmp", [128, 16], F32)); b_sttmp = ctx.buf(tag + "_sttmp")
    xt = [en(nc.sbuf_tensor(tag + "_xt%d" % i, [128, D], F32)) for i in range(2)]
    b_xt = [ctx.buf(tag + "_xt0"), ctx.buf(tag + "_xt1")]
    xnb = en(nc.sbuf_tensor(tag + "_xnb", [128, D], BF16)); b_xnb = ctx.buf(tag + "_xnb")
    t1 = en(nc.sbuf_tensor(tag + "_t1", [128, D], F32)); b_t1 = ctx.buf(tag + "_t1")
    cdiag = en(nc.sbuf_tensor(tag + "_cdiag", [128, 32 * 128], BF16)); b_cdiag = ctx.buf(tag + "_cdiag")
    hT2 = [en(nc.sbuf_tensor(tag + "_hT%d" % i, [128, (ZC // 128) * 512], BF16)) for i in range(2)]
    b_hT2 = [ctx.buf(tag + "_hT%d" % i) for i in range(2)]
    wcT = en(nc.sbuf_tensor(tag + "_wcT", [128, (ZC // 128) * D], BF16)); b_wcT = ctx.buf(tag + "_wcT")
    wcst = [en(nc.sbuf_tensor(tag + "_wcst%d" % i, [128, ZC], BF16)) for i in range(2)]
    b_wcst = [ctx.buf(tag + "_wcst0"), ctx.buf(tag + "_wcst1")]
    ao = en(nc.sbuf_tensor(tag + "_ao", [128, D], F32)); b_ao = ctx.buf(tag + "_ao")

    ptp = [en(nc.psum_tensor(tag + "_ptp%d" % i, [128, 512], BF16)) for i in range(2)]
    b_ptp = [ctx.buf(tag + "_ptp%d" % i) for i in range(2)]
    pzr = [en(nc.psum_tensor(tag + "_pzr%d" % i, [128, 512], F32)) for i in range(2)]
    b_pzr = [ctx.buf(tag + "_pzr%d" % i) for i in range(2)]
    ppr = [en(nc.psum_tensor(tag + "_ppr%d" % i, [128, 512], F32)) for i in range(2)]
    b_ppr = [ctx.buf(tag + "_ppr%d" % i) for i in range(2)]

    rmh, b_rmh = io["L1_ffn_rail_maker"], io["b_L1_ffn_rail_maker"]
    ruph, b_ruph = io["L1_ffn_rail_up_proj"], io["b_L1_ffn_rail_up_proj"]
    wch, b_wch = io["L1_ffn_w_cproj"], io["b_L1_ffn_w_cproj"]
    rm_flat = rmh.rearrange("h r i j -> (h r i) j")
    rup_flat = ruph.rearrange("h r k j -> (h r k) j")

    emit_ln_pass(ctx, consts, T, xin, b_xin, xt, b_xt, xnb, b_xnb, lnst, b_lnst,
                 sttmp, b_sttmp, xnT, b_xnT, ptp, b_ptp)

    bstack = ctx.sub(); bst = bstack.__enter__()
    enb = bst.enter_context
    NB3 = 4
    rmdiag4 = [enb(nc.sbuf_tensor(tag + "_rmd%d" % i, [128, 128], BF16)) for i in range(NB3)]
    b_rmd4 = [ctx.buf(tag + "_rmd%d" % i) for i in range(NB3)]
    rupd4 = [enb(nc.sbuf_tensor(tag + "_rupd%d" % i, [128, 128], BF16)) for i in range(NB3)]
    b_rupd4 = [ctx.buf(tag + "_rupd%d" % i) for i in range(NB3)]
    rst4 = [enb(nc.sbuf_tensor(tag + "_rst%d" % i, [128, 16 * 5], BF16)) for i in range(NB3)]
    b_rst4 = [ctx.buf(tag + "_rst%d" % i) for i in range(NB3)]
    rtr4 = [enb(nc.sbuf_tensor(tag + "_rtr%d" % i, [16, 128], BF16)) for i in range(NB3)]
    b_rtr4 = [ctx.buf(tag + "_rtr%d" % i) for i in range(NB3)]
    rmTt4 = [enb(nc.sbuf_tensor(tag + "_rmTt%d" % i, [128, 128], BF16)) for i in range(NB3)]
    b_rmTt4 = [ctx.buf(tag + "_rmTt%d" % i) for i in range(NB3)]
    csc4 = [enb(nc.sbuf_tensor(tag + "_csc%d" % i, [32, 128], BF16)) for i in range(NB3)]
    b_csc4 = [ctx.buf(tag + "_csc%d" % i) for i in range(NB3)]
    for h in range(NH):
        for g in range(NG):
            gi = h * NG + g
            rmdiag, b_rmd = rmdiag4[gi % NB3], b_rmd4[gi % NB3]
            rupd, b_rupd = rupd4[gi % NB3], b_rupd4[gi % NB3]
            rst, b_rst = rst4[gi % NB3], b_rst4[gi % NB3]
            rtr, b_rtr = rtr4[gi % NB3], b_rtr4[gi % NB3]
            rmTt, b_rmTt = rmTt4[gi % NB3], b_rmTt4[gi % NB3]
            csc, b_csc = csc4[gi % NB3], b_csc4[gi % NB3]
            ctx.pool.emit(lambda e, rmdiag=rmdiag: e.memset(rmdiag[:, :], 0.0), writes=[b_rmd])
            ctx.pool.emit(lambda e, rupd=rupd: e.memset(rupd[:, :], 0.0), writes=[b_rupd])
            row0 = (h * RAILS + g * 8) * 16
            ctx.pool.emit(lambda e, row0=row0, rst=rst: e.dma_start(out=rst[:, 0:16],
                                                                   in_=rm_flat[row0:row0 + 128, :]),
                          reads=[b_rmh], writes=[b_rst], q=ctx.qpool)
            for b in range(8):
                ctx.sp.emit(lambda e, b=b, rmdiag=rmdiag, rst=rst: e.dma_start(
                    out=rmdiag[16 * b:16 * (b + 1), 16 * b:16 * (b + 1)],
                    in_=rst[16 * b:16 * (b + 1), 0:16]),
                    reads=[b_rst], writes=[b_rmd], q=ctx.qsp)
            krow0 = (h * RAILS + g * 8) * 64
            for s in range(4):
                ctx.pool.emit(lambda e, krow0=krow0, s=s, rst=rst: e.dma_start(
                    out=rst[:, 16 * (s + 1):16 * (s + 2)],
                    in_=rup_flat[krow0 + s * 128: krow0 + (s + 1) * 128, :]),
                    reads=[b_ruph], writes=[b_rst], q=ctx.qpool)
                ctx.pe.emit(lambda e, s=s, rst=rst: e.transpose(
                    ptp[0][0:16, 0:128], rst[:, 16 * (s + 1):16 * (s + 2)], consts.id_bf[:, :]),
                    reads=[b_rst, consts.b_id_bf], writes=[b_ptp[0]])
                ctx.dve.emit(lambda e, rtr=rtr: e.tensor_copy(rtr[:, :], ptp[0][0:16, 0:128]),
                             reads=[b_ptp[0]], writes=[b_rtr])
                ctx.sp.emit(lambda e, s=s, rupd=rupd, rtr=rtr: e.dma_start(
                    out=rupd[32 * s:32 * s + 16, 0:64], in_=rtr[:, 0:64]),
                    reads=[b_rtr], writes=[b_rupd], q=ctx.qsp)
                ctx.sp.emit(lambda e, s=s, rupd=rupd, rtr=rtr: e.dma_start(
                    out=rupd[32 * s + 16:32 * s + 32, 64:128], in_=rtr[:, 64:128]),
                    reads=[b_rtr], writes=[b_rupd], q=ctx.qsp)
            ctx.pe.emit(lambda e, rmdiag=rmdiag: e.transpose(ptp[0][:, 0:128], rmdiag[:, :],
                                                             consts.id_bf[:, :]),
                        reads=[b_rmd, consts.b_id_bf], writes=[b_ptp[0]])
            ctx.dve.emit(lambda e, rmTt=rmTt: e.tensor_copy(rmTt[:, :], ptp[0][:, 0:128]),
                         reads=[b_ptp[0]], writes=[b_rmTt])
            for s in range(4):
                pzb, b_pzb = pzr[s % 2], b_pzr[s % 2]
                ctx.pe.emit(lambda e, s=s, pzb=pzb, rmTt=rmTt, rupd=rupd: e.matmul(
                    pzb[0:32, 0:128], rmTt[32 * s:32 * s + 32, 32 * s:32 * s + 32],
                    rupd[32 * s:32 * s + 32, 0:128],
                    start=True, stop=True, tile_position=(32 * s, 0)),
                    reads=[b_rmTt, b_rupd], writes=[b_pzb])
                if s < 3:
                    ctx.dve.emit(lambda e, gi=gi, s=s, pzb=pzb: e.tensor_copy(
                        cdiag[32 * s:32 * s + 32, gi * 128:(gi + 1) * 128], pzb[0:32, 0:128]),
                        reads=[b_pzb], writes=[b_cdiag])
                else:
                    ctx.dve.emit(lambda e, pzb=pzb, csc=csc: e.tensor_copy(csc[:, :], pzb[0:32, 0:128]),
                                 reads=[b_pzb], writes=[b_csc])
                    ctx.sp.emit(lambda e, gi=gi, csc=csc: e.dma_start(
                        out=cdiag[96:128, gi * 128:(gi + 1) * 128], in_=csc[:, :]),
                        reads=[b_csc], writes=[b_cdiag], q=ctx.qsp)
    bstack.__exit__(None, None, None)

    ctx.pool.emit(lambda e: e.memset(acc[:, :], 0.0), writes=[b_acc])

    nsub = ZC // 128
    ngc = ZC // 512
    for zc in range(nzc):
        h = zc // (nzc // NH)
        wcTr = wcT[:, :].rearrange("p (z d) -> p z d", z=nsub)
        nb = 0
        for ci in range(NCH):
            st, b_st = wcst[ci % 2], b_wcst[ci % 2]
            ctx.pool.emit(lambda e, ci=ci, zc=zc, st=st: e.dma_start(
                out=st[:, :], in_=wch[ci * 128:(ci + 1) * 128, zc * ZC:(zc + 1) * ZC]),
                reads=[b_wch], writes=[b_st], q=ctx.qpool)
            for z0 in range(0, nsub, 4):
                zn = min(4, nsub - z0)
                pb, bpb = ptp[nb % 2], b_ptp[nb % 2]
                nb += 1
                for j in range(zn):
                    ctx.pe.emit(lambda e, z0=z0, j=j, pb=pb, st=st: e.transpose(
                        pb[:, j * 128:(j + 1) * 128], st[:, (z0 + j) * 128:(z0 + j + 1) * 128],
                        consts.id_bf[:, :]),
                        reads=[b_st, consts.b_id_bf], writes=[bpb])
                ctx.dve.emit(lambda e, ci=ci, z0=z0, zn=zn, pb=pb: e.tensor_copy(
                    wcTr[:, z0:z0 + zn, ci * 128:(ci + 1) * 128], pb[:, 0:zn * 128]
                    .rearrange("p (z d) -> p z d", z=zn)),
                    reads=[bpb], writes=[b_wcT])
        def emit_rails(tc):
            hT, b_hT = hT2[tc % 2], b_hT2[tc % 2]
            for gl in range(ngc):
                g = (zc % (nzc // NH)) * ngc + gl
                gi = h * NG + g
                for s in range(4):
                    pzb, b_pzb = pzr[s % 2], b_pzr[s % 2]
                    ctx.pe.emit(lambda e, gi=gi, g=g, tc=tc, s=s, pzb=pzb: e.matmul(
                        pzb[:, :], cdiag[32 * s:32 * s + 32, gi * 128:(gi + 1) * 128],
                        xnT[32 * s:32 * s + 32, g * T + tc * 512: g * T + (tc + 1) * 512],
                        start=True, stop=True, tile_position=(32 * s, 0)),
                        reads=[b_cdiag, b_xnT], writes=[b_pzb])
                    ctx.act.emit(lambda e, gl=gl, s=s, pzb=pzb, hT=hT: e.activation(
                        out=hT[:, (gl * 4 + s) * 512:(gl * 4 + s + 1) * 512], in_=pzb[:, :],
                        func=AF.Gelu),
                        reads=[b_pzb], writes=[b_hT])

        emit_rails(0)
        for tc in range(ntc):
            if tc + 1 < ntc:
                emit_rails(tc + 1)
            hT, b_hT = hT2[tc % 2], b_hT2[tc % 2]
            for t4 in range(4):
                t = tc * 4 + t4
                for half in range(2):
                    pb, bpb = ppr[half], b_ppr[half]
                    fns = []
                    for zs in range(nsub):
                        fns.append(lambda e, zs=zs, t4=t4, half=half, pb=pb, hT=hT: e.matmul(
                            pb[:, :], hT[:, zs * 512 + t4 * 128: zs * 512 + (t4 + 1) * 128],
                            wcT[:, zs * D + half * 512: zs * D + (half + 1) * 512],
                            start=(zs == 0), stop=(zs == nsub - 1)))
                    ctx.pe.emit_multi(fns, reads=[b_hT, b_wcT], writes=[bpb])
                    ctx.dve.emit(lambda e, t=t, half=half, pb=pb: e.tensor_add(
                        acc[:, t * D + half * 512: t * D + (half + 1) * 512],
                        acc[:, t * D + half * 512: t * D + (half + 1) * 512], pb[:, :]),
                        reads=[bpb, b_acc], writes=[b_acc])

    for t in range(nt):
        ctx.dve.emit(lambda e, t=t: e.tensor_copy(ao[:, :], acc[:, t * D:(t + 1) * D]),
                     reads=[b_acc], writes=[b_ao])
        emit_finale(ctx, consts, T, t, ao[:, :], b_ao, xin, b_xin, xout, b_xout,
                    xt[t % 2], b_xt[t % 2], t1, b_t1, sttmp, b_sttmp)



PHASE_WEIGHTS = {
    0: ["L0_attn_Wq", "L0_attn_Wk", "L0_attn_Wv", "L0_attn_Wo", "L0_attn_bo"],
    1: ["L0_ffn_gate_w", "L0_ffn_w1", "L0_ffn_b1", "L0_ffn_w2", "L0_ffn_b2",
        "L0_ffn_w3", "L0_ffn_b3", "L0_ffn_sw1", "L0_ffn_sw2", "L0_ffn_sw3"],
    2: ["L1_attn_q", "L1_attn_k", "L1_attn_log_decay", "L1_attn_norm_weight", "L1_attn_o_w"],
    3: ["L1_ffn_rail_maker", "L1_ffn_rail_up_proj", "L1_ffn_w_cproj"],
}
_SHAPES = {
    "L0_attn_Wq": [1024, 1024], "L0_attn_Wk": [1024, 1024], "L0_attn_Wv": [1024, 1024],
    "L0_attn_Wo": [1024, 1024], "L0_attn_bo": [1024],
    "L0_ffn_gate_w": [8, 1024], "L0_ffn_w1": [8, 1024, 1024], "L0_ffn_b1": [8, 1024],
    "L0_ffn_w2": [8, 1024, 1024], "L0_ffn_b2": [8, 1024], "L0_ffn_w3": [8, 1024, 1024],
    "L0_ffn_b3": [8, 1024], "L0_ffn_sw1": [2048, 1024], "L0_ffn_sw2": [2048, 1024],
    "L0_ffn_sw3": [1024, 2048],
    "L1_attn_q": [1024], "L1_attn_k": [1024], "L1_attn_log_decay": [16],
    "L1_attn_norm_weight": [1024], "L1_attn_o_w": [1024, 1024],
    "L1_ffn_rail_maker": [4, 64, 16, 16], "L1_ffn_rail_up_proj": [4, 64, 64, 16],
    "L1_ffn_w_cproj": [1024, 16384],
}
ALL_WEIGHTS = [nm for p in range(4) for nm in PHASE_WEIGHTS[p]]


def build_fused(T):
    from contextlib import ExitStack
    nc = bass.Bass("TRN2", target_bir_lowering=False, debug=False)
    ctx = Ctx(nc)
    io = {}
    for nm in ALL_WEIGHTS:
        io[nm] = nc.dram_tensor(nm, _SHAPES[nm], F32, kind="ExternalInput").ap()
        io["b_" + nm] = ctx.buf(nm)
    x0 = nc.dram_tensor("x_in", [T, D], F32, kind="ExternalInput").ap()
    y = nc.dram_tensor("x_out", [T, D], F32, kind="ExternalOutput").ap()
    xmid = [nc.dram_tensor("xmid%d" % i, [T, D], F32, kind="Internal").ap() for i in range(3)]
    b_x = [ctx.buf("x%d" % i) for i in range(5)]
    chain = [x0, xmid[0], xmid[1], xmid[2], y]
    builders = [build_phase0, build_phase1, build_phase2, build_phase3]
    with ExitStack() as stack:
        consts = Consts(ctx, stack)
        for p in range(4):
            io["x_in"], io["b_x_in"] = chain[p], b_x[p]
            io["x_out"], io["b_x_out"] = chain[p + 1], b_x[p + 1]
            with ctx.sub() as pstack:
                builders[p](ctx, consts, pstack, T, io)
        ctx.finish()
        ctx.render_all()
    nc.finalize()
    return nc


def kernel(**inputs):
    from concourse.bass_utils import run_bass_kernel_spmd
    NCORES = 8
    T = 4096
    x = np.ascontiguousarray(np.asarray(inputs["L0_attn_x"], dtype=np.float32))
    nc = build_fused(T)
    wmap = {nm: np.ascontiguousarray(np.asarray(inputs[nm], dtype=np.float32))
            for nm in ALL_WEIGHTS}
    in_maps = []
    for i in range(NCORES):
        m = {"x_in": np.ascontiguousarray(x[i])}
        m.update(wmap)
        in_maps.append(m)
    res = run_bass_kernel_spmd(nc, in_maps, core_ids=list(range(NCORES)))
    return np.stack([np.asarray(res.results[i]["x_out"]) for i in range(NCORES)], axis=0)


# version 14
# speedup vs baseline: 1.0660x; 1.0127x over previous
import math
import numpy as np
import concourse.bass as bass
import concourse.mybir as mybir

F32 = mybir.dt.float32
BF16 = mybir.dt.bfloat16
FP16 = mybir.dt.float16
AF = mybir.ActivationFunctionType
ALU = mybir.AluOpType
AX = mybir.AxisListType

D = 1024
NCH = 8
H = 16
HD = 64
EPS_LN = 1e-5
EPS_RMS = 1e-6


def get_slopes(n):
    def pow2(n):
        start = 2 ** (-2 ** (-(math.log2(n) - 3)))
        return [start * start ** i for i in range(n)]
    if math.log2(n).is_integer():
        return pow2(n)
    c = 2 ** math.floor(math.log2(n))
    return pow2(c) + get_slopes(2 * c)[0::2][:n - c]


class Buf:
    __slots__ = ("name", "w", "r")

    def __init__(self, name):
        self.name = name
        self.w = None
        self.r = {}


class Q:

    def __init__(self, nc, name, nslots=12):
        self.slots = []
        for i in range(nslots):
            s = nc.alloc_semaphore("%s%d" % (name, i))
            _SemById[s.num] = s
            self.slots.append([s, 0])
        self.i = 0

    def take(self):
        slot = self.slots[self.i]
        self.i = (self.i + 1) % len(self.slots)
        return slot


class Eng:

    def __init__(self, ctx, name, ename, sem, self_ordered):
        self.ctx = ctx
        self.name = name
        self.ename = ename
        self.sem = sem
        self.n = 0
        self.self_ordered = self_ordered
        self.known = {}
        self.prog = []
        self._waits = []

    def _dep(self, ev):
        if ev is None:
            return
        sem, val = ev
        k = sem.num
        if sem is self.sem and self.self_ordered:
            return
        if self.known.get(k, 0) >= val:
            return
        self._waits.append((sem, val))
        snap = self.ctx.snapshots.get((k, val))
        if snap:
            for k2, v2 in snap.items():
                if self.known.get(k2, 0) < v2:
                    self.known[k2] = v2
        if self.known.get(k, 0) < val:
            self.known[k] = val

    def _pre(self, reads, writes):
        self._waits = []
        for b in reads:
            self._dep(b.w)
        for b in writes:
            for k, v in b.r.items():
                self._dep((_SemById[k], v))
            self._dep(b.w)

    def _snapshot(self, ev):
        snap = dict(self.known)
        if self.self_ordered:
            snap[self.sem.num] = self.n
        k, v = ev[0].num, ev[1]
        if snap.get(k, 0) < v:
            snap[k] = v
        self.ctx.snapshots[(k, v)] = snap

    def _post(self, ev, reads, writes):
        for b in reads:
            k = ev[0].num
            if b.r.get(k, 0) < ev[1]:
                b.r[k] = ev[1]
        for b in writes:
            b.w = ev
            b.r = {}

    def emit(self, fn, reads=(), writes=(), q=None):
        return self.emit_multi([fn], reads, writes, q)

    def emit_multi(self, fns, reads=(), writes=(), q=None):
        self._pre(reads, writes)
        if q is not None:
            slot = q.take()
            self._dep((slot[0], slot[1]))
            slot[1] += 16
            ev = (slot[0], slot[1])
            self.prog.append((self._waits, fns, slot[0], 16))
        else:
            self.n += 1
            ev = (self.sem, self.n)
            self.prog.append((self._waits, fns, self.sem, 1))
        self._waits = []
        self._snapshot(ev)
        self._post(ev, reads, writes)
        return ev

    def render(self, e):
        for waits, fns, incsem, incval in self.prog:
            for (s, v) in waits:
                e.wait_ge(s, v)
            inst = None
            for fn in fns:
                inst = fn(e)
            if inst is not None:
                inst.then_inc(incsem, incval)


_SemById = {}


class Ctx:
    def __init__(self, nc):
        self.nc = nc
        self.snapshots = {}
        self.sems = {}
        for nm in ("pe", "act", "dve", "pool", "sp"):
            s = nc.alloc_semaphore("s_" + nm)
            _SemById[s.num] = s
            self.sems[nm] = s
        self.pe = Eng(self, "pe", "tensor", self.sems["pe"], False)
        self.act = Eng(self, "act", "scalar", self.sems["act"], False)
        self.dve = Eng(self, "dve", "vector", self.sems["dve"], False)
        self.pool = Eng(self, "pool", "gpsimd", self.sems["pool"], False)
        self.sp = Eng(self, "sp", "sync", self.sems["sp"], False)
        self.engines = [self.pe, self.act, self.dve, self.pool, self.sp]
        self.qsp = Q(nc, "qsp")
        self.qact = Q(nc, "qact")
        self.qpool = Q(nc, "qpool")
        self.bufs = []

    def uname(self, n):
        self._uc = getattr(self, "_uc", 0) + 1
        return "%s_u%d" % (n, self._uc)

    def buf(self, name):
        b = Buf(name)
        b.r = dict(getattr(self, "barrier", {}))
        self.bufs.append(b)
        return b

    def scope_barrier(self):
        bar = {}
        for e in self.engines:
            if e.n > 0:
                bar[e.sem.num] = e.n
        for q in (self.qsp, self.qact, self.qpool):
            for slot in q.slots:
                if slot[1] > 0:
                    bar[slot[0].num] = slot[1]
        self.barrier = bar

    def sub(self):
        from contextlib import ExitStack as _ES

        class _Scope:
            def __init__(s):
                s.st = _ES()

            def __enter__(s):
                return s.st.__enter__()

            def __exit__(s, *a):
                r = s.st.__exit__(*a)
                self.scope_barrier()
                return r
        return _Scope()

    def finish(self):
        for e in self.engines:
            e._waits = []
            for other in self.engines:
                if other is not e and other.n > 0:
                    e._dep((other.sem, other.n))
            for q in (self.qsp, self.qact, self.qpool):
                for slot in q.slots:
                    if slot[1] > 0:
                        e._dep((slot[0], slot[1]))
            if e._waits:
                e.prog.append((e._waits, [], None, 0))
                e._waits = []

    def render_all(self):
        with self.nc.Block() as blk:
            for eng in self.engines:
                getattr(blk, eng.ename)(lambda e, eng=eng: eng.render(e))


class Consts:

    def __init__(self, ctx, stack):
        nc = ctx.nc
        en = stack.enter_context
        self.id_bf = en(nc.sbuf_tensor("c_id_bf", [128, 128], BF16))
        self.id_f = en(nc.sbuf_tensor("c_id_f", [128, 128], F32))
        self.ones_col = en(nc.sbuf_tensor("c_ones_col", [128, 1], F32))
        self.ones_row = en(nc.sbuf_tensor("c_ones_row", [1, 512], F32))
        self.b_id_bf = ctx.buf("id_bf")
        self.b_id_f = ctx.buf("id_f")
        self.b_ones_col = ctx.buf("ones_col")
        self.b_ones_row = ctx.buf("ones_row")
        p, b = ctx.pool, self
        p.emit(lambda e: e.memset(self.id_bf[:, :], 0.0), writes=[b.b_id_bf])
        p.emit(lambda e: e.affine_select(out=self.id_bf[:, :], in_=self.id_bf[:, :],
                                         compare_op=ALU.not_equal, fill=1.0, base=0,
                                         pattern=[[-1, 128]], channel_multiplier=1),
               reads=[b.b_id_bf], writes=[b.b_id_bf])
        p.emit(lambda e: e.memset(self.id_f[:, :], 0.0), writes=[b.b_id_f])
        p.emit(lambda e: e.affine_select(out=self.id_f[:, :], in_=self.id_f[:, :],
                                         compare_op=ALU.not_equal, fill=1.0, base=0,
                                         pattern=[[-1, 128]], channel_multiplier=1),
               reads=[b.b_id_f], writes=[b.b_id_f])
        p.emit(lambda e: e.memset(self.ones_col[:, :], 1.0), writes=[b.b_ones_col])
        p.emit(lambda e: e.memset(self.ones_row[:, :], 1.0), writes=[b.b_ones_row])
        self.ones_row_bf = en(nc.sbuf_tensor("c_ones_row_bf", [1, 512], BF16))
        self.b_ones_row_bf = ctx.buf("ones_row_bf")
        p.emit(lambda e: e.memset(self.ones_row_bf[:, :], 1.0), writes=[b.b_ones_row_bf])
        self.ones_col_bf = en(nc.sbuf_tensor("c_ones_col_bf", [128, 1], BF16))
        self.b_ones_col_bf = ctx.buf("ones_col_bf")
        p.emit(lambda e: e.memset(self.ones_col_bf[:, :], 1.0), writes=[b.b_ones_col_bf])
        self.negbig = en(nc.sbuf_tensor("c_negbig", [1, 512], F32))
        self.b_negbig = ctx.buf("negbig")
        p.emit(lambda e: e.memset(self.negbig[:, :], -1e9), writes=[b.b_negbig])
        self.fullmask = en(nc.sbuf_tensor("c_fullmask", [128, 128], BF16))
        self.b_fullmask = ctx.buf("fullmask")
        p.emit(lambda e: e.memset(self.fullmask[:, :], -1e9), writes=[b.b_fullmask])
        self.masklt = en(nc.sbuf_tensor("c_masklt", [128, 128], BF16))
        self.b_masklt = ctx.buf("masklt")
        p.emit(lambda e: e.memset(self.masklt[:, :], 0.0), writes=[b.b_masklt])
        p.emit(lambda e: e.affine_select(out=self.masklt[:, :], in_=self.masklt[:, :],
                                         compare_op=ALU.is_ge, fill=-1e9, base=0,
                                         pattern=[[1, 128]], channel_multiplier=-1),
               reads=[b.b_masklt], writes=[b.b_masklt])
        self.maskut = en(nc.sbuf_tensor("c_maskut", [128, 128], BF16))
        self.b_maskut = ctx.buf("maskut")
        p.emit(lambda e: e.memset(self.maskut[:, :], 0.0), writes=[b.b_maskut])
        p.emit(lambda e: e.affine_select(out=self.maskut[:, :], in_=self.maskut[:, :],
                                         compare_op=ALU.is_ge, fill=-1e9, base=0,
                                         pattern=[[-1, 128]], channel_multiplier=1),
               reads=[b.b_maskut], writes=[b.b_maskut])
        self.eps = en(nc.sbuf_tensor("c_eps", [128, 2], F32))
        self.b_eps = ctx.buf("eps")
        p.emit(lambda e: e.memset(self.eps[:, 0:1], 1e-5), writes=[b.b_eps])
        p.emit(lambda e: e.memset(self.eps[:, 1:2], 1e-6), reads=[b.b_eps], writes=[b.b_eps])


def ln_stats(ctx, consts, xt_ap, b_xt, mr_ap, b_mr, tmp_ap, b_tmp, width=D):
    nchunk = (width + 511) // 512
    for i in range(nchunk):
        lo, hi = i * 512, min((i + 1) * 512, width)
        ctx.dve.emit(lambda e, i=i, lo=lo, hi=hi: e.bn_stats(tmp_ap[:, 6 * i:6 * i + 6], xt_ap[:, lo:hi]),
                     reads=[b_xt], writes=[b_tmp])
    ctx.dve.emit(lambda e: e.bn_aggr(mr_ap[:, 0:2], tmp_ap[:, 0:6 * nchunk]),
                 reads=[b_tmp], writes=[b_mr])
    ctx.act.emit(lambda e: e.activation(out=tmp_ap[:, 12:13], in_=mr_ap[:, 1:2], func=AF.Ln,
                                        bias=consts.eps[0:mr_ap.shape[0], 0:1]),
                 reads=[b_mr, consts.b_eps], writes=[b_tmp])
    ctx.act.emit(lambda e: e.activation(out=mr_ap[:, 1:2], in_=tmp_ap[:, 12:13], func=AF.Exp, scale=-0.5),
                 reads=[b_tmp], writes=[b_mr])


def transpose128(ctx, consts, dst_ap, b_dst, src_ap, b_src, psum_ap, b_psum, dtype="bf"):
    ident = consts.id_bf if dtype == "bf" else consts.id_f
    b_id = consts.b_id_bf if dtype == "bf" else consts.b_id_f
    ctx.pe.emit(lambda e: e.transpose(psum_ap, src_ap, ident[:, :]),
                reads=[b_src, b_id], writes=[b_psum])
    ctx.dve.emit(lambda e: e.tensor_copy(dst_ap, psum_ap),
                 reads=[b_psum], writes=[b_dst])


def load_weight_T(ctx, consts, stack, name, w_hbm, b_whbm, rows, cols, tp_psum, b_tp):
    nc = ctx.nc
    ra, ca = rows // 128, cols // 128
    nat = [stack.enter_context(nc.sbuf_tensor(name + "_nat%d" % i, [128, cols], BF16)) for i in range(2)]
    b_nat = [ctx.buf(name + "_nat%d" % i) for i in range(2)]
    wT = stack.enter_context(nc.sbuf_tensor(name + "_T", [128, ca * rows], BF16))
    b_wT = ctx.buf(name + "_T")
    for ai in range(ra):
        st, b_st = nat[ai % 2], b_nat[ai % 2]
        ctx.pool.emit(lambda e, ai=ai, st=st: e.dma_start(out=st[:, :], in_=w_hbm[ai * 128:(ai + 1) * 128, :]),
                      reads=[b_whbm], writes=[b_st], q=ctx.qpool)
        for bi in range(ca):
            transpose128(ctx, consts,
                         wT[:, bi * rows + ai * 128: bi * rows + (ai + 1) * 128], b_wT,
                         st[:, bi * 128:(bi + 1) * 128], b_st,
                         tp_psum, b_tp, "bf")
    return wT, b_wT


def ln_apply_cast(ctx, out_ap, b_out, x_ap, b_x, mr_ap, b_mr):
    ctx.dve.emit(lambda e: e.tensor_scalar(out_ap, x_ap, mr_ap[:, 0:1], mr_ap[:, 1:2],
                                           ALU.subtract, ALU.mult),
                 reads=[b_x, b_mr], writes=[b_out])


def build_phase2(ctx, consts, stack, T, io):
    nc = ctx.nc
    nt = T // 128
    ntc = T // 512
    en = stack.enter_context
    xin, b_xin = io["x_in"], io["b_x_in"]
    xout, b_xout = io["x_out"], io["b_x_out"]

    xn2T = en(nc.sbuf_tensor("p2_xnT", [128, NCH * T], BF16)); b_xn2T = ctx.buf("p2_xnT")
    obuf, b_obuf = xn2T, b_xn2T
    u = en(nc.sbuf_tensor("p2_u", [128, T], F32)); b_u = ctx.buf("p2_u")
    xt = [en(nc.sbuf_tensor("p2_xt%d" % i, [128, D], F32)) for i in range(2)]
    b_xt = [ctx.buf("p2_xt0"), ctx.buf("p2_xt1")]
    xnb = en(nc.sbuf_tensor("p2_xnb", [128, D], BF16)); b_xnb = ctx.buf("p2_xnb")
    lnst = en(nc.sbuf_tensor("p2_lnst", [128, nt * 2], F32)); b_lnst = ctx.buf("p2_lnst")
    sttmp = en(nc.sbuf_tensor("p2_sttmp", [128, 16], F32)); b_sttmp = ctx.buf("p2_sttmp")
    rows = en(nc.sbuf_tensor("p2_rows", [1, 2 * D + 64], F32)); b_rows = ctx.buf("p2_rows")
    lamexp = en(nc.sbuf_tensor("p2_lamexp", [1, 128], F32)); b_lamexp = ctx.buf("p2_lamexp")
    lamcol = en(nc.sbuf_tensor("p2_lamcol", [128, NCH], F32)); b_lamcol = ctx.buf("p2_lamcol")
    qkcol = en(nc.sbuf_tensor("p2_qkcol", [128, NCH], F32)); b_qkcol = ctx.buf("p2_qkcol")
    sel = en(nc.sbuf_tensor("p2_sel", [128, 2], F32)); b_sel = ctx.buf("p2_sel")
    selT = en(nc.sbuf_tensor("p2_selT", [2, 128], F32)); b_selT = ctx.buf("p2_selT")
    s2 = en(nc.sbuf_tensor("p2_s2", [2, NCH], F32)); b_s2 = ctx.buf("p2_s2")
    ccol = en(nc.sbuf_tensor("p2_ccol", [128, NCH], F32)); b_ccol = ctx.buf("p2_ccol")
    ssq = en(nc.sbuf_tensor("p2_ssq", [1, T], F32)); b_ssq = ctx.buf("p2_ssq")
    rrow = en(nc.sbuf_tensor("p2_rrow", [1, T], F32)); b_rrow = ctx.buf("p2_rrow")
    rstdc = en(nc.sbuf_tensor("p2_rstdc", [128, nt], F32)); b_rstdc = ctx.buf("p2_rstdc")
    nwcol = en(nc.sbuf_tensor("p2_nwcol", [128, NCH], F32)); b_nwcol = ctx.buf("p2_nwcol")
    ao = en(nc.sbuf_tensor("p2_ao", [128, D], F32)); b_ao = ctx.buf("p2_ao")
    t1 = en(nc.sbuf_tensor("p2_t1", [128, D], F32)); b_t1 = ctx.buf("p2_t1")
    ptp = [en(nc.psum_tensor("p2_ptp%d" % i, [128, 128], BF16)) for i in range(2)]
    b_ptp = [ctx.buf("p2_ptp%d" % i) for i in range(2)]
    ptf = en(nc.psum_tensor("p2_ptf", [128, 512], F32)); b_ptf = ctx.buf("p2_ptf")
    pmm = [en(nc.psum_tensor("p2_pmm%d" % i, [128, 512], F32)) for i in range(2)]
    b_pmm = [ctx.buf("p2_pmm%d" % i) for i in range(2)]

    for t in range(nt):
        ctx.sp.emit(lambda e, t=t: e.dma_start(out=xt[t % 2][:, :], in_=xin[t * 128:(t + 1) * 128, :]),
                    reads=[b_xin], writes=[b_xt[t % 2]], q=ctx.qsp)
        mr = lnst[:, 2 * t:2 * t + 2]
        ln_stats(ctx, consts, xt[t % 2][:, :], b_xt[t % 2], mr, b_lnst, sttmp[:, :], b_sttmp)
        ctx.dve.emit(lambda e, t=t: e.tensor_scalar(xnb[:, :], xt[t % 2][:, :],
                                                    lnst[:, 2 * t:2 * t + 1], lnst[:, 2 * t + 1:2 * t + 2],
                                                    ALU.subtract, ALU.mult),
                     reads=[b_xt[t % 2], b_lnst], writes=[b_xnb])
        for c in range(NCH):
            transpose128(ctx, consts,
                         xn2T[:, c * T + t * 128: c * T + (t + 1) * 128], b_xn2T,
                         xnb[:, c * 128:(c + 1) * 128], b_xnb,
                         ptp[c % 2][:, :], b_ptp[c % 2], "bf")

    q_h, b_q = io["L1_attn_q"], io["b_L1_attn_q"]
    k_h, b_k = io["L1_attn_k"], io["b_L1_attn_k"]
    ld_h, b_ld = io["L1_attn_log_decay"], io["b_L1_attn_log_decay"]
    nw_h, b_nw = io["L1_attn_norm_weight"], io["b_L1_attn_norm_weight"]
    ow_h, b_ow = io["L1_attn_o_w"], io["b_L1_attn_o_w"]

    ctx.sp.emit(lambda e: e.dma_start(out=rows[0:1, 0:16], in_=ld_h[None, :]),
                reads=[b_ld], writes=[b_rows], q=ctx.qsp)
    ctx.sp.emit(lambda e: e.dma_start(out=rows[0:1, 64:64 + D], in_=q_h[None, :]),
                reads=[b_q], writes=[b_rows], q=ctx.qsp)
    ctx.sp.emit(lambda e: e.dma_start(out=rows[0:1, 64 + D:64 + 2 * D], in_=k_h[None, :]),
                reads=[b_k], writes=[b_rows], q=ctx.qsp)
    ctx.act.emit(lambda e: e.activation(out=rows[0:1, 16:32], in_=rows[0:1, 0:16], func=AF.Sigmoid),
                 reads=[b_rows], writes=[b_rows])
    ctx.dve.emit(lambda e: e.tensor_mul(rows[0:1, 64:64 + D], rows[0:1, 64:64 + D], rows[0:1, 64 + D:64 + 2 * D]),
                 reads=[b_rows], writes=[b_rows])
    for c in range(NCH):
        ctx.pe.emit_multi([
            (lambda e, c=c: e.matmul(ptf[0:1, 0:64], rows[0:1, 16 + 2 * c:17 + 2 * c],
                                     consts.ones_row[0:1, 0:64], start=True, stop=True)),
            (lambda e, c=c: e.matmul(ptf[0:1, 64:128], rows[0:1, 17 + 2 * c:18 + 2 * c],
                                     consts.ones_row[0:1, 0:64], start=True, stop=True)),
        ], reads=[b_rows, consts.b_ones_row], writes=[b_ptf])
        ctx.dve.emit(lambda e: e.tensor_copy(lamexp[0:1, :], ptf[0:1, 0:128]),
                     reads=[b_ptf], writes=[b_lamexp])
        ctx.pe.emit(lambda e: e.transpose(ptf[:, 0:1], lamexp[0:1, :], consts.id_f[0:1, 0:1]),
                    reads=[b_lamexp, consts.b_id_f], writes=[b_ptf])
        ctx.dve.emit(lambda e, c=c: e.tensor_copy(lamcol[:, c:c + 1], ptf[:, 0:1]),
                     reads=[b_ptf], writes=[b_lamcol])
    for c in range(NCH):
        ctx.pe.emit(lambda e, c=c: e.transpose(ptf[:, 0:1], rows[0:1, 64 + c * 128:64 + (c + 1) * 128],
                                               consts.id_f[0:1, 0:1]),
                    reads=[b_rows, consts.b_id_f], writes=[b_ptf])
        ctx.dve.emit(lambda e, c=c: e.tensor_copy(qkcol[:, c:c + 1], ptf[:, 0:1]),
                     reads=[b_ptf], writes=[b_qkcol])
    ctx.pool.emit(lambda e: e.memset(sel[:, :], 0.0), writes=[b_sel])
    ctx.pool.emit(lambda e: e.memset(sel[0:64, 0:1], 1.0), reads=[b_sel], writes=[b_sel])
    ctx.pool.emit(lambda e: e.memset(sel[64:128, 1:2], 1.0), reads=[b_sel], writes=[b_sel])
    ctx.pe.emit(lambda e: e.matmul(ptf[0:2, 0:NCH], sel[:, :], qkcol[:, :], start=True, stop=True),
                reads=[b_sel, b_qkcol], writes=[b_ptf])
    ctx.dve.emit(lambda e: e.tensor_copy(s2[:, :], ptf[0:2, 0:NCH]),
                 reads=[b_ptf], writes=[b_s2])
    ctx.pe.emit(lambda e: e.transpose(ptf[0:2, 0:128], sel[:, :], consts.id_f[:, :]),
                reads=[b_sel, consts.b_id_f], writes=[b_ptf])
    ctx.dve.emit(lambda e: e.tensor_copy(selT[:, :], ptf[0:2, 0:128]),
                 reads=[b_ptf], writes=[b_selT])
    ctx.pe.emit(lambda e: e.matmul(ptf[:, 0:NCH], selT[:, :], s2[:, :], start=True, stop=True),
                reads=[b_selT, b_s2], writes=[b_ptf])
    ctx.dve.emit(lambda e: e.tensor_copy(ccol[:, :], ptf[:, 0:NCH]),
                 reads=[b_ptf], writes=[b_ccol])

    for c in range(NCH):
        ctx.dve.emit(lambda e, c=c: e.tensor_tensor_scan(u[:, :], lamcol[:, c:c + 1].broadcast_to((128, T)),
                                                         xn2T[:, c * T:(c + 1) * T], 0.0, ALU.mult, ALU.add),
                     reads=[b_lamcol, b_xn2T], writes=[b_u])
        ctx.dve.emit(lambda e, c=c: e.tensor_scalar_mul(obuf[:, c * T:(c + 1) * T], u[:, :], ccol[:, c:c + 1]),
                     reads=[b_u, b_ccol], writes=[b_obuf])

    ctx.dve.emit(lambda e: e.memset(ssq[0:1, :], 0.0), writes=[b_ssq])
    for c in range(NCH):
        ctx.act.emit(lambda e, c=c: e.activation(out=u[:, :], in_=obuf[:, c * T:(c + 1) * T], func=AF.Square),
                     reads=[b_obuf], writes=[b_u])
        for tc in range(ntc):
            pm = pmm[tc % 2]
            bpm = b_pmm[tc % 2]
            ctx.pe.emit(lambda e, tc=tc, pm=pm: e.matmul(pm[0:1, :], consts.ones_col[:, :], u[:, tc * 512:(tc + 1) * 512],
                                                  start=True, stop=True),
                        reads=[b_u, consts.b_ones_col], writes=[bpm])
            ctx.dve.emit(lambda e, tc=tc, pm=pm: e.tensor_add(ssq[0:1, tc * 512:(tc + 1) * 512],
                                                       ssq[0:1, tc * 512:(tc + 1) * 512], pm[0:1, :]),
                         reads=[bpm, b_ssq], writes=[b_ssq])

    ctx.act.emit(lambda e: e.activation(out=rrow[0:1, 0:T], in_=ssq[0:1, :], func=AF.Ln, scale=1.0 / D,
                                        bias=consts.eps[0:1, 1:2]),
                 reads=[b_ssq, consts.b_eps], writes=[b_rrow])
    ctx.act.emit(lambda e: e.activation(out=rrow[0:1, 0:T], in_=rrow[0:1, 0:T], func=AF.Exp, scale=-0.5),
                 reads=[b_rrow], writes=[b_rrow])
    for t in range(nt):
        ctx.pe.emit(lambda e, t=t: e.transpose(ptf[:, 0:1], rrow[0:1, t * 128:(t + 1) * 128],
                                               consts.id_f[0:1, 0:1]),
                    reads=[b_rrow, consts.b_id_f], writes=[b_ptf])
        ctx.dve.emit(lambda e, t=t: e.tensor_copy(rstdc[:, t:t + 1], ptf[:, 0:1]),
                     reads=[b_ptf], writes=[b_rstdc])

    owT, b_owT = load_weight_T(ctx, consts, stack, "p2_ow", ow_h, b_ow, D, D, ptp[0][:, :], b_ptp[0])
    ctx.sp.emit(lambda e: e.dma_start(out=rows[0:1, 64:64 + D], in_=nw_h[None, :]),
                reads=[b_nw], writes=[b_rows], q=ctx.qsp)
    for c in range(NCH):
        ctx.pe.emit(lambda e, c=c: e.transpose(ptf[:, 0:1], rows[0:1, 64 + c * 128:64 + (c + 1) * 128],
                                               consts.id_f[0:1, 0:1]),
                    reads=[b_rows, consts.b_id_f], writes=[b_ptf])
        ctx.dve.emit(lambda e, c=c: e.tensor_copy(nwcol[:, c:c + 1], ptf[:, 0:1]),
                     reads=[b_ptf], writes=[b_nwcol])
    for bi in range(NCH):
        ctx.dve.emit(lambda e, bi=bi: e.tensor_scalar_mul(owT[:, bi * D:(bi + 1) * D],
                                                          owT[:, bi * D:(bi + 1) * D], nwcol[:, bi:bi + 1]),
                     reads=[b_owT, b_nwcol], writes=[b_owT])

    for t in range(nt):
        for half in range(2):
            pm, bpm = pmm[half], b_pmm[half]
            fns = []
            for ci in range(NCH):
                fns.append(lambda e, ci=ci, half=half, t=t, pm=pm: e.matmul(
                    pm[:, :],
                    obuf[:, ci * T + t * 128: ci * T + (t + 1) * 128],
                    owT[:, ci * D + half * 512: ci * D + (half + 1) * 512],
                    start=(ci == 0), stop=(ci == NCH - 1)))
            ctx.pe.emit_multi(fns, reads=[b_obuf, b_owT], writes=[bpm])
            ctx.dve.emit(lambda e, half=half, t=t, pm=pm: e.tensor_scalar_mul(
                ao[:, half * 512:(half + 1) * 512], pm[:, :], rstdc[:, t:t + 1]),
                reads=[bpm, b_rstdc], writes=[b_ao])
        mr = sttmp[:, 13:15]
        ln_stats(ctx, consts, ao[:, :], b_ao, mr, b_sttmp, sttmp[:, :], b_sttmp)
        ctx.dve.emit(lambda e: e.tensor_scalar(t1[:, :], ao[:, :], sttmp[:, 13:14], sttmp[:, 14:15],
                                               ALU.subtract, ALU.mult),
                     reads=[b_ao, b_sttmp], writes=[b_t1])
        ctx.sp.emit(lambda e, t=t: e.dma_start(out=xt[t % 2][:, :], in_=xin[t * 128:(t + 1) * 128, :]),
                    reads=[b_xin], writes=[b_xt[t % 2]], q=ctx.qsp)
        ctx.dve.emit(lambda e, t=t: e.tensor_add(t1[:, :], t1[:, :], xt[t % 2][:, :]),
                     reads=[b_xt[t % 2], b_t1], writes=[b_t1])
        ctx.sp.emit(lambda e, t=t: e.dma_start(out=xout[t * 128:(t + 1) * 128, :], in_=t1[:, :]),
                    reads=[b_t1], writes=[b_xout], q=ctx.qsp)


def emit_weight_T(ctx, consts, wT, b_wT, stage, b_stage, w_hbm, b_w, rows, cols, tp_psum, b_tp,
                  row0=0, col0=0):
    ra, ca = rows // 128, cols // 128
    for ai in range(ra):
        st, b_st = stage[ai % 2], b_stage[ai % 2]
        ctx.pool.emit(lambda e, ai=ai, st=st: e.dma_start(
            out=st[:, 0:cols], in_=w_hbm[row0 + ai * 128:row0 + (ai + 1) * 128, col0:col0 + cols]),
            reads=[b_w], writes=[b_st], q=ctx.qpool)
        for bi in range(ca):
            transpose128(ctx, consts,
                         wT[:, bi * rows + ai * 128: bi * rows + (ai + 1) * 128], b_wT,
                         st[:, bi * 128:(bi + 1) * 128], b_st,
                         tp_psum, b_tp, "bf")


def emit_weight_T3(ctx, consts, wT, b_wT, stages, b_stages, cnt, w_hbm, b_w, rows, cols,
                   ptp2, b_ptp2, row0=0, col0=0):
    ra, ca = rows // 128, cols // 128
    for ai in range(ra):
        k = cnt[0] % len(stages)
        cnt[0] += 1
        st, b_st = stages[k], b_stages[k]
        ctx.pool.emit(lambda e, ai=ai, st=st: e.dma_start(
            out=st[:, 0:cols], in_=w_hbm[row0 + ai * 128:row0 + (ai + 1) * 128, col0:col0 + cols]),
            reads=[b_w], writes=[b_st], q=ctx.qpool)
        for b0 in range(0, ca, 4):
            bn = min(4, ca - b0)
            pb, bpb = ptp2[(ai + b0 // 4) % 2], b_ptp2[(ai + b0 // 4) % 2]
            for j in range(bn):
                ctx.pe.emit(lambda e, b0=b0, j=j, pb=pb, st=st: e.transpose(
                    pb[:, j * 128:(j + 1) * 128],
                    st[:, (b0 + j) * 128:(b0 + j + 1) * 128],
                    consts.id_bf[:, :]),
                    reads=[b_st, consts.b_id_bf], writes=[bpb])
            wTr = wT[:, 0:ca * rows].rearrange("p (b a) -> p b a", b=ca)
            ctx.dve.emit(lambda e, ai=ai, b0=b0, bn=bn, pb=pb, wTr=wTr: e.tensor_copy(
                wTr[:, b0:b0 + bn, ai * 128:(ai + 1) * 128],
                pb[:, 0:bn * 128].rearrange("p (b a) -> p b a", b=bn)),
                reads=[bpb], writes=[b_wT])


def emit_weight_T2(ctx, consts, wT, b_wT, nat, b_nat, w_hbm, b_w, rows, cols,
                   ptp2, b_ptp2, row0=0, col0=0):
    ra, ca = rows // 128, cols // 128
    if col0 == 0 and cols == w_hbm.shape[1]:
        ctx.pool.emit(lambda e: e.dma_start(
            out=nat[:, 0:ra * cols].rearrange("p (a c) -> p a c", a=ra),
            in_=w_hbm[row0:row0 + rows, :].rearrange("(a p) c -> p a c", p=128)),
            reads=[b_w], writes=[b_nat], q=ctx.qpool)
    else:
        for ai in range(ra):
            ctx.pool.emit(lambda e, ai=ai: e.dma_start(
                out=nat[:, ai * cols:(ai + 1) * cols],
                in_=w_hbm[row0 + ai * 128:row0 + (ai + 1) * 128, col0:col0 + cols]),
                reads=[b_w], writes=[b_nat], q=ctx.qpool)
    wTr = wT[:, 0:ca * rows].rearrange("p (b a) -> p b a", b=ca)
    nb = 0
    for bi in range(ca):
        for a0 in range(0, ra, 4):
            an = min(4, ra - a0)
            pb, bpb = ptp2[nb % 2], b_ptp2[nb % 2]
            nb += 1
            for j in range(an):
                ctx.pe.emit(lambda e, bi=bi, a0=a0, j=j, pb=pb: e.transpose(
                    pb[:, j * 128:(j + 1) * 128],
                    nat[:, (a0 + j) * cols + bi * 128:(a0 + j) * cols + (bi + 1) * 128],
                    consts.id_bf[:, :]),
                    reads=[b_nat, consts.b_id_bf], writes=[bpb])
            ctx.dve.emit(lambda e, bi=bi, a0=a0, an=an, pb=pb: e.tensor_copy(
                wTr[:, bi, a0 * 128:(a0 + an) * 128], pb[:, 0:an * 128]),
                reads=[bpb], writes=[b_wT])


def emit_ln_pass(ctx, consts, T, xin, b_xin, xt, b_xt, xnb, b_xnb, lnst, b_lnst,
                 sttmp, b_sttmp, xnT, b_xnT, ptp, b_ptp, r0=0):
    nt = T // 128
    for t in range(nt):
        ctx.sp.emit(lambda e, t=t: e.dma_start(out=xt[t % 2][:, :],
                                               in_=xin[r0 + t * 128:r0 + (t + 1) * 128, :]),
                    reads=[b_xin], writes=[b_xt[t % 2]], q=ctx.qsp)
        mr = lnst[:, 2 * t:2 * t + 2]
        ln_stats(ctx, consts, xt[t % 2][:, :], b_xt[t % 2], mr, b_lnst, sttmp[:, :], b_sttmp)
        ctx.dve.emit(lambda e, t=t: e.tensor_scalar(xnb[:, :], xt[t % 2][:, :],
                                                    lnst[:, 2 * t:2 * t + 1], lnst[:, 2 * t + 1:2 * t + 2],
                                                    ALU.subtract, ALU.mult),
                     reads=[b_xt[t % 2], b_lnst], writes=[b_xnb])
        for c in range(NCH):
            transpose128(ctx, consts,
                         xnT[:, c * T + t * 128: c * T + (t + 1) * 128], b_xnT,
                         xnb[:, c * 128:(c + 1) * 128], b_xnb,
                         ptp[c % 2][:, 0:128], b_ptp[c % 2], "bf")


def emit_finale(ctx, consts, T, t, src_ap, b_src, xin, b_xin, xout, b_xout,
                xt, b_xt, t1, b_t1, sttmp, b_sttmp, r0=0):
    mr = sttmp[:, 13:15]
    ln_stats(ctx, consts, src_ap, b_src, mr, b_sttmp, sttmp[:, :], b_sttmp)
    ctx.dve.emit(lambda e: e.tensor_scalar(t1[:, :], src_ap, sttmp[:, 13:14], sttmp[:, 14:15],
                                           ALU.subtract, ALU.mult),
                 reads=[b_src, b_sttmp], writes=[b_t1])
    ctx.sp.emit(lambda e, t=t: e.dma_start(out=xt[:, :], in_=xin[r0 + t * 128:r0 + (t + 1) * 128, :]),
                reads=[b_xin], writes=[b_xt], q=ctx.qsp)
    ctx.dve.emit(lambda e: e.tensor_add(t1[:, :], t1[:, :], xt[:, :]),
                 reads=[b_xt, b_t1], writes=[b_t1])
    ctx.sp.emit(lambda e, t=t: e.dma_start(out=xout[r0 + t * 128:r0 + (t + 1) * 128, :], in_=t1[:, :]),
                reads=[b_t1], writes=[b_xout], q=ctx.qsp)


def build_phase1(ctx, consts, stack, Tfull, io):
    with ctx.sub() as hstack:
        _build_phase1_half(ctx, consts, hstack, Tfull, 0, io)


def _build_phase1_half(ctx, consts, stack, T, r0, io):
    nc = ctx.nc
    nt = T // 128
    ntc = T // 512
    en = stack.enter_context
    xin, b_xin = io["x_in"], io["b_x_in"]
    xout, b_xout = io["x_out"], io["b_x_out"]
    E = 8

    xnT = en(nc.sbuf_tensor(ctx.uname("p1_xnT"), [128, NCH * T], BF16)); b_xnT = ctx.buf("p1_xnT")
    y_hbm = nc.dram_tensor(ctx.uname("p1_yhbm"), [T, D], F32, kind="Internal").ap()
    b_yh = [ctx.buf("p1_yh%d" % t) for t in range(nt)]
    ctrb = [en(nc.sbuf_tensor(ctx.uname("p1_ctrb%d" % i), [128, D], F32)) for i in range(3)]
    b_ctrb = [ctx.buf("p1_ctrb%d" % i) for i in range(3)]
    ctrbi = [0]
    xt = [en(nc.sbuf_tensor(ctx.uname("p1_xt%d" % i), [128, D], F32)) for i in range(2)]
    b_xt = [ctx.buf("p1_xt0"), ctx.buf("p1_xt1")]
    xnb = en(nc.sbuf_tensor(ctx.uname("p1_xnb"), [128, D], BF16)); b_xnb = ctx.buf("p1_xnb")
    lnst = en(nc.sbuf_tensor(ctx.uname("p1_lnst"), [128, nt * 2], F32)); b_lnst = ctx.buf("p1_lnst")
    sttmp = en(nc.sbuf_tensor(ctx.uname("p1_sttmp"), [128, 16], F32)); b_sttmp = ctx.buf("p1_sttmp")
    t1 = en(nc.sbuf_tensor(ctx.uname("p1_t1"), [128, D], F32)); b_t1 = ctx.buf("p1_t1")
    gwT = en(nc.sbuf_tensor(ctx.uname("p1_gwT"), [128, NCH * E], BF16)); b_gwT = ctx.buf("p1_gwT")
    gnat = en(nc.sbuf_tensor(ctx.uname("p1_gnat"), [8, D], BF16)); b_gnat = ctx.buf("p1_gnat")
    sc = en(nc.sbuf_tensor(ctx.uname("p1_sc"), [128, 4 * E + 4], F32)); b_sc = ctx.buf("p1_sc")
    cbuf = en(nc.sbuf_tensor(ctx.uname("p1_cbuf"), [128, nt * E], F32)); b_cbuf = ctx.buf("p1_cbuf")
    mbuf = en(nc.sbuf_tensor(ctx.uname("p1_mbuf"), [128, nt * E], F32)); b_mbuf = ctx.buf("p1_mbuf")
    crow = en(nc.sbuf_tensor(ctx.uname("p1_crow"), [1, T], BF16)); b_crow = ctx.buf("p1_crow")
    w1T = en(nc.sbuf_tensor(ctx.uname("p1_w1T"), [128, NCH * D], BF16)); b_w1T = ctx.buf("p1_w1T")
    w2T = en(nc.sbuf_tensor(ctx.uname("p1_w2T"), [128, NCH * D], BF16)); b_w2T = ctx.buf("p1_w2T")
    w3T = en(nc.sbuf_tensor(ctx.uname("p1_w3T"), [128, NCH * D], BF16)); b_w3T = ctx.buf("p1_w3T")
    nat2 = [en(nc.sbuf_tensor(ctx.uname("p1_nat%d" % i), [128, D], BF16)) for i in range(3)]
    b_nat2 = [ctx.buf("p1_nat%d" % i) for i in range(3)]
    natcnt = [0]
    brow = en(nc.sbuf_tensor(ctx.uname("p1_brow"), [1, 2 * D], BF16)); b_brow = ctx.buf("p1_brow")
    b1col = en(nc.sbuf_tensor(ctx.uname("p1_b1col"), [128, NCH], F32)); b_b1col = ctx.buf("p1_b1col")
    xcT2 = [en(nc.sbuf_tensor(ctx.uname("p1_xcT%d" % i), [128, NCH * 512], BF16)) for i in range(2)]
    b_xcT2 = [ctx.buf("p1_xcT%d" % i) for i in range(2)]
    hT = en(nc.sbuf_tensor(ctx.uname("p1_hT"), [128, NCH * 512], BF16)); b_hT = ctx.buf("p1_hT")
    x3sb = en(nc.sbuf_tensor(ctx.uname("p1_x3sb"), [128, D], F32)); b_x3sb = ctx.buf("p1_x3sb")
    t2 = en(nc.sbuf_tensor(ctx.uname("p1_t2"), [128, 512], F32)); b_t2 = ctx.buf("p1_t2")
    ptp = [en(nc.psum_tensor(ctx.uname("p1_ptp%d" % i), [128, 512], BF16)) for i in range(2)]
    b_ptp = [ctx.buf("p1_ptp%d" % i) for i in range(2)]
    pcb = en(nc.psum_tensor(ctx.uname("p1_pcb"), [128, 512], F32)); b_pcb = ctx.buf("p1_pcb")
    pz = [en(nc.psum_tensor(ctx.uname("p1_pz%d" % i), [128, 512], F32)) for i in range(2)]
    b_pz = [ctx.buf("p1_pz%d" % i) for i in range(2)]
    pw = [en(nc.psum_tensor(ctx.uname("p1_pw%d" % i), [128, 512], F32)) for i in range(3)]
    b_pw = [ctx.buf("p1_pw%d" % i) for i in range(3)]

    emit_ln_pass(ctx, consts, T, xin, b_xin, xt, b_xt, xnb, b_xnb, lnst, b_lnst,
                 sttmp, b_sttmp, xnT, b_xnT, ptp, b_ptp, r0=r0)

    gw, b_gw = io["L0_ffn_gate_w"], io["b_L0_ffn_gate_w"]
    ctx.pool.emit(lambda e: e.dma_start(out=gnat[:, :], in_=gw[:, :]),
                  reads=[b_gw], writes=[b_gnat], q=ctx.qpool)
    for ci in range(NCH):
        ctx.pe.emit(lambda e, ci=ci: e.transpose(ptp[0][:, 0:8], gnat[0:8, ci * 128:(ci + 1) * 128],
                                                 consts.id_bf[0:8, 0:8]),
                    reads=[b_gnat, consts.b_id_bf], writes=[b_ptp[0]])
        ctx.dve.emit(lambda e, ci=ci: e.tensor_copy(gwT[:, ci * E:(ci + 1) * E], ptp[0][0:128, 0:8]),
                     reads=[b_ptp[0]], writes=[b_gwT])
    for t in range(nt):
        fns = []
        for ci in range(NCH):
            fns.append(lambda e, ci=ci, t=t: e.matmul(
                pcb[:, 0:E], xnT[:, ci * T + t * 128: ci * T + (t + 1) * 128],
                gwT[:, ci * E:(ci + 1) * E], start=(ci == 0), stop=(ci == NCH - 1)))
        ctx.pe.emit_multi(fns, reads=[b_xnT, b_gwT], writes=[b_pcb])
        L, M1, M2, Z = sc[:, 0:E], sc[:, 4 * E:4 * E + 1], sc[:, 4 * E + 1:4 * E + 2], sc[:, 4 * E + 2:4 * E + 3]
        A1, A2, MS = sc[:, E:2 * E], sc[:, 2 * E:3 * E], sc[:, 3 * E:4 * E]
        ctx.dve.emit(lambda e: e.tensor_copy(L, pcb[:, 0:E]), reads=[b_pcb], writes=[b_sc])
        ctx.dve.emit(lambda e: e.reduce_max(M1, L, axis=AX.X, negate=True), reads=[b_sc], writes=[b_sc])
        ctx.act.emit(lambda e: e.activation(out=L, in_=L, func=AF.Exp, bias=M1, accum_out=Z),
                     reads=[b_sc], writes=[b_sc])
        ctx.dve.emit(lambda e: e.reciprocal(M1, Z), reads=[b_sc], writes=[b_sc])
        ctx.dve.emit(lambda e: e.tensor_scalar_mul(L, L, M1), reads=[b_sc], writes=[b_sc])
        ctx.dve.emit(lambda e: e.reduce_max(M1, L, axis=AX.X), reads=[b_sc], writes=[b_sc])
        ctx.dve.emit(lambda e: e.tensor_single_scalar(A1, L, M1, ALU.is_ge), reads=[b_sc], writes=[b_sc])
        ctx.dve.emit(lambda e: e.scalar_tensor_tensor(MS, A1, -1e30, L, ALU.mult, ALU.add),
                     reads=[b_sc], writes=[b_sc])
        ctx.dve.emit(lambda e: e.reduce_max(M2, MS, axis=AX.X), reads=[b_sc], writes=[b_sc])
        ctx.dve.emit(lambda e: e.tensor_single_scalar(A2, MS, M2, ALU.is_ge), reads=[b_sc], writes=[b_sc])
        ctx.dve.emit(lambda e, t=t: e.tensor_scalar_mul(cbuf[:, t * E:(t + 1) * E], A1, M1),
                     reads=[b_sc], writes=[b_cbuf])
        ctx.dve.emit(lambda e, t=t: e.tensor_scalar(MS, A2, M2, None, ALU.mult),
                     reads=[b_sc], writes=[b_sc])
        ctx.dve.emit(lambda e, t=t: e.tensor_add(cbuf[:, t * E:(t + 1) * E], cbuf[:, t * E:(t + 1) * E], MS),
                     reads=[b_sc, b_cbuf], writes=[b_cbuf])
        ctx.dve.emit(lambda e, t=t: e.tensor_add(mbuf[:, t * E:(t + 1) * E], A1, A2),
                     reads=[b_sc], writes=[b_mbuf])


    w1h, b_w1h = io["L0_ffn_w1"], io["b_L0_ffn_w1"]
    w2h, b_w2h = io["L0_ffn_w2"], io["b_L0_ffn_w2"]
    w3h, b_w3h = io["L0_ffn_w3"], io["b_L0_ffn_w3"]
    b1h, b_b1h = io["L0_ffn_b1"], io["b_L0_ffn_b1"]
    b2h, b_b2h = io["L0_ffn_b2"], io["b_L0_ffn_b2"]
    b3h, b_b3h = io["L0_ffn_b3"], io["b_L0_ffn_b3"]
    for ex in range(E):
        ctx.pool.emit(lambda e, ex=ex: e.dma_start(out=brow[0:1, 0:D], in_=b1h[ex][None, :]),
                      reads=[b_b1h], writes=[b_brow], q=ctx.qpool)
        for c in range(NCH):
            ctx.pe.emit(lambda e, c=c: e.transpose(ptp[0][:, 0:1], brow[0:1, c * 128:(c + 1) * 128],
                                                   consts.id_bf[0:1, 0:1]),
                        reads=[b_brow, consts.b_id_bf], writes=[b_ptp[0]])
            ctx.dve.emit(lambda e, c=c: e.tensor_copy(b1col[:, c:c + 1], ptp[0][:, 0:1]),
                         reads=[b_ptp[0]], writes=[b_b1col])
        ctx.pool.emit(lambda e, ex=ex: e.dma_start(out=brow[0:1, 0:D], in_=b2h[ex][None, :]),
                      reads=[b_b2h, b_b1col], writes=[b_brow], q=ctx.qpool)
        ctx.pool.emit(lambda e, ex=ex: e.dma_start(out=brow[0:1, D:2 * D], in_=b3h[ex][None, :]),
                      reads=[b_b3h], writes=[b_brow], q=ctx.qpool)
        emit_weight_T3(ctx, consts, w1T, b_w1T, nat2, b_nat2, natcnt, w1h[ex], b_w1h, D, D, ptp, b_ptp)
        emit_weight_T3(ctx, consts, w2T, b_w2T, nat2, b_nat2, natcnt, w2h[ex], b_w2h, D, D, ptp, b_ptp)
        emit_weight_T3(ctx, consts, w3T, b_w3T, nat2, b_nat2, natcnt, w3h[ex], b_w3h, D, D, ptp, b_ptp)
        for t in range(nt):
            ctx.pe.emit(lambda e, t=t, ex=ex: e.transpose(pcb[0:1, 0:128],
                                                          cbuf[:, t * E + ex:t * E + ex + 1],
                                                          consts.id_f[:, :]),
                        reads=[b_cbuf, consts.b_id_f], writes=[b_pcb])
            ctx.dve.emit(lambda e, t=t: e.tensor_copy(crow[0:1, t * 128:(t + 1) * 128], pcb[0:1, 0:128]),
                         reads=[b_pcb], writes=[b_crow])
        def emit_xct(tc):
            xcT, b_xcT = xcT2[tc % 2], b_xcT2[tc % 2]
            ctx.pe.emit(lambda e, tc=tc: e.matmul(pcb[:, :], consts.ones_row_bf[0:1, 0:128],
                                                  crow[0:1, tc * 512:(tc + 1) * 512], start=True, stop=True),
                        reads=[b_crow, consts.b_ones_row_bf], writes=[b_pcb])
            for ci in range(NCH):
                ctx.dve.emit(lambda e, ci=ci, tc=tc, xcT=xcT: e.tensor_mul(
                    xcT[:, ci * 512:(ci + 1) * 512],
                    xnT[:, ci * T + tc * 512: ci * T + (tc + 1) * 512], pcb[:, :]),
                    reads=[b_xnT, b_pcb], writes=[b_xcT])

        emit_xct(0)
        for tc in range(ntc):
            xcT, b_xcT = xcT2[tc % 2], b_xcT2[tc % 2]
            for hc in range(NCH):
                pb, bpb = pz[hc % 2], b_pz[hc % 2]
                fns = []
                for ci in range(NCH):
                    fns.append(lambda e, ci=ci, hc=hc, pb=pb, xcT=xcT: e.matmul(
                        pb[:, :], w1T[:, ci * D + hc * 128: ci * D + (hc + 1) * 128],
                        xcT[:, ci * 512:(ci + 1) * 512], start=(ci == 0), stop=(ci == NCH - 1)))
                ctx.pe.emit_multi(fns, reads=[b_w1T, b_xcT], writes=[bpb])
                ctx.act.emit(lambda e, hc=hc, pb=pb: e.activation(
                    out=hT[:, hc * 512:(hc + 1) * 512], in_=pb[:, :], func=AF.Silu,
                    bias=b1col[:, hc:hc + 1]),
                    reads=[bpb, b_b1col], writes=[b_hT])
            if tc + 1 < ntc:
                emit_xct(tc + 1)
            for s4 in range(4):
                t = tc * 4 + s4
                for half in range(2):
                    pb, bpb = pw[half], b_pw[half]
                    fns = []
                    for ci in range(NCH):
                        fns.append(lambda e, ci=ci, s4=s4, half=half, pb=pb, xcT=xcT: e.matmul(
                            pb[:, :], xcT[:, ci * 512 + s4 * 128: ci * 512 + (s4 + 1) * 128],
                            w3T[:, ci * D + half * 512: ci * D + (half + 1) * 512],
                            start=(ci == 0), stop=False))
                    fns.append(lambda e, half=half, pb=pb: e.matmul(
                        pb[:, :], consts.ones_row_bf[0:1, 0:128],
                        brow[0:1, D + half * 512: D + (half + 1) * 512], start=False, stop=True))
                    ctx.pe.emit_multi(fns, reads=[b_xcT, b_w3T, b_brow, consts.b_ones_row_bf], writes=[bpb])
                    ctx.act.emit(lambda e, half=half, pb=pb: e.activation(
                        out=x3sb[:, half * 512:(half + 1) * 512], in_=pb[:, :], func=AF.Copy),
                        reads=[bpb], writes=[b_x3sb])
                for half in range(2):
                    pb, bpb = pw[2], b_pw[2]
                    fns = []
                    for hc in range(NCH):
                        fns.append(lambda e, hc=hc, s4=s4, half=half, pb=pb: e.matmul(
                            pb[:, :], hT[:, hc * 512 + s4 * 128: hc * 512 + (s4 + 1) * 128],
                            w2T[:, hc * D + half * 512: hc * D + (half + 1) * 512],
                            start=(hc == 0), stop=False))
                    fns.append(lambda e, half=half, pb=pb: e.matmul(
                        pb[:, :], consts.ones_row_bf[0:1, 0:128],
                        brow[0:1, 0 + half * 512: 0 + (half + 1) * 512], start=False, stop=True))
                    ctx.pe.emit_multi(fns, reads=[b_hT, b_w2T, b_brow, consts.b_ones_row_bf], writes=[bpb])
                    ctx.dve.emit(lambda e, half=half, pb=pb: e.tensor_mul(
                        t2[:, :], pb[:, :], x3sb[:, half * 512:(half + 1) * 512]),
                        reads=[bpb, b_x3sb], writes=[b_t2])
                    cb, b_cb = ctrb[ctrbi[0] % 3], b_ctrb[ctrbi[0] % 3]
                    ctx.dve.emit(lambda e, t=t, half=half, ex=ex, cb=cb: e.tensor_scalar_mul(
                        cb[:, half * 512:(half + 1) * 512],
                        t2[:, :], mbuf[:, t * E + ex:t * E + ex + 1]),
                        reads=[b_t2, b_mbuf], writes=[b_cb])
                ctrbi[0] += 1
                ctx.pool.emit(lambda e, t=t, ex=ex, cb=cb: e.dma_start(
                    out=y_hbm[t * 128:(t + 1) * 128, :], in_=cb[:, :],
                    accum_op=(ALU.bypass if ex == 0 else ALU.add)),
                    reads=[b_cb], writes=[b_yh[t]], q=ctx.qpool)

    sw1h, b_sw1h = io["L0_ffn_sw1"], io["b_L0_ffn_sw1"]
    sw2h, b_sw2h = io["L0_ffn_sw2"], io["b_L0_ffn_sw2"]
    sw3h, b_sw3h = io["L0_ffn_sw3"], io["b_L0_ffn_sw3"]
    IC = 256
    nic = (2 * D) // IC
    z1sT, b_z1sT = w1T, b_w1T
    z2sT, b_z2sT = w2T, b_w2T
    swT, b_swT = xcT, b_xcT
    sw2T_, b_sw2T_ = hT, b_hT
    sw3T_, b_sw3T_ = w3T, b_w3T
    for ic in range(nic):
        emit_weight_T3(ctx, consts, swT, b_swT, nat2, b_nat2, natcnt, sw1h, b_sw1h, IC, D, ptp, b_ptp, row0=ic * IC)
        emit_weight_T3(ctx, consts, sw2T_, b_sw2T_, nat2, b_nat2, natcnt, sw2h, b_sw2h, IC, D, ptp, b_ptp, row0=ic * IC)
        emit_weight_T3(ctx, consts, sw3T_, b_sw3T_, nat2, b_nat2, natcnt, sw3h, b_sw3h, D, IC, ptp, b_ptp, col0=ic * IC)
        for isub in range(IC // 128):
            for tc in range(ntc):
                pb, bpb = pz[tc % 2], b_pz[tc % 2]
                fns = []
                for ci in range(NCH):
                    fns.append(lambda e, ci=ci, isub=isub, tc=tc, pb=pb: e.matmul(
                        pb[:, :], swT[:, ci * IC + isub * 128: ci * IC + (isub + 1) * 128],
                        xnT[:, ci * T + tc * 512: ci * T + (tc + 1) * 512],
                        start=(ci == 0), stop=(ci == NCH - 1)))
                ctx.pe.emit_multi(fns, reads=[b_swT, b_xnT], writes=[bpb])
                ctx.act.emit(lambda e, isub=isub, tc=tc, pb=pb: e.activation(
                    out=z1sT[:, isub * T + tc * 512: isub * T + (tc + 1) * 512], in_=pb[:, :], func=AF.Silu),
                    reads=[bpb], writes=[b_z1sT])
                pb2, bpb2 = pw[tc % 2], b_pw[tc % 2]
                fns = []
                for ci in range(NCH):
                    fns.append(lambda e, ci=ci, isub=isub, tc=tc, pb2=pb2: e.matmul(
                        pb2[:, :], sw2T_[:, ci * IC + isub * 128: ci * IC + (isub + 1) * 128],
                        xnT[:, ci * T + tc * 512: ci * T + (tc + 1) * 512],
                        start=(ci == 0), stop=(ci == NCH - 1)))
                ctx.pe.emit_multi(fns, reads=[b_sw2T_, b_xnT], writes=[bpb2])
                ctx.dve.emit(lambda e, isub=isub, tc=tc, pb2=pb2: e.tensor_mul(
                    z1sT[:, isub * T + tc * 512: isub * T + (tc + 1) * 512],
                    z1sT[:, isub * T + tc * 512: isub * T + (tc + 1) * 512], pb2[:, :]),
                    reads=[b_z1sT, bpb2], writes=[b_z1sT])
        for t in range(nt):
            for half in range(2):
                pb, bpb = pw[2], b_pw[2]
                fns = []
                for isub in range(IC // 128):
                    fns.append(lambda e, isub=isub, t=t, half=half, pb=pb: e.matmul(
                        pb[:, :], z1sT[:, isub * T + t * 128: isub * T + (t + 1) * 128],
                        sw3T_[:, isub * D + half * 512: isub * D + (half + 1) * 512],
                        start=(isub == 0), stop=(isub == IC // 128 - 1)))
                ctx.pe.emit_multi(fns, reads=[b_z1sT, b_sw3T_], writes=[bpb])
                cb, b_cb = ctrb[ctrbi[0] % 3], b_ctrb[ctrbi[0] % 3]
                ctx.dve.emit(lambda e, half=half, pb=pb, cb=cb: e.tensor_copy(
                    cb[:, half * 512:(half + 1) * 512], pb[:, :]),
                    reads=[bpb], writes=[b_cb])
                if half == 1:
                    ctrbi[0] += 1
                    ctx.pool.emit(lambda e, t=t, cb=cb: e.dma_start(
                        out=y_hbm[t * 128:(t + 1) * 128, :], in_=cb[:, :], accum_op=ALU.add),
                        reads=[b_cb], writes=[b_yh[t]], q=ctx.qpool)

    for t in range(nt):
        yb, b_yb = ctrb[t % 2], b_ctrb[t % 2]
        ctx.sp.emit(lambda e, t=t, yb=yb: e.dma_start(out=yb[:, :], in_=y_hbm[t * 128:(t + 1) * 128, :]),
                    reads=[b_yh[t]], writes=[b_yb], q=ctx.qsp)
        emit_finale(ctx, consts, T, t, yb[:, :], b_yb, xin, b_xin, xout, b_xout,
                    xt[t % 2], b_xt[t % 2], t1, b_t1, sttmp, b_sttmp, r0=r0)


def build_phase0(ctx, consts, stack, T, io, tag="p0"):
    nc = ctx.nc
    nt = T // 128
    ntc = T // 512
    en = stack.enter_context
    xin, b_xin = io["x_in"], io["b_x_in"]
    xout, b_xout = io["x_out"], io["b_x_out"]
    slopes = get_slopes(H)

    qt_hbm = nc.dram_tensor(tag + "_qt", [D, T], BF16, kind="Internal").ap()
    kt_hbm = nc.dram_tensor(tag + "_kt", [D, T], BF16, kind="Internal").ap()
    v_hbm = nc.dram_tensor(tag + "_v", [T, D], BF16, kind="Internal").ap()
    b_qt = ctx.buf(tag + "_qt"); b_kt = ctx.buf(tag + "_kt"); b_v = ctx.buf(tag + "_v")

    lnst = en(nc.sbuf_tensor(tag + "_lnst", [128, nt * 2], F32)); b_lnst = ctx.buf(tag + "_lnst")
    sttmp = en(nc.sbuf_tensor(tag + "_sttmp", [128, 16], F32)); b_sttmp = ctx.buf(tag + "_sttmp")
    xt = [en(nc.sbuf_tensor(tag + "_xt%d" % i, [128, D], F32)) for i in range(2)]
    b_xt = [ctx.buf(tag + "_xt0"), ctx.buf(tag + "_xt1")]
    kpos = en(nc.sbuf_tensor(tag + "_kpos", [1, T], F32)); b_kpos = ctx.buf(tag + "_kpos")
    cones, b_cones = None, None
    ctx.pool.emit(lambda e: e.iota(kpos[0:1, :], pattern=[[1, T]], base=0, channel_multiplier=0,
                                   allow_small_or_imprecise_dtypes=True), writes=[b_kpos])

    with ctx.sub() as st0:
        en0 = st0.enter_context
        xnT = en0(nc.sbuf_tensor(tag + "_xnT", [128, NCH * T], BF16)); b_xnT = ctx.buf(tag + "_xnT")
        xnb = en0(nc.sbuf_tensor(tag + "_xnb", [128, D], BF16)); b_xnb = ctx.buf(tag + "_xnb")
        wq = en0(nc.sbuf_tensor(tag + "_wq", [128, NCH * D], BF16)); b_wq = ctx.buf(tag + "_wq")
        wk = en0(nc.sbuf_tensor(tag + "_wk", [128, NCH * D], BF16)); b_wk = ctx.buf(tag + "_wk")
        wv = en0(nc.sbuf_tensor(tag + "_wv", [128, NCH * D], BF16)); b_wv = ctx.buf(tag + "_wv")
        qst = en0(nc.sbuf_tensor(tag + "_qst", [128, 512], BF16)); b_qst = ctx.buf(tag + "_qst")
        kst = en0(nc.sbuf_tensor(tag + "_kst", [128, 512], BF16)); b_kst = ctx.buf(tag + "_kst")
        vst = en0(nc.sbuf_tensor(tag + "_vst", [128, 512], BF16)); b_vst = ctx.buf(tag + "_vst")
        ptp = [en0(nc.psum_tensor(tag + "_ptp%d" % i, [128, 128], BF16)) for i in range(2)]
        b_ptp = [ctx.buf(tag + "_ptp%d" % i) for i in range(2)]
        pqk = [en0(nc.psum_tensor(tag + "_pqk%d" % i, [128, 512], F32)) for i in range(2)]
        b_pqk = [ctx.buf(tag + "_pqk%d" % i) for i in range(2)]

        for name, wsb, b_wsb in (("L0_attn_Wq", wq, b_wq), ("L0_attn_Wk", wk, b_wk),
                                 ("L0_attn_Wv", wv, b_wv)):
            wh, b_wh = io[name], io["b_" + name]
            ctx.pool.emit(lambda e, wh=wh, wsb=wsb: e.dma_start(
                out=wsb[:, :].rearrange("p (a c) -> p a c", a=NCH),
                in_=wh[:, :].rearrange("(a p) c -> p a c", p=128)),
                reads=[b_wh], writes=[b_wsb], q=ctx.qpool)

        emit_ln_pass(ctx, consts, T, xin, b_xin, xt, b_xt, xnb, b_xnb, lnst, b_lnst,
                     sttmp, b_sttmp, xnT, b_xnT, ptp, b_ptp)

        for name, wsb, b_wsb, dst, b_dst, stg, b_stg, scale in (
                ("q", wq, b_wq, qt_hbm, b_qt, qst, b_qst, 0.125),
                ("k", wk, b_wk, kt_hbm, b_kt, kst, b_kst, 1.0)):
            for co in range(NCH):
                for tc in range(ntc):
                    pb, bpb = pqk[tc % 2], b_pqk[tc % 2]
                    fns = []
                    for ci in range(NCH):
                        fns.append(lambda e, ci=ci, co=co, tc=tc, pb=pb, wsb=wsb: e.matmul(
                            pb[:, :], wsb[:, ci * D + co * 128: ci * D + (co + 1) * 128],
                            xnT[:, ci * T + tc * 512: ci * T + (tc + 1) * 512],
                            start=(ci == 0), stop=(ci == NCH - 1)))
                    ctx.pe.emit_multi(fns, reads=[b_wsb, b_xnT], writes=[bpb])
                    ctx.act.emit(lambda e, pb=pb, stg=stg, scale=scale: e.activation(
                        out=stg[:, :], in_=pb[:, :], func=AF.Copy, scale=scale),
                        reads=[bpb], writes=[b_stg])
                    ctx.sp.emit(lambda e, co=co, tc=tc, dst=dst, stg=stg: e.dma_start(
                        out=dst[co * 128:(co + 1) * 128, tc * 512:(tc + 1) * 512], in_=stg[:, :]),
                        reads=[b_stg], writes=[b_dst], q=ctx.qsp)
        for t in range(nt):
            for half in range(2):
                pb, bpb = pqk[half], b_pqk[half]
                fns = []
                for ci in range(NCH):
                    fns.append(lambda e, ci=ci, t=t, half=half, pb=pb: e.matmul(
                        pb[:, :], xnT[:, ci * T + t * 128: ci * T + (t + 1) * 128],
                        wv[:, ci * D + half * 512: ci * D + (half + 1) * 512],
                        start=(ci == 0), stop=(ci == NCH - 1)))
                ctx.pe.emit_multi(fns, reads=[b_wv, b_xnT], writes=[bpb])
                ctx.dve.emit(lambda e, pb=pb: e.tensor_copy(vst[:, :], pb[:, :]),
                             reads=[bpb], writes=[b_vst])
                ctx.sp.emit(lambda e, t=t, half=half: e.dma_start(
                    out=v_hbm[t * 128:(t + 1) * 128, half * 512:(half + 1) * 512], in_=vst[:, :]),
                    reads=[b_vst], writes=[b_v], q=ctx.qsp)

    with ctx.sub() as st12:
        ctxfull = st12.enter_context(nc.sbuf_tensor(tag + "_ctx", [128, nt * D], BF16))
        b_ctx = ctx.buf(tag + "_ctx")
        _build_phase0_bc(ctx, consts, st12, T, io, tag, ctxfull, b_ctx, lnst, b_lnst,
                         sttmp, b_sttmp, xt, b_xt, kpos, b_kpos, cones, b_cones,
                         qt_hbm, b_qt, kt_hbm, b_kt, v_hbm, b_v)


def _build_phase0_bc(ctx, consts, stack, T, io, tag, ctxfull, b_ctx, lnst, b_lnst,
                     sttmp, b_sttmp, xt, b_xt, kpos, b_kpos, cones, b_cones,
                     qt_hbm, b_qt, kt_hbm, b_kt, v_hbm, b_v):
    nc = ctx.nc
    nt = T // 128
    nqc = T // 512
    slopes = get_slopes(H)
    xin, b_xin = io["x_in"], io["b_x_in"]
    xout, b_xout = io["x_out"], io["b_x_out"]
    with ctx.sub() as st1:
        en1 = st1.enter_context
        qaug = en1(nc.sbuf_tensor(tag + "_qaug", [128, 2 * T], BF16)); b_qaug = ctx.buf(tag + "_qaug")
        kaug = en1(nc.sbuf_tensor(tag + "_kaug", [128, 2 * T], BF16)); b_kaug = ctx.buf(tag + "_kaug")
        vp = en1(nc.sbuf_tensor(tag + "_vp", [128, nt * 128], BF16)); b_vp = ctx.buf(tag + "_vp")
        sq = en1(nc.sbuf_tensor(tag + "_sq", [64, T], BF16)); b_sq = ctx.buf(tag + "_sq")
        rows = en1(nc.sbuf_tensor(tag + "_rows", [1, 2 * T + 8], F32)); b_rows = ctx.buf(tag + "_rows")
        ckbf = en1(nc.sbuf_tensor(tag + "_ckbf", [1, T], BF16)); b_ckbf = ctx.buf(tag + "_ckbf")
        ptile = [[en1(nc.sbuf_tensor(tag + "_pt%d%d" % (i, j), [128, 512], BF16)) for j in range(2)]
                 for i in range(2)]
        b_ptile = [[ctx.buf(tag + "_pt%d%d" % (i, j)) for j in range(2)] for i in range(2)]
        zrow = en1(nc.sbuf_tensor(tag + "_zrow", [1, 2 * 512], F32)); b_zrow = ctx.buf(tag + "_zrow")
        zbc = en1(nc.sbuf_tensor(tag + "_zbc", [128, 512], F32)); b_zbc = ctx.buf(tag + "_zbc")
        mzc = en1(nc.sbuf_tensor(tag + "_mzc", [128, nt], F32)); b_mzc = ctx.buf(tag + "_mzc")
        pst = [[en1(nc.psum_tensor(tag + "_pst%d%d" % (i, j), [128, 512], F32)) for j in range(2)]
               for i in range(2)]
        b_pst = [[ctx.buf(tag + "_pst%d%d" % (i, j)) for j in range(2)] for i in range(2)]
        pctx = en1(nc.psum_tensor(tag + "_pctx", [128, 512], F32)); b_pctx = ctx.buf(tag + "_pctx")
        pz = en1(nc.psum_tensor(tag + "_pz", [128, 512], F32)); b_pz = ctx.buf(tag + "_pz")

        ctx.pool.emit(lambda e: e.memset(qaug[64:65, :], 1.0), writes=[b_qaug])
        for i in range(2 * T // 512):
            ctx.sp.emit(lambda e, i=i: e.dma_start(out=qaug[65:66, i * 512:(i + 1) * 512],
                                                   in_=consts.ones_row_bf[0:1, :]),
                        reads=[consts.b_ones_row_bf], writes=[b_qaug], q=ctx.qsp)
            ctx.sp.emit(lambda e, i=i: e.dma_start(out=kaug[66:67, i * 512:(i + 1) * 512],
                                                   in_=consts.ones_row_bf[0:1, :]),
                        reads=[consts.b_ones_row_bf], writes=[b_kaug], q=ctx.qsp)

        for pair in range(NCH):
            ctx.sp.emit(lambda e, pair=pair: e.dma_start(
                out=qaug[0:64, :].rearrange("p (two t) -> p two t", two=2),
                in_=qt_hbm[pair * 128:(pair + 1) * 128, :].rearrange("(two p) t -> p two t", p=64)),
                reads=[b_qt], writes=[b_qaug], q=ctx.qsp)
            ctx.sp.emit(lambda e, pair=pair: e.dma_start(
                out=kaug[0:64, :].rearrange("p (two t) -> p two t", two=2),
                in_=kt_hbm[pair * 128:(pair + 1) * 128, :].rearrange("(two p) t -> p two t", p=64)),
                reads=[b_kt], writes=[b_kaug], q=ctx.qsp)
            ctx.sp.emit(lambda e, pair=pair: e.dma_start(
                out=vp[:, :].rearrange("p (kt c) -> p kt c", c=128),
                in_=v_hbm[:, pair * 128:(pair + 1) * 128].rearrange("(kt p) c -> p kt c", p=128)),
                reads=[b_v], writes=[b_vp], q=ctx.qsp)
            for hb in range(2):
                c_h = slopes[2 * pair + hb] / 8.0
                ctx.act.emit(lambda e, c_h=c_h: e.activation(
                    out=rows[0:1, 0:T], in_=kpos[0:1, :], func=AF.Copy, scale=c_h),
                    reads=[b_kpos], writes=[b_rows])
                ctx.dve.emit(lambda e: e.tensor_copy(ckbf[0:1, :], rows[0:1, 0:T]),
                             reads=[b_rows], writes=[b_ckbf])
                ctx.dve.emit(lambda e, hb=hb: e.tensor_copy(kaug[64:65, hb * T:(hb + 1) * T],
                                                            ckbf[0:1, :]),
                             reads=[b_ckbf], writes=[b_kaug])
                ctx.dve.emit(lambda e: e.tensor_sub(rows[0:1, T:2 * T], rows[0:1, 0:T],
                                                    ckbf[0:1, :]),
                             reads=[b_rows, b_ckbf], writes=[b_rows])
                ctx.pool.emit(lambda e, hb=hb: e.dma_start(out=kaug[65:66, hb * T:(hb + 1) * T],
                                                           in_=rows[0:1, T:2 * T]),
                              reads=[b_rows], writes=[b_kaug], q=ctx.qpool)
                ctx.act.emit(lambda e, hb=hb: e.activation(
                    out=sq[0:64, :], in_=qaug[0:64, hb * T:(hb + 1) * T], func=AF.Square),
                    reads=[b_qaug], writes=[b_sq])
                for i in range(T // 512):
                    pb = pz if i % 2 == 0 else pctx
                    bpb = b_pz if i % 2 == 0 else b_pctx
                    ctx.pe.emit(lambda e, i=i, pb=pb: e.matmul(
                        pb[0:1, :], consts.ones_col_bf[0:64, 0:1], sq[0:64, i * 512:(i + 1) * 512],
                        start=True, stop=True),
                        reads=[b_sq, consts.b_ones_col_bf], writes=[bpb])
                    ctx.dve.emit(lambda e, i=i, pb=pb: e.tensor_copy(
                        rows[0:1, i * 512:(i + 1) * 512], pb[0:1, :]),
                        reads=[bpb], writes=[b_rows])
                ctx.act.emit(lambda e, hb=hb: e.activation(
                    out=sq[0:64, :], in_=kaug[0:64, hb * T:(hb + 1) * T], func=AF.Square),
                    reads=[b_kaug], writes=[b_sq])
                for i in range(T // 512):
                    pb = pz if i % 2 == 0 else pctx
                    bpb = b_pz if i % 2 == 0 else b_pctx
                    ctx.pe.emit(lambda e, i=i, pb=pb: e.matmul(
                        pb[0:1, :], consts.ones_col_bf[0:64, 0:1], sq[0:64, i * 512:(i + 1) * 512],
                        start=True, stop=True),
                        reads=[b_sq, consts.b_ones_col_bf], writes=[bpb])
                    ctx.dve.emit(lambda e, i=i, pb=pb: e.tensor_copy(
                        rows[0:1, T + i * 512:T + (i + 1) * 512], pb[0:1, :]),
                        reads=[bpb], writes=[b_rows])
                ctx.dve.emit(lambda e: e.reduce_max(rows[0:1, 2 * T:2 * T + 1],
                                                    rows[0:1, T:2 * T], axis=AX.X),
                             reads=[b_rows], writes=[b_rows])
                ctx.act.emit(lambda e: e.activation(out=rows[0:1, 2 * T + 1:2 * T + 2],
                                                    in_=rows[0:1, 2 * T:2 * T + 1], func=AF.Ln),
                             reads=[b_rows], writes=[b_rows])
                ctx.act.emit(lambda e: e.activation(out=rows[0:1, 2 * T + 1:2 * T + 2],
                                                    in_=rows[0:1, 2 * T + 1:2 * T + 2],
                                                    func=AF.Copy, scale=0.5),
                             reads=[b_rows], writes=[b_rows])
                ctx.act.emit(lambda e: e.activation(out=rows[0:1, 0:T], in_=rows[0:1, 0:T],
                                                    func=AF.Ln),
                             reads=[b_rows], writes=[b_rows])
                ctx.act.emit(lambda e: e.activation(out=rows[0:1, 0:T], in_=rows[0:1, 0:T],
                                                    func=AF.Exp, scale=0.5,
                                                    bias=rows[0:1, 2 * T + 1:2 * T + 2]),
                             reads=[b_rows], writes=[b_rows])
                ctx.act.emit(lambda e, c_h=c_h: e.activation(
                    out=rows[0:1, T:2 * T], in_=kpos[0:1, :], func=AF.Copy, scale=c_h),
                    reads=[b_kpos], writes=[b_rows])
                ctx.dve.emit(lambda e: e.tensor_add(rows[0:1, 0:T], rows[0:1, 0:T],
                                                    rows[0:1, T:2 * T]),
                             reads=[b_rows], writes=[b_rows])
                ctx.dve.emit(lambda e: e.tensor_scalar(rows[0:1, 0:T], rows[0:1, 0:T],
                                                       0.5, -1.0, ALU.add, ALU.mult),
                             reads=[b_rows], writes=[b_rows])
                ctx.pool.emit(lambda e, hb=hb: e.dma_start(
                    out=qaug[66:67, hb * T:(hb + 1) * T], in_=rows[0:1, 0:T]),
                    reads=[b_rows], writes=[b_qaug], q=ctx.qpool)

            for qc in range(nqc):
                q0 = qc * 512
                ctxsl = ctxfull[:, pair * T + q0: pair * T + q0 + 512]
                klast = 4 * qc + 3
                def emit_cz(kt):
                    par = kt % 2
                    ctx.pe.emit_multi([
                        (lambda e, kt=kt, par=par: e.matmul(
                            pctx[0:64, :], vp[:, kt * 128: kt * 128 + 64],
                            ptile[0][par][:, 0:512],
                            start=(kt == 0), stop=(kt == klast), tile_position=(0, 0))),
                        (lambda e, kt=kt, par=par: e.matmul(
                            pctx[64:128, :], vp[:, kt * 128 + 64: kt * 128 + 128],
                            ptile[1][par][:, 0:512],
                            start=(kt == 0), stop=(kt == klast), tile_position=(0, 64),
                            skip_group_check=True)),
                    ], reads=[b_vp, b_ptile[0][par], b_ptile[1][par]], writes=[b_pctx])
                    ctx.pe.emit_multi([
                        (lambda e, kt=kt, par=par: e.matmul(
                            pz[0:1, :], consts.ones_col_bf[:, 0:1], ptile[0][par][:, 0:512],
                            start=(kt == 0), stop=(kt == klast), tile_position=(0, 0))),
                        (lambda e, kt=kt, par=par: e.matmul(
                            pz[32:33, :], consts.ones_col_bf[:, 0:1], ptile[1][par][:, 0:512],
                            start=(kt == 0), stop=(kt == klast), tile_position=(0, 32),
                            skip_group_check=True)),
                    ], reads=[consts.b_ones_col_bf, b_ptile[0][par], b_ptile[1][par]],
                        writes=[b_pz])

                for kt in range(4 * qc + 4):
                    jd = kt - 4 * qc
                    par = kt % 2
                    for hb in range(2):
                        pb, bpb = pst[hb][par], b_pst[hb][par]
                        fns = [
                            (lambda e, kt=kt, q0=q0, hb=hb, pb=pb: e.matmul(
                                pb[:, :],
                                kaug[0:67, hb * T + kt * 128: hb * T + (kt + 1) * 128],
                                qaug[0:67, hb * T + q0: hb * T + q0 + 512],
                                start=True, stop=not (0 <= jd <= 3))),
                        ]
                        if 0 <= jd <= 3:
                            for jj in range(jd):
                                fns.append(lambda e, jj=jj, pb=pb: e.matmul(
                                    pb[:, jj * 128:(jj + 1) * 128],
                                    consts.id_bf[:, :], consts.fullmask[:, :],
                                    start=False, stop=False))
                            fns.append(lambda e, jd=jd, pb=pb: e.matmul(
                                pb[:, jd * 128:(jd + 1) * 128],
                                consts.id_bf[:, :], consts.masklt[:, :],
                                start=False, stop=True))
                        ctx.pe.emit_multi(fns, reads=[b_kaug, b_qaug, consts.b_id_bf,
                                                      consts.b_masklt, consts.b_fullmask],
                                          writes=[bpb])
                        ctx.act.emit(lambda e, hb=hb, par=par, pb=pb: e.activation(
                            out=ptile[hb][par][:, 0:512], in_=pb[:, :], func=AF.Exp),
                            reads=[bpb], writes=[b_ptile[hb][par]])
                    if kt > 0:
                        emit_cz(kt - 1)
                emit_cz(4 * qc + 3)
                ctx.dve.emit(lambda e: e.tensor_copy(zrow[0:1, 0:512], pz[0:1, :]),
                             reads=[b_pz], writes=[b_zrow])
                ctx.dve.emit(lambda e: e.tensor_copy(zrow[0:1, 512:1024], pz[32:33, :]),
                             reads=[b_pz], writes=[b_zrow])
                ctx.act.emit(lambda e: e.activation(out=zrow[0:1, :], in_=zrow[0:1, :], func=AF.Ln),
                             reads=[b_zrow], writes=[b_zrow])
                ctx.act.emit(lambda e: e.activation(out=zrow[0:1, :], in_=zrow[0:1, :], func=AF.Exp,
                                                    scale=-1.0),
                             reads=[b_zrow], writes=[b_zrow])
                ctx.pe.emit_multi([
                    (lambda e: e.matmul(pz[0:64, :], consts.ones_row[0:1, 0:64],
                                        zrow[0:1, 0:512], start=True, stop=True,
                                        tile_position=(0, 0))),
                    (lambda e: e.matmul(pz[64:128, :], consts.ones_row[0:1, 0:64],
                                        zrow[0:1, 512:1024], start=True, stop=True,
                                        tile_position=(0, 64), skip_group_check=True)),
                ], reads=[consts.b_ones_row, b_zrow], writes=[b_pz])
                ctx.dve.emit(lambda e: e.tensor_copy(zbc[:, :], pz[:, :]),
                             reads=[b_pz], writes=[b_zbc])
                ctx.dve.emit(lambda e, ctxsl=ctxsl: e.tensor_mul(ctxsl, pctx[:, :], zbc[:, :]),
                             reads=[b_pctx, b_zbc], writes=[b_ctx])

    with ctx.sub() as st2:
        en2 = st2.enter_context
        wo = en2(nc.sbuf_tensor(tag + "_wo", [128, NCH * D], BF16)); b_wo = ctx.buf(tag + "_wo")
        borow = en2(nc.sbuf_tensor(tag + "_bo", [1, D], F32)); b_bo = ctx.buf(tag + "_bo")
        cts = en2(nc.sbuf_tensor(tag + "_cts", [128, NCH * 128], BF16)); b_cts = ctx.buf(tag + "_cts")
        ao = en2(nc.sbuf_tensor(tag + "_ao", [128, D], F32)); b_ao = ctx.buf(tag + "_ao")
        t1 = en2(nc.sbuf_tensor(tag + "_t1", [128, D], F32)); b_t1 = ctx.buf(tag + "_t1")
        ptc = [en2(nc.psum_tensor(tag + "_ptc%d" % i, [128, 128], BF16)) for i in range(2)]
        b_ptc = [ctx.buf(tag + "_ptc%d" % i) for i in range(2)]
        ppr = [en2(nc.psum_tensor(tag + "_ppr%d" % i, [128, 512], F32)) for i in range(2)]
        b_ppr = [ctx.buf(tag + "_ppr%d" % i) for i in range(2)]

        woh, b_woh = io["L0_attn_Wo"], io["b_L0_attn_Wo"]
        boh, b_boh = io["L0_attn_bo"], io["b_L0_attn_bo"]
        ctx.pool.emit(lambda e: e.dma_start(
            out=wo[:, :].rearrange("p (a c) -> p a c", a=NCH),
            in_=woh[:, :].rearrange("(a p) c -> p a c", p=128)),
            reads=[b_woh], writes=[b_wo], q=ctx.qpool)
        ctx.sp.emit(lambda e: e.dma_start(out=borow[0:1, :], in_=boh[None, :]),
                    reads=[b_boh], writes=[b_bo], q=ctx.qsp)
        for t in range(nt):
            for half in range(2):
                pb, bpb = ppr[half], b_ppr[half]
                fns = []
                for ci in range(NCH):
                    fns.append(lambda e, ci=ci, t=t, half=half, pb=pb: e.matmul(
                        pb[:, :], ctxfull[:, ci * T + t * 128: ci * T + (t + 1) * 128],
                        wo[:, ci * D + half * 512: ci * D + (half + 1) * 512],
                        start=(ci == 0), stop=False))
                fns.append(lambda e, half=half, pb=pb: e.matmul(
                    pb[:, :], consts.ones_row[0:1, 0:128],
                    borow[0:1, half * 512:(half + 1) * 512], start=False, stop=True))
                ctx.pe.emit_multi(fns, reads=[b_ctx, b_wo, b_bo, consts.b_ones_row], writes=[bpb])
                ctx.dve.emit(lambda e, half=half, pb=pb: e.tensor_copy(ao[:, half * 512:(half + 1) * 512],
                                                                      pb[:, :]),
                             reads=[bpb], writes=[b_ao])
            emit_finale(ctx, consts, T, t, ao[:, :], b_ao, xin, b_xin, xout, b_xout,
                        xt[t % 2], b_xt[t % 2], t1, b_t1, sttmp, b_sttmp)


from contextlib import ExitStack as _ExitStack


def _substack(stack):
    return _ExitStack()


def build_phase3(ctx, consts, stack, T, io, tag="p3"):
    nc = ctx.nc
    nt = T // 128
    ntc = T // 512
    en = stack.enter_context
    xin, b_xin = io["x_in"], io["b_x_in"]
    xout, b_xout = io["x_out"], io["b_x_out"]
    NH, RAILS, RW = 4, 64, 16
    ZT = NH * RAILS * 64
    ZC = 1024
    nzc = ZT // ZC
    NG = RAILS // 8

    xnT = en(nc.sbuf_tensor(tag + "_xnT", [128, NCH * T], BF16)); b_xnT = ctx.buf(tag + "_xnT")
    acc = en(nc.sbuf_tensor(tag + "_acc", [128, nt * D], FP16)); b_acc = ctx.buf(tag + "_acc")
    lnst = en(nc.sbuf_tensor(tag + "_lnst", [128, nt * 2], F32)); b_lnst = ctx.buf(tag + "_lnst")
    sttmp = en(nc.sbuf_tensor(tag + "_sttmp", [128, 16], F32)); b_sttmp = ctx.buf(tag + "_sttmp")
    xt = [en(nc.sbuf_tensor(tag + "_xt%d" % i, [128, D], F32)) for i in range(2)]
    b_xt = [ctx.buf(tag + "_xt0"), ctx.buf(tag + "_xt1")]
    xnb = en(nc.sbuf_tensor(tag + "_xnb", [128, D], BF16)); b_xnb = ctx.buf(tag + "_xnb")
    t1 = en(nc.sbuf_tensor(tag + "_t1", [128, D], F32)); b_t1 = ctx.buf(tag + "_t1")
    cdiag = en(nc.sbuf_tensor(tag + "_cdiag", [128, 32 * 128], BF16)); b_cdiag = ctx.buf(tag + "_cdiag")
    hT2 = [en(nc.sbuf_tensor(tag + "_hT%d" % i, [128, (ZC // 128) * 512], BF16)) for i in range(2)]
    b_hT2 = [ctx.buf(tag + "_hT%d" % i) for i in range(2)]
    wcT = en(nc.sbuf_tensor(tag + "_wcT", [128, (ZC // 128) * D], BF16)); b_wcT = ctx.buf(tag + "_wcT")
    wcst = [en(nc.sbuf_tensor(tag + "_wcst%d" % i, [128, ZC], BF16)) for i in range(2)]
    b_wcst = [ctx.buf(tag + "_wcst0"), ctx.buf(tag + "_wcst1")]
    ao = en(nc.sbuf_tensor(tag + "_ao", [128, D], F32)); b_ao = ctx.buf(tag + "_ao")

    ptp = [en(nc.psum_tensor(tag + "_ptp%d" % i, [128, 512], BF16)) for i in range(2)]
    b_ptp = [ctx.buf(tag + "_ptp%d" % i) for i in range(2)]
    pzr = [en(nc.psum_tensor(tag + "_pzr%d" % i, [128, 512], F32)) for i in range(2)]
    b_pzr = [ctx.buf(tag + "_pzr%d" % i) for i in range(2)]
    ppr = [en(nc.psum_tensor(tag + "_ppr%d" % i, [128, 512], F32)) for i in range(2)]
    b_ppr = [ctx.buf(tag + "_ppr%d" % i) for i in range(2)]

    rmh, b_rmh = io["L1_ffn_rail_maker"], io["b_L1_ffn_rail_maker"]
    ruph, b_ruph = io["L1_ffn_rail_up_proj"], io["b_L1_ffn_rail_up_proj"]
    wch, b_wch = io["L1_ffn_w_cproj"], io["b_L1_ffn_w_cproj"]
    rm_flat = rmh.rearrange("h r i j -> (h r i) j")
    rup_flat = ruph.rearrange("h r k j -> (h r k) j")

    emit_ln_pass(ctx, consts, T, xin, b_xin, xt, b_xt, xnb, b_xnb, lnst, b_lnst,
                 sttmp, b_sttmp, xnT, b_xnT, ptp, b_ptp)

    bstack = ctx.sub(); bst = bstack.__enter__()
    enb = bst.enter_context
    NB3 = 4
    rmall = enb(nc.sbuf_tensor(tag + "_rmall", [128, 32 * 16], BF16)); b_rmall = ctx.buf(tag + "_rmall")
    rupall = enb(nc.sbuf_tensor(tag + "_rupall", [128, 128 * 16], BF16)); b_rupall = ctx.buf(tag + "_rupall")
    ctx.pool.emit(lambda e: e.dma_start(
        out=rmall[:, :].rearrange("p (a j) -> p a j", j=16),
        in_=rm_flat[:, :].rearrange("(a p) j -> p a j", p=128)),
        reads=[b_rmh], writes=[b_rmall], q=ctx.qpool)
    for hq in range(2):
        ctx.pool.emit(lambda e, hq=hq: e.dma_start(
            out=rupall[:, hq * 1024:(hq + 1) * 1024].rearrange("p (a j) -> p a j", j=16),
            in_=rup_flat[hq * 8192:(hq + 1) * 8192, :].rearrange("(a p) j -> p a j", p=128)),
            reads=[b_ruph], writes=[b_rupall], q=ctx.qpool)
    rmdiag4 = [enb(nc.sbuf_tensor(tag + "_rmd%d" % i, [128, 128], BF16)) for i in range(NB3)]
    b_rmd4 = [ctx.buf(tag + "_rmd%d" % i) for i in range(NB3)]
    rupd4 = [enb(nc.sbuf_tensor(tag + "_rupd%d" % i, [128, 128], BF16)) for i in range(NB3)]
    b_rupd4 = [ctx.buf(tag + "_rupd%d" % i) for i in range(NB3)]
    rst4 = [enb(nc.sbuf_tensor(tag + "_rst%d" % i, [128, 16 * 5], BF16)) for i in range(NB3)]
    b_rst4 = [ctx.buf(tag + "_rst%d" % i) for i in range(NB3)]
    rtr4 = [enb(nc.sbuf_tensor(tag + "_rtr%d" % i, [16, 128], BF16)) for i in range(NB3)]
    b_rtr4 = [ctx.buf(tag + "_rtr%d" % i) for i in range(NB3)]
    rmTt4 = [enb(nc.sbuf_tensor(tag + "_rmTt%d" % i, [128, 128], BF16)) for i in range(NB3)]
    b_rmTt4 = [ctx.buf(tag + "_rmTt%d" % i) for i in range(NB3)]
    csc4 = [enb(nc.sbuf_tensor(tag + "_csc%d" % i, [32, 128], BF16)) for i in range(NB3)]
    b_csc4 = [ctx.buf(tag + "_csc%d" % i) for i in range(NB3)]
    for h in range(NH):
        for g in range(NG):
            gi = h * NG + g
            rmdiag, b_rmd = rmdiag4[gi % NB3], b_rmd4[gi % NB3]
            rupd, b_rupd = rupd4[gi % NB3], b_rupd4[gi % NB3]
            rst, b_rst = rst4[gi % NB3], b_rst4[gi % NB3]
            rtr, b_rtr = rtr4[gi % NB3], b_rtr4[gi % NB3]
            rmTt, b_rmTt = rmTt4[gi % NB3], b_rmTt4[gi % NB3]
            csc, b_csc = csc4[gi % NB3], b_csc4[gi % NB3]
            ctx.pool.emit(lambda e, rmdiag=rmdiag: e.memset(rmdiag[:, :], 0.0), writes=[b_rmd])
            ctx.pool.emit(lambda e, rupd=rupd: e.memset(rupd[:, :], 0.0), writes=[b_rupd])
            for b in range(8):
                ctx.act.emit(lambda e, b=b, gi=gi, rmdiag=rmdiag: e.dma_start(
                    out=rmdiag[16 * b:16 * (b + 1), 16 * b:16 * (b + 1)],
                    in_=rmall[16 * b:16 * (b + 1), gi * 16:(gi + 1) * 16]),
                    reads=[b_rmall], writes=[b_rmd], q=ctx.qact)
            for s in range(4):
                a4 = gi * 4 + s
                ctx.pe.emit(lambda e, a4=a4: e.transpose(
                    ptp[0][0:16, 0:128], rupall[:, a4 * 16:(a4 + 1) * 16], consts.id_bf[:, :]),
                    reads=[b_rupall, consts.b_id_bf], writes=[b_ptp[0]])
                ctx.dve.emit(lambda e, rtr=rtr: e.tensor_copy(rtr[:, :], ptp[0][0:16, 0:128]),
                             reads=[b_ptp[0]], writes=[b_rtr])
                ctx.act.emit(lambda e, s=s, rupd=rupd, rtr=rtr: e.dma_start(
                    out=rupd[32 * s:32 * s + 16, 0:64], in_=rtr[:, 0:64]),
                    reads=[b_rtr], writes=[b_rupd], q=ctx.qact)
                ctx.act.emit(lambda e, s=s, rupd=rupd, rtr=rtr: e.dma_start(
                    out=rupd[32 * s + 16:32 * s + 32, 64:128], in_=rtr[:, 64:128]),
                    reads=[b_rtr], writes=[b_rupd], q=ctx.qact)
            ctx.pe.emit(lambda e, rmdiag=rmdiag: e.transpose(ptp[0][:, 0:128], rmdiag[:, :],
                                                             consts.id_bf[:, :]),
                        reads=[b_rmd, consts.b_id_bf], writes=[b_ptp[0]])
            ctx.dve.emit(lambda e, rmTt=rmTt: e.tensor_copy(rmTt[:, :], ptp[0][:, 0:128]),
                         reads=[b_ptp[0]], writes=[b_rmTt])
            for s in range(4):
                pzb, b_pzb = pzr[s % 2], b_pzr[s % 2]
                ctx.pe.emit(lambda e, s=s, pzb=pzb, rmTt=rmTt, rupd=rupd: e.matmul(
                    pzb[0:32, 0:128], rmTt[32 * s:32 * s + 32, 32 * s:32 * s + 32],
                    rupd[32 * s:32 * s + 32, 0:128],
                    start=True, stop=True, tile_position=(32 * s, 0)),
                    reads=[b_rmTt, b_rupd], writes=[b_pzb])
                if s < 3:
                    ctx.dve.emit(lambda e, gi=gi, s=s, pzb=pzb: e.tensor_copy(
                        cdiag[32 * s:32 * s + 32, gi * 128:(gi + 1) * 128], pzb[0:32, 0:128]),
                        reads=[b_pzb], writes=[b_cdiag])
                else:
                    ctx.dve.emit(lambda e, pzb=pzb, csc=csc: e.tensor_copy(csc[:, :], pzb[0:32, 0:128]),
                                 reads=[b_pzb], writes=[b_csc])
                    ctx.sp.emit(lambda e, gi=gi, csc=csc: e.dma_start(
                        out=cdiag[96:128, gi * 128:(gi + 1) * 128], in_=csc[:, :]),
                        reads=[b_csc], writes=[b_cdiag], q=ctx.qsp)
    bstack.__exit__(None, None, None)

    ctx.pool.emit(lambda e: e.memset(acc[:, :], 0.0), writes=[b_acc])

    nsub = ZC // 128
    ngc = ZC // 512
    for zc in range(nzc):
        h = zc // (nzc // NH)
        wcTr = wcT[:, :].rearrange("p (z d) -> p z d", z=nsub)
        nb = 0
        for ci in range(NCH):
            st, b_st = wcst[ci % 2], b_wcst[ci % 2]
            ctx.pool.emit(lambda e, ci=ci, zc=zc, st=st: e.dma_start(
                out=st[:, :], in_=wch[ci * 128:(ci + 1) * 128, zc * ZC:(zc + 1) * ZC]),
                reads=[b_wch], writes=[b_st], q=ctx.qpool)
            for z0 in range(0, nsub, 4):
                zn = min(4, nsub - z0)
                pb, bpb = ptp[nb % 2], b_ptp[nb % 2]
                nb += 1
                for j in range(zn):
                    ctx.pe.emit(lambda e, z0=z0, j=j, pb=pb, st=st: e.transpose(
                        pb[:, j * 128:(j + 1) * 128], st[:, (z0 + j) * 128:(z0 + j + 1) * 128],
                        consts.id_bf[:, :]),
                        reads=[b_st, consts.b_id_bf], writes=[bpb])
                ctx.dve.emit(lambda e, ci=ci, z0=z0, zn=zn, pb=pb: e.tensor_copy(
                    wcTr[:, z0:z0 + zn, ci * 128:(ci + 1) * 128], pb[:, 0:zn * 128]
                    .rearrange("p (z d) -> p z d", z=zn)),
                    reads=[bpb], writes=[b_wcT])
        def emit_rails(tc):
            hT, b_hT = hT2[tc % 2], b_hT2[tc % 2]
            for gl in range(ngc):
                g = (zc % (nzc // NH)) * ngc + gl
                gi = h * NG + g
                for s in range(4):
                    pzb, b_pzb = pzr[s % 2], b_pzr[s % 2]
                    ctx.pe.emit(lambda e, gi=gi, g=g, tc=tc, s=s, pzb=pzb: e.matmul(
                        pzb[:, :], cdiag[32 * s:32 * s + 32, gi * 128:(gi + 1) * 128],
                        xnT[32 * s:32 * s + 32, g * T + tc * 512: g * T + (tc + 1) * 512],
                        start=True, stop=True, tile_position=(32 * s, 0)),
                        reads=[b_cdiag, b_xnT], writes=[b_pzb])
                    ctx.act.emit(lambda e, gl=gl, s=s, pzb=pzb, hT=hT: e.activation(
                        out=hT[:, (gl * 4 + s) * 512:(gl * 4 + s + 1) * 512], in_=pzb[:, :],
                        func=AF.Gelu),
                        reads=[b_pzb], writes=[b_hT])

        emit_rails(0)
        for tc in range(ntc):
            if tc + 1 < ntc:
                emit_rails(tc + 1)
            hT, b_hT = hT2[tc % 2], b_hT2[tc % 2]
            for t4 in range(4):
                t = tc * 4 + t4
                for half in range(2):
                    pb, bpb = ppr[half], b_ppr[half]
                    fns = []
                    for zs in range(nsub):
                        fns.append(lambda e, zs=zs, t4=t4, half=half, pb=pb, hT=hT: e.matmul(
                            pb[:, :], hT[:, zs * 512 + t4 * 128: zs * 512 + (t4 + 1) * 128],
                            wcT[:, zs * D + half * 512: zs * D + (half + 1) * 512],
                            start=(zs == 0), stop=(zs == nsub - 1)))
                    ctx.pe.emit_multi(fns, reads=[b_hT, b_wcT], writes=[bpb])
                    ctx.dve.emit(lambda e, t=t, half=half, pb=pb: e.tensor_add(
                        acc[:, t * D + half * 512: t * D + (half + 1) * 512],
                        acc[:, t * D + half * 512: t * D + (half + 1) * 512], pb[:, :]),
                        reads=[bpb, b_acc], writes=[b_acc])

    for t in range(nt):
        ctx.dve.emit(lambda e, t=t: e.tensor_copy(ao[:, :], acc[:, t * D:(t + 1) * D]),
                     reads=[b_acc], writes=[b_ao])
        emit_finale(ctx, consts, T, t, ao[:, :], b_ao, xin, b_xin, xout, b_xout,
                    xt[t % 2], b_xt[t % 2], t1, b_t1, sttmp, b_sttmp)



PHASE_WEIGHTS = {
    0: ["L0_attn_Wq", "L0_attn_Wk", "L0_attn_Wv", "L0_attn_Wo", "L0_attn_bo"],
    1: ["L0_ffn_gate_w", "L0_ffn_w1", "L0_ffn_b1", "L0_ffn_w2", "L0_ffn_b2",
        "L0_ffn_w3", "L0_ffn_b3", "L0_ffn_sw1", "L0_ffn_sw2", "L0_ffn_sw3"],
    2: ["L1_attn_q", "L1_attn_k", "L1_attn_log_decay", "L1_attn_norm_weight", "L1_attn_o_w"],
    3: ["L1_ffn_rail_maker", "L1_ffn_rail_up_proj", "L1_ffn_w_cproj"],
}
_SHAPES = {
    "L0_attn_Wq": [1024, 1024], "L0_attn_Wk": [1024, 1024], "L0_attn_Wv": [1024, 1024],
    "L0_attn_Wo": [1024, 1024], "L0_attn_bo": [1024],
    "L0_ffn_gate_w": [8, 1024], "L0_ffn_w1": [8, 1024, 1024], "L0_ffn_b1": [8, 1024],
    "L0_ffn_w2": [8, 1024, 1024], "L0_ffn_b2": [8, 1024], "L0_ffn_w3": [8, 1024, 1024],
    "L0_ffn_b3": [8, 1024], "L0_ffn_sw1": [2048, 1024], "L0_ffn_sw2": [2048, 1024],
    "L0_ffn_sw3": [1024, 2048],
    "L1_attn_q": [1024], "L1_attn_k": [1024], "L1_attn_log_decay": [16],
    "L1_attn_norm_weight": [1024], "L1_attn_o_w": [1024, 1024],
    "L1_ffn_rail_maker": [4, 64, 16, 16], "L1_ffn_rail_up_proj": [4, 64, 64, 16],
    "L1_ffn_w_cproj": [1024, 16384],
}
ALL_WEIGHTS = [nm for p in range(4) for nm in PHASE_WEIGHTS[p]]


def build_fused(T):
    from contextlib import ExitStack
    nc = bass.Bass("TRN2", target_bir_lowering=False, debug=False)
    ctx = Ctx(nc)
    io = {}
    for nm in ALL_WEIGHTS:
        io[nm] = nc.dram_tensor(nm, _SHAPES[nm], F32, kind="ExternalInput").ap()
        io["b_" + nm] = ctx.buf(nm)
    x0 = nc.dram_tensor("x_in", [T, D], F32, kind="ExternalInput").ap()
    y = nc.dram_tensor("x_out", [T, D], F32, kind="ExternalOutput").ap()
    xmid = [nc.dram_tensor("xmid%d" % i, [T, D], F32, kind="Internal").ap() for i in range(3)]
    b_x = [ctx.buf("x%d" % i) for i in range(5)]
    chain = [x0, xmid[0], xmid[1], xmid[2], y]
    builders = [build_phase0, build_phase1, build_phase2, build_phase3]
    with ExitStack() as stack:
        consts = Consts(ctx, stack)
        for p in range(4):
            io["x_in"], io["b_x_in"] = chain[p], b_x[p]
            io["x_out"], io["b_x_out"] = chain[p + 1], b_x[p + 1]
            with ctx.sub() as pstack:
                builders[p](ctx, consts, pstack, T, io)
        ctx.finish()
        ctx.render_all()
    nc.finalize()
    return nc


def kernel(**inputs):
    from concourse.bass_utils import run_bass_kernel_spmd
    NCORES = 8
    T = 4096
    x = np.ascontiguousarray(np.asarray(inputs["L0_attn_x"], dtype=np.float32))
    nc = build_fused(T)
    wmap = {nm: np.ascontiguousarray(np.asarray(inputs[nm], dtype=np.float32))
            for nm in ALL_WEIGHTS}
    in_maps = []
    for i in range(NCORES):
        m = {"x_in": np.ascontiguousarray(x[i])}
        m.update(wmap)
        in_maps.append(m)
    res = run_bass_kernel_spmd(nc, in_maps, core_ids=list(range(NCORES)))
    return np.stack([np.asarray(res.results[i]["x_out"]) for i in range(NCORES)], axis=0)
